# version 03
# baseline (speedup 1.0000x reference)
import contextlib
import numpy as np
import ml_dtypes
import concourse.bass as bass
import concourse.mybir as mybir
from concourse.bass_utils import run_bass_kernel_spmd

F32 = mybir.dt.float32
BF16 = mybir.dt.bfloat16
I32 = mybir.dt.int32
U32 = mybir.dt.uint32
AF = mybir.ActivationFunctionType
ALU = mybir.AluOpType
AX = mybir.AxisListType
IOff = bass.IndirectOffsetOnAxis
BFNP = ml_dtypes.bfloat16

NCORE = 8
B_, S_, D = 4, 4096, 1024
T = 2048
NT = 16
NKC = 32
H, HD = 16, 64
QT = 256
NQT = T // QT
ENGINES = ("sync", "tensor", "vector", "scalar", "gpsimd")
PAIRS = [[0, 1], [2, 3], [4, 5], [6, 7]]
ALL8 = [list(range(8))]
GRP4 = [[0, 2, 4, 6], [1, 3, 5, 7]]


class Buf:
    __slots__ = ("name", "w", "r")

    def __init__(self, name):
        self.name = name
        self.w = None
        self.r = []


class Pr:
    def __init__(self, nc):
        self.nc = nc
        self.q = {e: [] for e in ENGINES}
        self.cnt = {e: 0 for e in ENGINES}
        self.bnd_regs = {}
        self.last_coll = None

    def _bound_reg(self, e, val):
        if val not in self.bnd_regs:
            self.bnd_regs[val] = e.to_reg(int(val))
        return self.bnd_regs[val]

    def op(self, eng, fn, reads=(), writes=(), inc=1, extra_deps=()):
        deps = list(extra_deps)
        for b in reads:
            if b.w is not None:
                deps.append(b.w)
        for b in writes:
            if b.w is not None:
                deps.append(b.w)
            deps.extend(b.r)
        dd = {}
        for (de, dv) in deps:
            if eng == "tensor" and de == "tensor":
                continue
            if dd.get(de, 0) < dv:
                dd[de] = dv
        self.cnt[eng] += inc
        tok = (eng, self.cnt[eng])
        self.q[eng].append((fn, tuple(dd.items()), inc))
        for b in writes:
            b.w = tok
            b.r = []
        for b in reads:
            b.r.append(tok)
        return tok

    def dma(self, eng, out, in_, reads=(), writes=(), **kw):
        try:
            if out.tensor.dtype != in_.tensor.dtype:
                eng = "gpsimd"
        except AttributeError:
            pass
        return self.op(eng, lambda e: e.dma_start(out=out, in_=in_, **kw),
                       reads=reads, writes=writes, inc=16)

    def mm(self, out, lhsT, rhs, start, stop, reads=(), writes=()):
        return self.op("tensor",
                       lambda e: e.matmul(out, lhsT, rhs, start=start, stop=stop),
                       reads=reads, writes=writes)

    def tr(self, out, in_, ident, reads=(), writes=()):
        return self.op("tensor", lambda e: e.transpose(out, in_, ident),
                       reads=reads, writes=writes)

    def act(self, out, in_, func, reads=(), writes=(), bias=0.0, scale=1.0):
        return self.op("scalar",
                       lambda e: e.activation(out, in_, func, bias=bias, scale=scale),
                       reads=reads, writes=writes)

    def tt(self, out, a, b2, alu, reads=(), writes=()):
        return self.op("vector", lambda e: e.tensor_tensor(out, a, b2, alu),
                       reads=reads, writes=writes)

    def ts(self, out, a, s1, s2, op0, op1=None, reads=(), writes=()):
        if s2 is None:
            return self.op("vector", lambda e: e.tensor_scalar(out, a, s1, None, op0),
                           reads=reads, writes=writes)
        return self.op("vector", lambda e: e.tensor_scalar(out, a, s1, s2, op0, op1),
                       reads=reads, writes=writes)

    def red(self, out, in_, alu, reads=(), writes=(), absval=None):
        return self.op("vector",
                       lambda e: e.tensor_reduce(out, in_, AX.X, alu,
                                                 apply_absolute_value=absval),
                       reads=reads, writes=writes)

    def copy(self, out, in_, reads=(), writes=()):
        return self.op("vector", lambda e: e.tensor_copy(out=out, in_=in_),
                       reads=reads, writes=writes)

    def memset(self, ap, v, writes=(), reads=()):
        return self.op("vector", lambda e: e.memset(ap, v), reads=reads, writes=writes)

    def coll(self, kind, alu, groups, in_ap, out_ap, reads=(), writes=()):
        extra = [self.last_coll] if self.last_coll is not None else []
        self.op("gpsimd", lambda e: e.collective_compute(
            kind, alu, replica_groups=groups, ins=[in_ap], outs=[out_ap]),
            reads=reads, writes=(), inc=1, extra_deps=extra)
        fence_in, fence_out = self.fence_bufs
        tok = self.op("gpsimd", lambda e: e.collective_compute(
            "AllReduce", ALU.add, replica_groups=groups,
            ins=[fence_in.ap()], outs=[fence_out.ap()]),
            reads=reads, writes=writes, inc=1)
        self.last_coll = tok
        return tok

    def igather(self, out, in_, idx_ap, bound, reads=(), writes=()):
        def f(e):
            return e.indirect_dma_start(
                out=out, out_offset=None, in_=in_, in_offset=IOff(ap=idx_ap, axis=0),
                bounds_check=self._bound_reg(e, bound), oob_is_err=False)
        return self.op("gpsimd", f, reads=reads, writes=writes, inc=16)

    def iscatter(self, out, idx_ap, in_, bound, reads=(), writes=()):
        def f(e):
            return e.indirect_dma_start(
                out=out, out_offset=IOff(ap=idx_ap, axis=0), in_=in_, in_offset=None,
                bounds_check=self._bound_reg(e, bound), oob_is_err=False)
        return self.op("gpsimd", f, reads=reads, writes=writes, inc=16)

    def emit(self, block, sems):
        def mk(eng_name):
            def run(e):
                wm = {f: 0 for f in ENGINES}
                for _qi, (fn, deps, inc) in enumerate(self.q[eng_name]):
                    for (de, dv) in deps:
                        if dv > wm[de]:
                            e.wait_ge(sems[de], dv)
                            wm[de] = dv
                    try:
                        fn(e).then_inc(sems[eng_name], inc)
                    except Exception:
                        print("EMIT FAIL engine", eng_name, "queue index", _qi,
                              "of", len(self.q[eng_name]))
                        raise
            return run
        block.sync(mk("sync"))
        block.tensor(mk("tensor"))
        block.vector(mk("vector"))
        block.scalar(mk("scalar"))
        block.gpsimd(mk("gpsimd"))


class Ctx:
    def __init__(self, nc, es, prog):
        self.nc = nc
        self.es = es
        self.p = prog
        self.t = {}
        self.b = {}

    def sb(self, name, shape, dtype):
        self.t[name] = self.es.enter_context(self.nc.sbuf_tensor(name, shape, dtype))
        self.b[name] = Buf(name)
        return self.t[name]

    def ps(self, name, shape, dtype=F32):
        self.t[name] = self.es.enter_context(self.nc.psum_tensor(name, shape, dtype))
        self.b[name] = Buf(name)
        return self.t[name]

    def dram(self, name, shape, dtype, kind=None):
        if kind:
            self.t[name] = self.nc.dram_tensor(name, shape, dtype, kind=kind)
        else:
            self.t[name] = self.nc.dram_tensor(name, shape, dtype)
        self.b[name] = Buf(name)
        return self.t[name]

    def tb(self, name):
        return self.t[name], self.b[name]



def emit_prelude(c, p):
    t, b = c.t, c.b
    p.memset(t["cst"][:, 0:1], 1e-5, writes=(b["cst"],))
    p.dma("sync", t["ident"][:, :], t["ident_in"][:, :], reads=(b["ident_in"],), writes=(b["ident"],))
    p.dma("sync", t["cum"][:, :], t["cum_in"][:, :], reads=(b["cum_in"],), writes=(b["cum"],))
    p.dma("gpsimd", t["identf"][:, :], t["ident_in"][:, :], reads=(b["ident_in"],), writes=(b["identf"],))
    p.memset(t["onesr"][:, :], 1.0, writes=(b["onesr"],))
    p.memset(t["onesrb"][:, :], 1.0, writes=(b["onesrb"],))
    p.memset(t["onesc"][:, :], 1.0, writes=(b["onesc"],))
    p.memset(t["onescb"][:, :], 1.0, writes=(b["onescb"],))


def emit_ln_phase(c, p):
    t, b = c.t, c.b
    for tt in range(NT):
        r = slice(tt * 128, (tt + 1) * 128)
        xa, xaB = t["a1"], b["a1"]
        p.dma("sync", xa[:, 0:D], t["x_in"][r, :], reads=(b["x_in"],), writes=(xaB,))
        emit_ln_stats(c, p, xa[:, 0:D], (xaB,), D, 1e-5)
        st, stB = t["stat"], b["stat"]
        xn, xnB = t["a2"], b["a2"]
        p.ts(xn[:, 0:D], xa[:, 0:D], st[:, 0:1], st[:, 3:4], ALU.subtract, ALU.mult,
             reads=(xaB, stB), writes=(xnB,))
        p.dma("gpsimd", t["xn32_d"][r, :], xn[:, 0:D], reads=(xnB,), writes=(b["xn32_d"],))
        xbb, xbbB = t["w16a"], b["w16a"]
        p.copy(xbb[:, 0:D], xn[:, 0:D], reads=(xnB,), writes=(xbbB,))
        p.dma("gpsimd", t["xn_tok_d"][r, :], xbb[:, 0:D], reads=(xbbB,), writes=(b["xn_tok_d"],))
        for ch in range(8):
            pt, ptB = t["psT"], b["psT"]
            p.tr(pt[:, 0:128], xbb[:, ch * 128:(ch + 1) * 128], t["ident"][:, :],
                 reads=(xbbB, b["ident"]), writes=(ptB,))
            o2, o2B = t["w16b"], b["w16b"]
            p.copy(o2[:, ch * 128:(ch + 1) * 128], pt[:, 0:128], reads=(ptB,), writes=(o2B,))
        for ch in range(8):
            p.dma("gpsimd", t["xn_fm_d"][ch * 128:(ch + 1) * 128, tt * 128:(tt + 1) * 128],
                  t["w16b"][:, ch * 128:(ch + 1) * 128], reads=(b["w16b"],), writes=(b["xn_fm_d"],))
    p.dma("gpsimd", t["xn_fm_d"][1024:1025, :], t["onesrb"][0:1, 0:T],
          reads=(b["onesrb"],), writes=(b["xn_fm_d"],))


def emit_ln_stats(c, p, src_ap, src_reads, n, eps):
    t, b = c.t, c.b
    st, stB = t["stat"], b["stat"]
    sq, sqB = t["lnsq"], b["lnsq"]
    p.red(st[:, 0:1], src_ap, ALU.add, reads=src_reads, writes=(stB,))
    p.ts(st[:, 0:1], st[:, 0:1], 1.0 / n, None, ALU.mult, reads=(stB,), writes=(stB,))
    p.act(sq[:, 0:n], src_ap, AF.Square, reads=src_reads, writes=(sqB,))
    p.red(st[:, 1:2], sq[:, 0:n], ALU.add, reads=(sqB,), writes=(stB,))
    p.ts(st[:, 1:2], st[:, 1:2], 1.0 / n, None, ALU.mult, reads=(stB,), writes=(stB,))
    p.act(sq[:, 0:1], st[:, 0:1], AF.Square, reads=(stB,), writes=(sqB,))
    p.tt(st[:, 1:2], st[:, 1:2], sq[:, 0:1], ALU.subtract, reads=(sqB, stB), writes=(stB,))
    p.act(st[:, 2:3], st[:, 1:2], AF.Sqrt, bias=float(eps), reads=(stB, c.b["cst"]), writes=(stB,))
    p.op("vector", lambda e, st=st: e.reciprocal(st[:, 3:4], st[:, 2:3]),
         reads=(stB,), writes=(stB,))


def load_wchunks(c, p, dst, dstB, src_name, rows, cols, dst_col0=0):
    t, b = c.t, c.b
    src, srcB = t[src_name], b[src_name]
    nch = (rows + 127) // 128
    for i in range(nch):
        r0, r1 = i * 128, min((i + 1) * 128, rows)
        p.dma("gpsimd", dst[0:r1 - r0, dst_col0 + i * cols: dst_col0 + i * cols + cols],
              src[r0:r1, :], reads=(srcB,), writes=(dstB,))


def fm_ld(c, p, dst, dstB, ch, c0, c1, src="xn_fm_d"):
    t, b = c.t, c.b
    rows = 1 if ch == 8 else 128
    p.dma("sync", dst[0:rows, 0:c1 - c0], t[src][ch * 128:ch * 128 + rows, c0:c1],
          reads=(b[src],), writes=(dstB,))



def emit_zn_phase(c, p):
    t, b = c.t, c.b
    for name, dst_lo in (("a3_lns", 0), ("a3_lnb", 1024)):
        pass
    gb, gbB = t["w16c"], b["w16c"]
    rsb, rsbB = t["lnsq"], b["lnsq"]
    p.dma("sync", rsb[0:1, 0:1024], t["l3_lns"][0:1, :], reads=(b["l3_lns"],), writes=(rsbB,))
    p.dma("sync", rsb[0:1, 1024:2048], t["l3_lnb"][0:1, :], reads=(b["l3_lnb"],), writes=(rsbB,))
    gbc, gbcB = t["zlng"], b["zlng"]
    for j in range(4):
        ps, psB = t["psB"], b["psB"]
        p.mm(ps[:, 0:512], t["onesr"][0:1, 0:128], rsb[0:1, j * 512:(j + 1) * 512],
             start=True, stop=True, reads=(rsbB, b["onesr"]), writes=(psB,))
        p.copy(gbc[:, j * 512:(j + 1) * 512], ps[:, 0:512], reads=(psB,), writes=(gbcB,))
    for tt in range(NT):
        r = slice(tt * 128, (tt + 1) * 128)
        xa, xaB = t["a1"], b["a1"]
        p.dma("sync", xa[:, 0:D], t["xn32_d"][r, :], reads=(b["xn32_d"],), writes=(xaB,))
        emit_ln_stats(c, p, xa[:, 0:D], (xaB,), D, 1e-5)
        st, stB = t["stat"], b["stat"]
        zn, znB = t["a2"], b["a2"]
        p.ts(zn[:, 0:D], xa[:, 0:D], st[:, 0:1], st[:, 3:4], ALU.subtract, ALU.mult,
             reads=(xaB, stB), writes=(znB,))
        p.tt(zn[:, 0:D], zn[:, 0:D], gbc[:, 0:1024], ALU.mult, reads=(gbcB,), writes=(znB,))
        p.tt(zn[:, 0:D], zn[:, 0:D], gbc[:, 1024:2048], ALU.add, reads=(gbcB,), writes=(znB,))
        xbb, xbbB = t["w16a"], b["w16a"]
        p.copy(xbb[:, 0:D], zn[:, 0:D], reads=(znB,), writes=(xbbB,))
        for ch in range(8):
            pt, ptB = t["psT"], b["psT"]
            p.tr(pt[:, 0:128], xbb[:, ch * 128:(ch + 1) * 128], t["ident"][:, :],
                 reads=(xbbB, b["ident"]), writes=(ptB,))
            o2, o2B = t["w16b"], b["w16b"]
            p.copy(o2[:, ch * 128:(ch + 1) * 128], pt[:, 0:128], reads=(ptB,), writes=(o2B,))
        for ch in range(8):
            p.dma("gpsimd", t["zn_fm_d"][ch * 128:(ch + 1) * 128, tt * 128:(tt + 1) * 128],
                  t["w16b"][:, ch * 128:(ch + 1) * 128], reads=(b["w16b"],), writes=(b["zn_fm_d"],))
    p.dma("gpsimd", t["zn_fm_d"][1024:1025, :], t["onesrb"][0:1, 0:T],
          reads=(b["onesrb"],), writes=(b["zn_fm_d"],))


def emit_attn(c, p, variant):
    t, b = c.t, c.b
    KR = 66 if variant == "l1" else 64
    if variant == "l3":
        emit_zn_phase(c, p)

    _attn_qkv_produce(c, p, variant)

    p.coll("AllGather", ALU.bypass, PAIRS, t["kx_d"][0:H * KR, :], t["kg_d"][0:2 * H * KR, :],
           reads=(b["kx_d"],), writes=(b["kg_d"],))
    p.coll("AllGather", ALU.bypass, PAIRS, t["vx_d"].ap(), t["vg_d"].ap(),
           reads=(b["vx_d"],), writes=(b["vg_d"],))

    if DEBUG and "dbg_qx" in t:
        p.dma("gpsimd", t["dbg_vx"][:, :], t["vx_d"].ap(), reads=(b["vx_d"],), writes=(b["dbg_vx"],))
        p.dma("gpsimd", t["dbg_kx"][:, :], t["kx_d"].ap(), reads=(b["kx_d"],), writes=(b["dbg_kx"],))
        p.dma("gpsimd", t["dbg_qx"][:, :], t["qx_d"].ap(), reads=(b["qx_d"],), writes=(b["dbg_qx"],))
        p.dma("gpsimd", t["dbg_kg"][:, :], t["kg_d"].ap(), reads=(b["kg_d"],), writes=(b["dbg_kg"],))
        p.dma("gpsimd", t["dbg_vg"][:, :], t["vg_d"].ap(), reads=(b["vg_d"],), writes=(b["dbg_vg"],))
    if variant == "l1":
        p.memset(t["bb"][:, 0:H], 0.0, writes=(b["bb"],))
    else:
        _attn_bounds(c, p, KR)

    _attn_core(c, p, variant, KR)
    _attn_epilogue(c, p, variant)


def _attn_qkv_produce(c, p, variant):
    if variant in ("l0", "l3"):
        _qkv_fm_direct(c, p, variant)
    else:
        _qkv_tokmajor(c, p, variant)
    _v_produce(c, p, variant)


def _qkv_fm_direct(c, p, variant):
    t, b = c.t, c.b
    scale_q = 0.125
    has_bias = (variant == "l0")
    if has_bias:
        for i, nm in enumerate(("bq_in", "bk_in")):
            for ch in range(8):
                p.dma("sync", t["bqk"][:, i * 8 + ch:i * 8 + ch + 1],
                      t[nm][ch * 128:(ch + 1) * 128, :], reads=(b[nm],), writes=(b["bqk"],))
        p.ts(t["bqk"][:, 0:8], t["bqk"][:, 0:8], scale_q, None, ALU.mult,
             reads=(b["bqk"],), writes=(b["bqk"],))
    fmsrc = "zn_fm_d" if variant == "l3" else "xn_fm_d"
    for which, wname, dram_out, scl in (("q", "wq_in", "qx_d", scale_q),
                                        ("k", "wk_in", "kx_d", 1.0)):
        wsb, wsbB = t["wk1"], b["wk1"]
        load_wchunks(c, p, wsb, wsbB, wname, 1024, 1024)
        for ho in range(8):
            stg, stgB = t["w16a"], b["w16a"]
            for jt in range(T // 512):
                ps, psB = t["psB"], b["psB"]
                for ch in range(8):
                    xf, xfB = t["xf"], b["xf"]
                    fm_ld(c, p, xf, xfB, ch, jt * 512, (jt + 1) * 512, src=fmsrc)
                    p.mm(ps[:, 0:512], wsb[:, ch * 1024 + ho * 128: ch * 1024 + (ho + 1) * 128],
                         xf[0:128, 0:512], start=(ch == 0), stop=(ch == 7),
                         reads=(wsbB, xfB), writes=(psB,))
                bias_ap = t["bqk"][:, (0 if which == "q" else 8) + ho:
                                   (0 if which == "q" else 8) + ho + 1] if has_bias else 0.0
                extra = (b["bqk"],) if has_bias else ()
                p.act(stg[:, jt * 512:(jt + 1) * 512], ps[:, 0:512], AF.Identity,
                      bias=bias_ap, scale=(scl if which == "q" else 1.0),
                      reads=(psB,) + extra, writes=(stgB,))
            h0, h1 = 2 * ho, 2 * ho + 1
            p.dma("gpsimd", t[dram_out][h0 * 64:h0 * 64 + 64, :], stg[0:64, 0:T],
                  reads=(stgB,), writes=(b[dram_out],))
            p.dma("gpsimd", t[dram_out][h1 * 64:h1 * 64 + 64, :], stg[64:128, 0:T],
                  reads=(stgB,), writes=(b[dram_out],))


def _qkv_tokmajor(c, p, variant):
    t, b = c.t, c.b
    KR = 66 if variant == "l1" else 64
    if variant == "l2":
        rsb, rsbB = t["lnsq"], b["lnsq"]
        p.dma("sync", rsb[0:1, 0:16], t["l2_temp"][0:1, :], reads=(b["l2_temp"],), writes=(rsbB,))
        p.op("vector", lambda e, rsb=rsb: e.reciprocal(rsb[0:1, 16:32], rsb[0:1, 0:16]),
             reads=(rsbB,), writes=(rsbB,))
        ps, psB = t["psB"], b["psB"]
        p.mm(ps[:, 0:16], t["onesr"][0:1, 0:128], rsb[0:1, 16:32], start=True, stop=True,
             reads=(rsbB, b["onesr"]), writes=(psB,))
        tb_, tbB = t["tempb"], b["tempb"]
        p.copy(tb_[:, 0:16], ps[:, 0:16], reads=(psB,), writes=(tbB,))
    for which, wname, dram_out in (("q", "wq_in", "qx_d"), ("k", "wk_in", "kx_d")):
        wsb, wsbB = t["wk1"], b["wk1"]
        rows = 1025 if variant == "l1" else 1024
        load_wchunks(c, p, wsb, wsbB, wname, rows, 1024)
        for tt in range(NT):
            qt_, qtB = t["w16a"], b["w16a"]
            for j2 in range(2):
                ps, psB = t["psB"], b["psB"]
                for ch in range(8):
                    xf, xfB = t["xf"], b["xf"]
                    fm_ld(c, p, xf, xfB, ch, tt * 128, (tt + 1) * 128)
                    p.mm(ps[:, 0:512], xf[0:128, 0:128],
                         wsb[:, ch * 1024 + j2 * 512: ch * 1024 + (j2 + 1) * 512],
                         start=(ch == 0), stop=(ch == 7 and variant != "l1"),
                         reads=(wsbB, xfB), writes=(psB,))
                if variant == "l1":
                    xf, xfB = t["xf"], b["xf"]
                    fm_ld(c, p, xf, xfB, 8, tt * 128, (tt + 1) * 128)
                    p.mm(ps[:, 0:512], xf[0:1, 0:128],
                         wsb[0:1, 8 * 1024 + j2 * 512: 8 * 1024 + (j2 + 1) * 512],
                         start=False, stop=True, reads=(wsbB, xfB), writes=(psB,))
                p.copy(qt_[:, j2 * 512:(j2 + 1) * 512], ps[:, 0:512], reads=(psB,), writes=(qtB,))
            if variant == "l1":
                sq, sqB = t["lnsq"], b["lnsq"]
                p.act(sq[:, 0:1024], qt_[:, 0:1024], AF.Square, reads=(qtB,), writes=(sqB,))
                q2, q2B = t["a3"], b["a3"]
                for h in range(H):
                    p.red(q2[:, h:h + 1], sq[:, h * 64:(h + 1) * 64], ALU.add,
                          reads=(sqB,), writes=(q2B,))
                pt, ptB = t["psB"], b["psB"]
                p.tr(pt[0:16, 0:128], q2[:, 0:16], t["identf"][:, :],
                     reads=(q2B, b["identf"]), writes=(ptB,))
                p.copy(t["q2row"][0:16, tt * 128:(tt + 1) * 128], pt[0:16, 0:128],
                       reads=(ptB,), writes=(b["q2row"],))
                if which == "q":
                    p.ts(qt_[:, 0:1024], qt_[:, 0:1024], -2.0, None, ALU.mult,
                         reads=(qtB,), writes=(qtB,))
            else:
                sq, sqB = t["lnsq"], b["lnsq"]
                p.act(sq[:, 0:1024], qt_[:, 0:1024], AF.Square, reads=(qtB,), writes=(sqB,))
                st_, stB = t["stat"], b["stat"]
                p.red(st_[:, 0:1], sq[:, 0:1024], ALU.add, reads=(sqB,), writes=(stB,))
                p.act(st_[:, 1:2], st_[:, 0:1], AF.Sqrt, reads=(stB,), writes=(stB,))
                p.ts(st_[:, 1:2], st_[:, 1:2], 1e-12, None, ALU.max, reads=(stB,), writes=(stB,))
                p.op("vector", lambda e, st_=st_: e.reciprocal(st_[:, 2:3], st_[:, 1:2]),
                     reads=(stB,), writes=(stB,))
                p.ts(st_[:, 2:3], st_[:, 2:3], 0.125, None, ALU.mult, reads=(stB,), writes=(stB,))
                p.ts(qt_[:, 0:1024], qt_[:, 0:1024], st_[:, 2:3], None, ALU.mult,
                     reads=(stB,), writes=(qtB,))
                if which == "q":
                    for h in range(H):
                        p.ts(qt_[:, h * 64:(h + 1) * 64], qt_[:, h * 64:(h + 1) * 64],
                             t["tempb"][:, h:h + 1], None, ALU.mult,
                             reads=(b["tempb"],), writes=(qtB,))
            for ch in range(8):
                pt, ptB = t["psT"], b["psT"]
                p.tr(pt[:, 0:128], qt_[:, ch * 128:(ch + 1) * 128], t["ident"][:, :],
                     reads=(qtB, b["ident"]), writes=(ptB,))
                o2, o2B = t["w16b"], b["w16b"]
                p.copy(o2[:, ch * 128:(ch + 1) * 128], pt[:, 0:128], reads=(ptB,), writes=(o2B,))
            for ch in range(8):
                h0, h1 = 2 * ch, 2 * ch + 1
                p.dma("gpsimd", t[dram_out][h0 * KR:h0 * KR + 64, tt * 128:(tt + 1) * 128],
                      t["w16b"][0:64, ch * 128:(ch + 1) * 128], reads=(b["w16b"],), writes=(b[dram_out],))
                p.dma("gpsimd", t[dram_out][h1 * KR:h1 * KR + 64, tt * 128:(tt + 1) * 128],
                      t["w16b"][64:128, ch * 128:(ch + 1) * 128], reads=(b["w16b"],), writes=(b[dram_out],))
        if variant == "l1":
            sqrow = 64 if which == "q" else 65
            onerow = 65 if which == "q" else 64
            p.dma("gpsimd",
                  t[dram_out].ap().rearrange("(h r) tk -> h r tk", r=KR)[:, sqrow:sqrow + 1, :],
                  t["q2row"][0:16, 0:T], reads=(b["q2row"],), writes=(b[dram_out],))
            for h in range(H):
                p.dma("gpsimd", t[dram_out][h * KR + onerow:h * KR + onerow + 1, :],
                      t["onesrb"][0:1, 0:T], reads=(b["onesrb"],), writes=(b[dram_out],))


def _v_produce(c, p, variant):
    t, b = c.t, c.b
    wsb, wsbB = t["wk1"], b["wk1"]
    has_bias = variant in ("l0", "l1")
    rows = 1025 if has_bias else 1024
    load_wchunks(c, p, wsb, wsbB, "wv_in", rows, 1024)
    fmsrc = "zn_fm_d" if variant == "l3" else "xn_fm_d"
    for tt in range(NT):
        stg, stgB = t["w16a"], b["w16a"]
        p.memset(stg[:, 0:1040], 1.0, writes=(stgB,))
        for j2 in range(2):
            ps, psB = t["psB"], b["psB"]
            for ch in range(8):
                xf, xfB = t["xf"], b["xf"]
                fm_ld(c, p, xf, xfB, ch, tt * 128, (tt + 1) * 128, src=fmsrc)
                p.mm(ps[:, 0:512], xf[0:128, 0:128], wsb[:, ch * 1024 + j2 * 512: ch * 1024 + (j2 + 1) * 512],
                     start=(ch == 0), stop=(ch == 7 and not has_bias),
                     reads=(wsbB, xfB), writes=(psB,))
            if has_bias:
                xf, xfB = t["xf"], b["xf"]
                fm_ld(c, p, xf, xfB, 8, tt * 128, (tt + 1) * 128, src=fmsrc)
                p.mm(ps[:, 0:512], xf[0:1, 0:128],
                     wsb[0:1, 8 * 1024 + j2 * 512: 8 * 1024 + (j2 + 1) * 512],
                     start=False, stop=True, reads=(wsbB, xfB), writes=(psB,))
            for hh in range(8):
                hgl = j2 * 8 + hh
                p.copy(stg[:, hgl * 65:hgl * 65 + 64], ps[:, hh * 64:(hh + 1) * 64],
                       reads=(psB,), writes=(stgB,))
        p.dma("gpsimd", t["vx_d"][tt * 128:(tt + 1) * 128, :], stg[:, 0:1040],
              reads=(stgB,), writes=(b["vx_d"],))


def _attn_bounds(c, p, KR):
    t, b = c.t, c.b
    row, rowB = t["row4"], b["row4"]
    for side, dram, nn, ncols in (("q", "qx_d", 0, T), ("k", "kg_d", 16, 2 * T)):
        for h in range(H):
            kf, kfB = t["kf"], b["kf"]
            if side == "q":
                p.dma("sync", kf[0:64, 0:T], t[dram][h * 64:h * 64 + 64, :],
                      reads=(b[dram],), writes=(kfB,))
            else:
                for r in range(2):
                    p.dma("sync", kf[0:64, r * T:(r + 1) * T],
                          t[dram][r * H * KR + h * KR: r * H * KR + h * KR + 64, :],
                          reads=(b[dram],), writes=(kfB,))
            sq, sqB = t["w16a"], b["w16a"]
            p.act(sq[0:64, 0:ncols], kf[0:64, 0:ncols], AF.Square, reads=(kfB,), writes=(sqB,))
            ps, psB = t["psS"], b["psS"]
            for half in range(ncols // 2048):
                for j in range(4):
                    p.mm(ps[0:1, j * 512:(j + 1) * 512], t["onescb"][0:64, 0:1],
                         sq[0:64, half * 2048 + j * 512: half * 2048 + (j + 1) * 512],
                         start=True, stop=True, reads=(sqB, b["onescb"]), writes=(psB,))
                if half == 0:
                    p.red(row[0:1, nn + h:nn + h + 1], ps[0:1, 0:2048], ALU.max,
                          reads=(psB,), writes=(rowB,))
                else:
                    p.red(row[0:1, 32 + h:32 + h + 1], ps[0:1, 0:2048], ALU.max,
                          reads=(psB,), writes=(rowB,))
                    p.tt(row[0:1, nn + h:nn + h + 1], row[0:1, nn + h:nn + h + 1],
                         row[0:1, 32 + h:32 + h + 1], ALU.max, reads=(rowB,), writes=(rowB,))
    p.tt(row[0:1, 32:48], row[0:1, 0:16], row[0:1, 16:32], ALU.mult, reads=(rowB,), writes=(rowB,))
    p.act(row[0:1, 32:48], row[0:1, 32:48], AF.Sqrt, reads=(rowB,), writes=(rowB,))
    p.ts(row[0:1, 32:48], row[0:1, 32:48], -1.0, None, ALU.mult, reads=(rowB,), writes=(rowB,))
    ps, psB = t["psB"], b["psB"]
    p.mm(ps[:, 0:H], t["onesr"][0:1, 0:128], row[0:1, 32:48], start=True, stop=True,
         reads=(rowB, b["onesr"]), writes=(psB,))
    p.copy(t["bb"][:, 0:H], ps[:, 0:H], reads=(psB,), writes=(b["bb"],))


def _attn_core(c, p, variant, KR):
    t, b = c.t, c.b
    for h in range(H):
        kf, kfB = t["kf"], b["kf"]
        for r in range(2):
            p.dma("sync", kf[0:KR, r * T:(r + 1) * T],
                  t["kg_d"][r * H * KR + h * KR: r * H * KR + (h + 1) * KR, :],
                  reads=(b["kg_d"],), writes=(kfB,))
        vh, vhB = t["vh"], b["vh"]
        for kc in range(NKC):
            p.dma("sync", vh[:, kc * 65:(kc + 1) * 65],
                  t["vg_d"][kc * 128:(kc + 1) * 128, h * 65:(h + 1) * 65],
                  reads=(b["vg_d"],), writes=(vhB,))
        qh, qhB = t["qh"], b["qh"]
        p.dma("sync", qh[0:KR, 0:T], t["qx_d"][h * KR:(h + 1) * KR, :],
              reads=(b["qx_d"],), writes=(qhB,))
        for qt in range(NQT):
            pb, pbB = t["pb"], b["pb"]
            for kc in range(NKC):
                ps, psB = t["psS"], b["psS"]
                bank = (kc % 4) * 512
                p.mm(ps[:, bank:bank + QT], kf[0:KR, kc * 128:(kc + 1) * 128],
                     qh[0:KR, qt * QT:(qt + 1) * QT], start=True, stop=True,
                     reads=(kfB, qhB), writes=(psB,))
                if variant == "l1":
                    u, uB = t["a3"], b["a3"]
                    p.act(u[:, 0:QT], ps[:, bank:bank + QT], AF.Abs, reads=(psB,), writes=(uB,))
                    p.act(u[:, QT:2 * QT], u[:, 0:QT], AF.Sqrt, reads=(uB,), writes=(uB,))
                    p.act(pb[:, kc * QT:(kc + 1) * QT], u[:, QT:2 * QT], AF.Exp, scale=-1.0,
                          reads=(uB,), writes=(pbB,))
                else:
                    p.act(pb[:, kc * QT:(kc + 1) * QT], ps[:, bank:bank + QT], AF.Exp,
                          bias=t["bb"][:, h:h + 1], reads=(psB, b["bb"]), writes=(pbB,))
            po, poB = t["psB"], b["psB"]
            for kc in range(NKC):
                p.mm(po[0:65, 0:QT], vh[:, kc * 65:(kc + 1) * 65],
                     pb[:, kc * QT:(kc + 1) * QT], start=(kc == 0), stop=(kc == NKC - 1),
                     reads=(vhB, pbB), writes=(poB,))
            hp, half = h // 2, h % 2
            ost, ostB = t["ostg"], b["ostg"]
            p.act(ost[0:64, 0:QT], po[0:64, 0:QT], AF.Copy, reads=(poB,), writes=(ostB,))
            rs, rsB = t["rsum"], b["rsum"]
            p.op("vector", lambda e, rs=rs, po=po: e.reciprocal(rs[0:1, 0:QT], po[64:65, 0:QT]),
                 reads=(poB,), writes=(rsB,))
            bc, bcB = t["psS"], b["psS"]
            p.mm(bc[0:64, 0:QT], t["onesr"][0:1, 0:64], rs[0:1, 0:QT],
                 start=True, stop=True, reads=(rsB, b["onesr"]), writes=(bcB,))
            p.tt(ost[0:64, 0:QT], ost[0:64, 0:QT], bc[0:64, 0:QT], ALU.mult,
                 reads=(bcB,), writes=(ostB,))
            p.dma("gpsimd", t["ofm_d"][h * 64:(h + 1) * 64, qt * QT:(qt + 1) * QT],
                  ost[0:64, 0:QT], reads=(ostB,), writes=(b["ofm_d"],))


def _attn_epilogue(c, p, variant):
    t, b = c.t, c.b
    wsb, wsbB = t["wk1"], b["wk1"]
    load_wchunks(c, p, wsb, wsbB, "wo_in", 1025, 1024)
    for tt in range(NT):
        so, soB = t["a1"], b["a1"]
        if variant == "l3":
            p.dma("sync", so[:, 0:D], t["xn32_d"][tt * 128:(tt + 1) * 128, :],
                  reads=(b["xn32_d"],), writes=(soB,))
        ofc, ofcB = t["w16a"], b["w16a"]
        for ch in range(8):
            p.dma("sync", ofc[:, ch * 128:(ch + 1) * 128],
                  t["ofm_d"][ch * 128:(ch + 1) * 128, tt * 128:(tt + 1) * 128],
                  reads=(b["ofm_d"],), writes=(ofcB,))
        for j2 in range(2):
            ps, psB = t["psS"], b["psS"]
            for ch in range(8):
                p.mm(ps[:, 0:512], ofc[:, ch * 128:(ch + 1) * 128],
                     wsb[:, ch * 1024 + j2 * 512: ch * 1024 + (j2 + 1) * 512],
                     start=(ch == 0), stop=False, reads=(ofcB, wsbB), writes=(psB,))
            p.mm(ps[:, 0:512], t["onesrb"][0:1, 0:128],
                 wsb[0:1, 8 * 1024 + j2 * 512: 8 * 1024 + (j2 + 1) * 512],
                 start=False, stop=True, reads=(b["onesrb"], wsbB), writes=(psB,))
            if variant == "l3":
                p.tt(so[:, j2 * 512:(j2 + 1) * 512], so[:, j2 * 512:(j2 + 1) * 512],
                     ps[:, 0:512], ALU.add, reads=(psB,), writes=(soB,))
            else:
                p.copy(so[:, j2 * 512:(j2 + 1) * 512], ps[:, 0:512], reads=(psB,), writes=(soB,))
        p.dma("gpsimd", t["s_d"][tt * 128:(tt + 1) * 128, :], so[:, 0:D],
              reads=(soB,), writes=(b["s_d"],))



def emit_ffn0(c, p):
    t, b = c.t, c.b
    CAP = 4096
    wsb, wsbB = t["wk1"], b["wk1"]
    load_wchunks(c, p, wsb, wsbB, "f0_wp_aug", 1025, 1024)
    for tt in range(NT):
        stg, stgB = t["w16a"], b["w16a"]
        for j2 in range(2):
            ps, psB = t["psB"], b["psB"]
            for ch in range(9):
                xf, xfB = t["xf"], b["xf"]
                fm_ld(c, p, xf, xfB, ch, tt * 128, (tt + 1) * 128)
                rr = 1 if ch == 8 else 128
                p.mm(ps[:, 0:512], xf[0:rr, 0:128],
                     wsb[0:rr, ch * 1024 + j2 * 512: ch * 1024 + (j2 + 1) * 512],
                     start=(ch == 0), stop=(ch == 8), reads=(wsbB, xfB), writes=(psB,))
            p.copy(stg[:, j2 * 512:(j2 + 1) * 512], ps[:, 0:512], reads=(psB,), writes=(stgB,))
        p.dma("gpsimd", t["xflat_d"][tt * 128:(tt + 1) * 128, :], stg[:, 0:D],
              reads=(stgB,), writes=(b["xflat_d"],))
    p.coll("AllGather", ALU.bypass, ALL8, t["xflat_d"].ap(), t["xflatall_d"].ap(),
           reads=(b["xflat_d"],), writes=(b["xflatall_d"],))

    gsb, gsbB = t["gf32"], b["gf32"]
    wg, wgB = t["wg32"], b["wg32"]
    for ch in range(8):
        p.dma("sync", wg[:, ch * 8:(ch + 1) * 8], t["f0_wg"][ch * 128:(ch + 1) * 128, :],
              reads=(b["f0_wg"],), writes=(wgB,))
        p.dma("sync", wg[:, 64 + ch:64 + ch + 1], t["f0_bp"][ch * 128:(ch + 1) * 128, :],
              reads=(b["f0_bp"],), writes=(wgB,))
    ps, psB = t["psB"], b["psB"]
    p.memset(gsb[:, 0:80], 0.0, writes=(gsbB,))
    for jch in range(8):
        wpt, wptB = t["wf32s"], b["wf32s"]
        p.dma("sync", wpt[:, 0:1024], t["f0_wpT"][jch * 128:(jch + 1) * 128, :],
              reads=(b["f0_wpT"],), writes=(wptB,))
        for ich in range(8):
            p.mm(ps[:, ich * 8:(ich + 1) * 8], wpt[:, ich * 128:(ich + 1) * 128],
                 wg[:, jch * 8:(jch + 1) * 8], start=True, stop=True,
                 reads=(wptB, wgB), writes=(psB,))
        p.mm(ps[0:1, 64:72], wg[:, 64 + jch:64 + jch + 1], wg[:, jch * 8:(jch + 1) * 8],
             start=True, stop=True, reads=(wgB,), writes=(psB,))
        p.tt(gsb[:, 0:64], gsb[:, 0:64], ps[:, 0:64], ALU.add, reads=(psB,), writes=(gsbB,))
        p.tt(gsb[0:1, 64:72], gsb[0:1, 64:72], ps[0:1, 64:72], ALU.add,
             reads=(psB,), writes=(gsbB,))
    if DEBUG:
        p.dma("gpsimd", t["dbg_G"][:, :], gsb[:, 0:80], reads=(gsbB,), writes=(b["dbg_G"],))
    lg, lgB = t["psS"], b["psS"]
    lgs, lgsB = t["lnsq"], b["lnsq"]
    p.memset(lgs[0:8, 0:T], 0.0, writes=(lgsB,))
    for ich in range(8):
        strip, stripB = t["a2"], b["a2"]
        for tt in range(NT):
            xa, xaB = t["a1"], b["a1"]
            p.dma("sync", xa[:, 0:128], t["xn32_d"][tt * 128:(tt + 1) * 128,
                                                    ich * 128:(ich + 1) * 128],
                  reads=(b["xn32_d"],), writes=(xaB,))
            pt, ptB = t["psB"], b["psB"]
            p.tr(pt[:, 0:128], xa[:, 0:128], t["identf"][:, :],
                 reads=(xaB, b["identf"]), writes=(ptB,))
            p.copy(strip[:, tt * 128:(tt + 1) * 128], pt[:, 0:128], reads=(ptB,), writes=(stripB,))
        if DEBUG and ich == 7:
            p.dma("gpsimd", t["dbg_strip"][:, :], strip[:, 0:2048],
                  reads=(stripB,), writes=(b["dbg_strip"],))
        for jt in range(4):
            p.mm(lg[0:8, jt * 512:(jt + 1) * 512], gsb[:, ich * 8:(ich + 1) * 8],
                 strip[:, jt * 512:(jt + 1) * 512], start=True, stop=True,
                 reads=(gsbB, stripB), writes=(lgB,))
        p.tt(lgs[0:8, 0:T], lgs[0:8, 0:T], lg[0:8, 0:T], ALU.add, reads=(lgB,), writes=(lgsB,))
    for jt in range(4):
        p.mm(lg[0:8, jt * 512:(jt + 1) * 512], gsb[0:1, 64:72],
             t["onesr"][0:1, 0:512], start=True, stop=True,
             reads=(gsbB, b["onesr"]), writes=(lgB,))
    p.tt(lgs[0:8, 0:T], lgs[0:8, 0:T], lg[0:8, 0:T], ALU.add, reads=(lgB,), writes=(lgsB,))
    lt, ltB = t["ltok"], b["ltok"]
    for tt in range(NT):
        pt, ptB = t["psB"], b["psB"]
        p.tr(pt[0:128, 0:8], lgs[0:8, tt * 128:(tt + 1) * 128], t["identf"][0:8, 0:8],
             reads=(lgsB, b["identf"]), writes=(ptB,))
        p.copy(lt[:, tt * 8:(tt + 1) * 8], pt[0:128, 0:8], reads=(ptB,), writes=(ltB,))

    _ffn0_route(c, p, CAP)

    _ffn0_expert(c, p, CAP)

    _ffn0_out(c, p, CAP)


def _ffn0_route(c, p, CAP):
    t, b = c.t, c.b
    lt, ltB = t["ltok"], b["ltok"]
    m8, m8B = t["m8"], b["m8"]
    est, estB = t["est"], b["est"]
    zz8, zz8B = t["zz8"], b["zz8"]
    p.memset(zz8[:, 0:8], 0.0, writes=(zz8B,))
    cps, cpsB = t["psB"], b["psB"]
    for tt in range(NT):
        lg8 = lt[:, tt * 8:(tt + 1) * 8]
        mx, mxB = t["sc8"], b["sc8"]
        p.op("vector", lambda e, mx=mx, lg8=lg8: e.max(mx[:, 0:8], lg8), reads=(ltB,), writes=(mxB,))
        eq, eqB = t["sc8b"], b["sc8b"]
        p.ts(eq[:, 0:8], lg8, mx[:, 0:1], None, ALU.is_equal, reads=(ltB, mxB), writes=(eqB,))
        pf, pfB = t["sc8c"], b["sc8c"]
        p.op("vector", lambda e, pf=pf, eq=eq, zz8=zz8: e.tensor_tensor_scan(
            pf[:, 0:8], eq[:, 0:8], zz8[:, 0:8], 0.0, ALU.add, ALU.add),
            reads=(eqB, zz8B), writes=(pfB,))
        p.ts(pf[:, 0:8], pf[:, 0:8], 1.0, None, ALU.is_equal, reads=(pfB,), writes=(pfB,))
        msk = m8[:, tt * 8:(tt + 1) * 8]
        p.tt(msk, pf[:, 0:8], eq[:, 0:8], ALU.mult, reads=(pfB, eqB), writes=(m8B,))
        p.tt(eq[:, 0:8], msk, t["iota8"][:, 0:8], ALU.mult, reads=(m8B, b["iota8"]), writes=(eqB,))
        p.red(est[:, tt:tt + 1], eq[:, 0:8], ALU.add, reads=(eqB,), writes=(estB,))
        p.mm(cps[0:1, 0:8], t["onesc"][:, 0:1], msk, start=(tt == 0), stop=(tt == NT - 1),
             reads=(b["onesc"], m8B), writes=(cpsB,))
    cnt, cntB = t["sc8"], b["sc8"]
    p.copy(cnt[0:1, 0:8], cps[0:1, 0:8], reads=(cpsB,), writes=(cntB,))
    p.dma("gpsimd", t["cnt_d"][0:1, :], cnt[0:1, 0:8], reads=(cntB,), writes=(b["cnt_d"],))
    p.coll("AllGather", ALU.bypass, ALL8, t["cnt_d"].ap(), t["cntall_d"].ap(),
           reads=(b["cnt_d"],), writes=(b["cntall_d"],))
    call, callB = t["sc8b"], b["sc8b"]
    p.dma("sync", call[0:8, 0:8], t["cntall_d"][:, :], reads=(b["cntall_d"],), writes=(callB,))
    ps, psB = t["psB"], b["psB"]
    p.mm(ps[0:1, 0:8], t["premask"][0:8, 0:1], call[0:8, 0:8], start=True, stop=True,
         reads=(callB, b["premask"]), writes=(psB,))
    offs, offsB = t["sc8c"], b["sc8c"]
    p.copy(offs[0:1, 0:8], ps[0:1, 0:8], reads=(psB,), writes=(offsB,))
    neg, negB = t["a1"], b["a1"]
    p.memset(neg[:, 0:256], -1.0, writes=(negB,))
    p.dma("gpsimd", t["idxsc_d"].ap().rearrange("(p f) one -> p (f one)", p=128),
          neg[:, 0:256], reads=(negB,), writes=(b["idxsc_d"],))
    for tt in range(NT):
        msk = t["m8"][:, tt * 8:(tt + 1) * 8]
        rps, rpsB = t["psB"], b["psB"]
        p.mm(rps[:, 0:8], t["cum"][:, :], msk, start=True, stop=False,
             reads=(b["cum"], b["m8"]), writes=(rpsB,))
        p.mm(rps[:, 0:8], t["onesr"][0:1, 0:128], offs[0:1, 0:8], start=False, stop=True,
             reads=(b["onesr"], offsB), writes=(rpsB,))
        rr, rrB = t["sc8"], b["sc8"]
        p.tt(rr[:, 0:8], rps[:, 0:8], msk, ALU.mult, reads=(rpsB, b["m8"]), writes=(rrB,))
        p.red(t["est"][:, NT + tt:NT + tt + 1], rr[:, 0:8], ALU.add, reads=(rrB,), writes=(b["est"],))
        tsp, tspB = t["psB"], b["psB"]
        p.mm(tsp[0:1, 8:16], t["onesc"][:, 0:1], msk, start=True, stop=True,
             reads=(b["onesc"], b["m8"]), writes=(tspB,))
        p.tt(offs[0:1, 0:8], offs[0:1, 0:8], tsp[0:1, 8:16], ALU.add,
             reads=(tspB,), writes=(offsB,))
    for tt in range(NT):
        pos, posB = t["sc8"], b["sc8"]
        rk = t["est"][:, NT + tt:NT + tt + 1]
        p.ts(pos[:, 0:1], t["est"][:, tt:tt + 1], float(CAP), None, ALU.mult,
             reads=(b["est"],), writes=(posB,))
        p.tt(pos[:, 0:1], pos[:, 0:1], rk, ALU.add, reads=(b["est"],), writes=(posB,))
        p.ts(pos[:, 1:2], rk, float(CAP), None, ALU.is_ge, reads=(b["est"],), writes=(posB,))
        p.ts(pos[:, 1:2], pos[:, 1:2], 1e9, None, ALU.mult, reads=(posB,), writes=(posB,))
        p.tt(pos[:, 0:1], pos[:, 0:1], pos[:, 1:2], ALU.add, reads=(posB,), writes=(posB,))
        pi, piB = t["idxm"], b["idxm"]
        p.copy(pi[:, 32 + tt:33 + tt], pos[:, 0:1], reads=(posB,), writes=(piB,))
        p.dma("sync", t["tid_sb"][:, 0:1],
              t["tok_ids"][tt * 128:(tt + 1) * 128, :], reads=(b["tok_ids"],), writes=(b["tid_sb"],))
        p.iscatter(t["idxsc_d"].ap(), pi[:, 32 + tt:33 + tt], t["tid_sb"][:, 0:1], 32767,
                   reads=(piB, b["tid_sb"], b["idxsc_d"]), writes=(b["idxsc_d"],))
    p.coll("AllReduce", ALU.max, ALL8, t["idxsc_d"].ap(), t["idxall_d"].ap(),
           reads=(b["idxsc_d"],), writes=(b["idxall_d"],))
    im, imB = t["idxm"], b["idxm"]
    tmp32, tmp32B = t["g32t"], b["g32t"]
    p.igather(tmp32[0:32, 0:128],
              t["idxall_d"].ap().rearrange("(r c) one -> r (c one)", c=128),
              t["rowidx_sb"][0:32, 0:1], 255,
              reads=(b["idxall_d"], b["rowidx_sb"]), writes=(tmp32B,))
    pt, ptB = t["psB"], b["psB"]
    p.tr(pt[0:128, 0:32], tmp32[0:32, 0:128], t["identf"][0:32, 0:32],
         reads=(tmp32B, b["identf"]), writes=(ptB,))
    imf, imfB = t["a1"], b["a1"]
    p.copy(imf[:, 0:32], pt[0:128, 0:32], reads=(ptB,), writes=(imfB,))
    p.ts(imf[:, 32:64], imf[:, 0:32], 0.0, None, ALU.is_lt, reads=(imfB,), writes=(imfB,))
    p.ts(imf[:, 32:64], imf[:, 32:64], 2e9, None, ALU.mult, reads=(imfB,), writes=(imfB,))
    p.tt(imf[:, 0:32], imf[:, 0:32], imf[:, 32:64], ALU.add, reads=(imfB,), writes=(imfB,))
    p.copy(im[:, 0:32], imf[:, 0:32], reads=(imfB,), writes=(imB,))


def _ffn0_expert(c, p, CAP):
    t, b = c.t, c.b
    g1, g1B = t["g1bc"], b["g1bc"]
    b1, b1B = t["b1bc"], b["b1bc"]
    for name, dst, dstB in (("f0_ln1g", g1, g1B), ("f0_ln1b", b1, b1B)):
        rsb, rsbB = t["lnsq"], b["lnsq"]
        p.dma("sync", rsb[0:1, 0:4096], t[name][0:1, :], reads=(b[name],), writes=(rsbB,))
        for j in range(8):
            ps, psB = t["psB"], b["psB"]
            p.mm(ps[:, 0:512], t["onesr"][0:1, 0:128], rsb[0:1, j * 512:(j + 1) * 512],
                 start=True, stop=True, reads=(rsbB, b["onesr"]), writes=(psB,))
            p.copy(dst[:, j * 512:(j + 1) * 512], ps[:, 0:512], reads=(psB,), writes=(dstB,))
    for half in range(2):
        w1, w1B = t["wbig"], b["wbig"]
        srcw, srcwB = c.tb("f0_w1_aug")
        for i in range(8):
            p.dma("gpsimd", w1[:, i * 2048:(i + 1) * 2048],
                  srcw[i * 128:(i + 1) * 128, half * 2048:(half + 1) * 2048],
                  reads=(srcwB,), writes=(w1B,))
        p.dma("gpsimd", w1[0:1, 8 * 2048:9 * 2048],
              srcw[1024:1025, half * 2048:(half + 1) * 2048], reads=(srcwB,), writes=(w1B,))
        for sc in range(32):
            tok, tokB = t["w16a"], b["w16a"]
            p.memset(tok[:, 0:1024], 0.0, writes=(tokB,))
            p.igather(tok[:, 0:1024], t["xflatall_d"].ap(), t["idxm"][:, sc:sc + 1], 16383,
                      reads=(b["xflatall_d"], b["idxm"]), writes=(tokB,))
            tfm, tfmB = t["w16b"], b["w16b"]
            for ch in range(8):
                pt, ptB = t["psT"], b["psT"]
                p.tr(pt[:, 0:128], tok[:, ch * 128:(ch + 1) * 128], t["ident"][:, :],
                     reads=(tokB, b["ident"]), writes=(ptB,))
                p.copy(tfm[:, ch * 128:(ch + 1) * 128], pt[:, 0:128], reads=(ptB,), writes=(tfmB,))
            ps, psB = t["psS"], b["psS"]
            for j in range(4):
                for ch in range(8):
                    p.mm(ps[:, j * 512:(j + 1) * 512], tfm[:, ch * 128:(ch + 1) * 128],
                         w1[:, ch * 2048 + j * 512: ch * 2048 + (j + 1) * 512],
                         start=(ch == 0), stop=False, reads=(tfmB, w1B), writes=(psB,))
                p.mm(ps[:, j * 512:(j + 1) * 512], t["onesrb"][0:1, 0:128],
                     w1[0:1, 8 * 2048 + j * 512: 8 * 2048 + (j + 1) * 512],
                     start=False, stop=True, reads=(b["onesrb"], w1B), writes=(psB,))
            hh, hhB = t["a2"], b["a2"]
            p.copy(hh[:, 0:2048], ps[:, 0:2048], reads=(psB,), writes=(hhB,))
            p.dma("gpsimd", t["hh_d"][sc * 128:(sc + 1) * 128, half * 2048:(half + 1) * 2048],
                  hh[:, 0:2048], reads=(hhB,), writes=(b["hh_d"],))
    for sc in range(32):
        hacc, haccB = t["a1"], b["a1"]
        p.dma("sync", hacc[:, 0:4096], t["hh_d"][sc * 128:(sc + 1) * 128, :],
              reads=(b["hh_d"],), writes=(haccB,))
        emit_ln_stats(c, p, hacc[:, 0:4096], (haccB,), 4096, 1e-5)
        st, stB = t["stat"], b["stat"]
        gel, gelB = t["w16a"], b["w16a"]
        p.ts(gel[:, 0:4096], hacc[:, 0:4096], st[:, 0:1], st[:, 3:4], ALU.subtract, ALU.mult,
             reads=(haccB, stB), writes=(gelB,))
        p.tt(gel[:, 0:4096], gel[:, 0:4096], g1[:, 0:4096], ALU.mult,
             reads=(g1B,), writes=(gelB,))
        p.tt(gel[:, 0:4096], gel[:, 0:4096], b1[:, 0:4096], ALU.add,
             reads=(b1B,), writes=(gelB,))
        p.act(gel[:, 0:4096], gel[:, 0:4096], AF.Gelu, reads=(gelB,), writes=(gelB,))
        p.dma("gpsimd", t["gel_d"][sc * 128:(sc + 1) * 128, :], gel[:, 0:4096],
              reads=(gelB,), writes=(b["gel_d"],))
    w2, w2B = t["wbig"], b["wbig"]
    load_wchunks(c, p, w2, w2B, "f0_w2_aug", 4097, 1024)
    for sc in range(32):
        gel, gelB = t["w16a"], b["w16a"]
        p.dma("sync", gel[:, 0:4096], t["gel_d"][sc * 128:(sc + 1) * 128, :],
              reads=(b["gel_d"],), writes=(gelB,))
        gfm, gfmB = t["w16b"], b["w16b"]
        for ch in range(32):
            pt, ptB = t["psT"], b["psT"]
            p.tr(pt[:, 0:128], gel[:, ch * 128:(ch + 1) * 128], t["ident"][:, :],
                 reads=(gelB, b["ident"]), writes=(ptB,))
            p.copy(gfm[:, ch * 128:(ch + 1) * 128], pt[:, 0:128], reads=(ptB,), writes=(gfmB,))
        stg, stgB = t["w16c"], b["w16c"]
        for j2 in range(2):
            ps, psB = t["psB"], b["psB"]
            for ch in range(32):
                p.mm(ps[:, 0:512], gfm[:, ch * 128:(ch + 1) * 128],
                     w2[:, ch * 1024 + j2 * 512: ch * 1024 + (j2 + 1) * 512],
                     start=(ch == 0), stop=False, reads=(gfmB, w2B), writes=(psB,))
            p.mm(ps[:, 0:512], t["onesrb"][0:1, 0:128],
                 w2[0:1, 32 * 1024 + j2 * 512: 32 * 1024 + (j2 + 1) * 512],
                 start=False, stop=True, reads=(b["onesrb"], w2B), writes=(psB,))
            p.copy(stg[:, j2 * 512:(j2 + 1) * 512], ps[:, 0:512], reads=(psB,), writes=(stgB,))
        p.dma("gpsimd", t["slots_d"][sc * 128:(sc + 1) * 128, :], stg[:, 0:D],
              reads=(stgB,), writes=(b["slots_d"],))
    p.coll("AllGather", ALU.bypass, ALL8, t["slots_d"].ap(), t["slotsall_d"].ap(),
           reads=(b["slots_d"],), writes=(b["slotsall_d"],))


def _ffn0_out(c, p, CAP):
    t, b = c.t, c.b
    gg, ggB = t["g1bc"], b["g1bc"]
    bb2, bb2B = t["b1bc"], b["b1bc"]
    for name, dst, dstB in (("f0_lng", gg, ggB), ("f0_lnb", bb2, bb2B)):
        rsb, rsbB = t["lnsq"], b["lnsq"]
        p.dma("sync", rsb[0:1, 0:1024], t[name][0:1, :], reads=(b[name],), writes=(rsbB,))
        for j in range(2):
            ps, psB = t["psB"], b["psB"]
            p.mm(ps[:, 0:512], t["onesr"][0:1, 0:128], rsb[0:1, j * 512:(j + 1) * 512],
                 start=True, stop=True, reads=(rsbB, b["onesr"]), writes=(psB,))
            p.copy(dst[:, j * 512:(j + 1) * 512], ps[:, 0:512], reads=(psB,), writes=(dstB,))
    pix, pixB = t["idxm"], b["idxm"]
    for tt in range(NT):
        og, ogB = t["w16a"], b["w16a"]
        p.memset(og[:, 0:1024], 0.0, writes=(ogB,))
        p.igather(og[:, 0:1024], t["slotsall_d"].ap(), pix[:, 32 + tt:33 + tt], 8 * CAP - 1,
                  reads=(b["slotsall_d"], pixB), writes=(ogB,))
        ofl, oflB = t["a1"], b["a1"]
        p.copy(ofl[:, 0:D], og[:, 0:D], reads=(ogB,), writes=(oflB,))
        emit_ln_stats(c, p, ofl[:, 0:D], (oflB,), D, 1e-5)
        st, stB = t["stat"], b["stat"]
        p.ts(ofl[:, 0:D], ofl[:, 0:D], st[:, 0:1], st[:, 3:4], ALU.subtract, ALU.mult,
             reads=(stB,), writes=(oflB,))
        p.tt(ofl[:, 0:D], ofl[:, 0:D], gg[:, 0:D], ALU.mult, reads=(ggB,), writes=(oflB,))
        p.tt(ofl[:, 0:D], ofl[:, 0:D], bb2[:, 0:D], ALU.add, reads=(bb2B,), writes=(oflB,))
        p.dma("gpsimd", t["g_d"][tt * 128:(tt + 1) * 128, :], ofl[:, 0:D],
              reads=(oflB,), writes=(b["g_d"],))


def _quantize_absmean(c, p, src_name, dst_name, rows, cols):
    t, b = c.t, c.b
    nch = rows // 128
    acc, accB = t["a3"], b["a3"]
    p.memset(acc[:, 0:1], 0.0, writes=(accB,))
    for i in range(nch):
        xa, xaB = t["a1"], b["a1"]
        p.dma("sync", xa[:, 0:cols], t[src_name][i * 128:(i + 1) * 128, :],
              reads=(b[src_name],), writes=(xaB,))
        p.op("vector", lambda e, xa=xa, cols=cols: e.tensor_reduce(
            t["stat"][:, 4:5], xa[:, 0:cols], AX.X, ALU.add, apply_absolute_value=True),
            reads=(xaB,), writes=(b["stat"],))
        p.tt(acc[:, 0:1], acc[:, 0:1], t["stat"][:, 4:5], ALU.add,
             reads=(b["stat"],), writes=(accB,))
    ps, psB = t["psB"], b["psB"]
    p.mm(ps[0:1, 0:1], acc[:, 0:1], t["onesc"][:, 0:1], start=True, stop=True,
         reads=(accB, b["onesc"]), writes=(psB,))
    th, thB = t["stat"], b["stat"]
    p.ts(th[0:1, 5:6], ps[0:1, 0:1], 0.5 / (rows * cols), None, ALU.mult,
         reads=(psB,), writes=(thB,))
    p.ts(th[0:1, 5:6], th[0:1, 5:6], 0.5e-6, None, ALU.add, reads=(thB,), writes=(thB,))
    p.mm(ps[:, 4:5], t["onesr"][0:1, 0:128], th[0:1, 5:6], start=True, stop=True,
         reads=(thB, b["onesr"]), writes=(psB,))
    thb, thbB = t["a3"], b["a3"]
    p.copy(thb[:, 1:2], ps[:, 4:5], reads=(psB,), writes=(thbB,))
    for i in range(nch):
        xa, xaB = t["a1"], b["a1"]
        p.dma("sync", xa[:, 0:cols], t[src_name][i * 128:(i + 1) * 128, :],
              reads=(b[src_name],), writes=(xaB,))
        sg, sgB = t["w16a"], b["w16a"]
        p.act(sg[:, 0:cols], xa[:, 0:cols], AF.Sign, reads=(xaB,), writes=(sgB,))
        aw, awB = t["lnsq"], b["lnsq"]
        p.act(aw[:, 0:cols], xa[:, 0:cols], AF.Abs, reads=(xaB,), writes=(awB,))
        p.ts(aw[:, 0:cols], aw[:, 0:cols], thb[:, 1:2], None, ALU.is_gt,
             reads=(thbB,), writes=(awB,))
        wq, wqB = t["w16b"], b["w16b"]
        p.tt(wq[:, 0:cols], sg[:, 0:cols], aw[:, 0:cols], ALU.mult,
             reads=(sgB, awB), writes=(wqB,))
        p.dma("gpsimd", t[dst_name][i * 128:(i + 1) * 128, :], wq[:, 0:cols],
              reads=(wqB,), writes=(b[dst_name],))


def emit_ffn1(c, p):
    t, b = c.t, c.b
    _quantize_absmean(c, p, "f1_wfcT", "f1_wq1_d", 1024, 4096)
    _quantize_absmean(c, p, "f1_wprojT", "f1_wq2_d", 4096, 1024)
    for j in range(8):
        wsb, wsbB = t["wk1"], b["wk1"]
        for ch in range(8):
            p.dma("gpsimd", wsb[:, ch * 512:(ch + 1) * 512],
                  t["f1_wq1_d"][ch * 128:(ch + 1) * 128, j * 512:(j + 1) * 512],
                  reads=(b["f1_wq1_d"],), writes=(wsbB,))
        for tt in range(NT):
            ps, psB = t["psB"], b["psB"]
            for ch in range(8):
                xf, xfB = t["xf"], b["xf"]
                fm_ld(c, p, xf, xfB, ch, tt * 128, (tt + 1) * 128)
                p.mm(ps[:, 0:512], xf[0:128, 0:128], wsb[:, ch * 512:(ch + 1) * 512],
                     start=(ch == 0), stop=(ch == 7), reads=(wsbB, xfB), writes=(psB,))
            hg, hgB = t["a2"], b["a2"]
            p.act(hg[:, 0:512], ps[:, 0:512], AF.Gelu, reads=(psB,), writes=(hgB,))
            p.dma("gpsimd", t["f1_h_d"][tt * 128:(tt + 1) * 128, j * 512:(j + 1) * 512],
                  hg[:, 0:512], reads=(hgB,), writes=(b["f1_h_d"],))
    for name, lo in (("f1_lnw", 0), ("f1_lnb", 4096)):
        rsb, rsbB = t["lnsq"], b["lnsq"]
        p.dma("sync", rsb[0:1, 0:4096], t[name][0:1, :], reads=(b[name],), writes=(rsbB,))
        for jj in range(8):
            ps, psB = t["psB"], b["psB"]
            p.mm(ps[:, 0:512], t["onesr"][0:1, 0:128], rsb[0:1, jj * 512:(jj + 1) * 512],
                 start=True, stop=True, reads=(rsbB, b["onesr"]), writes=(psB,))
            p.copy(t["wk1"][:, lo + jj * 512:lo + (jj + 1) * 512], ps[:, 0:512],
                   reads=(psB,), writes=(b["wk1"],))
    for tt in range(NT):
        ha, haB = t["a1"], b["a1"]
        p.dma("sync", ha[:, 0:4096], t["f1_h_d"][tt * 128:(tt + 1) * 128, :],
              reads=(b["f1_h_d"],), writes=(haB,))
        emit_ln_stats(c, p, ha[:, 0:4096], (haB,), 4096, 1e-5)
        st_, stB = t["stat"], b["stat"]
        hn, hnB = t["w16a"], b["w16a"]
        p.ts(hn[:, 0:4096], ha[:, 0:4096], st_[:, 0:1], st_[:, 3:4], ALU.subtract, ALU.mult,
             reads=(haB, stB), writes=(hnB,))
        p.tt(hn[:, 0:4096], hn[:, 0:4096], t["wk1"][:, 0:4096], ALU.mult,
             reads=(b["wk1"],), writes=(hnB,))
        p.tt(hn[:, 0:4096], hn[:, 0:4096], t["wk1"][:, 4096:8192], ALU.add,
             reads=(b["wk1"],), writes=(hnB,))
        o2, o2B = t["w16b"], b["w16b"]
        for ch in range(32):
            pt, ptB = t["psT"], b["psT"]
            p.tr(pt[:, 0:128], hn[:, ch * 128:(ch + 1) * 128], t["ident"][:, :],
                 reads=(hnB, b["ident"]), writes=(ptB,))
            p.copy(o2[:, ch * 128:(ch + 1) * 128], pt[:, 0:128], reads=(ptB,), writes=(o2B,))
        for ch in range(32):
            p.dma("gpsimd", t["f1_hnf_d"][ch * 128:(ch + 1) * 128, tt * 128:(tt + 1) * 128],
                  o2[:, ch * 128:(ch + 1) * 128], reads=(o2B,), writes=(b["f1_hnf_d"],))
    for j2 in range(2):
        wsb, wsbB = t["wbig1"], b["wbig1"]
        for ch in range(32):
            p.dma("gpsimd", wsb[:, ch * 512:(ch + 1) * 512],
                  t["f1_wq2_d"][ch * 128:(ch + 1) * 128, j2 * 512:(j2 + 1) * 512],
                  reads=(b["f1_wq2_d"],), writes=(wsbB,))
        for tt in range(NT):
            ps, psB = t["psB"], b["psB"]
            for ch in range(32):
                xf, xfB = t["xf"], b["xf"]
                p.dma("sync", xf[0:128, 0:128],
                      t["f1_hnf_d"][ch * 128:(ch + 1) * 128, tt * 128:(tt + 1) * 128],
                      reads=(b["f1_hnf_d"],), writes=(xfB,))
                p.mm(ps[:, 0:512], xf[0:128, 0:128], wsb[:, ch * 512:(ch + 1) * 512],
                     start=(ch == 0), stop=(ch == 31), reads=(wsbB, xfB), writes=(psB,))
            go, goB = t["a2"], b["a2"]
            p.copy(go[:, 0:512], ps[:, 0:512], reads=(psB,), writes=(goB,))
            p.dma("gpsimd", t["g_d"][tt * 128:(tt + 1) * 128, j2 * 512:(j2 + 1) * 512],
                  go[:, 0:512], reads=(goB,), writes=(b["g_d"],))


HJ2 = [512, 512, 512, 512, 512, 256]


def emit_ffn2(c, p):
    t, b = c.t, c.b
    NTOK = 8 * T
    p.coll("AllGather", ALU.bypass, ALL8, t["xn_fm_d"].ap(), t["xnfm_all_d"].ap(),
           reads=(b["xn_fm_d"],), writes=(b["xnfm_all_d"],))

    wg, wgB = t["wg32"], b["wg32"]
    for ch in range(8):
        p.dma("sync", wg[:, ch * 7:(ch + 1) * 7], t["f2_wr"][ch * 128:(ch + 1) * 128, :],
              reads=(b["f2_wr"],), writes=(wgB,))
    lgs, lgsB = t["lnsq"], b["lnsq"]
    p.memset(lgs[0:7, 0:T], 0.0, writes=(lgsB,))
    lg, lgB = t["psS"], b["psS"]
    for ich in range(8):
        strip, stripB = t["a2"], b["a2"]
        for tt in range(NT):
            xa, xaB = t["a1"], b["a1"]
            p.dma("sync", xa[:, 0:128], t["xn32_d"][tt * 128:(tt + 1) * 128,
                                                    ich * 128:(ich + 1) * 128],
                  reads=(b["xn32_d"],), writes=(xaB,))
            pt, ptB = t["psB"], b["psB"]
            p.tr(pt[:, 0:128], xa[:, 0:128], t["identf"][:, :],
                 reads=(xaB, b["identf"]), writes=(ptB,))
            p.copy(strip[:, tt * 128:(tt + 1) * 128], pt[:, 0:128], reads=(ptB,), writes=(stripB,))
        for jt in range(4):
            p.mm(lg[0:7, jt * 512:(jt + 1) * 512], wg[:, ich * 7:(ich + 1) * 7],
                 strip[:, jt * 512:(jt + 1) * 512], start=True, stop=True,
                 reads=(wgB, stripB), writes=(lgB,))
        p.tt(lgs[0:7, 0:T], lgs[0:7, 0:T], lg[0:7, 0:T], ALU.add, reads=(lgB,), writes=(lgsB,))
    rbc, rbcB = t["sc8"], b["sc8"]
    p.dma("sync", rbc[0:7, 0:1], t["f2_rb"][:, :], reads=(b["f2_rb"],), writes=(rbcB,))
    p.ts(lgs[0:7, 0:T], lgs[0:7, 0:T], rbc[0:7, 0:1], None, ALU.mult,
         reads=(rbcB,), writes=(lgsB,))
    p.act(lgs[0:7, 0:T], lgs[0:7, 0:T], AF.Sigmoid, reads=(lgsB,), writes=(lgsB,))
    zz8, zz8B = t["zz8"], b["zz8"]
    p.memset(zz8[:, 0:8], 0.0, writes=(zz8B,))
    for tt in range(NT):
        pt, ptB = t["psB"], b["psB"]
        p.tr(pt[0:128, 0:7], lgs[0:7, tt * 128:(tt + 1) * 128], t["identf"][0:7, 0:7],
             reads=(lgsB, b["identf"]), writes=(ptB,))
        pr, prB = t["ltok"], b["ltok"]
        p.memset(pr[:, tt * 8 + 7:tt * 8 + 8], -1e30, writes=(prB,))
        p.copy(pr[:, tt * 8:tt * 8 + 7], pt[0:128, 0:7], reads=(ptB,), writes=(prB,))
        lg8 = pr[:, tt * 8:(tt + 1) * 8]
        mx, mxB = t["sc8"], b["sc8"]
        p.op("vector", lambda e, mx=mx, lg8=lg8: e.max(mx[:, 0:8], lg8), reads=(prB,), writes=(mxB,))
        eq, eqB = t["sc8b"], b["sc8b"]
        pf, pfB = t["sc8c"], b["sc8c"]
        m1_, m1B = t["m8"], b["m8"]
        p.ts(eq[:, 0:8], lg8, mx[:, 0:1], None, ALU.is_equal, reads=(prB, mxB), writes=(eqB,))
        p.op("vector", lambda e, pf=pf, eq=eq, zz8=zz8: e.tensor_tensor_scan(
            pf[:, 0:8], eq[:, 0:8], zz8[:, 0:8], 0.0, ALU.add, ALU.add),
            reads=(eqB, zz8B), writes=(pfB,))
        msk1 = m1_[:, tt * 8:(tt + 1) * 8]
        p.ts(msk1, pf[:, 0:8], 1.0, None, ALU.is_equal, reads=(pfB,), writes=(m1B,))
        p.tt(msk1, msk1, eq[:, 0:8], ALU.mult, reads=(eqB,), writes=(m1B,))
        tie, tieB = t["stat"], b["stat"]
        p.ts(tie[:, 6:7], mx[:, 1:2], None, None, ALU.bypass, reads=(mxB,), writes=(tieB,)) \
            if False else None
        p.tt(tie[:, 6:7], mx[:, 0:1], mx[:, 1:2], ALU.is_equal, reads=(mxB,), writes=(tieB,))
        mt, mtB = t["est"], b["est"]
        p.ts(mt[:, 0:8], pf[:, 0:8], 2.0, None, ALU.is_equal, reads=(pfB,), writes=(mtB,))
        p.tt(mt[:, 0:8], mt[:, 0:8], eq[:, 0:8], ALU.mult, reads=(eqB,), writes=(mtB,))
        p.ts(eq[:, 0:8], lg8, mx[:, 1:2], None, ALU.is_equal, reads=(prB, mxB), writes=(eqB,))
        p.op("vector", lambda e, pf=pf, eq=eq, zz8=zz8: e.tensor_tensor_scan(
            pf[:, 0:8], eq[:, 0:8], zz8[:, 0:8], 0.0, ALU.add, ALU.add),
            reads=(eqB, zz8B), writes=(pfB,))
        p.ts(pf[:, 0:8], pf[:, 0:8], 1.0, None, ALU.is_equal, reads=(pfB,), writes=(pfB,))
        p.tt(pf[:, 0:8], pf[:, 0:8], eq[:, 0:8], ALU.mult, reads=(eqB,), writes=(pfB,))
        p.ts(mt[:, 0:8], mt[:, 0:8], tie[:, 6:7], None, ALU.mult, reads=(tieB,), writes=(mtB,))
        p.ts(tie[:, 7:8], tie[:, 6:7], -1.0, None, ALU.mult, ALU.bypass, reads=(tieB,), writes=(tieB,)) \
            if False else None
        p.ts(tie[:, 7:8], tie[:, 6:7], -1.0, None, ALU.mult, reads=(tieB,), writes=(tieB,))
        p.ts(tie[:, 7:8], tie[:, 7:8], 1.0, None, ALU.add, reads=(tieB,), writes=(tieB,))
        p.ts(pf[:, 0:8], pf[:, 0:8], tie[:, 7:8], None, ALU.mult, reads=(tieB,), writes=(pfB,))
        p.tt(mt[:, 0:8], mt[:, 0:8], pf[:, 0:8], ALU.add, reads=(pfB,), writes=(mtB,))
        p.tt(tie[:, 4:5], mx[:, 0:1], mx[:, 1:2], ALU.add, reads=(mxB,), writes=(tieB,))
        p.op("vector", lambda e, tie=tie: e.reciprocal(tie[:, 5:6], tie[:, 4:5]),
             reads=(tieB,), writes=(tieB,))
        p.tt(tie[:, 4:5], mx[:, 0:1], tie[:, 5:6], ALU.mult, reads=(mxB,), writes=(tieB,))
        p.tt(tie[:, 5:6], mx[:, 1:2], tie[:, 5:6], ALU.mult, reads=(mxB, tieB), writes=(tieB,))
        p.ts(msk1, msk1, tie[:, 4:5], None, ALU.mult, reads=(tieB,), writes=(m1B,))
        p.ts(mt[:, 0:8], mt[:, 0:8], tie[:, 5:6], None, ALU.mult, reads=(tieB,), writes=(mtB,))
        ww, wwB = t["sc8"], b["sc8"]
        p.tt(ww[:, 0:8], msk1, mt[:, 0:8], ALU.add, reads=(m1B, mtB), writes=(wwB,))
        p.memset(ww[:, 7:8], 1.0, writes=(wwB,))
        p.dma("gpsimd", t["wx_d"][tt * 128:(tt + 1) * 128, :], ww[:, 0:8],
              reads=(wwB,), writes=(b["wx_d"],))
    p.coll("AllGather", ALU.bypass, ALL8, t["wx_d"].ap(), t["wall_d"].ap(),
           reads=(b["wx_d"],), writes=(b["wall_d"],))
    selb, selbB = t["iota8"], b["iota8"]
    rsb, rsbB = t["lnsq"], b["lnsq"]
    p.dma("sync", rsb[0:1, 0:8], t["f2_sel"][0:1, :], reads=(b["f2_sel"],), writes=(rsbB,))
    ps, psB = t["psB"], b["psB"]
    p.mm(ps[:, 0:8], t["onesr"][0:1, 0:128], rsb[0:1, 0:8], start=True, stop=True,
         reads=(rsbB, b["onesr"]), writes=(psB,))
    p.copy(selb[:, 0:8], ps[:, 0:8], reads=(psB,), writes=(selbB,))

    col0 = 0
    for j, wj in enumerate(HJ2):
        wsb, wsbB = t["wk1"], b["wk1"]
        wub, wubB = t["wbig1"], b["wbig1"]
        for ch in range(8):
            p.dma("gpsimd", wsb[:, ch * wj:(ch + 1) * wj],
                  t["f2_wg_w"][ch * 128:(ch + 1) * 128, col0:col0 + wj],
                  reads=(b["f2_wg_w"],), writes=(wsbB,))
            p.dma("gpsimd", wub[:, ch * wj:(ch + 1) * wj],
                  t["f2_wu_w"][ch * 128:(ch + 1) * 128, col0:col0 + wj],
                  reads=(b["f2_wu_w"],), writes=(wubB,))
        for tg in range(NTOK // 128):
            rr, tt2 = tg // 16, tg % 16
            ps, psB = t["psS"], b["psS"]
            for ch in range(8):
                xf, xfB = t["xf"], b["xf"]
                p.dma("sync", xf[0:128, 0:128],
                      t["xnfm_all_d"][rr * 1025 + ch * 128: rr * 1025 + (ch + 1) * 128,
                                      tt2 * 128:(tt2 + 1) * 128],
                      reads=(b["xnfm_all_d"],), writes=(xfB,))
                p.mm(ps[:, 0:wj], xf[0:128, 0:128], wsb[:, ch * wj:(ch + 1) * wj],
                     start=(ch == 0), stop=(ch == 7), reads=(wsbB, xfB), writes=(psB,))
            for ch in range(8):
                xf, xfB = t["xf"], b["xf"]
                p.dma("sync", xf[0:128, 0:128],
                      t["xnfm_all_d"][rr * 1025 + ch * 128: rr * 1025 + (ch + 1) * 128,
                                      tt2 * 128:(tt2 + 1) * 128],
                      reads=(b["xnfm_all_d"],), writes=(xfB,))
                p.mm(ps[:, 512:512 + wj], xf[0:128, 0:128], wub[:, ch * wj:(ch + 1) * wj],
                     start=(ch == 0), stop=(ch == 7), reads=(wubB, xfB), writes=(psB,))
            sg, sgB = t["w16a"], b["w16a"]
            p.act(sg[:, 0:wj], ps[:, 0:wj], AF.Silu, reads=(psB,), writes=(sgB,))
            p.tt(sg[:, 1024:1024 + wj], sg[:, 0:wj], ps[:, 512:512 + wj], ALU.mult,
                 reads=(psB, sgB), writes=(sgB,))
            p.dma("gpsimd", t["f2_mid_d"][tg * 128:(tg + 1) * 128, col0:col0 + wj],
                  sg[:, 1024:1024 + wj], reads=(sgB,), writes=(b["f2_mid_d"],))
        col0 += wj

    for tg in range(NTOK // 128):
        md, mdB = t["w16a"], b["w16a"]
        p.dma("sync", md[:, 0:2816], t["f2_mid_d"][tg * 128:(tg + 1) * 128, :],
              reads=(b["f2_mid_d"],), writes=(mdB,))
        o2, o2B = t["w16b"], b["w16b"]
        for ch in range(22):
            pt, ptB = t["psT"], b["psT"]
            p.tr(pt[:, 0:128], md[:, ch * 128:(ch + 1) * 128], t["ident"][:, :],
                 reads=(mdB, b["ident"]), writes=(ptB,))
            p.copy(o2[:, ch * 128:(ch + 1) * 128], pt[:, 0:128], reads=(ptB,), writes=(o2B,))
        for ch in range(22):
            p.dma("gpsimd", t["f2_midf_d"][ch * 128:(ch + 1) * 128, tg * 128:(tg + 1) * 128],
                  o2[:, ch * 128:(ch + 1) * 128], reads=(o2B,), writes=(b["f2_midf_d"],))

    for j2 in range(2):
        wsb, wsbB = t["wbig1"], b["wbig1"]
        for ch in range(22):
            p.dma("gpsimd", wsb[:, ch * 512:(ch + 1) * 512],
                  t["f2_wd_w"][ch * 128:(ch + 1) * 128, j2 * 512:(j2 + 1) * 512],
                  reads=(b["f2_wd_w"],), writes=(wsbB,))
        for tg in range(NTOK // 128):
            ps, psB = t["psB"], b["psB"]
            for ch in range(22):
                xf, xfB = t["xf"], b["xf"]
                p.dma("sync", xf[0:128, 0:128],
                      t["f2_midf_d"][ch * 128:(ch + 1) * 128, tg * 128:(tg + 1) * 128],
                      reads=(b["f2_midf_d"],), writes=(xfB,))
                p.mm(ps[:, 0:512], xf[0:128, 0:128], wsb[:, ch * 512:(ch + 1) * 512],
                     start=(ch == 0), stop=(ch == 21), reads=(wsbB, xfB), writes=(psB,))
            wv8, wv8B = t["sc8"], b["sc8"]
            p.dma("sync", wv8[:, 0:8], t["wall_d"][tg * 128:(tg + 1) * 128, :],
                  reads=(b["wall_d"],), writes=(wv8B,))
            p.tt(wv8[:, 0:8], wv8[:, 0:8], t["iota8"][:, 0:8], ALU.mult,
                 reads=(b["iota8"],), writes=(wv8B,))
            p.red(t["stat"][:, 6:7], wv8[:, 0:8], ALU.add, reads=(wv8B,), writes=(b["stat"],))
            oo, ooB = t["a2"], b["a2"]
            p.ts(oo[:, 0:512], ps[:, 0:512], t["stat"][:, 6:7], None, ALU.mult,
                 reads=(psB, b["stat"]), writes=(ooB,))
            p.dma("gpsimd", t["f2_out_d"][tg * 128:(tg + 1) * 128, j2 * 512:(j2 + 1) * 512],
                  oo[:, 0:512], reads=(ooB,), writes=(b["f2_out_d"],))
    p.coll("ReduceScatter", ALU.add, ALL8, t["f2_out_d"].ap(), t["g_d"].ap(),
           reads=(b["f2_out_d"],), writes=(b["g_d"],))


def emit_ffn3(c, p):
    t, b = c.t, c.b
    NP = 2048
    wph, wphB = t["wbig1"], b["wbig1"]
    load_wchunks(c, p, wph, wphB, "f3_phi", 1024, NP)
    nrm, nrmB = t["row4"], b["row4"]
    acc, accB = t["psS"], b["psS"]
    sq, sqB = t["lnsq"], b["lnsq"]
    lacc, laccB = t["a2"], b["a2"]
    p.memset(lacc[0:1, 0:NP], 0.0, writes=(laccB,))
    for ch in range(8):
        p.act(sq[:, 0:NP], wph[:, ch * NP:(ch + 1) * NP], AF.Square, reads=(wphB,), writes=(sqB,))
        for jt in range(4):
            p.mm(acc[0:1, jt * 512:(jt + 1) * 512], t["onesc"][:, 0:1],
                 sq[:, jt * 512:(jt + 1) * 512], start=True, stop=True,
                 reads=(sqB, b["onesc"]), writes=(accB,))
        p.tt(lacc[0:1, 0:NP], lacc[0:1, 0:NP], acc[0:1, 0:NP], ALU.add,
             reads=(accB,), writes=(laccB,))
    p.red(nrm[0:1, 0:1], lacc[0:1, 0:NP], ALU.max, reads=(laccB,), writes=(nrmB,))
    p.act(nrm[0:1, 0:1], nrm[0:1, 0:1], AF.Sqrt, reads=(nrmB,), writes=(nrmB,))
    p.ts(nrm[0:1, 0:1], nrm[0:1, 0:1], 1.0001, None, ALU.mult, reads=(nrmB,), writes=(nrmB,))
    ps, psB = t["psB"], b["psB"]
    p.mm(ps[:, 16:17], t["onesr"][0:1, 0:128], nrm[0:1, 0:1], start=True, stop=True,
         reads=(nrmB, b["onesr"]), writes=(psB,))
    phm, phmB = t["stat"], b["stat"]
    p.copy(phm[:, 6:7], ps[:, 16:17], reads=(psB,), writes=(phmB,))
    for tt in range(NT):
        ps2, ps2B = t["psS"], b["psS"]
        for jt in range(4):
            for ch in range(8):
                xf, xfB = t["xf"], b["xf"]
                fm_ld(c, p, xf, xfB, ch, tt * 128, (tt + 1) * 128)
                p.mm(ps2[:, jt * 512:(jt + 1) * 512], xf[0:128, 0:128],
                     wph[:, ch * NP + jt * 512: ch * NP + (jt + 1) * 512],
                     start=(ch == 0), stop=(ch == 7), reads=(wphB, xfB), writes=(ps2B,))
        la, laB = t["a1"], b["a1"]
        p.copy(la[:, 0:NP], ps2[:, 0:NP], reads=(ps2B,), writes=(laB,))
        p.dma("gpsimd", t["l_d"][tt * 128:(tt + 1) * 128, :], la[:, 0:NP],
              reads=(laB,), writes=(b["l_d"],))
        xa, xaB = t["a2"], b["a2"]
        p.dma("sync", xa[:, 0:D], t["xn32_d"][tt * 128:(tt + 1) * 128, :],
              reads=(b["xn32_d"],), writes=(xaB,))
        p.act(sq[:, 0:D], xa[:, 0:D], AF.Square, reads=(xaB,), writes=(sqB,))
        p.red(t["stat"][:, 4:5], sq[:, 0:D], ALU.add, reads=(sqB,), writes=(b["stat"],))
        p.act(t["stat"][:, 4:5], t["stat"][:, 4:5], AF.Sqrt, reads=(b["stat"],), writes=(b["stat"],))
        p.tt(t["stat"][:, 5:6], t["stat"][:, 4:5], phm[:, 6:7], ALU.mult,
             reads=(phmB,), writes=(b["stat"],))
        p.dma("gpsimd", t["bm_d"][tt * 128:(tt + 1) * 128, :], t["stat"][:, 5:6],
              reads=(b["stat"],), writes=(b["bm_d"],))
    p.coll("AllReduce", ALU.max, GRP4, t["bm_d"].ap(), t["bmax_d"].ap(),
           reads=(b["bm_d"],), writes=(b["bmax_d"],))
    for tt in range(NT):
        la, laB = t["a1"], b["a1"]
        p.dma("sync", la[:, 0:NP], t["l_d"][tt * 128:(tt + 1) * 128, :],
              reads=(b["l_d"],), writes=(laB,))
        bm, bmB = t["stat"], b["stat"]
        p.dma("sync", bm[:, 4:5], t["bmax_d"][tt * 128:(tt + 1) * 128, :],
              reads=(b["bmax_d"],), writes=(bmB,))
        p.ts(bm[:, 5:6], bm[:, 4:5], -1.0, None, ALU.mult, reads=(bmB,), writes=(bmB,))
        ea, eaB = t["a2"], b["a2"]
        p.act(ea[:, 0:NP], la[:, 0:NP], AF.Exp, bias=bm[:, 5:6], reads=(laB, bmB), writes=(eaB,))
        p.dma("gpsimd", t["e_d"][tt * 128:(tt + 1) * 128, :], ea[:, 0:NP],
              reads=(eaB,), writes=(b["e_d"],))
    p.coll("AllReduce", ALU.add, GRP4, t["e_d"].ap(), t["esum_d"].ap(),
           reads=(b["e_d"],), writes=(b["esum_d"],))
    for tt in range(NT):
        ea, eaB = t["a1"], b["a1"]
        p.dma("sync", ea[:, 0:NP], t["e_d"][tt * 128:(tt + 1) * 128, :],
              reads=(b["e_d"],), writes=(eaB,))
        es, esB = t["a2"], b["a2"]
        p.dma("sync", es[:, 0:NP], t["esum_d"][tt * 128:(tt + 1) * 128, :],
              reads=(b["esum_d"],), writes=(esB,))
        p.op("vector", lambda e, es=es: e.reciprocal(es[:, 0:NP], es[:, 0:NP]),
             reads=(esB,), writes=(esB,))
        dd, ddB = t["w16a"], b["w16a"]
        p.tt(dd[:, 0:NP], ea[:, 0:NP], es[:, 0:NP], ALU.mult, reads=(eaB, esB), writes=(ddB,))
        p.dma("gpsimd", t["d_d"][tt * 128:(tt + 1) * 128, :], dd[:, 0:NP],
              reads=(ddB,), writes=(b["d_d"],))
    for ch in range(16):
        for jt in range(2):
            ps3, ps3B = t["psB"], b["psB"]
            for tt in range(NT):
                df, dfB = t["w16c"], b["w16c"]
                p.dma("sync", df[0:128, 0:128],
                      t["d_d"][tt * 128:(tt + 1) * 128, ch * 128:(ch + 1) * 128],
                      reads=(b["d_d"],), writes=(dfB,))
                xf, xfB = t["xf"], b["xf"]
                p.dma("sync", xf[0:128, 0:512],
                      t["xn_tok_d"][tt * 128:(tt + 1) * 128, jt * 512:(jt + 1) * 512],
                      reads=(b["xn_tok_d"],), writes=(xfB,))
                p.mm(ps3[:, 0:512], df[0:128, 0:128], xf[0:128, 0:512],
                     start=(tt == 0), stop=(tt == NT - 1), reads=(xfB, dfB), writes=(ps3B,))
            so, soB = t["a2"], b["a2"]
            p.copy(so[:, 0:512], ps3[:, 0:512], reads=(ps3B,), writes=(soB,))
            p.dma("gpsimd", t["slf_d"][ch * 128:(ch + 1) * 128, jt * 512:(jt + 1) * 512],
                  so[:, 0:512], reads=(soB,), writes=(b["slf_d"],))
    p.coll("AllReduce", ALU.add, PAIRS, t["slf_d"].ap(), t["slfsum_d"].ap(),
           reads=(b["slf_d"],), writes=(b["slfsum_d"],))
    p.coll("AllGather", ALU.bypass, GRP4, t["slfsum_d"].ap(), t["slall_d"].ap(),
           reads=(b["slfsum_d"],), writes=(b["slall_d"],))
    for g in range(8):
        ms, msB = t["a1"], b["a1"]
        p.igather(ms[:, 0:1024], t["slall_d"].ap(), t["yidx_sb"][:, 16 + g:17 + g], 8191,
                  reads=(b["slall_d"], b["yidx_sb"]), writes=(msB,))
        mb, mbB = t["w16a"], b["w16a"]
        p.copy(mb[:, 0:1024], ms[:, 0:1024], reads=(msB,), writes=(mbB,))
        for ch in range(8):
            pt, ptB = t["psT"], b["psT"]
            p.tr(pt[:, 0:128], mb[:, ch * 128:(ch + 1) * 128], t["ident"][:, :],
                 reads=(mbB, b["ident"]), writes=(ptB,))
            p.copy(t["wbig1"][:, ch * 1024 + g * 128:ch * 1024 + (g + 1) * 128], pt[:, 0:128],
                   reads=(ptB,), writes=(b["wbig1"],))
    for nloc in range(8):
        wsb, wsbB = t["wk1"], b["wk1"]
        for ch in range(8):
            p.dma("gpsimd", wsb[:, ch * 1024:(ch + 1) * 1024],
                  t["f3_w"][nloc * 1024 + ch * 128: nloc * 1024 + (ch + 1) * 128, :],
                  reads=(b["f3_w"],), writes=(wsbB,))
        brow, browB = t["w16c"], b["w16c"]
        p.dma("gpsimd", brow[0:1, 0:1024], t["f3_b"][nloc:nloc + 1, :],
              reads=(b["f3_b"],), writes=(browB,))
        for bb_ in range(4):
            loc0 = bb_ * 256 + nloc * 32
            for j2 in range(2):
                psy, psyB = t["psB"], b["psB"]
                for ch in range(8):
                    p.mm(psy[0:32, 0:512],
                         t["wbig1"][:, ch * 1024 + loc0: ch * 1024 + loc0 + 32],
                         wsb[:, ch * 1024 + j2 * 512: ch * 1024 + (j2 + 1) * 512],
                         start=(ch == 0), stop=False, reads=(b["wbig1"], wsbB), writes=(psyB,))
                p.mm(psy[0:32, 0:512], t["onesrb"][0:1, 0:32],
                     brow[0:1, j2 * 512:(j2 + 1) * 512], start=False, stop=True,
                     reads=(b["onesrb"], browB), writes=(psyB,))
                yo, yoB = t["w16a"], b["w16a"]
                p.copy(yo[0:32, 0:512], psy[0:32, 0:512], reads=(psyB,), writes=(yoB,))
                p.dma("gpsimd",
                      t["y_loc_d"][bb_ * 256 + nloc * 32: bb_ * 256 + (nloc + 1) * 32,
                                   j2 * 512:(j2 + 1) * 512],
                      yo[0:32, 0:512], reads=(yoB,), writes=(b["y_loc_d"],))
    p.coll("AllGather", ALU.bypass, ALL8, t["y_loc_d"].ap(), t["yall_d"].ap(),
           reads=(b["y_loc_d"],), writes=(b["yall_d"],))
    for j in range(16):
        yb, ybB = t["w16a"], b["w16a"]
        p.igather(yb[:, 0:1024], t["yall_d"].ap(), t["yidx_sb"][:, j:j + 1], 8191,
                  reads=(b["yall_d"], b["yidx_sb"]), writes=(ybB,))
        p.dma("gpsimd", t["yb_d"][j * 128:(j + 1) * 128, :], yb[:, 0:1024],
              reads=(ybB,), writes=(b["yb_d"],))
    for tt in range(NT):
        la, laB = t["a1"], b["a1"]
        p.dma("sync", la[:, 0:NP], t["l_d"][tt * 128:(tt + 1) * 128, :],
              reads=(b["l_d"],), writes=(laB,))
        st_, stB = t["stat"], b["stat"]
        p.red(st_[:, 4:5], la[:, 0:NP], ALU.max, reads=(laB,), writes=(stB,))
        p.ts(st_[:, 5:6], st_[:, 4:5], -1.0, None, ALU.mult, reads=(stB,), writes=(stB,))
        ca, caB = t["a2"], b["a2"]
        p.act(ca[:, 0:NP], la[:, 0:NP], AF.Exp, bias=st_[:, 5:6], reads=(laB, stB), writes=(caB,))
        p.red(st_[:, 6:7], ca[:, 0:NP], ALU.add, reads=(caB,), writes=(stB,))
        p.op("vector", lambda e, st_=st_: e.reciprocal(st_[:, 7:8], st_[:, 6:7]),
             reads=(stB,), writes=(stB,))
        cb, cbB = t["w16a"], b["w16a"]
        p.ts(cb[:, 0:NP], ca[:, 0:NP], st_[:, 7:8], None, ALU.mult, reads=(caB, stB), writes=(cbB,))
        cf, cfB = t["w16b"], b["w16b"]
        for ch in range(16):
            pt, ptB = t["psT"], b["psT"]
            p.tr(pt[:, 0:128], cb[:, ch * 128:(ch + 1) * 128], t["ident"][:, :],
                 reads=(cbB, b["ident"]), writes=(ptB,))
            p.copy(cf[:, ch * 128:(ch + 1) * 128], pt[:, 0:128], reads=(ptB,), writes=(cfB,))
        for j2 in range(2):
            pso, psoB = t["psB"], b["psB"]
            for j in range(16):
                yf, yfB = t["w16c"], b["w16c"]
                p.dma("sync", yf[0:128, 0:512],
                      t["yb_d"][j * 128:(j + 1) * 128, j2 * 512:(j2 + 1) * 512],
                      reads=(b["yb_d"],), writes=(yfB,))
                p.mm(pso[:, 0:512], cf[:, j * 128:(j + 1) * 128], yf[0:128, 0:512],
                     start=(j == 0), stop=(j == 15), reads=(cfB, yfB), writes=(psoB,))
            go, goB = t["a2"], b["a2"]
            p.copy(go[:, 0:512], pso[:, 0:512], reads=(psoB,), writes=(goB,))
            p.dma("gpsimd", t["g_d"][tt * 128:(tt + 1) * 128, j2 * 512:(j2 + 1) * 512],
                  go[:, 0:512], reads=(goB,), writes=(b["g_d"],))


DEBUG = False


def emit_debug_dumps0(c, p):
    t, b = c.t, c.b
    p.dma("gpsimd", t["dbg_xflat"][:, :], t["xflat_d"].ap(), reads=(b["xflat_d"],),
          writes=(b["dbg_xflat"],))
    for tt in range(NT):
        p.dma("gpsimd", t["dbg_lg"][tt * 128:(tt + 1) * 128, :], t["ltok"][:, tt * 8:(tt + 1) * 8],
              reads=(b["ltok"],), writes=(b["dbg_lg"],))
        p.dma("gpsimd", t["dbg_est"][tt * 128:(tt + 1) * 128, 0:1], t["est"][:, tt:tt + 1],
              reads=(b["est"],), writes=(b["dbg_est"],), allow_slow_non_contiguous=True)
        p.dma("gpsimd", t["dbg_est"][tt * 128:(tt + 1) * 128, 1:2], t["est"][:, NT + tt:NT + tt + 1],
              reads=(b["est"],), writes=(b["dbg_est"],), allow_slow_non_contiguous=True)
    p.dma("gpsimd", t["dbg_slots"][:, :], t["slots_d"].ap(), reads=(b["slots_d"],),
          writes=(b["dbg_slots"],))
    p.dma("gpsimd", t["dbg_idx"][:, :], t["idxall_d"].ap(), reads=(b["idxall_d"],),
          writes=(b["dbg_idx"],))


def emit_debug_dumps(c, p):
    t, b = c.t, c.b
    for nm in ("s", "g", "xn32"):
        if "dbg_" + nm not in t:
            continue
        p.dma("gpsimd", t["dbg_" + nm][:, :], t[nm.replace("xn32", "xn32") + "_d"].ap(),
              reads=(b[nm + "_d"] if nm != "xn32" else b["xn32_d"],),
              writes=(b["dbg_" + nm],))


def emit_combine(c, p):
    t, b = c.t, c.b
    if DEBUG:
        emit_debug_dumps(c, p)
        if "dbg_xflat" in c.t:
            emit_debug_dumps0(c, p)
    for tt in range(NT):
        r = slice(tt * 128, (tt + 1) * 128)
        xa, xaB = t["a1"], b["a1"]
        sa, saB = t["a2"], b["a2"]
        ga, gaB = t["a3"], b["a3"]
        p.dma("sync", xa[:, 0:D], t["x_in"][r, :], reads=(b["x_in"],), writes=(xaB,))
        p.dma("sync", sa[:, 0:D], t["s_d"][r, :], reads=(b["s_d"],), writes=(saB,))
        p.dma("sync", ga[:, 0:D], t["g_d"][r, :], reads=(b["g_d"],), writes=(gaB,))
        p.act(ga[:, 0:D], ga[:, 0:D], AF.Silu, reads=(gaB,), writes=(gaB,))
        p.tt(sa[:, 0:D], sa[:, 0:D], ga[:, 0:D], ALU.mult, reads=(gaB,), writes=(saB,))
        p.tt(xa[:, 0:D], xa[:, 0:D], sa[:, 0:D], ALU.add, reads=(saB,), writes=(xaB,))
        p.dma("gpsimd", t["x_out"][r, :], xa[:, 0:D], reads=(xaB,), writes=(b["x_out"],))



def build_block_program(blk):
    nc = bass.Bass("TRN2", target_bir_lowering=False, debug=False)
    es = contextlib.ExitStack()
    p = Pr(nc)
    c = Ctx(nc, es, p)

    c.dram("x_in", [T, D], F32, kind="ExternalInput")
    c.dram("x_out", [T, D], F32, kind="ExternalOutput")
    c.dram("ident_in", [128, 128], F32, kind="ExternalInput")
    c.dram("cum_in", [128, 128], F32, kind="ExternalInput")
    c.dram("xn_tok_d", [T, D], BF16)
    c.dram("xn_fm_d", [1025, T], BF16)
    c.dram("xn32_d", [T, D], F32)
    c.dram("s_d", [T, D], F32)
    c.dram("g_d", [T, D], F32)
    p.fence_bufs = (c.dram("fence_in", [1, 8], F32), c.dram("fence_out", [1, 8], F32))
    if DEBUG:
        KRD = 66 if blk == 1 else 64
        for nm in ("dbg_s", "dbg_g", "dbg_xn32"):
            c.dram(nm, [T, D], F32, kind="ExternalOutput")
        c.dram("dbg_qx", [H * KRD, T], F32, kind="ExternalOutput")
        c.dram("dbg_kx", [H * KRD, T], F32, kind="ExternalOutput")
        c.dram("dbg_vx", [T, 1040], F32, kind="ExternalOutput")
        c.dram("dbg_kg", [2 * H * KRD, T], F32, kind="ExternalOutput")
        c.dram("dbg_vg", [2 * T, 1040], F32, kind="ExternalOutput")
        if blk == 0:
            c.dram("dbg_G", [128, 80], F32, kind="ExternalOutput")
            c.dram("dbg_strip", [128, 2048], F32, kind="ExternalOutput")
            c.dram("dbg_xflat", [T, D], F32, kind="ExternalOutput")
            c.dram("dbg_lg", [T, 8], F32, kind="ExternalOutput")
            c.dram("dbg_est", [T, 2], F32, kind="ExternalOutput")
            c.dram("dbg_slots", [4096, D], F32, kind="ExternalOutput")
            c.dram("dbg_idx", [8 * 4096, 1], F32, kind="ExternalOutput")

    KR = 66 if blk == 1 else 64
    c.dram("qx_d", [H * KR, T], BF16)
    c.dram("kx_d", [H * KR, T], BF16)
    c.dram("vx_d", [T, 1040], BF16)
    c.dram("kg_d", [2 * H * KR, T], BF16)
    c.dram("vg_d", [2 * T, 1040], BF16)
    c.dram("ofm_d", [1024, T], BF16)

    if blk == 0:
        for nm, shp in (("wq_in", [1024, 1024]), ("wk_in", [1024, 1024]),
                        ("wv_in", [1025, 1024]), ("wo_in", [1025, 1024]),
                        ("bq_in", [1024, 1]), ("bk_in", [1024, 1])):
            c.dram(nm, shp, F32, kind="ExternalInput")
    elif blk == 1:
        for nm, shp in (("wq_in", [1025, 1024]), ("wk_in", [1025, 1024]),
                        ("wv_in", [1025, 1024]), ("wo_in", [1025, 1024]),
                        ("f1_wfcT", [1024, 4096]), ("f1_wprojT", [4096, 1024]),
                        ("f1_lnw", [1, 4096]), ("f1_lnb", [1, 4096])):
            c.dram(nm, shp, F32, kind="ExternalInput")
        c.dram("f1_wq1_d", [1024, 4096], BF16)
        c.dram("f1_wq2_d", [4096, 1024], BF16)
        c.dram("f1_h_d", [T, 4096], F32)
        c.dram("f1_hnf_d", [4096, T], BF16)
    elif blk == 2:
        for nm, shp in (("wq_in", [1024, 1024]), ("wk_in", [1024, 1024]),
                        ("wv_in", [1024, 1024]), ("wo_in", [1025, 1024]),
                        ("l2_temp", [1, 16]),
                        ("f2_wr", [1024, 7]), ("f2_rb", [7, 1]), ("f2_sel", [1, 8]),
                        ("f2_wg_w", [1024, 2816]), ("f2_wu_w", [1024, 2816]),
                        ("f2_wd_w", [2816, 1024])):
            c.dram(nm, shp, F32, kind="ExternalInput")
        c.dram("xnfm_all_d", [8 * 1025, T], BF16)
        c.dram("wx_d", [T, 8], F32)
        c.dram("wall_d", [8 * T, 8], F32)
        c.dram("f2_mid_d", [8 * T, 2816], BF16)
        c.dram("f2_midf_d", [2816, 8 * T], BF16)
        c.dram("f2_out_d", [8 * T, 1024], F32)
    elif blk == 3:
        for nm, shp in (("wq_in", [1024, 1024]), ("wk_in", [1024, 1024]),
                        ("wv_in", [1024, 1024]), ("wo_in", [1025, 1024]),
                        ("l3_lns", [1, 1024]), ("l3_lnb", [1, 1024]),
                        ("f3_phi", [1024, 2048]), ("f3_w", [8 * 1024, 1024]),
                        ("f3_b", [8, 1024])):
            c.dram(nm, shp, F32, kind="ExternalInput")
        c.dram("f3_yidx", [128, 32], I32, kind="ExternalInput")
        c.dram("zn_fm_d", [1025, T], BF16)
        c.dram("l_d", [T, 2048], F32)
        c.dram("bm_d", [T, 1], F32)
        c.dram("bmax_d", [T, 1], F32)
        c.dram("e_d", [T, 2048], F32)
        c.dram("esum_d", [T, 2048], F32)
        c.dram("d_d", [T, 2048], BF16)
        c.dram("slf_d", [2048, 1024], F32)
        c.dram("slfsum_d", [2048, 1024], F32)
        c.dram("slall_d", [8192, 1024], F32)
        c.dram("y_loc_d", [1024, 1024], BF16)
        c.dram("yall_d", [8192, 1024], BF16)
        c.dram("yb_d", [T, 1024], BF16)
    if blk == 0:
        for nm, shp in (("f0_wp_aug", [1025, 1024]), ("f0_wpT", [1024, 1024]),
                        ("f0_wg", [1024, 8]), ("f0_bp", [1024, 1]),
                        ("f0_w1_aug", [1025, 4096]), ("f0_w2_aug", [4097, 1024]),
                        ("f0_ln1g", [1, 4096]), ("f0_ln1b", [1, 4096]),
                        ("f0_lng", [1, 1024]), ("f0_lnb", [1, 1024]),
                        ("tok_ids", [T, 1]), ("premask", [8, 1]), ("iota8_in", [1, 8])):
            c.dram(nm, shp, F32, kind="ExternalInput")
        c.dram("rowidx", [32, 1], I32, kind="ExternalInput")
        c.dram("xflat_d", [T, D], BF16)
        c.dram("xflatall_d", [8 * T, D], BF16)
        c.dram("cnt_d", [1, 8], F32)
        c.dram("cntall_d", [8, 8], F32)
        c.dram("idxsc_d", [8 * 4096, 1], F32)
        c.dram("idxall_d", [8 * 4096, 1], F32)
        c.dram("hh_d", [4096, 4096], F32)
        c.dram("gel_d", [4096, 4096], BF16)
        c.dram("slots_d", [4096, D], BF16)
        c.dram("slotsall_d", [8 * 4096, D], BF16)

    c.sb("cst", [128, 1], F32)
    nc.const_aps.register(F32, 1e-5, c.t["cst"][:, 0:1])
    c.sb("ident", [128, 128], BF16)
    c.sb("identf", [128, 128], F32)
    c.sb("cum", [128, 128], F32)
    c.sb("onesr", [1, 512], F32)
    c.sb("onesrb", [1, T], BF16)
    c.sb("onesc", [128, 1], F32)
    c.sb("onescb", [128, 1], BF16)
    c.sb("stat", [128, 8], F32)
    c.sb("lnsq", [128, 4096], F32)
    c.sb("a1", [128, 4096], F32)
    c.sb("a2", [128, 2048], F32)
    c.sb("a3", [128, 1024], F32)
    c.sb("rsum", [1, 256], F32)
    c.sb("w16a", [128, 4096], BF16)
    c.sb("w16b", [128, 4096], BF16)
    c.sb("w16c", [128, 1024], BF16)
    c.sb("wk1", [128, 9216], BF16)
    c.sb("xf", [128, 512], BF16)
    c.sb("kf", [128, 2 * T], BF16)
    c.sb("qh", [128, T], BF16)
    c.sb("vh", [128, NKC * 65], BF16)
    c.sb("pb", [128, NKC * QT], BF16)
    c.sb("ostg", [64, 256], BF16)
    c.sb("bb", [128, H], F32)
    c.sb("row4", [1, 64], F32)
    c.sb("bqk", [128, 16], F32)
    if blk == 0:
        c.sb("gf32", [128, 80], F32)
        c.sb("wf32s", [128, 1024], F32)
        c.sb("wg32", [128, 72], F32)
        c.sb("ltok", [128, NT * 8], F32)
        c.sb("m8", [128, NT * 8], F32)
        c.sb("est", [128, 2 * NT], F32)
        c.sb("zz8", [128, 8], F32)
        c.sb("sc8", [128, 8], F32)
        c.sb("sc8b", [128, 8], F32)
        c.sb("sc8c", [128, 8], F32)
        c.sb("iota8", [128, 8], F32)
        c.sb("idxm", [128, 64], I32)
        c.sb("rowidx_sb", [32, 1], I32)
        c.sb("g32t", [32, 128], F32)
        c.sb("tid_sb", [128, 1], F32)
        c.t["g1bc"] = c.t["wk1"][:, 0:4096]
        c.b["g1bc"] = c.b["wk1"]
        c.t["b1bc"] = c.t["wk1"][:, 4096:8192]
        c.b["b1bc"] = c.b["wk1"]
        c.sb("wbig", [128, 9 * 2048 + 16 * 1024], BF16)
        c.sb("premask_sb", [8, 1], F32)
    if blk == 1:
        c.sb("q2row", [16, T], F32)
        c.sb("wbig1", [128, 32 * 512], BF16)
    if blk == 2:
        c.sb("wg32", [128, 72], F32)
        c.sb("ltok", [128, NT * 8], F32)
        c.sb("m8", [128, NT * 8], F32)
        c.sb("est", [128, 2 * NT], F32)
        c.sb("zz8", [128, 8], F32)
        c.sb("sc8", [128, 8], F32)
        c.sb("sc8b", [128, 8], F32)
        c.sb("sc8c", [128, 8], F32)
        c.sb("iota8", [128, 8], F32)
        c.sb("tempb", [128, 16], F32)
        c.sb("wbig1", [128, 22 * 512], BF16)
    if blk == 3:
        c.sb("zlng", [128, 2048], F32)
        c.sb("wbig1", [128, 8 * 2048], BF16)
        c.sb("yidx_sb", [128, 32], I32)
    c.ps("psS", [128, 2048], F32)
    c.ps("psB", [128, 1024], F32)
    c.ps("psT", [128, 512], BF16)

    emit_prelude(c, p)
    if blk == 0:
        rsb, rsbB = c.t["a3"], c.b["a3"]
        p.dma("sync", rsb[0:1, 0:8], c.t["iota8_in"][0:1, :],
              reads=(c.b["iota8_in"],), writes=(rsbB,))
        ps, psB = c.t["psB"], c.b["psB"]
        p.mm(ps[:, 0:8], c.t["onesr"][0:1, 0:128], rsb[0:1, 0:8], start=True, stop=True,
             reads=(rsbB, c.b["onesr"]), writes=(psB,))
        p.copy(c.t["iota8"][:, 0:8], ps[:, 0:8], reads=(psB,), writes=(c.b["iota8"],))
        p.dma("sync", c.t["premask_sb"][0:8, 0:1], c.t["premask"][:, :],
              reads=(c.b["premask"],), writes=(c.b["premask_sb"],))
        c.t["premask"], c.b["premask"] = c.t["premask_sb"], c.b["premask_sb"]
        p.dma("sync", c.t["rowidx_sb"][0:32, 0:1], c.t["rowidx"][:, :],
              reads=(c.b["rowidx"],), writes=(c.b["rowidx_sb"],))

    if blk == 3:
        p.dma("sync", c.t["yidx_sb"][:, 0:32], c.t["f3_yidx"][:, :],
              reads=(c.b["f3_yidx"],), writes=(c.b["yidx_sb"],))
    emit_ln_phase(c, p)
    if blk == 0:
        emit_attn(c, p, "l0")
        emit_ffn0(c, p)
    elif blk == 1:
        emit_attn(c, p, "l1")
        emit_ffn1(c, p)
    elif blk == 2:
        emit_attn(c, p, "l2")
        emit_ffn2(c, p)
    else:
        emit_attn(c, p, "l3")
        emit_ffn3(c, p)
    emit_combine(c, p)

    with contextlib.ExitStack() as es2:
        sems = {e: es2.enter_context(nc.semaphore("sem_" + e)) for e in ENGINES}
        block = es2.enter_context(nc.Block())
        p.emit(block, sems)
    es.close()
    return nc



def _host_consts(core):
    ident = np.eye(128, dtype=np.float32)
    cum = np.triu(np.ones((128, 128), np.float32), 1)
    tok_ids = (np.arange(T, dtype=np.float32) + core * T).reshape(T, 1)
    premask = (np.arange(8) < core).astype(np.float32).reshape(8, 1)
    iota8 = np.arange(8, dtype=np.float32).reshape(1, 8)
    rowidx = (core * 32 + np.arange(32, dtype=np.int32)).reshape(32, 1)
    return {"ident_in": ident, "cum_in": cum, "tok_ids": tok_ids,
            "premask": premask, "iota8_in": iota8, "rowidx": rowidx}


def _block0_weights(inputs, core):
    f32 = np.float32
    wqkv = np.asarray(inputs["L0_attn_w_qkv"], f32)
    bqkv = np.asarray(inputs["L0_attn_b_qkv"], f32)
    d = {
        "wq_in": np.ascontiguousarray(wqkv[:, 0:1024]),
        "wk_in": np.ascontiguousarray(wqkv[:, 1024:2048]),
        "wv_in": np.ascontiguousarray(
            np.vstack([wqkv[:, 2048:3072], bqkv[2048:3072][None, :]])),
        "wo_in": np.ascontiguousarray(
            np.vstack([np.asarray(inputs["L0_attn_w_proj"], f32),
                       np.asarray(inputs["L0_attn_b_proj"], f32)[None, :]])),
        "bq_in": np.ascontiguousarray(bqkv[0:1024].reshape(1024, 1)),
        "bk_in": np.ascontiguousarray(bqkv[1024:2048].reshape(1024, 1)),
        "f0_wp_aug": np.ascontiguousarray(
            np.vstack([np.asarray(inputs["L0_ffn_W_proj"], f32),
                       np.asarray(inputs["L0_ffn_b_proj"], f32)[None, :]])),
        "f0_wpT": np.ascontiguousarray(np.asarray(inputs["L0_ffn_W_proj"], f32).T),
        "f0_wg": np.ascontiguousarray(np.asarray(inputs["L0_ffn_W_gate"], f32)),
        "f0_bp": np.ascontiguousarray(
            np.asarray(inputs["L0_ffn_b_proj"], f32).reshape(1024, 1)),
        "f0_w1_aug": np.ascontiguousarray(
            np.vstack([np.asarray(inputs["L0_ffn_W1"], f32)[core],
                       np.asarray(inputs["L0_ffn_b1"], f32)[core][None, :]])),
        "f0_w2_aug": np.ascontiguousarray(
            np.vstack([np.asarray(inputs["L0_ffn_W2"], f32)[core],
                       np.asarray(inputs["L0_ffn_b2"], f32)[core][None, :]])),
        "f0_ln1g": np.asarray(inputs["L0_ffn_ln1_g"], f32)[core][None, :].copy(),
        "f0_ln1b": np.asarray(inputs["L0_ffn_ln1_b"], f32)[core][None, :].copy(),
        "f0_lng": np.asarray(inputs["L0_ffn_ln_g"], f32)[None, :].copy(),
        "f0_lnb": np.asarray(inputs["L0_ffn_ln_b"], f32)[None, :].copy(),
    }
    return d


def shard_x(x):
    out = []
    for c in range(NCORE):
        bb, hh = c // 2, c % 2
        out.append(np.ascontiguousarray(np.asarray(x, np.float32)[bb, hh * T:(hh + 1) * T, :]))
    return out


def unshard_x(shards):
    x = np.empty((B_, S_, D), np.float32)
    for c in range(NCORE):
        bb, hh = c // 2, c % 2
        x[bb, hh * T:(hh + 1) * T, :] = shards[c]
    return x


def _block1_weights(inputs, core):
    f32 = np.float32
    def aug(w, bias):
        return np.ascontiguousarray(np.vstack([np.asarray(w, f32).T,
                                               np.asarray(bias, f32)[None, :]]))
    return {
        "wq_in": aug(inputs["L1_attn_wq"], inputs["L1_attn_bq"]),
        "wk_in": aug(inputs["L1_attn_wk"], inputs["L1_attn_bk"]),
        "wv_in": aug(inputs["L1_attn_wv"], inputs["L1_attn_bv"]),
        "wo_in": aug(inputs["L1_attn_wo"], inputs["L1_attn_bo"]),
        "f1_wfcT": np.ascontiguousarray(np.asarray(inputs["L1_ffn_W_fc"], f32).T),
        "f1_wprojT": np.ascontiguousarray(np.asarray(inputs["L1_ffn_W_proj"], f32).T),
        "f1_lnw": np.asarray(inputs["L1_ffn_ln_w"], f32)[None, :].copy(),
        "f1_lnb": np.asarray(inputs["L1_ffn_ln_b"], f32)[None, :].copy(),
    }


def _block2_weights(inputs, core):
    f32 = np.float32
    qkvT = np.ascontiguousarray(np.asarray(inputs["L2_attn_qkv_w"], f32).T)
    if core < 7:
        wg = np.asarray(inputs["L2_ffn_routed_gate"], f32)[core]
        wu = np.asarray(inputs["L2_ffn_routed_up"], f32)[core]
        wd = np.asarray(inputs["L2_ffn_routed_down"], f32)[core]
    else:
        wg = np.asarray(inputs["L2_ffn_shared_gate"], f32)
        wu = np.asarray(inputs["L2_ffn_shared_up"], f32)
        wd = np.asarray(inputs["L2_ffn_shared_down"], f32)
    sel = np.zeros((1, 8), f32)
    sel[0, min(core, 7)] = 1.0
    return {
        "wq_in": np.ascontiguousarray(qkvT[:, 0:1024]),
        "wk_in": np.ascontiguousarray(qkvT[:, 1024:2048]),
        "wv_in": np.ascontiguousarray(qkvT[:, 2048:3072]),
        "wo_in": np.ascontiguousarray(
            np.vstack([np.asarray(inputs["L2_attn_proj_w"], f32).T,
                       np.asarray(inputs["L2_attn_proj_b"], f32)[None, :]])),
        "l2_temp": np.asarray(inputs["L2_attn_temperature"], f32).reshape(1, 16).copy(),
        "f2_wr": np.ascontiguousarray(np.asarray(inputs["L2_ffn_Wr"], f32)),
        "f2_rb": np.asarray(inputs["L2_ffn_routing_bias"], f32).reshape(7, 1).copy(),
        "f2_sel": sel,
        "f2_wg_w": np.ascontiguousarray(wg),
        "f2_wu_w": np.ascontiguousarray(wu),
        "f2_wd_w": np.ascontiguousarray(wd),
    }


def _block3_weights(inputs, core):
    f32 = np.float32
    wqkv = np.asarray(inputs["L3_attn_w_qkv"], f32).reshape(1024, 16, 64, 3)
    bcore = core // 2
    yidx = np.zeros((128, 32), np.int32)
    for j in range(16):
        npidx = j * 128 + np.arange(128)
        n = npidx // 32
        pp = npidx % 32
        r = n // 8
        yidx[:, j] = r * 1024 + bcore * 256 + (n % 8) * 32 + pp
    for g in range(8):
        rloc = g * 128 + np.arange(128)
        bb = rloc // 256
        rem = rloc % 256
        yidx[:, 16 + g] = bb * 2048 + core * 256 + rem
    return {
        "wq_in": np.ascontiguousarray(wqkv[..., 0].reshape(1024, 1024)),
        "wk_in": np.ascontiguousarray(wqkv[..., 1].reshape(1024, 1024)),
        "wv_in": np.ascontiguousarray(wqkv[..., 2].reshape(1024, 1024)),
        "wo_in": np.ascontiguousarray(
            np.vstack([np.asarray(inputs["L3_attn_w_proj"], f32),
                       np.asarray(inputs["L3_attn_b_proj"], f32)[None, :]])),
        "l3_lns": np.asarray(inputs["L3_attn_ln_scale"], f32)[None, :].copy(),
        "l3_lnb": np.asarray(inputs["L3_attn_ln_bias"], f32)[None, :].copy(),
        "f3_phi": np.ascontiguousarray(np.asarray(inputs["L3_ffn_phi"], f32).reshape(1024, 2048)),
        "f3_w": np.ascontiguousarray(
            np.asarray(inputs["L3_ffn_W"], f32)[core * 8:(core + 1) * 8].reshape(8 * 1024, 1024)),
        "f3_b": np.ascontiguousarray(np.asarray(inputs["L3_ffn_b"], f32)[core * 8:(core + 1) * 8]),
        "f3_yidx": yidx,
    }


_BLOCKS = [0, 1, 2, 3]
FUSED = True


ATTN_WNAMES = {
    0: ("wq_in", "wk_in", "wv_in", "wo_in", "bq_in", "bk_in"),
    1: ("wq_in", "wk_in", "wv_in", "wo_in"),
    2: ("wq_in", "wk_in", "wv_in", "wo_in", "l2_temp", "f2_wr", "f2_rb", "f2_sel",
        "f2_wg_w", "f2_wu_w", "f2_wd_w"),
    3: ("wq_in", "wk_in", "wv_in", "wo_in", "l3_lns", "l3_lnb", "f3_phi", "f3_w",
        "f3_b", "f3_yidx"),
}
WSHAPES = {
    0: {"wq_in": [1024, 1024], "wk_in": [1024, 1024], "wv_in": [1025, 1024],
        "wo_in": [1025, 1024], "bq_in": [1024, 1], "bk_in": [1024, 1]},
    1: {"wq_in": [1025, 1024], "wk_in": [1025, 1024], "wv_in": [1025, 1024],
        "wo_in": [1025, 1024]},
    2: {"wq_in": [1024, 1024], "wk_in": [1024, 1024], "wv_in": [1024, 1024],
        "wo_in": [1025, 1024], "l2_temp": [1, 16], "f2_wr": [1024, 7],
        "f2_rb": [7, 1], "f2_sel": [1, 8], "f2_wg_w": [1024, 2816],
        "f2_wu_w": [1024, 2816], "f2_wd_w": [2816, 1024]},
    3: {"wq_in": [1024, 1024], "wk_in": [1024, 1024], "wv_in": [1024, 1024],
        "wo_in": [1025, 1024], "l3_lns": [1, 1024], "l3_lnb": [1, 1024],
        "f3_phi": [1024, 2048], "f3_w": [8 * 1024, 1024], "f3_b": [8, 1024]},
}


def build_full_program():
    nc = bass.Bass("TRN2", target_bir_lowering=False, debug=False)
    es = contextlib.ExitStack()
    p = Pr(nc)
    c = Ctx(nc, es, p)

    c.dram("x_in0", [T, D], F32, kind="ExternalInput")
    c.dram("x_out3", [T, D], F32, kind="ExternalOutput")
    c.dram("xh1", [T, D], F32)
    c.dram("xh2", [T, D], F32)
    c.dram("xh3", [T, D], F32)
    c.dram("ident_in", [128, 128], F32, kind="ExternalInput")
    c.dram("cum_in", [128, 128], F32, kind="ExternalInput")
    c.dram("xn_tok_d", [T, D], BF16)
    c.dram("xn_fm_d", [1025, T], BF16)
    c.dram("xn32_d", [T, D], F32)
    c.dram("s_d", [T, D], F32)
    c.dram("g_d", [T, D], F32)
    p.fence_bufs = (c.dram("fence_in", [1, 8], F32), c.dram("fence_out", [1, 8], F32))

    KRX = 66
    c.dram("qx_d", [H * KRX, T], BF16)
    c.dram("kx_d", [H * KRX, T], BF16)
    c.dram("vx_d", [T, 1040], BF16)
    c.dram("kg_d", [2 * H * KRX, T], BF16)
    c.dram("vg_d", [2 * T, 1040], BF16)
    c.dram("ofm_d", [1024, T], BF16)

    for blk in range(4):
        for nm in ATTN_WNAMES[blk]:
            if nm == "f3_yidx":
                c.dram(f"b{blk}_{nm}", [128, 32], I32, kind="ExternalInput")
            else:
                c.dram(f"b{blk}_{nm}", WSHAPES[blk][nm], F32, kind="ExternalInput")
    for nm, shp in (("f0_wp_aug", [1025, 1024]), ("f0_wpT", [1024, 1024]),
                    ("f0_wg", [1024, 8]), ("f0_bp", [1024, 1]),
                    ("f0_w1_aug", [1025, 4096]), ("f0_w2_aug", [4097, 1024]),
                    ("f0_ln1g", [1, 4096]), ("f0_ln1b", [1, 4096]),
                    ("f0_lng", [1, 1024]), ("f0_lnb", [1, 1024]),
                    ("tok_ids", [T, 1]), ("premask", [8, 1]), ("iota8_in", [1, 8])):
        c.dram(nm, shp, F32, kind="ExternalInput")
    c.dram("rowidx", [32, 1], I32, kind="ExternalInput")
    for nm, shp in (("f1_wfcT", [1024, 4096]), ("f1_wprojT", [4096, 1024]),
                    ("f1_lnw", [1, 4096]), ("f1_lnb", [1, 4096])):
        c.dram(nm, shp, F32, kind="ExternalInput")
    c.dram("xflat_d", [T, D], BF16)
    c.dram("xflatall_d", [8 * T, D], BF16)
    c.dram("cnt_d", [1, 8], F32)
    c.dram("cntall_d", [8, 8], F32)
    c.dram("idxsc_d", [8 * 4096, 1], F32)
    c.dram("idxall_d", [8 * 4096, 1], F32)
    c.dram("hh_d", [4096, 4096], F32)
    c.dram("gel_d", [4096, 4096], BF16)
    c.dram("slots_d", [4096, D], BF16)
    c.dram("slotsall_d", [8 * 4096, D], BF16)
    c.dram("f1_wq1_d", [1024, 4096], BF16)
    c.dram("f1_wq2_d", [4096, 1024], BF16)
    c.dram("f1_h_d", [T, 4096], F32)
    c.dram("f1_hnf_d", [4096, T], BF16)
    c.dram("xnfm_all_d", [8 * 1025, T], BF16)
    c.dram("wx_d", [T, 8], F32)
    c.dram("wall_d", [8 * T, 8], F32)
    c.dram("f2_mid_d", [8 * T, 2816], BF16)
    c.dram("f2_midf_d", [2816, 8 * T], BF16)
    c.dram("f2_out_d", [8 * T, 1024], F32)
    c.dram("zn_fm_d", [1025, T], BF16)
    c.dram("l_d", [T, 2048], F32)
    c.dram("bm_d", [T, 1], F32)
    c.dram("bmax_d", [T, 1], F32)
    c.dram("e_d", [T, 2048], F32)
    c.dram("esum_d", [T, 2048], F32)
    c.dram("d_d", [T, 2048], BF16)
    c.dram("slf_d", [2048, 1024], F32)
    c.dram("slfsum_d", [2048, 1024], F32)
    c.dram("slall_d", [8192, 1024], F32)
    c.dram("y_loc_d", [1024, 1024], BF16)
    c.dram("yall_d", [8192, 1024], BF16)
    c.dram("yb_d", [T, 1024], BF16)

    c.sb("cst", [128, 1], F32)
    nc.const_aps.register(F32, 1e-5, c.t["cst"][:, 0:1])
    c.sb("ident", [128, 128], BF16)
    c.sb("identf", [128, 128], F32)
    c.sb("cum", [128, 128], F32)
    c.sb("onesr", [1, 512], F32)
    c.sb("onesrb", [1, T], BF16)
    c.sb("onesc", [128, 1], F32)
    c.sb("onescb", [128, 1], BF16)
    c.sb("stat", [128, 8], F32)
    c.sb("lnsq", [128, 4096], F32)
    c.sb("a1", [128, 4096], F32)
    c.sb("a2", [128, 2048], F32)
    c.sb("a3", [128, 1024], F32)
    c.sb("rsum", [1, 256], F32)
    c.sb("w16a", [128, 4096], BF16)
    c.sb("w16b", [128, 4096], BF16)
    c.sb("w16c", [128, 1024], BF16)
    c.sb("wk1", [128, 9216], BF16)
    c.sb("xf", [128, 512], BF16)
    c.sb("kf", [128, 2 * T], BF16)
    c.sb("qh", [128, T], BF16)
    c.sb("vh", [128, NKC * 65], BF16)
    c.sb("pb", [128, NKC * QT], BF16)
    c.sb("ostg", [64, 256], BF16)
    c.sb("bb", [128, H], F32)
    c.sb("row4", [1, 64], F32)
    c.sb("bqk", [128, 16], F32)
    c.sb("gf32", [128, 80], F32)
    c.sb("wf32s", [128, 1024], F32)
    c.sb("wg32", [128, 72], F32)
    c.sb("ltok", [128, NT * 8], F32)
    c.sb("m8", [128, NT * 8], F32)
    c.sb("est", [128, 2 * NT], F32)
    c.sb("zz8", [128, 8], F32)
    c.sb("sc8", [128, 8], F32)
    c.sb("sc8b", [128, 8], F32)
    c.sb("sc8c", [128, 8], F32)
    c.sb("iota8", [128, 8], F32)
    c.sb("idxm", [128, 64], I32)
    c.sb("rowidx_sb", [32, 1], I32)
    c.sb("g32t", [32, 128], F32)
    c.sb("tid_sb", [128, 1], F32)
    c.t["g1bc"] = c.t["wk1"][:, 0:4096]
    c.b["g1bc"] = c.b["wk1"]
    c.t["b1bc"] = c.t["wk1"][:, 4096:8192]
    c.b["b1bc"] = c.b["wk1"]
    c.sb("premask_sb", [8, 1], F32)
    c.sb("wbig", [128, 9 * 2048 + 16 * 1024], BF16)
    c.t["wbig1"] = c.t["wbig"][:, 0:16 * 1024]
    c.b["wbig1"] = c.b["wbig"]
    c.sb("aux8k", [128, 2048], F32)
    c.t["q2row"] = c.t["aux8k"][0:16, 0:T]
    c.b["q2row"] = c.b["aux8k"]
    c.t["zlng"] = c.t["aux8k"]
    c.b["zlng"] = c.b["aux8k"]
    c.sb("tempb", [128, 16], F32)
    c.sb("yidx_sb", [128, 32], I32)
    c.ps("psS", [128, 2048], F32)
    c.ps("psB", [128, 1024], F32)
    c.ps("psT", [128, 512], BF16)

    emit_prelude(c, p)
    rsb, rsbB = c.t["a3"], c.b["a3"]
    p.dma("sync", rsb[0:1, 0:8], c.t["iota8_in"][0:1, :], reads=(c.b["iota8_in"],), writes=(rsbB,))
    ps, psB = c.t["psB"], c.b["psB"]
    p.mm(ps[:, 0:8], c.t["onesr"][0:1, 0:128], rsb[0:1, 0:8], start=True, stop=True,
         reads=(rsbB, c.b["onesr"]), writes=(psB,))
    p.copy(c.t["iota8"][:, 0:8], ps[:, 0:8], reads=(psB,), writes=(c.b["iota8"],))
    p.dma("sync", c.t["premask_sb"][0:8, 0:1], c.t["premask"][:, :],
          reads=(c.b["premask"],), writes=(c.b["premask_sb"],))
    c.t["premask"], c.b["premask"] = c.t["premask_sb"], c.b["premask_sb"]
    p.dma("sync", c.t["rowidx_sb"][0:32, 0:1], c.t["rowidx"][:, :],
          reads=(c.b["rowidx"],), writes=(c.b["rowidx_sb"],))

    xins = ["x_in0", "xh1", "xh2", "xh3"]
    xouts = ["xh1", "xh2", "xh3", "x_out3"]
    variants = ["l0", "l1", "l2", "l3"]
    iota_tok = None
    for blk in range(4):
        c.t["x_in"], c.b["x_in"] = c.t[xins[blk]], c.b[xins[blk]]
        c.t["x_out"], c.b["x_out"] = c.t[xouts[blk]], c.b[xouts[blk]]
        for nm in ATTN_WNAMES[blk]:
            c.t[nm], c.b[nm] = c.t[f"b{blk}_{nm}"], c.b[f"b{blk}_{nm}"]
        if blk == 3:
            p.dma("sync", c.t["yidx_sb"][:, 0:32], c.t["f3_yidx"][:, :],
                  reads=(c.b["f3_yidx"],), writes=(c.b["yidx_sb"],))
        emit_ln_phase(c, p)
        emit_attn(c, p, variants[blk])
        [emit_ffn0, emit_ffn1, emit_ffn2, emit_ffn3][blk](c, p)
        emit_combine(c, p)

    with contextlib.ExitStack() as es2:
        sems = {e: es2.enter_context(nc.semaphore("sem_" + e)) for e in ENGINES}
        block = es2.enter_context(nc.Block())
        p.emit(block, sems)
    es.close()
    return nc


def kernel(**inputs):
    xs = shard_x(inputs["L0_attn_x"])
    if FUSED and _BLOCKS == [0, 1, 2, 3]:
        nc = build_full_program()
        wfns = {0: _block0_weights, 1: _block1_weights, 2: _block2_weights,
                3: _block3_weights}
        in_maps = []
        for core in range(NCORE):
            m = {"x_in0": xs[core]}
            m.update(_host_consts(core))
            for blk in range(4):
                wd = wfns[blk](inputs, core)
                for k, v in wd.items():
                    m[f"b{blk}_{k}" if k in ATTN_WNAMES[blk] else k] = v
            in_maps.append(m)
        res = run_bass_kernel_spmd(nc, in_maps, core_ids=list(range(NCORE)))
        return unshard_x([np.asarray(r["x_out3"], np.float32) for r in res.results])
    for blk in _BLOCKS:
        nc = build_block_program(blk)
        wfn = {0: _block0_weights, 1: _block1_weights, 2: _block2_weights,
               3: _block3_weights}[blk]
        in_maps = []
        for core in range(NCORE):
            m = {"x_in": xs[core]}
            m.update(_host_consts(core))
            m.update(wfn(inputs, core))
            in_maps.append(m)
        res = run_bass_kernel_spmd(nc, in_maps, core_ids=list(range(NCORE)))
        xs = [np.asarray(r["x_out"], np.float32) for r in res.results]
    return unshard_x(xs)


# version 04
# speedup vs baseline: 1.7706x; 1.7706x over previous
import contextlib
import numpy as np
import ml_dtypes
import concourse.bass as bass
import concourse.mybir as mybir
from concourse.bass_utils import run_bass_kernel_spmd

F32 = mybir.dt.float32
BF16 = mybir.dt.bfloat16
I32 = mybir.dt.int32
U32 = mybir.dt.uint32
AF = mybir.ActivationFunctionType
ALU = mybir.AluOpType
AX = mybir.AxisListType
IOff = bass.IndirectOffsetOnAxis
BFNP = ml_dtypes.bfloat16

NCORE = 8
B_, S_, D = 4, 4096, 1024
T = 2048
NT = 16
NKC = 32
H, HD = 16, 64
QT = 256
NQT = T // QT
ENGINES = ("sync", "tensor", "vector", "scalar", "gpsimd")
PAIRS = [[0, 1], [2, 3], [4, 5], [6, 7]]
ALL8 = [list(range(8))]
GRP4 = [[0, 2, 4, 6], [1, 3, 5, 7]]


class Buf:
    __slots__ = ("name", "w", "r")

    def __init__(self, name):
        self.name = name
        self.w = None
        self.r = []


class Pr:
    def __init__(self, nc):
        self.nc = nc
        self.q = {e: [] for e in ENGINES}
        self.cnt = {e: 0 for e in ENGINES}
        self.bnd_regs = {}
        self.last_coll = None

    def _bound_reg(self, e, val):
        if val not in self.bnd_regs:
            self.bnd_regs[val] = e.to_reg(int(val))
        return self.bnd_regs[val]

    def op(self, eng, fn, reads=(), writes=(), inc=1, extra_deps=()):
        deps = list(extra_deps)
        for b in reads:
            if b.w is not None:
                deps.append(b.w)
        for b in writes:
            if b.w is not None:
                deps.append(b.w)
            deps.extend(b.r)
        dd = {}
        for (de, dv) in deps:
            if eng == "tensor" and de == "tensor":
                continue
            if dd.get(de, 0) < dv:
                dd[de] = dv
        self.cnt[eng] += inc
        tok = (eng, self.cnt[eng])
        self.q[eng].append((fn, tuple(dd.items()), inc))
        for b in writes:
            b.w = tok
            b.r = []
        for b in reads:
            b.r.append(tok)
        return tok

    def dma(self, eng, out, in_, reads=(), writes=(), **kw):
        try:
            if out.tensor.dtype != in_.tensor.dtype:
                eng = "gpsimd"
        except AttributeError:
            pass
        return self.op(eng, lambda e: e.dma_start(out=out, in_=in_, **kw),
                       reads=reads, writes=writes, inc=16)

    def mm(self, out, lhsT, rhs, start, stop, reads=(), writes=()):
        return self.op("tensor",
                       lambda e: e.matmul(out, lhsT, rhs, start=start, stop=stop),
                       reads=reads, writes=writes)

    def tr(self, out, in_, ident, reads=(), writes=()):
        return self.op("tensor", lambda e: e.transpose(out, in_, ident),
                       reads=reads, writes=writes)

    def act(self, out, in_, func, reads=(), writes=(), bias=0.0, scale=1.0):
        return self.op("scalar",
                       lambda e: e.activation(out, in_, func, bias=bias, scale=scale),
                       reads=reads, writes=writes)

    def tt(self, out, a, b2, alu, reads=(), writes=()):
        return self.op("vector", lambda e: e.tensor_tensor(out, a, b2, alu),
                       reads=reads, writes=writes)

    def ts(self, out, a, s1, s2, op0, op1=None, reads=(), writes=()):
        if s2 is None:
            return self.op("vector", lambda e: e.tensor_scalar(out, a, s1, None, op0),
                           reads=reads, writes=writes)
        return self.op("vector", lambda e: e.tensor_scalar(out, a, s1, s2, op0, op1),
                       reads=reads, writes=writes)

    def red(self, out, in_, alu, reads=(), writes=(), absval=None):
        return self.op("vector",
                       lambda e: e.tensor_reduce(out, in_, AX.X, alu,
                                                 apply_absolute_value=absval),
                       reads=reads, writes=writes)

    def copy(self, out, in_, reads=(), writes=()):
        return self.op("vector", lambda e: e.tensor_copy(out=out, in_=in_),
                       reads=reads, writes=writes)

    def memset(self, ap, v, writes=(), reads=()):
        return self.op("vector", lambda e: e.memset(ap, v), reads=reads, writes=writes)

    def coll(self, kind, alu, groups, in_ap, out_ap, reads=(), writes=()):
        extra = [self.last_coll] if self.last_coll is not None else []
        self.op("gpsimd", lambda e: e.collective_compute(
            kind, alu, replica_groups=groups, ins=[in_ap], outs=[out_ap]),
            reads=reads, writes=(), inc=1, extra_deps=extra)
        fence_in, fence_out = self.fence_bufs
        tok = self.op("gpsimd", lambda e: e.collective_compute(
            "AllReduce", ALU.add, replica_groups=groups,
            ins=[fence_in.ap()], outs=[fence_out.ap()]),
            reads=reads, writes=writes, inc=1)
        self.last_coll = tok
        return tok

    def igather(self, out, in_, idx_ap, bound, reads=(), writes=()):
        def f(e):
            return e.indirect_dma_start(
                out=out, out_offset=None, in_=in_, in_offset=IOff(ap=idx_ap, axis=0),
                bounds_check=self._bound_reg(e, bound), oob_is_err=False)
        return self.op("gpsimd", f, reads=reads, writes=writes, inc=16)

    def iscatter(self, out, idx_ap, in_, bound, reads=(), writes=()):
        def f(e):
            return e.indirect_dma_start(
                out=out, out_offset=IOff(ap=idx_ap, axis=0), in_=in_, in_offset=None,
                bounds_check=self._bound_reg(e, bound), oob_is_err=False)
        return self.op("gpsimd", f, reads=reads, writes=writes, inc=16)

    def emit(self, block, sems):
        def mk(eng_name):
            def run(e):
                wm = {f: 0 for f in ENGINES}
                for _qi, (fn, deps, inc) in enumerate(self.q[eng_name]):
                    for (de, dv) in deps:
                        if dv > wm[de]:
                            e.wait_ge(sems[de], dv)
                            wm[de] = dv
                    try:
                        fn(e).then_inc(sems[eng_name], inc)
                    except Exception:
                        print("EMIT FAIL engine", eng_name, "queue index", _qi,
                              "of", len(self.q[eng_name]))
                        raise
            return run
        block.sync(mk("sync"))
        block.tensor(mk("tensor"))
        block.vector(mk("vector"))
        block.scalar(mk("scalar"))
        block.gpsimd(mk("gpsimd"))


class Ctx:
    def __init__(self, nc, es, prog):
        self.nc = nc
        self.es = es
        self.p = prog
        self.t = {}
        self.b = {}

    def sb(self, name, shape, dtype):
        self.t[name] = self.es.enter_context(self.nc.sbuf_tensor(name, shape, dtype))
        self.b[name] = Buf(name)
        return self.t[name]

    def ps(self, name, shape, dtype=F32):
        self.t[name] = self.es.enter_context(self.nc.psum_tensor(name, shape, dtype))
        self.b[name] = Buf(name)
        return self.t[name]

    def psbank(self, j):
        return self.t["psS"][:, j * 512:(j + 1) * 512], self.b["psS%d" % j]

    def dram(self, name, shape, dtype, kind=None):
        if kind:
            self.t[name] = self.nc.dram_tensor(name, shape, dtype, kind=kind)
        else:
            self.t[name] = self.nc.dram_tensor(name, shape, dtype)
        self.b[name] = Buf(name)
        return self.t[name]

    def tb(self, name):
        return self.t[name], self.b[name]



def emit_prelude(c, p):
    t, b = c.t, c.b
    p.memset(t["cst"][:, 0:1], 1e-5, writes=(b["cst"],))
    p.dma("sync", t["ident"][:, :], t["ident_in"][:, :], reads=(b["ident_in"],), writes=(b["ident"],))
    p.dma("sync", t["cum"][:, :], t["cum_in"][:, :], reads=(b["cum_in"],), writes=(b["cum"],))
    p.dma("gpsimd", t["identf"][:, :], t["ident_in"][:, :], reads=(b["ident_in"],), writes=(b["identf"],))
    p.memset(t["onesr"][:, :], 1.0, writes=(b["onesr"],))
    p.memset(t["onesrb"][:, :], 1.0, writes=(b["onesrb"],))
    p.memset(t["onesc"][:, :], 1.0, writes=(b["onesc"],))
    p.memset(t["onescb"][:, :], 1.0, writes=(b["onescb"],))


def emit_ln_phase(c, p):
    t, b = c.t, c.b
    for tt in range(NT):
        r = slice(tt * 128, (tt + 1) * 128)
        xa, xaB = t["a1"], b["a1"]
        p.dma("sync", xa[:, 0:D], t["x_in"][r, :], reads=(b["x_in"],), writes=(xaB,))
        emit_ln_stats(c, p, xa[:, 0:D], (xaB,), D, 1e-5)
        st, stB = t["stat"], b["stat"]
        xn, xnB = t["a2"], b["a2"]
        p.ts(xn[:, 0:D], xa[:, 0:D], st[:, 0:1], st[:, 3:4], ALU.subtract, ALU.mult,
             reads=(xaB, stB), writes=(xnB,))
        p.dma("gpsimd", t["xn32_d"][r, :], xn[:, 0:D], reads=(xnB,), writes=(b["xn32_d"],))
        xbb, xbbB = t["w16a"], b["w16a"]
        p.copy(xbb[:, 0:D], xn[:, 0:D], reads=(xnB,), writes=(xbbB,))
        p.dma("gpsimd", t["xn_tok_d"][r, :], xbb[:, 0:D], reads=(xbbB,), writes=(b["xn_tok_d"],))
        for ch in range(8):
            pt, ptB = t["psT"], b["psT"]
            p.tr(pt[:, 0:128], xbb[:, ch * 128:(ch + 1) * 128], t["ident"][:, :],
                 reads=(xbbB, b["ident"]), writes=(ptB,))
            o2, o2B = t["w16b"], b["w16b"]
            p.copy(o2[:, ch * 128:(ch + 1) * 128], pt[:, 0:128], reads=(ptB,), writes=(o2B,))
        for ch in range(8):
            p.dma("gpsimd", t["xn_fm_d"][ch * 128:(ch + 1) * 128, tt * 128:(tt + 1) * 128],
                  t["w16b"][:, ch * 128:(ch + 1) * 128], reads=(b["w16b"],), writes=(b["xn_fm_d"],))
    p.dma("gpsimd", t["xn_fm_d"][1024:1025, :], t["onesrb"][0:1, 0:T],
          reads=(b["onesrb"],), writes=(b["xn_fm_d"],))


def emit_ln_stats(c, p, src_ap, src_reads, n, eps):
    t, b = c.t, c.b
    st, stB = t["stat"], b["stat"]
    sq, sqB = t["lnsq"], b["lnsq"]
    p.red(st[:, 0:1], src_ap, ALU.add, reads=src_reads, writes=(stB,))
    p.ts(st[:, 0:1], st[:, 0:1], 1.0 / n, None, ALU.mult, reads=(stB,), writes=(stB,))
    p.act(sq[:, 0:n], src_ap, AF.Square, reads=src_reads, writes=(sqB,))
    p.red(st[:, 1:2], sq[:, 0:n], ALU.add, reads=(sqB,), writes=(stB,))
    p.ts(st[:, 1:2], st[:, 1:2], 1.0 / n, None, ALU.mult, reads=(stB,), writes=(stB,))
    p.act(sq[:, 0:1], st[:, 0:1], AF.Square, reads=(stB,), writes=(sqB,))
    p.tt(st[:, 1:2], st[:, 1:2], sq[:, 0:1], ALU.subtract, reads=(sqB, stB), writes=(stB,))
    p.act(st[:, 2:3], st[:, 1:2], AF.Sqrt, bias=float(eps), reads=(stB, c.b["cst"]), writes=(stB,))
    p.op("vector", lambda e, st=st: e.reciprocal(st[:, 3:4], st[:, 2:3]),
         reads=(stB,), writes=(stB,))


def load_wchunks(c, p, dst, dstB, src_name, rows, cols, dst_col0=0):
    t, b = c.t, c.b
    src, srcB = t[src_name], b[src_name]
    nch = (rows + 127) // 128
    for i in range(nch):
        r0, r1 = i * 128, min((i + 1) * 128, rows)
        p.dma("gpsimd", dst[0:r1 - r0, dst_col0 + i * cols: dst_col0 + i * cols + cols],
              src[r0:r1, :], reads=(srcB,), writes=(dstB,))


def fm_ld(c, p, dst, dstB, ch, c0, c1, src="xn_fm_d"):
    t, b = c.t, c.b
    rows = 1 if ch == 8 else 128
    p.dma("sync", dst[0:rows, 0:c1 - c0], t[src][ch * 128:ch * 128 + rows, c0:c1],
          reads=(b[src],), writes=(dstB,))



def emit_zn_phase(c, p):
    t, b = c.t, c.b
    for name, dst_lo in (("a3_lns", 0), ("a3_lnb", 1024)):
        pass
    gb, gbB = t["w16c"], b["w16c"]
    rsb, rsbB = t["lnsq"], b["lnsq"]
    p.dma("sync", rsb[0:1, 0:1024], t["l3_lns"][0:1, :], reads=(b["l3_lns"],), writes=(rsbB,))
    p.dma("sync", rsb[0:1, 1024:2048], t["l3_lnb"][0:1, :], reads=(b["l3_lnb"],), writes=(rsbB,))
    gbc, gbcB = t["zlng"], b["zlng"]
    for j in range(4):
        ps, psB = t["psB"], b["psB"]
        p.mm(ps[:, 0:512], t["onesr"][0:1, 0:128], rsb[0:1, j * 512:(j + 1) * 512],
             start=True, stop=True, reads=(rsbB, b["onesr"]), writes=(psB,))
        p.copy(gbc[:, j * 512:(j + 1) * 512], ps[:, 0:512], reads=(psB,), writes=(gbcB,))
    for tt in range(NT):
        r = slice(tt * 128, (tt + 1) * 128)
        xa, xaB = t["a1"], b["a1"]
        p.dma("sync", xa[:, 0:D], t["xn32_d"][r, :], reads=(b["xn32_d"],), writes=(xaB,))
        emit_ln_stats(c, p, xa[:, 0:D], (xaB,), D, 1e-5)
        st, stB = t["stat"], b["stat"]
        zn, znB = t["a2"], b["a2"]
        p.ts(zn[:, 0:D], xa[:, 0:D], st[:, 0:1], st[:, 3:4], ALU.subtract, ALU.mult,
             reads=(xaB, stB), writes=(znB,))
        p.tt(zn[:, 0:D], zn[:, 0:D], gbc[:, 0:1024], ALU.mult, reads=(gbcB,), writes=(znB,))
        p.tt(zn[:, 0:D], zn[:, 0:D], gbc[:, 1024:2048], ALU.add, reads=(gbcB,), writes=(znB,))
        xbb, xbbB = t["w16a"], b["w16a"]
        p.copy(xbb[:, 0:D], zn[:, 0:D], reads=(znB,), writes=(xbbB,))
        for ch in range(8):
            pt, ptB = t["psT"], b["psT"]
            p.tr(pt[:, 0:128], xbb[:, ch * 128:(ch + 1) * 128], t["ident"][:, :],
                 reads=(xbbB, b["ident"]), writes=(ptB,))
            o2, o2B = t["w16b"], b["w16b"]
            p.copy(o2[:, ch * 128:(ch + 1) * 128], pt[:, 0:128], reads=(ptB,), writes=(o2B,))
        for ch in range(8):
            p.dma("gpsimd", t["zn_fm_d"][ch * 128:(ch + 1) * 128, tt * 128:(tt + 1) * 128],
                  t["w16b"][:, ch * 128:(ch + 1) * 128], reads=(b["w16b"],), writes=(b["zn_fm_d"],))
    p.dma("gpsimd", t["zn_fm_d"][1024:1025, :], t["onesrb"][0:1, 0:T],
          reads=(b["onesrb"],), writes=(b["zn_fm_d"],))


def emit_attn(c, p, variant):
    t, b = c.t, c.b
    KR = 66 if variant == "l1" else 64
    if variant == "l3":
        emit_zn_phase(c, p)

    _attn_qkv_produce(c, p, variant)

    p.coll("AllGather", ALU.bypass, PAIRS, t["kx_d"][0:H * KR, :], t["kg_d"][0:2 * H * KR, :],
           reads=(b["kx_d"],), writes=(b["kg_d"],))
    p.coll("AllGather", ALU.bypass, PAIRS, t["vx_d"].ap(), t["vg_d"].ap(),
           reads=(b["vx_d"],), writes=(b["vg_d"],))

    if DEBUG and "dbg_qx" in t:
        p.dma("gpsimd", t["dbg_vx"][:, :], t["vx_d"].ap(), reads=(b["vx_d"],), writes=(b["dbg_vx"],))
        p.dma("gpsimd", t["dbg_kx"][:, :], t["kx_d"].ap(), reads=(b["kx_d"],), writes=(b["dbg_kx"],))
        p.dma("gpsimd", t["dbg_qx"][:, :], t["qx_d"].ap(), reads=(b["qx_d"],), writes=(b["dbg_qx"],))
        p.dma("gpsimd", t["dbg_kg"][:, :], t["kg_d"].ap(), reads=(b["kg_d"],), writes=(b["dbg_kg"],))
        p.dma("gpsimd", t["dbg_vg"][:, :], t["vg_d"].ap(), reads=(b["vg_d"],), writes=(b["dbg_vg"],))
    if variant == "l1":
        p.memset(t["bb"][:, 0:H], 0.0, writes=(b["bb"],))
    else:
        _attn_bounds(c, p, KR)

    _attn_core(c, p, variant, KR)
    _attn_epilogue(c, p, variant)


def _attn_qkv_produce(c, p, variant):
    if variant in ("l0", "l3"):
        _qkv_fm_direct(c, p, variant)
    else:
        _qkv_tokmajor(c, p, variant)
    _v_produce(c, p, variant)


def _qkv_fm_direct(c, p, variant):
    t, b = c.t, c.b
    scale_q = 0.125
    has_bias = (variant == "l0")
    if has_bias:
        for i, nm in enumerate(("bq_in", "bk_in")):
            for ch in range(8):
                p.dma("sync", t["bqk"][:, i * 8 + ch:i * 8 + ch + 1],
                      t[nm][ch * 128:(ch + 1) * 128, :], reads=(b[nm],), writes=(b["bqk"],))
        p.ts(t["bqk"][:, 0:8], t["bqk"][:, 0:8], scale_q, None, ALU.mult,
             reads=(b["bqk"],), writes=(b["bqk"],))
    fmsrc = "zn_fm_d" if variant == "l3" else "xn_fm_d"
    for which, wname, dram_out, scl in (("q", "wq_in", "qx_d", scale_q),
                                        ("k", "wk_in", "kx_d", 1.0)):
        wsb, wsbB = t["wk1"], b["wk1"]
        load_wchunks(c, p, wsb, wsbB, wname, 1024, 1024)
        for ho in range(8):
            stg, stgB = t["w16a"], b["w16a"]
            for jt in range(T // 512):
                ps, psB = t["psB"], b["psB"]
                for ch in range(8):
                    xf, xfB = t["xf"], b["xf"]
                    fm_ld(c, p, xf, xfB, ch, jt * 512, (jt + 1) * 512, src=fmsrc)
                    p.mm(ps[:, 0:512], wsb[:, ch * 1024 + ho * 128: ch * 1024 + (ho + 1) * 128],
                         xf[0:128, 0:512], start=(ch == 0), stop=(ch == 7),
                         reads=(wsbB, xfB), writes=(psB,))
                bias_ap = t["bqk"][:, (0 if which == "q" else 8) + ho:
                                   (0 if which == "q" else 8) + ho + 1] if has_bias else 0.0
                extra = (b["bqk"],) if has_bias else ()
                p.act(stg[:, jt * 512:(jt + 1) * 512], ps[:, 0:512], AF.Identity,
                      bias=bias_ap, scale=(scl if which == "q" else 1.0),
                      reads=(psB,) + extra, writes=(stgB,))
            h0, h1 = 2 * ho, 2 * ho + 1
            p.dma("gpsimd", t[dram_out][h0 * 64:h0 * 64 + 64, :], stg[0:64, 0:T],
                  reads=(stgB,), writes=(b[dram_out],))
            p.dma("gpsimd", t[dram_out][h1 * 64:h1 * 64 + 64, :], stg[64:128, 0:T],
                  reads=(stgB,), writes=(b[dram_out],))


def _qkv_tokmajor(c, p, variant):
    t, b = c.t, c.b
    KR = 66 if variant == "l1" else 64
    if variant == "l2":
        rsb, rsbB = t["lnsq"], b["lnsq"]
        p.dma("sync", rsb[0:1, 0:16], t["l2_temp"][0:1, :], reads=(b["l2_temp"],), writes=(rsbB,))
        p.op("vector", lambda e, rsb=rsb: e.reciprocal(rsb[0:1, 16:32], rsb[0:1, 0:16]),
             reads=(rsbB,), writes=(rsbB,))
        ps, psB = t["psB"], b["psB"]
        p.mm(ps[:, 0:16], t["onesr"][0:1, 0:128], rsb[0:1, 16:32], start=True, stop=True,
             reads=(rsbB, b["onesr"]), writes=(psB,))
        tb_, tbB = t["tempb"], b["tempb"]
        p.copy(tb_[:, 0:16], ps[:, 0:16], reads=(psB,), writes=(tbB,))
    for which, wname, dram_out in (("q", "wq_in", "qx_d"), ("k", "wk_in", "kx_d")):
        wsb, wsbB = t["wk1"], b["wk1"]
        rows = 1025 if variant == "l1" else 1024
        load_wchunks(c, p, wsb, wsbB, wname, rows, 1024)
        for tt in range(NT):
            qt_, qtB = t["w16a"], b["w16a"]
            for j2 in range(2):
                ps, psB = t["psB"], b["psB"]
                for ch in range(8):
                    xf, xfB = t["xf"], b["xf"]
                    fm_ld(c, p, xf, xfB, ch, tt * 128, (tt + 1) * 128)
                    p.mm(ps[:, 0:512], xf[0:128, 0:128],
                         wsb[:, ch * 1024 + j2 * 512: ch * 1024 + (j2 + 1) * 512],
                         start=(ch == 0), stop=(ch == 7 and variant != "l1"),
                         reads=(wsbB, xfB), writes=(psB,))
                if variant == "l1":
                    xf, xfB = t["xf"], b["xf"]
                    fm_ld(c, p, xf, xfB, 8, tt * 128, (tt + 1) * 128)
                    p.mm(ps[:, 0:512], xf[0:1, 0:128],
                         wsb[0:1, 8 * 1024 + j2 * 512: 8 * 1024 + (j2 + 1) * 512],
                         start=False, stop=True, reads=(wsbB, xfB), writes=(psB,))
                p.copy(qt_[:, j2 * 512:(j2 + 1) * 512], ps[:, 0:512], reads=(psB,), writes=(qtB,))
            if variant == "l1":
                sq, sqB = t["lnsq"], b["lnsq"]
                p.act(sq[:, 0:1024], qt_[:, 0:1024], AF.Square, reads=(qtB,), writes=(sqB,))
                q2, q2B = t["a3"], b["a3"]
                for h in range(H):
                    p.red(q2[:, h:h + 1], sq[:, h * 64:(h + 1) * 64], ALU.add,
                          reads=(sqB,), writes=(q2B,))
                pt, ptB = t["psB"], b["psB"]
                p.tr(pt[0:16, 0:128], q2[:, 0:16], t["identf"][:, :],
                     reads=(q2B, b["identf"]), writes=(ptB,))
                p.copy(t["q2row"][0:16, tt * 128:(tt + 1) * 128], pt[0:16, 0:128],
                       reads=(ptB,), writes=(b["q2row"],))
                if which == "q":
                    p.ts(qt_[:, 0:1024], qt_[:, 0:1024], -2.0, None, ALU.mult,
                         reads=(qtB,), writes=(qtB,))
            else:
                sq, sqB = t["lnsq"], b["lnsq"]
                p.act(sq[:, 0:1024], qt_[:, 0:1024], AF.Square, reads=(qtB,), writes=(sqB,))
                st_, stB = t["stat"], b["stat"]
                p.red(st_[:, 0:1], sq[:, 0:1024], ALU.add, reads=(sqB,), writes=(stB,))
                p.act(st_[:, 1:2], st_[:, 0:1], AF.Sqrt, reads=(stB,), writes=(stB,))
                p.ts(st_[:, 1:2], st_[:, 1:2], 1e-12, None, ALU.max, reads=(stB,), writes=(stB,))
                p.op("vector", lambda e, st_=st_: e.reciprocal(st_[:, 2:3], st_[:, 1:2]),
                     reads=(stB,), writes=(stB,))
                p.ts(st_[:, 2:3], st_[:, 2:3], 0.125, None, ALU.mult, reads=(stB,), writes=(stB,))
                p.ts(qt_[:, 0:1024], qt_[:, 0:1024], st_[:, 2:3], None, ALU.mult,
                     reads=(stB,), writes=(qtB,))
                if which == "q":
                    for h in range(H):
                        p.ts(qt_[:, h * 64:(h + 1) * 64], qt_[:, h * 64:(h + 1) * 64],
                             t["tempb"][:, h:h + 1], None, ALU.mult,
                             reads=(b["tempb"],), writes=(qtB,))
            for ch in range(8):
                pt, ptB = t["psT"], b["psT"]
                p.tr(pt[:, 0:128], qt_[:, ch * 128:(ch + 1) * 128], t["ident"][:, :],
                     reads=(qtB, b["ident"]), writes=(ptB,))
                o2, o2B = t["w16b"], b["w16b"]
                p.copy(o2[:, ch * 128:(ch + 1) * 128], pt[:, 0:128], reads=(ptB,), writes=(o2B,))
            for ch in range(8):
                h0, h1 = 2 * ch, 2 * ch + 1
                p.dma("gpsimd", t[dram_out][h0 * KR:h0 * KR + 64, tt * 128:(tt + 1) * 128],
                      t["w16b"][0:64, ch * 128:(ch + 1) * 128], reads=(b["w16b"],), writes=(b[dram_out],))
                p.dma("gpsimd", t[dram_out][h1 * KR:h1 * KR + 64, tt * 128:(tt + 1) * 128],
                      t["w16b"][64:128, ch * 128:(ch + 1) * 128], reads=(b["w16b"],), writes=(b[dram_out],))
        if variant == "l1":
            sqrow = 64 if which == "q" else 65
            onerow = 65 if which == "q" else 64
            p.dma("gpsimd",
                  t[dram_out].ap().rearrange("(h r) tk -> h r tk", r=KR)[:, sqrow:sqrow + 1, :],
                  t["q2row"][0:16, 0:T], reads=(b["q2row"],), writes=(b[dram_out],))
            for h in range(H):
                p.dma("gpsimd", t[dram_out][h * KR + onerow:h * KR + onerow + 1, :],
                      t["onesrb"][0:1, 0:T], reads=(b["onesrb"],), writes=(b[dram_out],))


def _v_produce(c, p, variant):
    t, b = c.t, c.b
    wsb, wsbB = t["wk1"], b["wk1"]
    has_bias = variant in ("l0", "l1")
    rows = 1025 if has_bias else 1024
    load_wchunks(c, p, wsb, wsbB, "wv_in", rows, 1024)
    fmsrc = "zn_fm_d" if variant == "l3" else "xn_fm_d"
    for tt in range(NT):
        stg, stgB = t["w16a"], b["w16a"]
        p.memset(stg[:, 0:1040], 1.0, writes=(stgB,))
        for j2 in range(2):
            ps, psB = t["psB"], b["psB"]
            for ch in range(8):
                xf, xfB = t["xf"], b["xf"]
                fm_ld(c, p, xf, xfB, ch, tt * 128, (tt + 1) * 128, src=fmsrc)
                p.mm(ps[:, 0:512], xf[0:128, 0:128], wsb[:, ch * 1024 + j2 * 512: ch * 1024 + (j2 + 1) * 512],
                     start=(ch == 0), stop=(ch == 7 and not has_bias),
                     reads=(wsbB, xfB), writes=(psB,))
            if has_bias:
                xf, xfB = t["xf"], b["xf"]
                fm_ld(c, p, xf, xfB, 8, tt * 128, (tt + 1) * 128, src=fmsrc)
                p.mm(ps[:, 0:512], xf[0:1, 0:128],
                     wsb[0:1, 8 * 1024 + j2 * 512: 8 * 1024 + (j2 + 1) * 512],
                     start=False, stop=True, reads=(wsbB, xfB), writes=(psB,))
            for hh in range(8):
                hgl = j2 * 8 + hh
                p.copy(stg[:, hgl * 65:hgl * 65 + 64], ps[:, hh * 64:(hh + 1) * 64],
                       reads=(psB,), writes=(stgB,))
        p.dma("gpsimd", t["vx_d"][tt * 128:(tt + 1) * 128, :], stg[:, 0:1040],
              reads=(stgB,), writes=(b["vx_d"],))


def _attn_bounds(c, p, KR):
    t, b = c.t, c.b
    row, rowB = t["row4"], b["row4"]
    for side, dram, nn, ncols in (("q", "qx_d", 0, T), ("k", "kg_d", 16, 2 * T)):
        for h in range(H):
            kf, kfB = t["kf"], b["kf"]
            if side == "q":
                p.dma("sync", kf[0:64, 0:T], t[dram][h * 64:h * 64 + 64, :],
                      reads=(b[dram],), writes=(kfB,))
            else:
                for r in range(2):
                    p.dma("sync", kf[0:64, r * T:(r + 1) * T],
                          t[dram][r * H * KR + h * KR: r * H * KR + h * KR + 64, :],
                          reads=(b[dram],), writes=(kfB,))
            sq, sqB = t["w16a"], b["w16a"]
            p.act(sq[0:64, 0:ncols], kf[0:64, 0:ncols], AF.Square, reads=(kfB,), writes=(sqB,))
            ps, psB = t["psS"], b["psS"]
            for half in range(ncols // 2048):
                for j in range(4):
                    p.mm(ps[0:1, j * 512:(j + 1) * 512], t["onescb"][0:64, 0:1],
                         sq[0:64, half * 2048 + j * 512: half * 2048 + (j + 1) * 512],
                         start=True, stop=True, reads=(sqB, b["onescb"]), writes=(psB,))
                if half == 0:
                    p.red(row[0:1, nn + h:nn + h + 1], ps[0:1, 0:2048], ALU.max,
                          reads=(psB,), writes=(rowB,))
                else:
                    p.red(row[0:1, 32 + h:32 + h + 1], ps[0:1, 0:2048], ALU.max,
                          reads=(psB,), writes=(rowB,))
                    p.tt(row[0:1, nn + h:nn + h + 1], row[0:1, nn + h:nn + h + 1],
                         row[0:1, 32 + h:32 + h + 1], ALU.max, reads=(rowB,), writes=(rowB,))
    p.tt(row[0:1, 32:48], row[0:1, 0:16], row[0:1, 16:32], ALU.mult, reads=(rowB,), writes=(rowB,))
    p.act(row[0:1, 32:48], row[0:1, 32:48], AF.Sqrt, reads=(rowB,), writes=(rowB,))
    p.ts(row[0:1, 32:48], row[0:1, 32:48], -1.0, None, ALU.mult, reads=(rowB,), writes=(rowB,))
    ps, psB = t["psB"], b["psB"]
    p.mm(ps[:, 0:H], t["onesr"][0:1, 0:128], row[0:1, 32:48], start=True, stop=True,
         reads=(rowB, b["onesr"]), writes=(psB,))
    p.copy(t["bb"][:, 0:H], ps[:, 0:H], reads=(psB,), writes=(b["bb"],))


def _attn_core(c, p, variant, KR):
    t, b = c.t, c.b
    for h in range(H):
        kf, kfB = t["kf"], b["kf"]
        for r in range(2):
            p.dma("sync", kf[0:KR, r * T:(r + 1) * T],
                  t["kg_d"][r * H * KR + h * KR: r * H * KR + (h + 1) * KR, :],
                  reads=(b["kg_d"],), writes=(kfB,))
        vh, vhB = t["vh"], b["vh"]
        for kc in range(NKC):
            p.dma("sync", vh[:, kc * 65:(kc + 1) * 65],
                  t["vg_d"][kc * 128:(kc + 1) * 128, h * 65:(h + 1) * 65],
                  reads=(b["vg_d"],), writes=(vhB,))
        qh, qhB = t["qh"], b["qh"]
        p.dma("sync", qh[0:KR, 0:T], t["qx_d"][h * KR:(h + 1) * KR, :],
              reads=(b["qx_d"],), writes=(qhB,))
        for qt in range(NQT):
            pb, pbB = t["pb"], b["pb"]
            for kc in range(NKC):
                psA, psAB = c.psbank(kc % 4)
                p.mm(psA[:, 0:QT], kf[0:KR, kc * 128:(kc + 1) * 128],
                     qh[0:KR, qt * QT:(qt + 1) * QT], start=True, stop=True,
                     reads=(kfB, qhB, b["psS"]), writes=(psAB,))
                if variant == "l1":
                    u, uB = t["a3"], b["a3"]
                    p.act(u[:, 0:QT], psA[:, 0:QT], AF.Abs, reads=(psAB,), writes=(uB,))
                    p.act(u[:, QT:2 * QT], u[:, 0:QT], AF.Sqrt, reads=(uB,), writes=(uB,))
                    p.act(pb[:, kc * QT:(kc + 1) * QT], u[:, QT:2 * QT], AF.Exp, scale=-1.0,
                          reads=(uB,), writes=(pbB,))
                else:
                    p.act(pb[:, kc * QT:(kc + 1) * QT], psA[:, 0:QT], AF.Exp,
                          bias=t["bb"][:, h:h + 1], reads=(psAB, b["bb"]), writes=(pbB,))
            po, poB = t["psB"], b["psB"]
            for kc in range(NKC):
                p.mm(po[0:65, 0:QT], vh[:, kc * 65:(kc + 1) * 65],
                     pb[:, kc * QT:(kc + 1) * QT], start=(kc == 0), stop=(kc == NKC - 1),
                     reads=(vhB, pbB), writes=(poB,))
            hp, half = h // 2, h % 2
            ost, ostB = t["ostg"], b["ostg"]
            p.act(ost[0:64, 0:QT], po[0:64, 0:QT], AF.Copy, reads=(poB,), writes=(ostB,))
            rs, rsB = t["rsum"], b["rsum"]
            p.op("vector", lambda e, rs=rs, po=po: e.reciprocal(rs[0:1, 0:QT], po[64:65, 0:QT]),
                 reads=(poB,), writes=(rsB,))
            bc, bcB = c.psbank(0)
            p.mm(bc[0:64, 0:QT], t["onesr"][0:1, 0:64], rs[0:1, 0:QT],
                 start=True, stop=True, reads=(rsB, b["onesr"], b["psS"]), writes=(bcB,))
            p.tt(ost[0:64, 0:QT], ost[0:64, 0:QT], bc[0:64, 0:QT], ALU.mult,
                 reads=(bcB,), writes=(ostB,))
            p.dma("gpsimd", t["ofm_d"][h * 64:(h + 1) * 64, qt * QT:(qt + 1) * QT],
                  ost[0:64, 0:QT], reads=(ostB,), writes=(b["ofm_d"],))


def _attn_epilogue(c, p, variant):
    t, b = c.t, c.b
    wsb, wsbB = t["wk1"], b["wk1"]
    load_wchunks(c, p, wsb, wsbB, "wo_in", 1025, 1024)
    for tt in range(NT):
        so, soB = t["a1"], b["a1"]
        if variant == "l3":
            p.dma("sync", so[:, 0:D], t["xn32_d"][tt * 128:(tt + 1) * 128, :],
                  reads=(b["xn32_d"],), writes=(soB,))
        ofc, ofcB = t["w16a"], b["w16a"]
        for ch in range(8):
            p.dma("sync", ofc[:, ch * 128:(ch + 1) * 128],
                  t["ofm_d"][ch * 128:(ch + 1) * 128, tt * 128:(tt + 1) * 128],
                  reads=(b["ofm_d"],), writes=(ofcB,))
        for j2 in range(2):
            ps, psB = t["psS"], b["psS"]
            for ch in range(8):
                p.mm(ps[:, 0:512], ofc[:, ch * 128:(ch + 1) * 128],
                     wsb[:, ch * 1024 + j2 * 512: ch * 1024 + (j2 + 1) * 512],
                     start=(ch == 0), stop=False, reads=(ofcB, wsbB), writes=(psB,))
            p.mm(ps[:, 0:512], t["onesrb"][0:1, 0:128],
                 wsb[0:1, 8 * 1024 + j2 * 512: 8 * 1024 + (j2 + 1) * 512],
                 start=False, stop=True, reads=(b["onesrb"], wsbB), writes=(psB,))
            if variant == "l3":
                p.tt(so[:, j2 * 512:(j2 + 1) * 512], so[:, j2 * 512:(j2 + 1) * 512],
                     ps[:, 0:512], ALU.add, reads=(psB,), writes=(soB,))
            else:
                p.copy(so[:, j2 * 512:(j2 + 1) * 512], ps[:, 0:512], reads=(psB,), writes=(soB,))
        p.dma("gpsimd", t["s_d"][tt * 128:(tt + 1) * 128, :], so[:, 0:D],
              reads=(soB,), writes=(b["s_d"],))



def emit_ffn0(c, p):
    t, b = c.t, c.b
    CAP = 4096
    wsb, wsbB = t["wk1"], b["wk1"]
    load_wchunks(c, p, wsb, wsbB, "f0_wp_aug", 1025, 1024)
    for tt in range(NT):
        stg, stgB = t["w16a"], b["w16a"]
        for j2 in range(2):
            ps, psB = t["psB"], b["psB"]
            for ch in range(9):
                xf, xfB = t["xf"], b["xf"]
                fm_ld(c, p, xf, xfB, ch, tt * 128, (tt + 1) * 128)
                rr = 1 if ch == 8 else 128
                p.mm(ps[:, 0:512], xf[0:rr, 0:128],
                     wsb[0:rr, ch * 1024 + j2 * 512: ch * 1024 + (j2 + 1) * 512],
                     start=(ch == 0), stop=(ch == 8), reads=(wsbB, xfB), writes=(psB,))
            p.copy(stg[:, j2 * 512:(j2 + 1) * 512], ps[:, 0:512], reads=(psB,), writes=(stgB,))
        p.dma("gpsimd", t["xflat_d"][tt * 128:(tt + 1) * 128, :], stg[:, 0:D],
              reads=(stgB,), writes=(b["xflat_d"],))
    p.coll("AllGather", ALU.bypass, ALL8, t["xflat_d"].ap(), t["xflatall_d"].ap(),
           reads=(b["xflat_d"],), writes=(b["xflatall_d"],))

    gsb, gsbB = t["gf32"], b["gf32"]
    wg, wgB = t["wg32"], b["wg32"]
    for ch in range(8):
        p.dma("sync", wg[:, ch * 8:(ch + 1) * 8], t["f0_wg"][ch * 128:(ch + 1) * 128, :],
              reads=(b["f0_wg"],), writes=(wgB,))
        p.dma("sync", wg[:, 64 + ch:64 + ch + 1], t["f0_bp"][ch * 128:(ch + 1) * 128, :],
              reads=(b["f0_bp"],), writes=(wgB,))
    ps, psB = t["psB"], b["psB"]
    p.memset(gsb[:, 0:80], 0.0, writes=(gsbB,))
    for jch in range(8):
        wpt, wptB = t["wf32s"], b["wf32s"]
        p.dma("sync", wpt[:, 0:1024], t["f0_wpT"][jch * 128:(jch + 1) * 128, :],
              reads=(b["f0_wpT"],), writes=(wptB,))
        for ich in range(8):
            p.mm(ps[:, ich * 8:(ich + 1) * 8], wpt[:, ich * 128:(ich + 1) * 128],
                 wg[:, jch * 8:(jch + 1) * 8], start=True, stop=True,
                 reads=(wptB, wgB), writes=(psB,))
        p.mm(ps[0:1, 64:72], wg[:, 64 + jch:64 + jch + 1], wg[:, jch * 8:(jch + 1) * 8],
             start=True, stop=True, reads=(wgB,), writes=(psB,))
        p.tt(gsb[:, 0:64], gsb[:, 0:64], ps[:, 0:64], ALU.add, reads=(psB,), writes=(gsbB,))
        p.tt(gsb[0:1, 64:72], gsb[0:1, 64:72], ps[0:1, 64:72], ALU.add,
             reads=(psB,), writes=(gsbB,))
    if DEBUG:
        p.dma("gpsimd", t["dbg_G"][:, :], gsb[:, 0:80], reads=(gsbB,), writes=(b["dbg_G"],))
    lg, lgB = t["psS"], b["psS"]
    lgs, lgsB = t["lnsq"], b["lnsq"]
    p.memset(lgs[0:8, 0:T], 0.0, writes=(lgsB,))
    for ich in range(8):
        strip, stripB = t["a2"], b["a2"]
        for tt in range(NT):
            xa, xaB = t["a1"], b["a1"]
            p.dma("sync", xa[:, 0:128], t["xn32_d"][tt * 128:(tt + 1) * 128,
                                                    ich * 128:(ich + 1) * 128],
                  reads=(b["xn32_d"],), writes=(xaB,))
            pt, ptB = t["psB"], b["psB"]
            p.tr(pt[:, 0:128], xa[:, 0:128], t["identf"][:, :],
                 reads=(xaB, b["identf"]), writes=(ptB,))
            p.copy(strip[:, tt * 128:(tt + 1) * 128], pt[:, 0:128], reads=(ptB,), writes=(stripB,))
        if DEBUG and ich == 7:
            p.dma("gpsimd", t["dbg_strip"][:, :], strip[:, 0:2048],
                  reads=(stripB,), writes=(b["dbg_strip"],))
        for jt in range(4):
            p.mm(lg[0:8, jt * 512:(jt + 1) * 512], gsb[:, ich * 8:(ich + 1) * 8],
                 strip[:, jt * 512:(jt + 1) * 512], start=True, stop=True,
                 reads=(gsbB, stripB), writes=(lgB,))
        p.tt(lgs[0:8, 0:T], lgs[0:8, 0:T], lg[0:8, 0:T], ALU.add, reads=(lgB,), writes=(lgsB,))
    for jt in range(4):
        p.mm(lg[0:8, jt * 512:(jt + 1) * 512], gsb[0:1, 64:72],
             t["onesr"][0:1, 0:512], start=True, stop=True,
             reads=(gsbB, b["onesr"]), writes=(lgB,))
    p.tt(lgs[0:8, 0:T], lgs[0:8, 0:T], lg[0:8, 0:T], ALU.add, reads=(lgB,), writes=(lgsB,))
    lt, ltB = t["ltok"], b["ltok"]
    for tt in range(NT):
        pt, ptB = t["psB"], b["psB"]
        p.tr(pt[0:128, 0:8], lgs[0:8, tt * 128:(tt + 1) * 128], t["identf"][0:8, 0:8],
             reads=(lgsB, b["identf"]), writes=(ptB,))
        p.copy(lt[:, tt * 8:(tt + 1) * 8], pt[0:128, 0:8], reads=(ptB,), writes=(ltB,))

    _ffn0_route(c, p, CAP)

    _ffn0_expert(c, p, CAP)

    _ffn0_out(c, p, CAP)


def _ffn0_route(c, p, CAP):
    t, b = c.t, c.b
    lt, ltB = t["ltok"], b["ltok"]
    m8, m8B = t["m8"], b["m8"]
    est, estB = t["est"], b["est"]
    zz8, zz8B = t["zz8"], b["zz8"]
    p.memset(zz8[:, 0:8], 0.0, writes=(zz8B,))
    cps, cpsB = t["psB"], b["psB"]
    for tt in range(NT):
        lg8 = lt[:, tt * 8:(tt + 1) * 8]
        mx, mxB = t["sc8"], b["sc8"]
        p.op("vector", lambda e, mx=mx, lg8=lg8: e.max(mx[:, 0:8], lg8), reads=(ltB,), writes=(mxB,))
        eq, eqB = t["sc8b"], b["sc8b"]
        p.ts(eq[:, 0:8], lg8, mx[:, 0:1], None, ALU.is_equal, reads=(ltB, mxB), writes=(eqB,))
        pf, pfB = t["sc8c"], b["sc8c"]
        p.op("vector", lambda e, pf=pf, eq=eq, zz8=zz8: e.tensor_tensor_scan(
            pf[:, 0:8], eq[:, 0:8], zz8[:, 0:8], 0.0, ALU.add, ALU.add),
            reads=(eqB, zz8B), writes=(pfB,))
        p.ts(pf[:, 0:8], pf[:, 0:8], 1.0, None, ALU.is_equal, reads=(pfB,), writes=(pfB,))
        msk = m8[:, tt * 8:(tt + 1) * 8]
        p.tt(msk, pf[:, 0:8], eq[:, 0:8], ALU.mult, reads=(pfB, eqB), writes=(m8B,))
        p.tt(eq[:, 0:8], msk, t["iota8"][:, 0:8], ALU.mult, reads=(m8B, b["iota8"]), writes=(eqB,))
        p.red(est[:, tt:tt + 1], eq[:, 0:8], ALU.add, reads=(eqB,), writes=(estB,))
        p.mm(cps[0:1, 0:8], t["onesc"][:, 0:1], msk, start=(tt == 0), stop=(tt == NT - 1),
             reads=(b["onesc"], m8B), writes=(cpsB,))
    cnt, cntB = t["sc8"], b["sc8"]
    p.copy(cnt[0:1, 0:8], cps[0:1, 0:8], reads=(cpsB,), writes=(cntB,))
    p.dma("gpsimd", t["cnt_d"][0:1, :], cnt[0:1, 0:8], reads=(cntB,), writes=(b["cnt_d"],))
    p.coll("AllGather", ALU.bypass, ALL8, t["cnt_d"].ap(), t["cntall_d"].ap(),
           reads=(b["cnt_d"],), writes=(b["cntall_d"],))
    call, callB = t["sc8b"], b["sc8b"]
    p.dma("sync", call[0:8, 0:8], t["cntall_d"][:, :], reads=(b["cntall_d"],), writes=(callB,))
    ps, psB = t["psB"], b["psB"]
    p.mm(ps[0:1, 0:8], t["premask"][0:8, 0:1], call[0:8, 0:8], start=True, stop=True,
         reads=(callB, b["premask"]), writes=(psB,))
    offs, offsB = t["sc8c"], b["sc8c"]
    p.copy(offs[0:1, 0:8], ps[0:1, 0:8], reads=(psB,), writes=(offsB,))
    neg, negB = t["a1"], b["a1"]
    p.memset(neg[:, 0:256], -1.0, writes=(negB,))
    p.dma("gpsimd", t["idxsc_d"].ap().rearrange("(p f) one -> p (f one)", p=128),
          neg[:, 0:256], reads=(negB,), writes=(b["idxsc_d"],))
    for tt in range(NT):
        msk = t["m8"][:, tt * 8:(tt + 1) * 8]
        rps, rpsB = t["psB"], b["psB"]
        p.mm(rps[:, 0:8], t["cum"][:, :], msk, start=True, stop=False,
             reads=(b["cum"], b["m8"]), writes=(rpsB,))
        p.mm(rps[:, 0:8], t["onesr"][0:1, 0:128], offs[0:1, 0:8], start=False, stop=True,
             reads=(b["onesr"], offsB), writes=(rpsB,))
        rr, rrB = t["sc8"], b["sc8"]
        p.tt(rr[:, 0:8], rps[:, 0:8], msk, ALU.mult, reads=(rpsB, b["m8"]), writes=(rrB,))
        p.red(t["est"][:, NT + tt:NT + tt + 1], rr[:, 0:8], ALU.add, reads=(rrB,), writes=(b["est"],))
        tsp, tspB = t["psB"], b["psB"]
        p.mm(tsp[0:1, 8:16], t["onesc"][:, 0:1], msk, start=True, stop=True,
             reads=(b["onesc"], b["m8"]), writes=(tspB,))
        p.tt(offs[0:1, 0:8], offs[0:1, 0:8], tsp[0:1, 8:16], ALU.add,
             reads=(tspB,), writes=(offsB,))
    for tt in range(NT):
        pos, posB = t["sc8"], b["sc8"]
        rk = t["est"][:, NT + tt:NT + tt + 1]
        p.ts(pos[:, 0:1], t["est"][:, tt:tt + 1], float(CAP), None, ALU.mult,
             reads=(b["est"],), writes=(posB,))
        p.tt(pos[:, 0:1], pos[:, 0:1], rk, ALU.add, reads=(b["est"],), writes=(posB,))
        p.ts(pos[:, 1:2], rk, float(CAP), None, ALU.is_ge, reads=(b["est"],), writes=(posB,))
        p.ts(pos[:, 1:2], pos[:, 1:2], 1e9, None, ALU.mult, reads=(posB,), writes=(posB,))
        p.tt(pos[:, 0:1], pos[:, 0:1], pos[:, 1:2], ALU.add, reads=(posB,), writes=(posB,))
        pi, piB = t["idxm"], b["idxm"]
        p.copy(pi[:, 32 + tt:33 + tt], pos[:, 0:1], reads=(posB,), writes=(piB,))
        p.dma("sync", t["tid_sb"][:, 0:1],
              t["tok_ids"][tt * 128:(tt + 1) * 128, :], reads=(b["tok_ids"],), writes=(b["tid_sb"],))
        p.iscatter(t["idxsc_d"].ap(), pi[:, 32 + tt:33 + tt], t["tid_sb"][:, 0:1], 32767,
                   reads=(piB, b["tid_sb"], b["idxsc_d"]), writes=(b["idxsc_d"],))
    p.coll("AllReduce", ALU.max, ALL8, t["idxsc_d"].ap(), t["idxall_d"].ap(),
           reads=(b["idxsc_d"],), writes=(b["idxall_d"],))
    im, imB = t["idxm"], b["idxm"]
    tmp32, tmp32B = t["g32t"], b["g32t"]
    p.igather(tmp32[0:32, 0:128],
              t["idxall_d"].ap().rearrange("(r c) one -> r (c one)", c=128),
              t["rowidx_sb"][0:32, 0:1], 255,
              reads=(b["idxall_d"], b["rowidx_sb"]), writes=(tmp32B,))
    pt, ptB = t["psB"], b["psB"]
    p.tr(pt[0:128, 0:32], tmp32[0:32, 0:128], t["identf"][0:32, 0:32],
         reads=(tmp32B, b["identf"]), writes=(ptB,))
    imf, imfB = t["a1"], b["a1"]
    p.copy(imf[:, 0:32], pt[0:128, 0:32], reads=(ptB,), writes=(imfB,))
    p.ts(imf[:, 32:64], imf[:, 0:32], 0.0, None, ALU.is_lt, reads=(imfB,), writes=(imfB,))
    p.ts(imf[:, 32:64], imf[:, 32:64], 2e9, None, ALU.mult, reads=(imfB,), writes=(imfB,))
    p.tt(imf[:, 0:32], imf[:, 0:32], imf[:, 32:64], ALU.add, reads=(imfB,), writes=(imfB,))
    p.copy(im[:, 0:32], imf[:, 0:32], reads=(imfB,), writes=(imB,))


def _ffn0_expert(c, p, CAP):
    t, b = c.t, c.b
    g1, g1B = t["g1bc"], b["g1bc"]
    b1, b1B = t["b1bc"], b["b1bc"]
    for name, dst, dstB in (("f0_ln1g", g1, g1B), ("f0_ln1b", b1, b1B)):
        rsb, rsbB = t["lnsq"], b["lnsq"]
        p.dma("sync", rsb[0:1, 0:4096], t[name][0:1, :], reads=(b[name],), writes=(rsbB,))
        for j in range(8):
            ps, psB = t["psB"], b["psB"]
            p.mm(ps[:, 0:512], t["onesr"][0:1, 0:128], rsb[0:1, j * 512:(j + 1) * 512],
                 start=True, stop=True, reads=(rsbB, b["onesr"]), writes=(psB,))
            p.copy(dst[:, j * 512:(j + 1) * 512], ps[:, 0:512], reads=(psB,), writes=(dstB,))
    for half in range(2):
        w1, w1B = t["wbig"], b["wbig"]
        srcw, srcwB = c.tb("f0_w1_aug")
        for i in range(8):
            p.dma("gpsimd", w1[:, i * 2048:(i + 1) * 2048],
                  srcw[i * 128:(i + 1) * 128, half * 2048:(half + 1) * 2048],
                  reads=(srcwB,), writes=(w1B,))
        p.dma("gpsimd", w1[0:1, 8 * 2048:9 * 2048],
              srcw[1024:1025, half * 2048:(half + 1) * 2048], reads=(srcwB,), writes=(w1B,))
        for sc in range(32):
            tok, tokB = t["w16a"], b["w16a"]
            p.memset(tok[:, 0:1024], 0.0, writes=(tokB,))
            p.igather(tok[:, 0:1024], t["xflatall_d"].ap(), t["idxm"][:, sc:sc + 1], 16383,
                      reads=(b["xflatall_d"], b["idxm"]), writes=(tokB,))
            tfm, tfmB = t["w16b"], b["w16b"]
            for ch in range(8):
                pt, ptB = t["psT"], b["psT"]
                p.tr(pt[:, 0:128], tok[:, ch * 128:(ch + 1) * 128], t["ident"][:, :],
                     reads=(tokB, b["ident"]), writes=(ptB,))
                p.copy(tfm[:, ch * 128:(ch + 1) * 128], pt[:, 0:128], reads=(ptB,), writes=(tfmB,))
            ps, psB = t["psS"], b["psS"]
            for j in range(4):
                for ch in range(8):
                    p.mm(ps[:, j * 512:(j + 1) * 512], tfm[:, ch * 128:(ch + 1) * 128],
                         w1[:, ch * 2048 + j * 512: ch * 2048 + (j + 1) * 512],
                         start=(ch == 0), stop=False, reads=(tfmB, w1B), writes=(psB,))
                p.mm(ps[:, j * 512:(j + 1) * 512], t["onesrb"][0:1, 0:128],
                     w1[0:1, 8 * 2048 + j * 512: 8 * 2048 + (j + 1) * 512],
                     start=False, stop=True, reads=(b["onesrb"], w1B), writes=(psB,))
            hh, hhB = t["a2"], b["a2"]
            p.copy(hh[:, 0:2048], ps[:, 0:2048], reads=(psB,), writes=(hhB,))
            p.dma("gpsimd", t["hh_d"][sc * 128:(sc + 1) * 128, half * 2048:(half + 1) * 2048],
                  hh[:, 0:2048], reads=(hhB,), writes=(b["hh_d"],))
    for sc in range(32):
        hacc, haccB = t["a1"], b["a1"]
        p.dma("sync", hacc[:, 0:4096], t["hh_d"][sc * 128:(sc + 1) * 128, :],
              reads=(b["hh_d"],), writes=(haccB,))
        emit_ln_stats(c, p, hacc[:, 0:4096], (haccB,), 4096, 1e-5)
        st, stB = t["stat"], b["stat"]
        gel, gelB = t["w16a"], b["w16a"]
        p.ts(gel[:, 0:4096], hacc[:, 0:4096], st[:, 0:1], st[:, 3:4], ALU.subtract, ALU.mult,
             reads=(haccB, stB), writes=(gelB,))
        p.tt(gel[:, 0:4096], gel[:, 0:4096], g1[:, 0:4096], ALU.mult,
             reads=(g1B,), writes=(gelB,))
        p.tt(gel[:, 0:4096], gel[:, 0:4096], b1[:, 0:4096], ALU.add,
             reads=(b1B,), writes=(gelB,))
        p.act(gel[:, 0:4096], gel[:, 0:4096], AF.Gelu, reads=(gelB,), writes=(gelB,))
        p.dma("gpsimd", t["gel_d"][sc * 128:(sc + 1) * 128, :], gel[:, 0:4096],
              reads=(gelB,), writes=(b["gel_d"],))
    w2, w2B = t["wbig"], b["wbig"]
    load_wchunks(c, p, w2, w2B, "f0_w2_aug", 4097, 1024)
    for sc in range(32):
        gel, gelB = t["w16a"], b["w16a"]
        p.dma("sync", gel[:, 0:4096], t["gel_d"][sc * 128:(sc + 1) * 128, :],
              reads=(b["gel_d"],), writes=(gelB,))
        gfm, gfmB = t["w16b"], b["w16b"]
        for ch in range(32):
            pt, ptB = t["psT"], b["psT"]
            p.tr(pt[:, 0:128], gel[:, ch * 128:(ch + 1) * 128], t["ident"][:, :],
                 reads=(gelB, b["ident"]), writes=(ptB,))
            p.copy(gfm[:, ch * 128:(ch + 1) * 128], pt[:, 0:128], reads=(ptB,), writes=(gfmB,))
        stg, stgB = t["w16c"], b["w16c"]
        for j2 in range(2):
            ps, psB = t["psB"], b["psB"]
            for ch in range(32):
                p.mm(ps[:, 0:512], gfm[:, ch * 128:(ch + 1) * 128],
                     w2[:, ch * 1024 + j2 * 512: ch * 1024 + (j2 + 1) * 512],
                     start=(ch == 0), stop=False, reads=(gfmB, w2B), writes=(psB,))
            p.mm(ps[:, 0:512], t["onesrb"][0:1, 0:128],
                 w2[0:1, 32 * 1024 + j2 * 512: 32 * 1024 + (j2 + 1) * 512],
                 start=False, stop=True, reads=(b["onesrb"], w2B), writes=(psB,))
            p.copy(stg[:, j2 * 512:(j2 + 1) * 512], ps[:, 0:512], reads=(psB,), writes=(stgB,))
        p.dma("gpsimd", t["slots_d"][sc * 128:(sc + 1) * 128, :], stg[:, 0:D],
              reads=(stgB,), writes=(b["slots_d"],))
    p.coll("AllGather", ALU.bypass, ALL8, t["slots_d"].ap(), t["slotsall_d"].ap(),
           reads=(b["slots_d"],), writes=(b["slotsall_d"],))


def _ffn0_out(c, p, CAP):
    t, b = c.t, c.b
    gg, ggB = t["g1bc"], b["g1bc"]
    bb2, bb2B = t["b1bc"], b["b1bc"]
    for name, dst, dstB in (("f0_lng", gg, ggB), ("f0_lnb", bb2, bb2B)):
        rsb, rsbB = t["lnsq"], b["lnsq"]
        p.dma("sync", rsb[0:1, 0:1024], t[name][0:1, :], reads=(b[name],), writes=(rsbB,))
        for j in range(2):
            ps, psB = t["psB"], b["psB"]
            p.mm(ps[:, 0:512], t["onesr"][0:1, 0:128], rsb[0:1, j * 512:(j + 1) * 512],
                 start=True, stop=True, reads=(rsbB, b["onesr"]), writes=(psB,))
            p.copy(dst[:, j * 512:(j + 1) * 512], ps[:, 0:512], reads=(psB,), writes=(dstB,))
    pix, pixB = t["idxm"], b["idxm"]
    for tt in range(NT):
        og, ogB = t["w16a"], b["w16a"]
        p.memset(og[:, 0:1024], 0.0, writes=(ogB,))
        p.igather(og[:, 0:1024], t["slotsall_d"].ap(), pix[:, 32 + tt:33 + tt], 8 * CAP - 1,
                  reads=(b["slotsall_d"], pixB), writes=(ogB,))
        ofl, oflB = t["a1"], b["a1"]
        p.copy(ofl[:, 0:D], og[:, 0:D], reads=(ogB,), writes=(oflB,))
        emit_ln_stats(c, p, ofl[:, 0:D], (oflB,), D, 1e-5)
        st, stB = t["stat"], b["stat"]
        p.ts(ofl[:, 0:D], ofl[:, 0:D], st[:, 0:1], st[:, 3:4], ALU.subtract, ALU.mult,
             reads=(stB,), writes=(oflB,))
        p.tt(ofl[:, 0:D], ofl[:, 0:D], gg[:, 0:D], ALU.mult, reads=(ggB,), writes=(oflB,))
        p.tt(ofl[:, 0:D], ofl[:, 0:D], bb2[:, 0:D], ALU.add, reads=(bb2B,), writes=(oflB,))
        p.dma("gpsimd", t["g_d"][tt * 128:(tt + 1) * 128, :], ofl[:, 0:D],
              reads=(oflB,), writes=(b["g_d"],))


def _quantize_absmean(c, p, src_name, dst_name, rows, cols):
    t, b = c.t, c.b
    nch = rows // 128
    acc, accB = t["a3"], b["a3"]
    p.memset(acc[:, 0:1], 0.0, writes=(accB,))
    for i in range(nch):
        xa, xaB = t["a1"], b["a1"]
        p.dma("sync", xa[:, 0:cols], t[src_name][i * 128:(i + 1) * 128, :],
              reads=(b[src_name],), writes=(xaB,))
        p.op("vector", lambda e, xa=xa, cols=cols: e.tensor_reduce(
            t["stat"][:, 4:5], xa[:, 0:cols], AX.X, ALU.add, apply_absolute_value=True),
            reads=(xaB,), writes=(b["stat"],))
        p.tt(acc[:, 0:1], acc[:, 0:1], t["stat"][:, 4:5], ALU.add,
             reads=(b["stat"],), writes=(accB,))
    ps, psB = t["psB"], b["psB"]
    p.mm(ps[0:1, 0:1], acc[:, 0:1], t["onesc"][:, 0:1], start=True, stop=True,
         reads=(accB, b["onesc"]), writes=(psB,))
    th, thB = t["stat"], b["stat"]
    p.ts(th[0:1, 5:6], ps[0:1, 0:1], 0.5 / (rows * cols), None, ALU.mult,
         reads=(psB,), writes=(thB,))
    p.ts(th[0:1, 5:6], th[0:1, 5:6], 0.5e-6, None, ALU.add, reads=(thB,), writes=(thB,))
    p.mm(ps[:, 4:5], t["onesr"][0:1, 0:128], th[0:1, 5:6], start=True, stop=True,
         reads=(thB, b["onesr"]), writes=(psB,))
    thb, thbB = t["a3"], b["a3"]
    p.copy(thb[:, 1:2], ps[:, 4:5], reads=(psB,), writes=(thbB,))
    for i in range(nch):
        xa, xaB = t["a1"], b["a1"]
        p.dma("sync", xa[:, 0:cols], t[src_name][i * 128:(i + 1) * 128, :],
              reads=(b[src_name],), writes=(xaB,))
        sg, sgB = t["w16a"], b["w16a"]
        p.act(sg[:, 0:cols], xa[:, 0:cols], AF.Sign, reads=(xaB,), writes=(sgB,))
        aw, awB = t["lnsq"], b["lnsq"]
        p.act(aw[:, 0:cols], xa[:, 0:cols], AF.Abs, reads=(xaB,), writes=(awB,))
        p.ts(aw[:, 0:cols], aw[:, 0:cols], thb[:, 1:2], None, ALU.is_gt,
             reads=(thbB,), writes=(awB,))
        wq, wqB = t["w16b"], b["w16b"]
        p.tt(wq[:, 0:cols], sg[:, 0:cols], aw[:, 0:cols], ALU.mult,
             reads=(sgB, awB), writes=(wqB,))
        p.dma("gpsimd", t[dst_name][i * 128:(i + 1) * 128, :], wq[:, 0:cols],
              reads=(wqB,), writes=(b[dst_name],))


def emit_ffn1(c, p):
    t, b = c.t, c.b
    _quantize_absmean(c, p, "f1_wfcT", "f1_wq1_d", 1024, 4096)
    _quantize_absmean(c, p, "f1_wprojT", "f1_wq2_d", 4096, 1024)
    for j in range(8):
        wsb, wsbB = t["wk1"], b["wk1"]
        for ch in range(8):
            p.dma("gpsimd", wsb[:, ch * 512:(ch + 1) * 512],
                  t["f1_wq1_d"][ch * 128:(ch + 1) * 128, j * 512:(j + 1) * 512],
                  reads=(b["f1_wq1_d"],), writes=(wsbB,))
        for tt in range(NT):
            ps, psB = t["psB"], b["psB"]
            for ch in range(8):
                xf, xfB = t["xf"], b["xf"]
                fm_ld(c, p, xf, xfB, ch, tt * 128, (tt + 1) * 128)
                p.mm(ps[:, 0:512], xf[0:128, 0:128], wsb[:, ch * 512:(ch + 1) * 512],
                     start=(ch == 0), stop=(ch == 7), reads=(wsbB, xfB), writes=(psB,))
            hg, hgB = t["a2"], b["a2"]
            p.act(hg[:, 0:512], ps[:, 0:512], AF.Gelu, reads=(psB,), writes=(hgB,))
            p.dma("gpsimd", t["f1_h_d"][tt * 128:(tt + 1) * 128, j * 512:(j + 1) * 512],
                  hg[:, 0:512], reads=(hgB,), writes=(b["f1_h_d"],))
    for name, lo in (("f1_lnw", 0), ("f1_lnb", 4096)):
        rsb, rsbB = t["lnsq"], b["lnsq"]
        p.dma("sync", rsb[0:1, 0:4096], t[name][0:1, :], reads=(b[name],), writes=(rsbB,))
        for jj in range(8):
            ps, psB = t["psB"], b["psB"]
            p.mm(ps[:, 0:512], t["onesr"][0:1, 0:128], rsb[0:1, jj * 512:(jj + 1) * 512],
                 start=True, stop=True, reads=(rsbB, b["onesr"]), writes=(psB,))
            p.copy(t["wk1"][:, lo + jj * 512:lo + (jj + 1) * 512], ps[:, 0:512],
                   reads=(psB,), writes=(b["wk1"],))
    for tt in range(NT):
        ha, haB = t["a1"], b["a1"]
        p.dma("sync", ha[:, 0:4096], t["f1_h_d"][tt * 128:(tt + 1) * 128, :],
              reads=(b["f1_h_d"],), writes=(haB,))
        emit_ln_stats(c, p, ha[:, 0:4096], (haB,), 4096, 1e-5)
        st_, stB = t["stat"], b["stat"]
        hn, hnB = t["w16a"], b["w16a"]
        p.ts(hn[:, 0:4096], ha[:, 0:4096], st_[:, 0:1], st_[:, 3:4], ALU.subtract, ALU.mult,
             reads=(haB, stB), writes=(hnB,))
        p.tt(hn[:, 0:4096], hn[:, 0:4096], t["wk1"][:, 0:4096], ALU.mult,
             reads=(b["wk1"],), writes=(hnB,))
        p.tt(hn[:, 0:4096], hn[:, 0:4096], t["wk1"][:, 4096:8192], ALU.add,
             reads=(b["wk1"],), writes=(hnB,))
        o2, o2B = t["w16b"], b["w16b"]
        for ch in range(32):
            pt, ptB = t["psT"], b["psT"]
            p.tr(pt[:, 0:128], hn[:, ch * 128:(ch + 1) * 128], t["ident"][:, :],
                 reads=(hnB, b["ident"]), writes=(ptB,))
            p.copy(o2[:, ch * 128:(ch + 1) * 128], pt[:, 0:128], reads=(ptB,), writes=(o2B,))
        for ch in range(32):
            p.dma("gpsimd", t["f1_hnf_d"][ch * 128:(ch + 1) * 128, tt * 128:(tt + 1) * 128],
                  o2[:, ch * 128:(ch + 1) * 128], reads=(o2B,), writes=(b["f1_hnf_d"],))
    for j2 in range(2):
        wsb, wsbB = t["wbig1"], b["wbig1"]
        for ch in range(32):
            p.dma("gpsimd", wsb[:, ch * 512:(ch + 1) * 512],
                  t["f1_wq2_d"][ch * 128:(ch + 1) * 128, j2 * 512:(j2 + 1) * 512],
                  reads=(b["f1_wq2_d"],), writes=(wsbB,))
        for tt in range(NT):
            ps, psB = t["psB"], b["psB"]
            for ch in range(32):
                xf, xfB = t["xf"], b["xf"]
                p.dma("sync", xf[0:128, 0:128],
                      t["f1_hnf_d"][ch * 128:(ch + 1) * 128, tt * 128:(tt + 1) * 128],
                      reads=(b["f1_hnf_d"],), writes=(xfB,))
                p.mm(ps[:, 0:512], xf[0:128, 0:128], wsb[:, ch * 512:(ch + 1) * 512],
                     start=(ch == 0), stop=(ch == 31), reads=(wsbB, xfB), writes=(psB,))
            go, goB = t["a2"], b["a2"]
            p.copy(go[:, 0:512], ps[:, 0:512], reads=(psB,), writes=(goB,))
            p.dma("gpsimd", t["g_d"][tt * 128:(tt + 1) * 128, j2 * 512:(j2 + 1) * 512],
                  go[:, 0:512], reads=(goB,), writes=(b["g_d"],))


HJ2 = [512, 512, 512, 512, 512, 256]


def emit_ffn2(c, p):
    t, b = c.t, c.b
    NTOK = 8 * T
    p.coll("AllGather", ALU.bypass, ALL8, t["xn_fm_d"].ap(), t["xnfm_all_d"].ap(),
           reads=(b["xn_fm_d"],), writes=(b["xnfm_all_d"],))

    wg, wgB = t["wg32"], b["wg32"]
    for ch in range(8):
        p.dma("sync", wg[:, ch * 7:(ch + 1) * 7], t["f2_wr"][ch * 128:(ch + 1) * 128, :],
              reads=(b["f2_wr"],), writes=(wgB,))
    lgs, lgsB = t["lnsq"], b["lnsq"]
    p.memset(lgs[0:7, 0:T], 0.0, writes=(lgsB,))
    lg, lgB = t["psS"], b["psS"]
    for ich in range(8):
        strip, stripB = t["a2"], b["a2"]
        for tt in range(NT):
            xa, xaB = t["a1"], b["a1"]
            p.dma("sync", xa[:, 0:128], t["xn32_d"][tt * 128:(tt + 1) * 128,
                                                    ich * 128:(ich + 1) * 128],
                  reads=(b["xn32_d"],), writes=(xaB,))
            pt, ptB = t["psB"], b["psB"]
            p.tr(pt[:, 0:128], xa[:, 0:128], t["identf"][:, :],
                 reads=(xaB, b["identf"]), writes=(ptB,))
            p.copy(strip[:, tt * 128:(tt + 1) * 128], pt[:, 0:128], reads=(ptB,), writes=(stripB,))
        for jt in range(4):
            p.mm(lg[0:7, jt * 512:(jt + 1) * 512], wg[:, ich * 7:(ich + 1) * 7],
                 strip[:, jt * 512:(jt + 1) * 512], start=True, stop=True,
                 reads=(wgB, stripB), writes=(lgB,))
        p.tt(lgs[0:7, 0:T], lgs[0:7, 0:T], lg[0:7, 0:T], ALU.add, reads=(lgB,), writes=(lgsB,))
    rbc, rbcB = t["sc8"], b["sc8"]
    p.dma("sync", rbc[0:7, 0:1], t["f2_rb"][:, :], reads=(b["f2_rb"],), writes=(rbcB,))
    p.ts(lgs[0:7, 0:T], lgs[0:7, 0:T], rbc[0:7, 0:1], None, ALU.mult,
         reads=(rbcB,), writes=(lgsB,))
    p.act(lgs[0:7, 0:T], lgs[0:7, 0:T], AF.Sigmoid, reads=(lgsB,), writes=(lgsB,))
    zz8, zz8B = t["zz8"], b["zz8"]
    p.memset(zz8[:, 0:8], 0.0, writes=(zz8B,))
    for tt in range(NT):
        pt, ptB = t["psB"], b["psB"]
        p.tr(pt[0:128, 0:7], lgs[0:7, tt * 128:(tt + 1) * 128], t["identf"][0:7, 0:7],
             reads=(lgsB, b["identf"]), writes=(ptB,))
        pr, prB = t["ltok"], b["ltok"]
        p.memset(pr[:, tt * 8 + 7:tt * 8 + 8], -1e30, writes=(prB,))
        p.copy(pr[:, tt * 8:tt * 8 + 7], pt[0:128, 0:7], reads=(ptB,), writes=(prB,))
        lg8 = pr[:, tt * 8:(tt + 1) * 8]
        mx, mxB = t["sc8"], b["sc8"]
        p.op("vector", lambda e, mx=mx, lg8=lg8: e.max(mx[:, 0:8], lg8), reads=(prB,), writes=(mxB,))
        eq, eqB = t["sc8b"], b["sc8b"]
        pf, pfB = t["sc8c"], b["sc8c"]
        m1_, m1B = t["m8"], b["m8"]
        p.ts(eq[:, 0:8], lg8, mx[:, 0:1], None, ALU.is_equal, reads=(prB, mxB), writes=(eqB,))
        p.op("vector", lambda e, pf=pf, eq=eq, zz8=zz8: e.tensor_tensor_scan(
            pf[:, 0:8], eq[:, 0:8], zz8[:, 0:8], 0.0, ALU.add, ALU.add),
            reads=(eqB, zz8B), writes=(pfB,))
        msk1 = m1_[:, tt * 8:(tt + 1) * 8]
        p.ts(msk1, pf[:, 0:8], 1.0, None, ALU.is_equal, reads=(pfB,), writes=(m1B,))
        p.tt(msk1, msk1, eq[:, 0:8], ALU.mult, reads=(eqB,), writes=(m1B,))
        tie, tieB = t["stat"], b["stat"]
        p.ts(tie[:, 6:7], mx[:, 1:2], None, None, ALU.bypass, reads=(mxB,), writes=(tieB,)) \
            if False else None
        p.tt(tie[:, 6:7], mx[:, 0:1], mx[:, 1:2], ALU.is_equal, reads=(mxB,), writes=(tieB,))
        mt, mtB = t["est"], b["est"]
        p.ts(mt[:, 0:8], pf[:, 0:8], 2.0, None, ALU.is_equal, reads=(pfB,), writes=(mtB,))
        p.tt(mt[:, 0:8], mt[:, 0:8], eq[:, 0:8], ALU.mult, reads=(eqB,), writes=(mtB,))
        p.ts(eq[:, 0:8], lg8, mx[:, 1:2], None, ALU.is_equal, reads=(prB, mxB), writes=(eqB,))
        p.op("vector", lambda e, pf=pf, eq=eq, zz8=zz8: e.tensor_tensor_scan(
            pf[:, 0:8], eq[:, 0:8], zz8[:, 0:8], 0.0, ALU.add, ALU.add),
            reads=(eqB, zz8B), writes=(pfB,))
        p.ts(pf[:, 0:8], pf[:, 0:8], 1.0, None, ALU.is_equal, reads=(pfB,), writes=(pfB,))
        p.tt(pf[:, 0:8], pf[:, 0:8], eq[:, 0:8], ALU.mult, reads=(eqB,), writes=(pfB,))
        p.ts(mt[:, 0:8], mt[:, 0:8], tie[:, 6:7], None, ALU.mult, reads=(tieB,), writes=(mtB,))
        p.ts(tie[:, 7:8], tie[:, 6:7], -1.0, None, ALU.mult, ALU.bypass, reads=(tieB,), writes=(tieB,)) \
            if False else None
        p.ts(tie[:, 7:8], tie[:, 6:7], -1.0, None, ALU.mult, reads=(tieB,), writes=(tieB,))
        p.ts(tie[:, 7:8], tie[:, 7:8], 1.0, None, ALU.add, reads=(tieB,), writes=(tieB,))
        p.ts(pf[:, 0:8], pf[:, 0:8], tie[:, 7:8], None, ALU.mult, reads=(tieB,), writes=(pfB,))
        p.tt(mt[:, 0:8], mt[:, 0:8], pf[:, 0:8], ALU.add, reads=(pfB,), writes=(mtB,))
        p.tt(tie[:, 4:5], mx[:, 0:1], mx[:, 1:2], ALU.add, reads=(mxB,), writes=(tieB,))
        p.op("vector", lambda e, tie=tie: e.reciprocal(tie[:, 5:6], tie[:, 4:5]),
             reads=(tieB,), writes=(tieB,))
        p.tt(tie[:, 4:5], mx[:, 0:1], tie[:, 5:6], ALU.mult, reads=(mxB,), writes=(tieB,))
        p.tt(tie[:, 5:6], mx[:, 1:2], tie[:, 5:6], ALU.mult, reads=(mxB, tieB), writes=(tieB,))
        p.ts(msk1, msk1, tie[:, 4:5], None, ALU.mult, reads=(tieB,), writes=(m1B,))
        p.ts(mt[:, 0:8], mt[:, 0:8], tie[:, 5:6], None, ALU.mult, reads=(tieB,), writes=(mtB,))
        ww, wwB = t["sc8"], b["sc8"]
        p.tt(ww[:, 0:8], msk1, mt[:, 0:8], ALU.add, reads=(m1B, mtB), writes=(wwB,))
        p.memset(ww[:, 7:8], 1.0, writes=(wwB,))
        p.dma("gpsimd", t["wx_d"][tt * 128:(tt + 1) * 128, :], ww[:, 0:8],
              reads=(wwB,), writes=(b["wx_d"],))
    p.coll("AllGather", ALU.bypass, ALL8, t["wx_d"].ap(), t["wall_d"].ap(),
           reads=(b["wx_d"],), writes=(b["wall_d"],))
    selb, selbB = t["iota8"], b["iota8"]
    rsb, rsbB = t["lnsq"], b["lnsq"]
    p.dma("sync", rsb[0:1, 0:8], t["f2_sel"][0:1, :], reads=(b["f2_sel"],), writes=(rsbB,))
    ps, psB = t["psB"], b["psB"]
    p.mm(ps[:, 0:8], t["onesr"][0:1, 0:128], rsb[0:1, 0:8], start=True, stop=True,
         reads=(rsbB, b["onesr"]), writes=(psB,))
    p.copy(selb[:, 0:8], ps[:, 0:8], reads=(psB,), writes=(selbB,))

    col0 = 0
    for j, wj in enumerate(HJ2):
        wsb, wsbB = t["wk1"], b["wk1"]
        wub, wubB = t["wbig1"], b["wbig1"]
        for ch in range(8):
            p.dma("gpsimd", wsb[:, ch * wj:(ch + 1) * wj],
                  t["f2_wg_w"][ch * 128:(ch + 1) * 128, col0:col0 + wj],
                  reads=(b["f2_wg_w"],), writes=(wsbB,))
            p.dma("gpsimd", wub[:, ch * wj:(ch + 1) * wj],
                  t["f2_wu_w"][ch * 128:(ch + 1) * 128, col0:col0 + wj],
                  reads=(b["f2_wu_w"],), writes=(wubB,))
        for tg in range(NTOK // 128):
            rr, tt2 = tg // 16, tg % 16
            xfn = "xq8a" if tg % 2 == 0 else "xq8b"
            xq, xqB = t[xfn], b[xfn]
            p.dma("sync", xq[:, 0:1024].rearrange("p (c tk) -> p c tk", c=8),
                  t["xnfm_all_d"][rr * 1025: rr * 1025 + 1024,
                                  tt2 * 128:(tt2 + 1) * 128].rearrange(
                      "(c p) tk -> p c tk", p=128),
                  reads=(b["xnfm_all_d"],), writes=(xqB,))
            psg, psgB = c.psbank(0)
            psu, psuB = c.psbank(1)
            for ch in range(8):
                p.mm(psg[:, 0:wj], xq[:, ch * 128:(ch + 1) * 128], wsb[:, ch * wj:(ch + 1) * wj],
                     start=(ch == 0), stop=(ch == 7),
                     reads=(wsbB, xqB, b["psS"]), writes=(psgB,))
            for ch in range(8):
                p.mm(psu[:, 0:wj], xq[:, ch * 128:(ch + 1) * 128], wub[:, ch * wj:(ch + 1) * wj],
                     start=(ch == 0), stop=(ch == 7),
                     reads=(wubB, xqB, b["psS"]), writes=(psuB,))
            sg, sgB = t["w16a"], b["w16a"]
            p.act(sg[:, 0:wj], psg[:, 0:wj], AF.Silu, reads=(psgB,), writes=(sgB,))
            p.tt(sg[:, 1024:1024 + wj], sg[:, 0:wj], psu[:, 0:wj], ALU.mult,
                 reads=(psuB, sgB), writes=(sgB,))
            p.dma("gpsimd", t["f2_mid_d"][tg * 128:(tg + 1) * 128, col0:col0 + wj],
                  sg[:, 1024:1024 + wj], reads=(sgB,), writes=(b["f2_mid_d"],))
        col0 += wj

    for tg in range(NTOK // 128):
        md, mdB = t["w16a"], b["w16a"]
        p.dma("sync", md[:, 0:2816], t["f2_mid_d"][tg * 128:(tg + 1) * 128, :],
              reads=(b["f2_mid_d"],), writes=(mdB,))
        o2, o2B = t["w16b"], b["w16b"]
        for ch in range(22):
            pt, ptB = t["psT"], b["psT"]
            p.tr(pt[:, 0:128], md[:, ch * 128:(ch + 1) * 128], t["ident"][:, :],
                 reads=(mdB, b["ident"]), writes=(ptB,))
            p.copy(o2[:, ch * 128:(ch + 1) * 128], pt[:, 0:128], reads=(ptB,), writes=(o2B,))
        for ch in range(22):
            p.dma("gpsimd", t["f2_midf_d"][ch * 128:(ch + 1) * 128, tg * 128:(tg + 1) * 128],
                  o2[:, ch * 128:(ch + 1) * 128], reads=(o2B,), writes=(b["f2_midf_d"],))

    for j2 in range(2):
        wsb, wsbB = t["wbig1"], b["wbig1"]
        for ch in range(22):
            p.dma("gpsimd", wsb[:, ch * 512:(ch + 1) * 512],
                  t["f2_wd_w"][ch * 128:(ch + 1) * 128, j2 * 512:(j2 + 1) * 512],
                  reads=(b["f2_wd_w"],), writes=(wsbB,))
        for tg in range(NTOK // 128):
            ps, psB = t["psB"], b["psB"]
            xfn = "xq8a" if tg % 2 == 0 else "xq8b"
            xq, xqB = t[xfn], b[xfn]
            p.dma("sync", xq[:, 0:22 * 128].rearrange("p (c tk) -> p c tk", c=22),
                  t["f2_midf_d"][0:2816, tg * 128:(tg + 1) * 128].rearrange(
                      "(c p) tk -> p c tk", p=128),
                  reads=(b["f2_midf_d"],), writes=(xqB,))
            for ch in range(22):
                p.mm(ps[:, 0:512], xq[:, ch * 128:(ch + 1) * 128], wsb[:, ch * 512:(ch + 1) * 512],
                     start=(ch == 0), stop=(ch == 21), reads=(wsbB, xqB), writes=(psB,))
            wv8, wv8B = t["sc8"], b["sc8"]
            p.dma("sync", wv8[:, 0:8], t["wall_d"][tg * 128:(tg + 1) * 128, :],
                  reads=(b["wall_d"],), writes=(wv8B,))
            p.tt(wv8[:, 0:8], wv8[:, 0:8], t["iota8"][:, 0:8], ALU.mult,
                 reads=(b["iota8"],), writes=(wv8B,))
            p.red(t["stat"][:, 6:7], wv8[:, 0:8], ALU.add, reads=(wv8B,), writes=(b["stat"],))
            oo, ooB = t["a2"], b["a2"]
            p.ts(oo[:, 0:512], ps[:, 0:512], t["stat"][:, 6:7], None, ALU.mult,
                 reads=(psB, b["stat"]), writes=(ooB,))
            p.dma("gpsimd", t["f2_out_d"][tg * 128:(tg + 1) * 128, j2 * 512:(j2 + 1) * 512],
                  oo[:, 0:512], reads=(ooB,), writes=(b["f2_out_d"],))
    p.coll("ReduceScatter", ALU.add, ALL8, t["f2_out_d"].ap(), t["g_d"].ap(),
           reads=(b["f2_out_d"],), writes=(b["g_d"],))


def emit_ffn3(c, p):
    t, b = c.t, c.b
    NP = 2048
    wph, wphB = t["wbig1"], b["wbig1"]
    load_wchunks(c, p, wph, wphB, "f3_phi", 1024, NP)
    nrm, nrmB = t["row4"], b["row4"]
    acc, accB = t["psS"], b["psS"]
    sq, sqB = t["lnsq"], b["lnsq"]
    lacc, laccB = t["a2"], b["a2"]
    p.memset(lacc[0:1, 0:NP], 0.0, writes=(laccB,))
    for ch in range(8):
        p.act(sq[:, 0:NP], wph[:, ch * NP:(ch + 1) * NP], AF.Square, reads=(wphB,), writes=(sqB,))
        for jt in range(4):
            p.mm(acc[0:1, jt * 512:(jt + 1) * 512], t["onesc"][:, 0:1],
                 sq[:, jt * 512:(jt + 1) * 512], start=True, stop=True,
                 reads=(sqB, b["onesc"]), writes=(accB,))
        p.tt(lacc[0:1, 0:NP], lacc[0:1, 0:NP], acc[0:1, 0:NP], ALU.add,
             reads=(accB,), writes=(laccB,))
    p.red(nrm[0:1, 0:1], lacc[0:1, 0:NP], ALU.max, reads=(laccB,), writes=(nrmB,))
    p.act(nrm[0:1, 0:1], nrm[0:1, 0:1], AF.Sqrt, reads=(nrmB,), writes=(nrmB,))
    p.ts(nrm[0:1, 0:1], nrm[0:1, 0:1], 1.0001, None, ALU.mult, reads=(nrmB,), writes=(nrmB,))
    ps, psB = t["psB"], b["psB"]
    p.mm(ps[:, 16:17], t["onesr"][0:1, 0:128], nrm[0:1, 0:1], start=True, stop=True,
         reads=(nrmB, b["onesr"]), writes=(psB,))
    phm, phmB = t["stat"], b["stat"]
    p.copy(phm[:, 6:7], ps[:, 16:17], reads=(psB,), writes=(phmB,))
    for tt in range(NT):
        ps2, ps2B = t["psS"], b["psS"]
        for jt in range(4):
            for ch in range(8):
                xf, xfB = t["xf"], b["xf"]
                fm_ld(c, p, xf, xfB, ch, tt * 128, (tt + 1) * 128)
                p.mm(ps2[:, jt * 512:(jt + 1) * 512], xf[0:128, 0:128],
                     wph[:, ch * NP + jt * 512: ch * NP + (jt + 1) * 512],
                     start=(ch == 0), stop=(ch == 7), reads=(wphB, xfB), writes=(ps2B,))
        la, laB = t["a1"], b["a1"]
        p.copy(la[:, 0:NP], ps2[:, 0:NP], reads=(ps2B,), writes=(laB,))
        p.dma("gpsimd", t["l_d"][tt * 128:(tt + 1) * 128, :], la[:, 0:NP],
              reads=(laB,), writes=(b["l_d"],))
        xa, xaB = t["a2"], b["a2"]
        p.dma("sync", xa[:, 0:D], t["xn32_d"][tt * 128:(tt + 1) * 128, :],
              reads=(b["xn32_d"],), writes=(xaB,))
        p.act(sq[:, 0:D], xa[:, 0:D], AF.Square, reads=(xaB,), writes=(sqB,))
        p.red(t["stat"][:, 4:5], sq[:, 0:D], ALU.add, reads=(sqB,), writes=(b["stat"],))
        p.act(t["stat"][:, 4:5], t["stat"][:, 4:5], AF.Sqrt, reads=(b["stat"],), writes=(b["stat"],))
        p.tt(t["stat"][:, 5:6], t["stat"][:, 4:5], phm[:, 6:7], ALU.mult,
             reads=(phmB,), writes=(b["stat"],))
        p.dma("gpsimd", t["bm_d"][tt * 128:(tt + 1) * 128, :], t["stat"][:, 5:6],
              reads=(b["stat"],), writes=(b["bm_d"],))
    p.coll("AllReduce", ALU.max, GRP4, t["bm_d"].ap(), t["bmax_d"].ap(),
           reads=(b["bm_d"],), writes=(b["bmax_d"],))
    for tt in range(NT):
        la, laB = t["a1"], b["a1"]
        p.dma("sync", la[:, 0:NP], t["l_d"][tt * 128:(tt + 1) * 128, :],
              reads=(b["l_d"],), writes=(laB,))
        bm, bmB = t["stat"], b["stat"]
        p.dma("sync", bm[:, 4:5], t["bmax_d"][tt * 128:(tt + 1) * 128, :],
              reads=(b["bmax_d"],), writes=(bmB,))
        p.ts(bm[:, 5:6], bm[:, 4:5], -1.0, None, ALU.mult, reads=(bmB,), writes=(bmB,))
        ea, eaB = t["a2"], b["a2"]
        p.act(ea[:, 0:NP], la[:, 0:NP], AF.Exp, bias=bm[:, 5:6], reads=(laB, bmB), writes=(eaB,))
        p.dma("gpsimd", t["e_d"][tt * 128:(tt + 1) * 128, :], ea[:, 0:NP],
              reads=(eaB,), writes=(b["e_d"],))
    p.coll("AllReduce", ALU.add, GRP4, t["e_d"].ap(), t["esum_d"].ap(),
           reads=(b["e_d"],), writes=(b["esum_d"],))
    for tt in range(NT):
        ea, eaB = t["a1"], b["a1"]
        p.dma("sync", ea[:, 0:NP], t["e_d"][tt * 128:(tt + 1) * 128, :],
              reads=(b["e_d"],), writes=(eaB,))
        es, esB = t["a2"], b["a2"]
        p.dma("sync", es[:, 0:NP], t["esum_d"][tt * 128:(tt + 1) * 128, :],
              reads=(b["esum_d"],), writes=(esB,))
        p.op("vector", lambda e, es=es: e.reciprocal(es[:, 0:NP], es[:, 0:NP]),
             reads=(esB,), writes=(esB,))
        dd, ddB = t["w16a"], b["w16a"]
        p.tt(dd[:, 0:NP], ea[:, 0:NP], es[:, 0:NP], ALU.mult, reads=(eaB, esB), writes=(ddB,))
        p.dma("gpsimd", t["d_d"][tt * 128:(tt + 1) * 128, :], dd[:, 0:NP],
              reads=(ddB,), writes=(b["d_d"],))
    for ch in range(16):
        for jt in range(2):
            ps3, ps3B = t["psB"], b["psB"]
            for tt in range(NT):
                df, dfB = t["w16c"], b["w16c"]
                p.dma("sync", df[0:128, 0:128],
                      t["d_d"][tt * 128:(tt + 1) * 128, ch * 128:(ch + 1) * 128],
                      reads=(b["d_d"],), writes=(dfB,))
                xf, xfB = t["xf"], b["xf"]
                p.dma("sync", xf[0:128, 0:512],
                      t["xn_tok_d"][tt * 128:(tt + 1) * 128, jt * 512:(jt + 1) * 512],
                      reads=(b["xn_tok_d"],), writes=(xfB,))
                p.mm(ps3[:, 0:512], df[0:128, 0:128], xf[0:128, 0:512],
                     start=(tt == 0), stop=(tt == NT - 1), reads=(xfB, dfB), writes=(ps3B,))
            so, soB = t["a2"], b["a2"]
            p.copy(so[:, 0:512], ps3[:, 0:512], reads=(ps3B,), writes=(soB,))
            p.dma("gpsimd", t["slf_d"][ch * 128:(ch + 1) * 128, jt * 512:(jt + 1) * 512],
                  so[:, 0:512], reads=(soB,), writes=(b["slf_d"],))
    p.coll("AllReduce", ALU.add, PAIRS, t["slf_d"].ap(), t["slfsum_d"].ap(),
           reads=(b["slf_d"],), writes=(b["slfsum_d"],))
    p.coll("AllGather", ALU.bypass, GRP4, t["slfsum_d"].ap(), t["slall_d"].ap(),
           reads=(b["slfsum_d"],), writes=(b["slall_d"],))
    for g in range(8):
        ms, msB = t["a1"], b["a1"]
        p.igather(ms[:, 0:1024], t["slall_d"].ap(), t["yidx_sb"][:, 16 + g:17 + g], 8191,
                  reads=(b["slall_d"], b["yidx_sb"]), writes=(msB,))
        mb, mbB = t["w16a"], b["w16a"]
        p.copy(mb[:, 0:1024], ms[:, 0:1024], reads=(msB,), writes=(mbB,))
        for ch in range(8):
            pt, ptB = t["psT"], b["psT"]
            p.tr(pt[:, 0:128], mb[:, ch * 128:(ch + 1) * 128], t["ident"][:, :],
                 reads=(mbB, b["ident"]), writes=(ptB,))
            p.copy(t["wbig1"][:, ch * 1024 + g * 128:ch * 1024 + (g + 1) * 128], pt[:, 0:128],
                   reads=(ptB,), writes=(b["wbig1"],))
    for nloc in range(8):
        wsb, wsbB = t["wk1"], b["wk1"]
        for ch in range(8):
            p.dma("gpsimd", wsb[:, ch * 1024:(ch + 1) * 1024],
                  t["f3_w"][nloc * 1024 + ch * 128: nloc * 1024 + (ch + 1) * 128, :],
                  reads=(b["f3_w"],), writes=(wsbB,))
        brow, browB = t["w16c"], b["w16c"]
        p.dma("gpsimd", brow[0:1, 0:1024], t["f3_b"][nloc:nloc + 1, :],
              reads=(b["f3_b"],), writes=(browB,))
        for bb_ in range(4):
            loc0 = bb_ * 256 + nloc * 32
            for j2 in range(2):
                psy, psyB = t["psB"], b["psB"]
                for ch in range(8):
                    p.mm(psy[0:32, 0:512],
                         t["wbig1"][:, ch * 1024 + loc0: ch * 1024 + loc0 + 32],
                         wsb[:, ch * 1024 + j2 * 512: ch * 1024 + (j2 + 1) * 512],
                         start=(ch == 0), stop=False, reads=(b["wbig1"], wsbB), writes=(psyB,))
                p.mm(psy[0:32, 0:512], t["onesrb"][0:1, 0:32],
                     brow[0:1, j2 * 512:(j2 + 1) * 512], start=False, stop=True,
                     reads=(b["onesrb"], browB), writes=(psyB,))
                yo, yoB = t["w16a"], b["w16a"]
                p.copy(yo[0:32, 0:512], psy[0:32, 0:512], reads=(psyB,), writes=(yoB,))
                p.dma("gpsimd",
                      t["y_loc_d"][bb_ * 256 + nloc * 32: bb_ * 256 + (nloc + 1) * 32,
                                   j2 * 512:(j2 + 1) * 512],
                      yo[0:32, 0:512], reads=(yoB,), writes=(b["y_loc_d"],))
    p.coll("AllGather", ALU.bypass, ALL8, t["y_loc_d"].ap(), t["yall_d"].ap(),
           reads=(b["y_loc_d"],), writes=(b["yall_d"],))
    for j in range(16):
        yb, ybB = t["w16a"], b["w16a"]
        p.igather(yb[:, 0:1024], t["yall_d"].ap(), t["yidx_sb"][:, j:j + 1], 8191,
                  reads=(b["yall_d"], b["yidx_sb"]), writes=(ybB,))
        p.dma("gpsimd", t["yb_d"][j * 128:(j + 1) * 128, :], yb[:, 0:1024],
              reads=(ybB,), writes=(b["yb_d"],))
    for tt in range(NT):
        la, laB = t["a1"], b["a1"]
        p.dma("sync", la[:, 0:NP], t["l_d"][tt * 128:(tt + 1) * 128, :],
              reads=(b["l_d"],), writes=(laB,))
        st_, stB = t["stat"], b["stat"]
        p.red(st_[:, 4:5], la[:, 0:NP], ALU.max, reads=(laB,), writes=(stB,))
        p.ts(st_[:, 5:6], st_[:, 4:5], -1.0, None, ALU.mult, reads=(stB,), writes=(stB,))
        ca, caB = t["a2"], b["a2"]
        p.act(ca[:, 0:NP], la[:, 0:NP], AF.Exp, bias=st_[:, 5:6], reads=(laB, stB), writes=(caB,))
        p.red(st_[:, 6:7], ca[:, 0:NP], ALU.add, reads=(caB,), writes=(stB,))
        p.op("vector", lambda e, st_=st_: e.reciprocal(st_[:, 7:8], st_[:, 6:7]),
             reads=(stB,), writes=(stB,))
        cb, cbB = t["w16a"], b["w16a"]
        p.ts(cb[:, 0:NP], ca[:, 0:NP], st_[:, 7:8], None, ALU.mult, reads=(caB, stB), writes=(cbB,))
        cf, cfB = t["w16b"], b["w16b"]
        for ch in range(16):
            pt, ptB = t["psT"], b["psT"]
            p.tr(pt[:, 0:128], cb[:, ch * 128:(ch + 1) * 128], t["ident"][:, :],
                 reads=(cbB, b["ident"]), writes=(ptB,))
            p.copy(cf[:, ch * 128:(ch + 1) * 128], pt[:, 0:128], reads=(ptB,), writes=(cfB,))
        for j2 in range(2):
            pso, psoB = t["psB"], b["psB"]
            for j in range(16):
                yf, yfB = t["w16c"], b["w16c"]
                p.dma("sync", yf[0:128, 0:512],
                      t["yb_d"][j * 128:(j + 1) * 128, j2 * 512:(j2 + 1) * 512],
                      reads=(b["yb_d"],), writes=(yfB,))
                p.mm(pso[:, 0:512], cf[:, j * 128:(j + 1) * 128], yf[0:128, 0:512],
                     start=(j == 0), stop=(j == 15), reads=(cfB, yfB), writes=(psoB,))
            go, goB = t["a2"], b["a2"]
            p.copy(go[:, 0:512], pso[:, 0:512], reads=(psoB,), writes=(goB,))
            p.dma("gpsimd", t["g_d"][tt * 128:(tt + 1) * 128, j2 * 512:(j2 + 1) * 512],
                  go[:, 0:512], reads=(goB,), writes=(b["g_d"],))


DEBUG = False


def emit_debug_dumps0(c, p):
    t, b = c.t, c.b
    p.dma("gpsimd", t["dbg_xflat"][:, :], t["xflat_d"].ap(), reads=(b["xflat_d"],),
          writes=(b["dbg_xflat"],))
    for tt in range(NT):
        p.dma("gpsimd", t["dbg_lg"][tt * 128:(tt + 1) * 128, :], t["ltok"][:, tt * 8:(tt + 1) * 8],
              reads=(b["ltok"],), writes=(b["dbg_lg"],))
        p.dma("gpsimd", t["dbg_est"][tt * 128:(tt + 1) * 128, 0:1], t["est"][:, tt:tt + 1],
              reads=(b["est"],), writes=(b["dbg_est"],), allow_slow_non_contiguous=True)
        p.dma("gpsimd", t["dbg_est"][tt * 128:(tt + 1) * 128, 1:2], t["est"][:, NT + tt:NT + tt + 1],
              reads=(b["est"],), writes=(b["dbg_est"],), allow_slow_non_contiguous=True)
    p.dma("gpsimd", t["dbg_slots"][:, :], t["slots_d"].ap(), reads=(b["slots_d"],),
          writes=(b["dbg_slots"],))
    p.dma("gpsimd", t["dbg_idx"][:, :], t["idxall_d"].ap(), reads=(b["idxall_d"],),
          writes=(b["dbg_idx"],))


def emit_debug_dumps(c, p):
    t, b = c.t, c.b
    for nm in ("s", "g", "xn32"):
        if "dbg_" + nm not in t:
            continue
        p.dma("gpsimd", t["dbg_" + nm][:, :], t[nm.replace("xn32", "xn32") + "_d"].ap(),
              reads=(b[nm + "_d"] if nm != "xn32" else b["xn32_d"],),
              writes=(b["dbg_" + nm],))


def emit_combine(c, p):
    t, b = c.t, c.b
    if DEBUG:
        emit_debug_dumps(c, p)
        if "dbg_xflat" in c.t:
            emit_debug_dumps0(c, p)
    for tt in range(NT):
        r = slice(tt * 128, (tt + 1) * 128)
        xa, xaB = t["a1"], b["a1"]
        sa, saB = t["a2"], b["a2"]
        ga, gaB = t["a3"], b["a3"]
        p.dma("sync", xa[:, 0:D], t["x_in"][r, :], reads=(b["x_in"],), writes=(xaB,))
        p.dma("sync", sa[:, 0:D], t["s_d"][r, :], reads=(b["s_d"],), writes=(saB,))
        p.dma("sync", ga[:, 0:D], t["g_d"][r, :], reads=(b["g_d"],), writes=(gaB,))
        p.act(ga[:, 0:D], ga[:, 0:D], AF.Silu, reads=(gaB,), writes=(gaB,))
        p.tt(sa[:, 0:D], sa[:, 0:D], ga[:, 0:D], ALU.mult, reads=(gaB,), writes=(saB,))
        p.tt(xa[:, 0:D], xa[:, 0:D], sa[:, 0:D], ALU.add, reads=(saB,), writes=(xaB,))
        p.dma("gpsimd", t["x_out"][r, :], xa[:, 0:D], reads=(xaB,), writes=(b["x_out"],))



def build_block_program(blk):
    nc = bass.Bass("TRN2", target_bir_lowering=False, debug=False)
    es = contextlib.ExitStack()
    p = Pr(nc)
    c = Ctx(nc, es, p)

    c.dram("x_in", [T, D], F32, kind="ExternalInput")
    c.dram("x_out", [T, D], F32, kind="ExternalOutput")
    c.dram("ident_in", [128, 128], F32, kind="ExternalInput")
    c.dram("cum_in", [128, 128], F32, kind="ExternalInput")
    c.dram("xn_tok_d", [T, D], BF16)
    c.dram("xn_fm_d", [1025, T], BF16)
    c.dram("xn32_d", [T, D], F32)
    c.dram("s_d", [T, D], F32)
    c.dram("g_d", [T, D], F32)
    p.fence_bufs = (c.dram("fence_in", [1, 8], F32), c.dram("fence_out", [1, 8], F32))
    if DEBUG:
        KRD = 66 if blk == 1 else 64
        for nm in ("dbg_s", "dbg_g", "dbg_xn32"):
            c.dram(nm, [T, D], F32, kind="ExternalOutput")
        c.dram("dbg_qx", [H * KRD, T], F32, kind="ExternalOutput")
        c.dram("dbg_kx", [H * KRD, T], F32, kind="ExternalOutput")
        c.dram("dbg_vx", [T, 1040], F32, kind="ExternalOutput")
        c.dram("dbg_kg", [2 * H * KRD, T], F32, kind="ExternalOutput")
        c.dram("dbg_vg", [2 * T, 1040], F32, kind="ExternalOutput")
        if blk == 0:
            c.dram("dbg_G", [128, 80], F32, kind="ExternalOutput")
            c.dram("dbg_strip", [128, 2048], F32, kind="ExternalOutput")
            c.dram("dbg_xflat", [T, D], F32, kind="ExternalOutput")
            c.dram("dbg_lg", [T, 8], F32, kind="ExternalOutput")
            c.dram("dbg_est", [T, 2], F32, kind="ExternalOutput")
            c.dram("dbg_slots", [4096, D], F32, kind="ExternalOutput")
            c.dram("dbg_idx", [8 * 4096, 1], F32, kind="ExternalOutput")

    KR = 66 if blk == 1 else 64
    c.dram("qx_d", [H * KR, T], BF16)
    c.dram("kx_d", [H * KR, T], BF16)
    c.dram("vx_d", [T, 1040], BF16)
    c.dram("kg_d", [2 * H * KR, T], BF16)
    c.dram("vg_d", [2 * T, 1040], BF16)
    c.dram("ofm_d", [1024, T], BF16)

    if blk == 0:
        for nm, shp in (("wq_in", [1024, 1024]), ("wk_in", [1024, 1024]),
                        ("wv_in", [1025, 1024]), ("wo_in", [1025, 1024]),
                        ("bq_in", [1024, 1]), ("bk_in", [1024, 1])):
            c.dram(nm, shp, F32, kind="ExternalInput")
    elif blk == 1:
        for nm, shp in (("wq_in", [1025, 1024]), ("wk_in", [1025, 1024]),
                        ("wv_in", [1025, 1024]), ("wo_in", [1025, 1024]),
                        ("f1_wfcT", [1024, 4096]), ("f1_wprojT", [4096, 1024]),
                        ("f1_lnw", [1, 4096]), ("f1_lnb", [1, 4096])):
            c.dram(nm, shp, F32, kind="ExternalInput")
        c.dram("f1_wq1_d", [1024, 4096], BF16)
        c.dram("f1_wq2_d", [4096, 1024], BF16)
        c.dram("f1_h_d", [T, 4096], F32)
        c.dram("f1_hnf_d", [4096, T], BF16)
    elif blk == 2:
        for nm, shp in (("wq_in", [1024, 1024]), ("wk_in", [1024, 1024]),
                        ("wv_in", [1024, 1024]), ("wo_in", [1025, 1024]),
                        ("l2_temp", [1, 16]),
                        ("f2_wr", [1024, 7]), ("f2_rb", [7, 1]), ("f2_sel", [1, 8]),
                        ("f2_wg_w", [1024, 2816]), ("f2_wu_w", [1024, 2816]),
                        ("f2_wd_w", [2816, 1024])):
            c.dram(nm, shp, F32, kind="ExternalInput")
        c.dram("xnfm_all_d", [8 * 1025, T], BF16)
        c.dram("wx_d", [T, 8], F32)
        c.dram("wall_d", [8 * T, 8], F32)
        c.dram("f2_mid_d", [8 * T, 2816], BF16)
        c.dram("f2_midf_d", [2816, 8 * T], BF16)
        c.dram("f2_out_d", [8 * T, 1024], F32)
    elif blk == 3:
        for nm, shp in (("wq_in", [1024, 1024]), ("wk_in", [1024, 1024]),
                        ("wv_in", [1024, 1024]), ("wo_in", [1025, 1024]),
                        ("l3_lns", [1, 1024]), ("l3_lnb", [1, 1024]),
                        ("f3_phi", [1024, 2048]), ("f3_w", [8 * 1024, 1024]),
                        ("f3_b", [8, 1024])):
            c.dram(nm, shp, F32, kind="ExternalInput")
        c.dram("f3_yidx", [128, 32], I32, kind="ExternalInput")
        c.dram("zn_fm_d", [1025, T], BF16)
        c.dram("l_d", [T, 2048], F32)
        c.dram("bm_d", [T, 1], F32)
        c.dram("bmax_d", [T, 1], F32)
        c.dram("e_d", [T, 2048], F32)
        c.dram("esum_d", [T, 2048], F32)
        c.dram("d_d", [T, 2048], BF16)
        c.dram("slf_d", [2048, 1024], F32)
        c.dram("slfsum_d", [2048, 1024], F32)
        c.dram("slall_d", [8192, 1024], F32)
        c.dram("y_loc_d", [1024, 1024], BF16)
        c.dram("yall_d", [8192, 1024], BF16)
        c.dram("yb_d", [T, 1024], BF16)
    if blk == 0:
        for nm, shp in (("f0_wp_aug", [1025, 1024]), ("f0_wpT", [1024, 1024]),
                        ("f0_wg", [1024, 8]), ("f0_bp", [1024, 1]),
                        ("f0_w1_aug", [1025, 4096]), ("f0_w2_aug", [4097, 1024]),
                        ("f0_ln1g", [1, 4096]), ("f0_ln1b", [1, 4096]),
                        ("f0_lng", [1, 1024]), ("f0_lnb", [1, 1024]),
                        ("tok_ids", [T, 1]), ("premask", [8, 1]), ("iota8_in", [1, 8])):
            c.dram(nm, shp, F32, kind="ExternalInput")
        c.dram("rowidx", [32, 1], I32, kind="ExternalInput")
        c.dram("xflat_d", [T, D], BF16)
        c.dram("xflatall_d", [8 * T, D], BF16)
        c.dram("cnt_d", [1, 8], F32)
        c.dram("cntall_d", [8, 8], F32)
        c.dram("idxsc_d", [8 * 4096, 1], F32)
        c.dram("idxall_d", [8 * 4096, 1], F32)
        c.dram("hh_d", [4096, 4096], F32)
        c.dram("gel_d", [4096, 4096], BF16)
        c.dram("slots_d", [4096, D], BF16)
        c.dram("slotsall_d", [8 * 4096, D], BF16)

    c.sb("cst", [128, 1], F32)
    nc.const_aps.register(F32, 1e-5, c.t["cst"][:, 0:1])
    c.sb("ident", [128, 128], BF16)
    c.sb("identf", [128, 128], F32)
    c.sb("cum", [128, 128], F32)
    c.sb("onesr", [1, 512], F32)
    c.sb("onesrb", [1, T], BF16)
    c.sb("onesc", [128, 1], F32)
    c.sb("onescb", [128, 1], BF16)
    c.sb("stat", [128, 8], F32)
    c.sb("lnsq", [128, 4096], F32)
    c.sb("a1", [128, 4096], F32)
    c.sb("a2", [128, 2048], F32)
    c.sb("a3", [128, 1024], F32)
    c.sb("rsum", [1, 256], F32)
    c.sb("w16a", [128, 4096], BF16)
    c.sb("w16b", [128, 4096], BF16)
    c.sb("w16c", [128, 1024], BF16)
    c.sb("wk1", [128, 9216], BF16)
    c.sb("xf", [128, 512], BF16)
    c.sb("kf", [128, 2 * T], BF16)
    c.sb("qh", [128, T], BF16)
    c.sb("vh", [128, NKC * 65], BF16)
    c.sb("pb", [128, NKC * QT], BF16)
    c.sb("ostg", [64, 256], BF16)
    c.sb("bb", [128, H], F32)
    c.sb("row4", [1, 64], F32)
    c.sb("bqk", [128, 16], F32)
    if blk == 0:
        c.sb("gf32", [128, 80], F32)
        c.sb("wf32s", [128, 1024], F32)
        c.sb("wg32", [128, 72], F32)
        c.sb("ltok", [128, NT * 8], F32)
        c.sb("m8", [128, NT * 8], F32)
        c.sb("est", [128, 2 * NT], F32)
        c.sb("zz8", [128, 8], F32)
        c.sb("sc8", [128, 8], F32)
        c.sb("sc8b", [128, 8], F32)
        c.sb("sc8c", [128, 8], F32)
        c.sb("iota8", [128, 8], F32)
        c.sb("idxm", [128, 64], I32)
        c.sb("rowidx_sb", [32, 1], I32)
        c.sb("g32t", [32, 128], F32)
        c.sb("tid_sb", [128, 1], F32)
        c.t["g1bc"] = c.t["wk1"][:, 0:4096]
        c.b["g1bc"] = c.b["wk1"]
        c.t["b1bc"] = c.t["wk1"][:, 4096:8192]
        c.b["b1bc"] = c.b["wk1"]
        c.sb("wbig", [128, 9 * 2048 + 16 * 1024], BF16)
        c.sb("premask_sb", [8, 1], F32)
    if blk == 1:
        c.sb("q2row", [16, T], F32)
        c.sb("wbig1", [128, 32 * 512], BF16)
    if blk == 2:
        c.sb("wg32", [128, 72], F32)
        c.sb("ltok", [128, NT * 8], F32)
        c.sb("m8", [128, NT * 8], F32)
        c.sb("est", [128, 2 * NT], F32)
        c.sb("zz8", [128, 8], F32)
        c.sb("sc8", [128, 8], F32)
        c.sb("sc8b", [128, 8], F32)
        c.sb("sc8c", [128, 8], F32)
        c.sb("iota8", [128, 8], F32)
        c.sb("tempb", [128, 16], F32)
        c.sb("wbig1", [128, 22 * 512], BF16)
    if blk == 3:
        c.sb("zlng", [128, 2048], F32)
        c.sb("wbig1", [128, 8 * 2048], BF16)
        c.sb("yidx_sb", [128, 32], I32)
    c.ps("psS", [128, 2048], F32)
    for _j in range(4):
        c.b["psS%d" % _j] = Buf("psS%d" % _j)
    c.ps("psB", [128, 1024], F32)
    c.ps("psT", [128, 512], BF16)

    emit_prelude(c, p)
    if blk == 0:
        rsb, rsbB = c.t["a3"], c.b["a3"]
        p.dma("sync", rsb[0:1, 0:8], c.t["iota8_in"][0:1, :],
              reads=(c.b["iota8_in"],), writes=(rsbB,))
        ps, psB = c.t["psB"], c.b["psB"]
        p.mm(ps[:, 0:8], c.t["onesr"][0:1, 0:128], rsb[0:1, 0:8], start=True, stop=True,
             reads=(rsbB, c.b["onesr"]), writes=(psB,))
        p.copy(c.t["iota8"][:, 0:8], ps[:, 0:8], reads=(psB,), writes=(c.b["iota8"],))
        p.dma("sync", c.t["premask_sb"][0:8, 0:1], c.t["premask"][:, :],
              reads=(c.b["premask"],), writes=(c.b["premask_sb"],))
        c.t["premask"], c.b["premask"] = c.t["premask_sb"], c.b["premask_sb"]
        p.dma("sync", c.t["rowidx_sb"][0:32, 0:1], c.t["rowidx"][:, :],
              reads=(c.b["rowidx"],), writes=(c.b["rowidx_sb"],))

    if blk == 3:
        p.dma("sync", c.t["yidx_sb"][:, 0:32], c.t["f3_yidx"][:, :],
              reads=(c.b["f3_yidx"],), writes=(c.b["yidx_sb"],))
    emit_ln_phase(c, p)
    if blk == 0:
        emit_attn(c, p, "l0")
        emit_ffn0(c, p)
    elif blk == 1:
        emit_attn(c, p, "l1")
        emit_ffn1(c, p)
    elif blk == 2:
        emit_attn(c, p, "l2")
        emit_ffn2(c, p)
    else:
        emit_attn(c, p, "l3")
        emit_ffn3(c, p)
    emit_combine(c, p)

    with contextlib.ExitStack() as es2:
        sems = {e: es2.enter_context(nc.semaphore("sem_" + e)) for e in ENGINES}
        block = es2.enter_context(nc.Block())
        p.emit(block, sems)
    es.close()
    return nc



def _host_consts(core):
    ident = np.eye(128, dtype=np.float32)
    cum = np.triu(np.ones((128, 128), np.float32), 1)
    tok_ids = (np.arange(T, dtype=np.float32) + core * T).reshape(T, 1)
    premask = (np.arange(8) < core).astype(np.float32).reshape(8, 1)
    iota8 = np.arange(8, dtype=np.float32).reshape(1, 8)
    rowidx = (core * 32 + np.arange(32, dtype=np.int32)).reshape(32, 1)
    return {"ident_in": ident, "cum_in": cum, "tok_ids": tok_ids,
            "premask": premask, "iota8_in": iota8, "rowidx": rowidx}


def _block0_weights(inputs, core):
    f32 = np.float32
    wqkv = np.asarray(inputs["L0_attn_w_qkv"], f32)
    bqkv = np.asarray(inputs["L0_attn_b_qkv"], f32)
    d = {
        "wq_in": np.ascontiguousarray(wqkv[:, 0:1024]),
        "wk_in": np.ascontiguousarray(wqkv[:, 1024:2048]),
        "wv_in": np.ascontiguousarray(
            np.vstack([wqkv[:, 2048:3072], bqkv[2048:3072][None, :]])),
        "wo_in": np.ascontiguousarray(
            np.vstack([np.asarray(inputs["L0_attn_w_proj"], f32),
                       np.asarray(inputs["L0_attn_b_proj"], f32)[None, :]])),
        "bq_in": np.ascontiguousarray(bqkv[0:1024].reshape(1024, 1)),
        "bk_in": np.ascontiguousarray(bqkv[1024:2048].reshape(1024, 1)),
        "f0_wp_aug": np.ascontiguousarray(
            np.vstack([np.asarray(inputs["L0_ffn_W_proj"], f32),
                       np.asarray(inputs["L0_ffn_b_proj"], f32)[None, :]])),
        "f0_wpT": np.ascontiguousarray(np.asarray(inputs["L0_ffn_W_proj"], f32).T),
        "f0_wg": np.ascontiguousarray(np.asarray(inputs["L0_ffn_W_gate"], f32)),
        "f0_bp": np.ascontiguousarray(
            np.asarray(inputs["L0_ffn_b_proj"], f32).reshape(1024, 1)),
        "f0_w1_aug": np.ascontiguousarray(
            np.vstack([np.asarray(inputs["L0_ffn_W1"], f32)[core],
                       np.asarray(inputs["L0_ffn_b1"], f32)[core][None, :]])),
        "f0_w2_aug": np.ascontiguousarray(
            np.vstack([np.asarray(inputs["L0_ffn_W2"], f32)[core],
                       np.asarray(inputs["L0_ffn_b2"], f32)[core][None, :]])),
        "f0_ln1g": np.asarray(inputs["L0_ffn_ln1_g"], f32)[core][None, :].copy(),
        "f0_ln1b": np.asarray(inputs["L0_ffn_ln1_b"], f32)[core][None, :].copy(),
        "f0_lng": np.asarray(inputs["L0_ffn_ln_g"], f32)[None, :].copy(),
        "f0_lnb": np.asarray(inputs["L0_ffn_ln_b"], f32)[None, :].copy(),
    }
    return d


def shard_x(x):
    out = []
    for c in range(NCORE):
        bb, hh = c // 2, c % 2
        out.append(np.ascontiguousarray(np.asarray(x, np.float32)[bb, hh * T:(hh + 1) * T, :]))
    return out


def unshard_x(shards):
    x = np.empty((B_, S_, D), np.float32)
    for c in range(NCORE):
        bb, hh = c // 2, c % 2
        x[bb, hh * T:(hh + 1) * T, :] = shards[c]
    return x


def _block1_weights(inputs, core):
    f32 = np.float32
    def aug(w, bias):
        return np.ascontiguousarray(np.vstack([np.asarray(w, f32).T,
                                               np.asarray(bias, f32)[None, :]]))
    return {
        "wq_in": aug(inputs["L1_attn_wq"], inputs["L1_attn_bq"]),
        "wk_in": aug(inputs["L1_attn_wk"], inputs["L1_attn_bk"]),
        "wv_in": aug(inputs["L1_attn_wv"], inputs["L1_attn_bv"]),
        "wo_in": aug(inputs["L1_attn_wo"], inputs["L1_attn_bo"]),
        "f1_wfcT": np.ascontiguousarray(np.asarray(inputs["L1_ffn_W_fc"], f32).T),
        "f1_wprojT": np.ascontiguousarray(np.asarray(inputs["L1_ffn_W_proj"], f32).T),
        "f1_lnw": np.asarray(inputs["L1_ffn_ln_w"], f32)[None, :].copy(),
        "f1_lnb": np.asarray(inputs["L1_ffn_ln_b"], f32)[None, :].copy(),
    }


def _block2_weights(inputs, core):
    f32 = np.float32
    qkvT = np.ascontiguousarray(np.asarray(inputs["L2_attn_qkv_w"], f32).T)
    if core < 7:
        wg = np.asarray(inputs["L2_ffn_routed_gate"], f32)[core]
        wu = np.asarray(inputs["L2_ffn_routed_up"], f32)[core]
        wd = np.asarray(inputs["L2_ffn_routed_down"], f32)[core]
    else:
        wg = np.asarray(inputs["L2_ffn_shared_gate"], f32)
        wu = np.asarray(inputs["L2_ffn_shared_up"], f32)
        wd = np.asarray(inputs["L2_ffn_shared_down"], f32)
    sel = np.zeros((1, 8), f32)
    sel[0, min(core, 7)] = 1.0
    return {
        "wq_in": np.ascontiguousarray(qkvT[:, 0:1024]),
        "wk_in": np.ascontiguousarray(qkvT[:, 1024:2048]),
        "wv_in": np.ascontiguousarray(qkvT[:, 2048:3072]),
        "wo_in": np.ascontiguousarray(
            np.vstack([np.asarray(inputs["L2_attn_proj_w"], f32).T,
                       np.asarray(inputs["L2_attn_proj_b"], f32)[None, :]])),
        "l2_temp": np.asarray(inputs["L2_attn_temperature"], f32).reshape(1, 16).copy(),
        "f2_wr": np.ascontiguousarray(np.asarray(inputs["L2_ffn_Wr"], f32)),
        "f2_rb": np.asarray(inputs["L2_ffn_routing_bias"], f32).reshape(7, 1).copy(),
        "f2_sel": sel,
        "f2_wg_w": np.ascontiguousarray(wg),
        "f2_wu_w": np.ascontiguousarray(wu),
        "f2_wd_w": np.ascontiguousarray(wd),
    }


def _block3_weights(inputs, core):
    f32 = np.float32
    wqkv = np.asarray(inputs["L3_attn_w_qkv"], f32).reshape(1024, 16, 64, 3)
    bcore = core // 2
    yidx = np.zeros((128, 32), np.int32)
    for j in range(16):
        npidx = j * 128 + np.arange(128)
        n = npidx // 32
        pp = npidx % 32
        r = n // 8
        yidx[:, j] = r * 1024 + bcore * 256 + (n % 8) * 32 + pp
    for g in range(8):
        rloc = g * 128 + np.arange(128)
        bb = rloc // 256
        rem = rloc % 256
        yidx[:, 16 + g] = bb * 2048 + core * 256 + rem
    return {
        "wq_in": np.ascontiguousarray(wqkv[..., 0].reshape(1024, 1024)),
        "wk_in": np.ascontiguousarray(wqkv[..., 1].reshape(1024, 1024)),
        "wv_in": np.ascontiguousarray(wqkv[..., 2].reshape(1024, 1024)),
        "wo_in": np.ascontiguousarray(
            np.vstack([np.asarray(inputs["L3_attn_w_proj"], f32),
                       np.asarray(inputs["L3_attn_b_proj"], f32)[None, :]])),
        "l3_lns": np.asarray(inputs["L3_attn_ln_scale"], f32)[None, :].copy(),
        "l3_lnb": np.asarray(inputs["L3_attn_ln_bias"], f32)[None, :].copy(),
        "f3_phi": np.ascontiguousarray(np.asarray(inputs["L3_ffn_phi"], f32).reshape(1024, 2048)),
        "f3_w": np.ascontiguousarray(
            np.asarray(inputs["L3_ffn_W"], f32)[core * 8:(core + 1) * 8].reshape(8 * 1024, 1024)),
        "f3_b": np.ascontiguousarray(np.asarray(inputs["L3_ffn_b"], f32)[core * 8:(core + 1) * 8]),
        "f3_yidx": yidx,
    }


_BLOCKS = [0, 1, 2, 3]
FUSED = True


ATTN_WNAMES = {
    0: ("wq_in", "wk_in", "wv_in", "wo_in", "bq_in", "bk_in"),
    1: ("wq_in", "wk_in", "wv_in", "wo_in"),
    2: ("wq_in", "wk_in", "wv_in", "wo_in", "l2_temp", "f2_wr", "f2_rb", "f2_sel",
        "f2_wg_w", "f2_wu_w", "f2_wd_w"),
    3: ("wq_in", "wk_in", "wv_in", "wo_in", "l3_lns", "l3_lnb", "f3_phi", "f3_w",
        "f3_b", "f3_yidx"),
}
WSHAPES = {
    0: {"wq_in": [1024, 1024], "wk_in": [1024, 1024], "wv_in": [1025, 1024],
        "wo_in": [1025, 1024], "bq_in": [1024, 1], "bk_in": [1024, 1]},
    1: {"wq_in": [1025, 1024], "wk_in": [1025, 1024], "wv_in": [1025, 1024],
        "wo_in": [1025, 1024]},
    2: {"wq_in": [1024, 1024], "wk_in": [1024, 1024], "wv_in": [1024, 1024],
        "wo_in": [1025, 1024], "l2_temp": [1, 16], "f2_wr": [1024, 7],
        "f2_rb": [7, 1], "f2_sel": [1, 8], "f2_wg_w": [1024, 2816],
        "f2_wu_w": [1024, 2816], "f2_wd_w": [2816, 1024]},
    3: {"wq_in": [1024, 1024], "wk_in": [1024, 1024], "wv_in": [1024, 1024],
        "wo_in": [1025, 1024], "l3_lns": [1, 1024], "l3_lnb": [1, 1024],
        "f3_phi": [1024, 2048], "f3_w": [8 * 1024, 1024], "f3_b": [8, 1024]},
}


def build_full_program():
    nc = bass.Bass("TRN2", target_bir_lowering=False, debug=False)
    es = contextlib.ExitStack()
    p = Pr(nc)
    c = Ctx(nc, es, p)

    c.dram("x_in0", [T, D], F32, kind="ExternalInput")
    c.dram("x_out3", [T, D], F32, kind="ExternalOutput")
    c.dram("xh1", [T, D], F32)
    c.dram("xh2", [T, D], F32)
    c.dram("xh3", [T, D], F32)
    c.dram("ident_in", [128, 128], F32, kind="ExternalInput")
    c.dram("cum_in", [128, 128], F32, kind="ExternalInput")
    c.dram("xn_tok_d", [T, D], BF16)
    c.dram("xn_fm_d", [1025, T], BF16)
    c.dram("xn32_d", [T, D], F32)
    c.dram("s_d", [T, D], F32)
    c.dram("g_d", [T, D], F32)
    p.fence_bufs = (c.dram("fence_in", [1, 8], F32), c.dram("fence_out", [1, 8], F32))

    KRX = 66
    c.dram("qx_d", [H * KRX, T], BF16)
    c.dram("kx_d", [H * KRX, T], BF16)
    c.dram("vx_d", [T, 1040], BF16)
    c.dram("kg_d", [2 * H * KRX, T], BF16)
    c.dram("vg_d", [2 * T, 1040], BF16)
    c.dram("ofm_d", [1024, T], BF16)

    for blk in range(4):
        for nm in ATTN_WNAMES[blk]:
            if nm == "f3_yidx":
                c.dram(f"b{blk}_{nm}", [128, 32], I32, kind="ExternalInput")
            else:
                c.dram(f"b{blk}_{nm}", WSHAPES[blk][nm], F32, kind="ExternalInput")
    for nm, shp in (("f0_wp_aug", [1025, 1024]), ("f0_wpT", [1024, 1024]),
                    ("f0_wg", [1024, 8]), ("f0_bp", [1024, 1]),
                    ("f0_w1_aug", [1025, 4096]), ("f0_w2_aug", [4097, 1024]),
                    ("f0_ln1g", [1, 4096]), ("f0_ln1b", [1, 4096]),
                    ("f0_lng", [1, 1024]), ("f0_lnb", [1, 1024]),
                    ("tok_ids", [T, 1]), ("premask", [8, 1]), ("iota8_in", [1, 8])):
        c.dram(nm, shp, F32, kind="ExternalInput")
    c.dram("rowidx", [32, 1], I32, kind="ExternalInput")
    for nm, shp in (("f1_wfcT", [1024, 4096]), ("f1_wprojT", [4096, 1024]),
                    ("f1_lnw", [1, 4096]), ("f1_lnb", [1, 4096])):
        c.dram(nm, shp, F32, kind="ExternalInput")
    c.dram("xflat_d", [T, D], BF16)
    c.dram("xflatall_d", [8 * T, D], BF16)
    c.dram("cnt_d", [1, 8], F32)
    c.dram("cntall_d", [8, 8], F32)
    c.dram("idxsc_d", [8 * 4096, 1], F32)
    c.dram("idxall_d", [8 * 4096, 1], F32)
    c.dram("hh_d", [4096, 4096], F32)
    c.dram("gel_d", [4096, 4096], BF16)
    c.dram("slots_d", [4096, D], BF16)
    c.dram("slotsall_d", [8 * 4096, D], BF16)
    c.dram("f1_wq1_d", [1024, 4096], BF16)
    c.dram("f1_wq2_d", [4096, 1024], BF16)
    c.dram("f1_h_d", [T, 4096], F32)
    c.dram("f1_hnf_d", [4096, T], BF16)
    c.dram("xnfm_all_d", [8 * 1025, T], BF16)
    c.dram("wx_d", [T, 8], F32)
    c.dram("wall_d", [8 * T, 8], F32)
    c.dram("f2_mid_d", [8 * T, 2816], BF16)
    c.dram("f2_midf_d", [2816, 8 * T], BF16)
    c.dram("f2_out_d", [8 * T, 1024], F32)
    c.dram("zn_fm_d", [1025, T], BF16)
    c.dram("l_d", [T, 2048], F32)
    c.dram("bm_d", [T, 1], F32)
    c.dram("bmax_d", [T, 1], F32)
    c.dram("e_d", [T, 2048], F32)
    c.dram("esum_d", [T, 2048], F32)
    c.dram("d_d", [T, 2048], BF16)
    c.dram("slf_d", [2048, 1024], F32)
    c.dram("slfsum_d", [2048, 1024], F32)
    c.dram("slall_d", [8192, 1024], F32)
    c.dram("y_loc_d", [1024, 1024], BF16)
    c.dram("yall_d", [8192, 1024], BF16)
    c.dram("yb_d", [T, 1024], BF16)

    c.sb("cst", [128, 1], F32)
    nc.const_aps.register(F32, 1e-5, c.t["cst"][:, 0:1])
    c.sb("ident", [128, 128], BF16)
    c.sb("identf", [128, 128], F32)
    c.sb("cum", [128, 128], F32)
    c.sb("onesr", [1, 512], F32)
    c.sb("onesrb", [1, T], BF16)
    c.sb("onesc", [128, 1], F32)
    c.sb("onescb", [128, 1], BF16)
    c.sb("stat", [128, 8], F32)
    c.sb("lnsq", [128, 4096], F32)
    c.sb("a1", [128, 4096], F32)
    c.sb("a2", [128, 2048], F32)
    c.sb("a3", [128, 1024], F32)
    c.sb("rsum", [1, 256], F32)
    c.sb("w16a", [128, 4096], BF16)
    c.sb("w16b", [128, 4096], BF16)
    c.sb("w16c", [128, 1024], BF16)
    c.sb("wk1", [128, 9216], BF16)
    c.sb("xf", [128, 512], BF16)
    c.sb("kf", [128, 2 * T], BF16)
    c.sb("qh", [128, T], BF16)
    c.sb("vh", [128, NKC * 65], BF16)
    c.sb("pb", [128, NKC * QT], BF16)
    c.sb("ostg", [64, 256], BF16)
    c.sb("bb", [128, H], F32)
    c.sb("row4", [1, 64], F32)
    c.sb("bqk", [128, 16], F32)
    c.sb("gf32", [128, 80], F32)
    c.sb("wf32s", [128, 1024], F32)
    c.sb("wg32", [128, 72], F32)
    c.sb("ltok", [128, NT * 8], F32)
    c.sb("m8", [128, NT * 8], F32)
    c.sb("est", [128, 2 * NT], F32)
    c.sb("zz8", [128, 8], F32)
    c.sb("sc8", [128, 8], F32)
    c.sb("sc8b", [128, 8], F32)
    c.sb("sc8c", [128, 8], F32)
    c.sb("iota8", [128, 8], F32)
    c.sb("idxm", [128, 64], I32)
    c.sb("rowidx_sb", [32, 1], I32)
    c.sb("g32t", [32, 128], F32)
    c.sb("tid_sb", [128, 1], F32)
    c.t["g1bc"] = c.t["wk1"][:, 0:4096]
    c.b["g1bc"] = c.b["wk1"]
    c.t["b1bc"] = c.t["wk1"][:, 4096:8192]
    c.b["b1bc"] = c.b["wk1"]
    c.sb("premask_sb", [8, 1], F32)
    c.sb("wbig", [128, 9 * 2048 + 16 * 1024], BF16)
    c.t["wbig1"] = c.t["wbig"][:, 0:16 * 1024]
    c.b["wbig1"] = c.b["wbig"]
    c.t["xq8a"] = c.t["kf"][:, 0:2816]
    c.b["xq8a"] = c.b["kf"]
    c.t["xq8b"] = c.t["pb"][:, 0:2816]
    c.b["xq8b"] = c.b["pb"]
    c.sb("aux8k", [128, 2048], F32)
    c.t["q2row"] = c.t["aux8k"][0:16, 0:T]
    c.b["q2row"] = c.b["aux8k"]
    c.t["zlng"] = c.t["aux8k"]
    c.b["zlng"] = c.b["aux8k"]
    c.sb("tempb", [128, 16], F32)
    c.sb("yidx_sb", [128, 32], I32)
    c.ps("psS", [128, 2048], F32)
    for _j in range(4):
        c.b["psS%d" % _j] = Buf("psS%d" % _j)
    c.ps("psB", [128, 1024], F32)
    c.ps("psT", [128, 512], BF16)

    emit_prelude(c, p)
    rsb, rsbB = c.t["a3"], c.b["a3"]
    p.dma("sync", rsb[0:1, 0:8], c.t["iota8_in"][0:1, :], reads=(c.b["iota8_in"],), writes=(rsbB,))
    ps, psB = c.t["psB"], c.b["psB"]
    p.mm(ps[:, 0:8], c.t["onesr"][0:1, 0:128], rsb[0:1, 0:8], start=True, stop=True,
         reads=(rsbB, c.b["onesr"]), writes=(psB,))
    p.copy(c.t["iota8"][:, 0:8], ps[:, 0:8], reads=(psB,), writes=(c.b["iota8"],))
    p.dma("sync", c.t["premask_sb"][0:8, 0:1], c.t["premask"][:, :],
          reads=(c.b["premask"],), writes=(c.b["premask_sb"],))
    c.t["premask"], c.b["premask"] = c.t["premask_sb"], c.b["premask_sb"]
    p.dma("sync", c.t["rowidx_sb"][0:32, 0:1], c.t["rowidx"][:, :],
          reads=(c.b["rowidx"],), writes=(c.b["rowidx_sb"],))

    xins = ["x_in0", "xh1", "xh2", "xh3"]
    xouts = ["xh1", "xh2", "xh3", "x_out3"]
    variants = ["l0", "l1", "l2", "l3"]
    iota_tok = None
    for blk in range(4):
        c.t["x_in"], c.b["x_in"] = c.t[xins[blk]], c.b[xins[blk]]
        c.t["x_out"], c.b["x_out"] = c.t[xouts[blk]], c.b[xouts[blk]]
        for nm in ATTN_WNAMES[blk]:
            c.t[nm], c.b[nm] = c.t[f"b{blk}_{nm}"], c.b[f"b{blk}_{nm}"]
        if blk == 3:
            p.dma("sync", c.t["yidx_sb"][:, 0:32], c.t["f3_yidx"][:, :],
                  reads=(c.b["f3_yidx"],), writes=(c.b["yidx_sb"],))
        emit_ln_phase(c, p)
        emit_attn(c, p, variants[blk])
        [emit_ffn0, emit_ffn1, emit_ffn2, emit_ffn3][blk](c, p)
        emit_combine(c, p)

    with contextlib.ExitStack() as es2:
        sems = {e: es2.enter_context(nc.semaphore("sem_" + e)) for e in ENGINES}
        block = es2.enter_context(nc.Block())
        p.emit(block, sems)
    es.close()
    return nc


def kernel(**inputs):
    xs = shard_x(inputs["L0_attn_x"])
    if FUSED and _BLOCKS == [0, 1, 2, 3]:
        nc = build_full_program()
        wfns = {0: _block0_weights, 1: _block1_weights, 2: _block2_weights,
                3: _block3_weights}
        in_maps = []
        for core in range(NCORE):
            m = {"x_in0": xs[core]}
            m.update(_host_consts(core))
            for blk in range(4):
                wd = wfns[blk](inputs, core)
                for k, v in wd.items():
                    m[f"b{blk}_{k}" if k in ATTN_WNAMES[blk] else k] = v
            in_maps.append(m)
        res = run_bass_kernel_spmd(nc, in_maps, core_ids=list(range(NCORE)))
        return unshard_x([np.asarray(r["x_out3"], np.float32) for r in res.results])
    for blk in _BLOCKS:
        nc = build_block_program(blk)
        wfn = {0: _block0_weights, 1: _block1_weights, 2: _block2_weights,
               3: _block3_weights}[blk]
        in_maps = []
        for core in range(NCORE):
            m = {"x_in": xs[core]}
            m.update(_host_consts(core))
            m.update(wfn(inputs, core))
            in_maps.append(m)
        res = run_bass_kernel_spmd(nc, in_maps, core_ids=list(range(NCORE)))
        xs = [np.asarray(r["x_out"], np.float32) for r in res.results]
    return unshard_x(xs)


# version 05 (final)
# speedup vs baseline: 2.0518x; 1.1588x over previous
import contextlib
import numpy as np
import ml_dtypes
import concourse.bass as bass
import concourse.mybir as mybir
from concourse.bass_utils import run_bass_kernel_spmd

F32 = mybir.dt.float32
BF16 = mybir.dt.bfloat16
I32 = mybir.dt.int32
U32 = mybir.dt.uint32
AF = mybir.ActivationFunctionType
ALU = mybir.AluOpType
AX = mybir.AxisListType
IOff = bass.IndirectOffsetOnAxis
BFNP = ml_dtypes.bfloat16

NCORE = 8
B_, S_, D = 4, 4096, 1024
T = 2048
NT = 16
NKC = 32
H, HD = 16, 64
QT = 256
NQT = T // QT
ENGINES = ("sync", "tensor", "vector", "scalar", "gpsimd")
PAIRS = [[0, 1], [2, 3], [4, 5], [6, 7]]
ALL8 = [list(range(8))]
GRP4 = [[0, 2, 4, 6], [1, 3, 5, 7]]


class Buf:
    __slots__ = ("name", "w", "r")

    def __init__(self, name):
        self.name = name
        self.w = None
        self.r = []


class Pr:
    def __init__(self, nc):
        self.nc = nc
        self.q = {e: [] for e in ENGINES}
        self.cnt = {e: 0 for e in ENGINES}
        self.bnd_regs = {}
        self.last_coll = None

    def _bound_reg(self, e, val):
        if val not in self.bnd_regs:
            self.bnd_regs[val] = e.to_reg(int(val))
        return self.bnd_regs[val]

    def op(self, eng, fn, reads=(), writes=(), inc=1, extra_deps=()):
        deps = list(extra_deps)
        for b in reads:
            if b.w is not None:
                deps.append(b.w)
        for b in writes:
            if b.w is not None:
                deps.append(b.w)
            deps.extend(b.r)
        dd = {}
        for (de, dv) in deps:
            if eng == "tensor" and de == "tensor":
                continue
            if dd.get(de, 0) < dv:
                dd[de] = dv
        self.cnt[eng] += inc
        tok = (eng, self.cnt[eng])
        self.q[eng].append((fn, tuple(dd.items()), inc))
        for b in writes:
            b.w = tok
            b.r = []
        for b in reads:
            b.r.append(tok)
        return tok

    def dma(self, eng, out, in_, reads=(), writes=(), **kw):
        try:
            if out.tensor.dtype != in_.tensor.dtype:
                eng = "gpsimd"
        except AttributeError:
            pass
        return self.op(eng, lambda e: e.dma_start(out=out, in_=in_, **kw),
                       reads=reads, writes=writes, inc=16)

    def mm(self, out, lhsT, rhs, start, stop, reads=(), writes=()):
        return self.op("tensor",
                       lambda e: e.matmul(out, lhsT, rhs, start=start, stop=stop),
                       reads=reads, writes=writes)

    def tr(self, out, in_, ident, reads=(), writes=()):
        return self.op("tensor", lambda e: e.transpose(out, in_, ident),
                       reads=reads, writes=writes)

    def act(self, out, in_, func, reads=(), writes=(), bias=0.0, scale=1.0):
        return self.op("scalar",
                       lambda e: e.activation(out, in_, func, bias=bias, scale=scale),
                       reads=reads, writes=writes)

    def tt(self, out, a, b2, alu, reads=(), writes=()):
        return self.op("vector", lambda e: e.tensor_tensor(out, a, b2, alu),
                       reads=reads, writes=writes)

    def ts(self, out, a, s1, s2, op0, op1=None, reads=(), writes=()):
        if s2 is None:
            return self.op("vector", lambda e: e.tensor_scalar(out, a, s1, None, op0),
                           reads=reads, writes=writes)
        return self.op("vector", lambda e: e.tensor_scalar(out, a, s1, s2, op0, op1),
                       reads=reads, writes=writes)

    def red(self, out, in_, alu, reads=(), writes=(), absval=None):
        return self.op("vector",
                       lambda e: e.tensor_reduce(out, in_, AX.X, alu,
                                                 apply_absolute_value=absval),
                       reads=reads, writes=writes)

    def copy(self, out, in_, reads=(), writes=()):
        return self.op("vector", lambda e: e.tensor_copy(out=out, in_=in_),
                       reads=reads, writes=writes)

    def memset(self, ap, v, writes=(), reads=()):
        return self.op("vector", lambda e: e.memset(ap, v), reads=reads, writes=writes)

    def coll(self, kind, alu, groups, in_ap, out_ap, reads=(), writes=()):
        extra = [self.last_coll] if self.last_coll is not None else []
        self.op("gpsimd", lambda e: e.collective_compute(
            kind, alu, replica_groups=groups, ins=[in_ap], outs=[out_ap]),
            reads=reads, writes=(), inc=1, extra_deps=extra)
        fence_in, fence_out = self.fence_bufs
        tok = self.op("gpsimd", lambda e: e.collective_compute(
            "AllReduce", ALU.add, replica_groups=groups,
            ins=[fence_in.ap()], outs=[fence_out.ap()]),
            reads=reads, writes=writes, inc=1)
        self.last_coll = tok
        return tok

    def igather(self, out, in_, idx_ap, bound, reads=(), writes=()):
        def f(e):
            return e.indirect_dma_start(
                out=out, out_offset=None, in_=in_, in_offset=IOff(ap=idx_ap, axis=0),
                bounds_check=self._bound_reg(e, bound), oob_is_err=False)
        return self.op("gpsimd", f, reads=reads, writes=writes, inc=16)

    def iscatter(self, out, idx_ap, in_, bound, reads=(), writes=()):
        def f(e):
            return e.indirect_dma_start(
                out=out, out_offset=IOff(ap=idx_ap, axis=0), in_=in_, in_offset=None,
                bounds_check=self._bound_reg(e, bound), oob_is_err=False)
        return self.op("gpsimd", f, reads=reads, writes=writes, inc=16)

    def emit(self, block, sems):
        def mk(eng_name):
            def run(e):
                wm = {f: 0 for f in ENGINES}
                for _qi, (fn, deps, inc) in enumerate(self.q[eng_name]):
                    for (de, dv) in deps:
                        if dv > wm[de]:
                            e.wait_ge(sems[de], dv)
                            wm[de] = dv
                    try:
                        fn(e).then_inc(sems[eng_name], inc)
                    except Exception:
                        print("EMIT FAIL engine", eng_name, "queue index", _qi,
                              "of", len(self.q[eng_name]))
                        raise
            return run
        block.sync(mk("sync"))
        block.tensor(mk("tensor"))
        block.vector(mk("vector"))
        block.scalar(mk("scalar"))
        block.gpsimd(mk("gpsimd"))


class Ctx:
    def __init__(self, nc, es, prog):
        self.nc = nc
        self.es = es
        self.p = prog
        self.t = {}
        self.b = {}

    def sb(self, name, shape, dtype):
        self.t[name] = self.es.enter_context(self.nc.sbuf_tensor(name, shape, dtype))
        self.b[name] = Buf(name)
        return self.t[name]

    def ps(self, name, shape, dtype=F32):
        self.t[name] = self.es.enter_context(self.nc.psum_tensor(name, shape, dtype))
        self.b[name] = Buf(name)
        return self.t[name]

    def psbank(self, j):
        return self.t["psS"][:, j * 512:(j + 1) * 512], self.b["psS%d" % j]

    def dram(self, name, shape, dtype, kind=None):
        if kind:
            self.t[name] = self.nc.dram_tensor(name, shape, dtype, kind=kind)
        else:
            self.t[name] = self.nc.dram_tensor(name, shape, dtype)
        self.b[name] = Buf(name)
        return self.t[name]

    def tb(self, name):
        return self.t[name], self.b[name]



def emit_prelude(c, p):
    t, b = c.t, c.b
    p.memset(t["cst"][:, 0:1], 1e-5, writes=(b["cst"],))
    p.dma("sync", t["ident"][:, :], t["ident_in"][:, :], reads=(b["ident_in"],), writes=(b["ident"],))
    p.dma("sync", t["cum"][:, :], t["cum_in"][:, :], reads=(b["cum_in"],), writes=(b["cum"],))
    p.dma("gpsimd", t["identf"][:, :], t["ident_in"][:, :], reads=(b["ident_in"],), writes=(b["identf"],))
    p.memset(t["onesr"][:, :], 1.0, writes=(b["onesr"],))
    p.memset(t["onesrb"][:, :], 1.0, writes=(b["onesrb"],))
    p.memset(t["onesc"][:, :], 1.0, writes=(b["onesc"],))
    p.memset(t["onescb"][:, :], 1.0, writes=(b["onescb"],))


def emit_ln_phase(c, p):
    t, b = c.t, c.b
    for tt in range(NT):
        r = slice(tt * 128, (tt + 1) * 128)
        xa, xaB = t["a1"], b["a1"]
        p.dma("sync", xa[:, 0:D], t["x_in"][r, :], reads=(b["x_in"],), writes=(xaB,))
        emit_ln_stats(c, p, xa[:, 0:D], (xaB,), D, 1e-5)
        st, stB = t["stat"], b["stat"]
        xn, xnB = t["a2"], b["a2"]
        p.ts(xn[:, 0:D], xa[:, 0:D], st[:, 0:1], st[:, 3:4], ALU.subtract, ALU.mult,
             reads=(xaB, stB), writes=(xnB,))
        p.dma("gpsimd", t["xn32_d"][r, :], xn[:, 0:D], reads=(xnB,), writes=(b["xn32_d"],))
        xbb, xbbB = t["w16a"], b["w16a"]
        p.copy(xbb[:, 0:D], xn[:, 0:D], reads=(xnB,), writes=(xbbB,))
        p.dma("gpsimd", t["xn_tok_d"][r, :], xbb[:, 0:D], reads=(xbbB,), writes=(b["xn_tok_d"],))
        for ch in range(8):
            pt, ptB = t["psT"], b["psT"]
            p.tr(pt[:, 0:128], xbb[:, ch * 128:(ch + 1) * 128], t["ident"][:, :],
                 reads=(xbbB, b["ident"]), writes=(ptB,))
            o2, o2B = t["w16b"], b["w16b"]
            p.copy(o2[:, ch * 128:(ch + 1) * 128], pt[:, 0:128], reads=(ptB,), writes=(o2B,))
        for ch in range(8):
            p.dma("gpsimd", t["xn_fm_d"][ch * 128:(ch + 1) * 128, tt * 128:(tt + 1) * 128],
                  t["w16b"][:, ch * 128:(ch + 1) * 128], reads=(b["w16b"],), writes=(b["xn_fm_d"],))
    p.dma("gpsimd", t["xn_fm_d"][1024:1025, :], t["onesrb"][0:1, 0:T],
          reads=(b["onesrb"],), writes=(b["xn_fm_d"],))


def emit_ln_stats(c, p, src_ap, src_reads, n, eps):
    t, b = c.t, c.b
    st, stB = t["stat"], b["stat"]
    sq, sqB = t["lnsq"], b["lnsq"]
    p.red(st[:, 0:1], src_ap, ALU.add, reads=src_reads, writes=(stB,))
    p.ts(st[:, 0:1], st[:, 0:1], 1.0 / n, None, ALU.mult, reads=(stB,), writes=(stB,))
    p.act(sq[:, 0:n], src_ap, AF.Square, reads=src_reads, writes=(sqB,))
    p.red(st[:, 1:2], sq[:, 0:n], ALU.add, reads=(sqB,), writes=(stB,))
    p.ts(st[:, 1:2], st[:, 1:2], 1.0 / n, None, ALU.mult, reads=(stB,), writes=(stB,))
    p.act(sq[:, 0:1], st[:, 0:1], AF.Square, reads=(stB,), writes=(sqB,))
    p.tt(st[:, 1:2], st[:, 1:2], sq[:, 0:1], ALU.subtract, reads=(sqB, stB), writes=(stB,))
    p.act(st[:, 2:3], st[:, 1:2], AF.Sqrt, bias=float(eps), reads=(stB, c.b["cst"]), writes=(stB,))
    p.op("vector", lambda e, st=st: e.reciprocal(st[:, 3:4], st[:, 2:3]),
         reads=(stB,), writes=(stB,))


def load_wchunks(c, p, dst, dstB, src_name, rows, cols, dst_col0=0):
    t, b = c.t, c.b
    src, srcB = t[src_name], b[src_name]
    nch = (rows + 127) // 128
    for i in range(nch):
        r0, r1 = i * 128, min((i + 1) * 128, rows)
        p.dma("gpsimd", dst[0:r1 - r0, dst_col0 + i * cols: dst_col0 + i * cols + cols],
              src[r0:r1, :], reads=(srcB,), writes=(dstB,))


def fm_ld(c, p, dst, dstB, ch, c0, c1, src="xn_fm_d"):
    t, b = c.t, c.b
    rows = 1 if ch == 8 else 128
    p.dma("sync", dst[0:rows, 0:c1 - c0], t[src][ch * 128:ch * 128 + rows, c0:c1],
          reads=(b[src],), writes=(dstB,))



def emit_zn_phase(c, p):
    t, b = c.t, c.b
    for name, dst_lo in (("a3_lns", 0), ("a3_lnb", 1024)):
        pass
    gb, gbB = t["w16c"], b["w16c"]
    rsb, rsbB = t["lnsq"], b["lnsq"]
    p.dma("sync", rsb[0:1, 0:1024], t["l3_lns"][0:1, :], reads=(b["l3_lns"],), writes=(rsbB,))
    p.dma("sync", rsb[0:1, 1024:2048], t["l3_lnb"][0:1, :], reads=(b["l3_lnb"],), writes=(rsbB,))
    gbc, gbcB = t["zlng"], b["zlng"]
    for j in range(4):
        ps, psB = t["psB"], b["psB"]
        p.mm(ps[:, 0:512], t["onesr"][0:1, 0:128], rsb[0:1, j * 512:(j + 1) * 512],
             start=True, stop=True, reads=(rsbB, b["onesr"]), writes=(psB,))
        p.copy(gbc[:, j * 512:(j + 1) * 512], ps[:, 0:512], reads=(psB,), writes=(gbcB,))
    for tt in range(NT):
        r = slice(tt * 128, (tt + 1) * 128)
        xa, xaB = t["a1"], b["a1"]
        p.dma("sync", xa[:, 0:D], t["xn32_d"][r, :], reads=(b["xn32_d"],), writes=(xaB,))
        emit_ln_stats(c, p, xa[:, 0:D], (xaB,), D, 1e-5)
        st, stB = t["stat"], b["stat"]
        zn, znB = t["a2"], b["a2"]
        p.ts(zn[:, 0:D], xa[:, 0:D], st[:, 0:1], st[:, 3:4], ALU.subtract, ALU.mult,
             reads=(xaB, stB), writes=(znB,))
        p.tt(zn[:, 0:D], zn[:, 0:D], gbc[:, 0:1024], ALU.mult, reads=(gbcB,), writes=(znB,))
        p.tt(zn[:, 0:D], zn[:, 0:D], gbc[:, 1024:2048], ALU.add, reads=(gbcB,), writes=(znB,))
        xbb, xbbB = t["w16a"], b["w16a"]
        p.copy(xbb[:, 0:D], zn[:, 0:D], reads=(znB,), writes=(xbbB,))
        for ch in range(8):
            pt, ptB = t["psT"], b["psT"]
            p.tr(pt[:, 0:128], xbb[:, ch * 128:(ch + 1) * 128], t["ident"][:, :],
                 reads=(xbbB, b["ident"]), writes=(ptB,))
            o2, o2B = t["w16b"], b["w16b"]
            p.copy(o2[:, ch * 128:(ch + 1) * 128], pt[:, 0:128], reads=(ptB,), writes=(o2B,))
        for ch in range(8):
            p.dma("gpsimd", t["zn_fm_d"][ch * 128:(ch + 1) * 128, tt * 128:(tt + 1) * 128],
                  t["w16b"][:, ch * 128:(ch + 1) * 128], reads=(b["w16b"],), writes=(b["zn_fm_d"],))
    p.dma("gpsimd", t["zn_fm_d"][1024:1025, :], t["onesrb"][0:1, 0:T],
          reads=(b["onesrb"],), writes=(b["zn_fm_d"],))


def emit_attn(c, p, variant):
    t, b = c.t, c.b
    KR = 66 if variant == "l1" else 64
    if variant == "l3":
        emit_zn_phase(c, p)

    _attn_qkv_produce(c, p, variant)

    p.coll("AllGather", ALU.bypass, PAIRS, t["kx_d"][0:H * KR, :], t["kg_d"][0:2 * H * KR, :],
           reads=(b["kx_d"],), writes=(b["kg_d"],))
    p.coll("AllGather", ALU.bypass, PAIRS, t["vx_d"].ap(), t["vg_d"].ap(),
           reads=(b["vx_d"],), writes=(b["vg_d"],))

    if DEBUG and "dbg_qx" in t:
        p.dma("gpsimd", t["dbg_vx"][:, :], t["vx_d"].ap(), reads=(b["vx_d"],), writes=(b["dbg_vx"],))
        p.dma("gpsimd", t["dbg_kx"][:, :], t["kx_d"].ap(), reads=(b["kx_d"],), writes=(b["dbg_kx"],))
        p.dma("gpsimd", t["dbg_qx"][:, :], t["qx_d"].ap(), reads=(b["qx_d"],), writes=(b["dbg_qx"],))
        p.dma("gpsimd", t["dbg_kg"][:, :], t["kg_d"].ap(), reads=(b["kg_d"],), writes=(b["dbg_kg"],))
        p.dma("gpsimd", t["dbg_vg"][:, :], t["vg_d"].ap(), reads=(b["vg_d"],), writes=(b["dbg_vg"],))
    if variant == "l1":
        p.memset(t["bb"][:, 0:H], 0.0, writes=(b["bb"],))
    else:
        _attn_bounds(c, p, KR)

    _attn_core(c, p, variant, KR)
    _attn_epilogue(c, p, variant)


def _attn_qkv_produce(c, p, variant):
    if variant in ("l0", "l3"):
        _qkv_fm_direct(c, p, variant)
    else:
        _qkv_tokmajor(c, p, variant)
    _v_produce(c, p, variant)


def _qkv_fm_direct(c, p, variant):
    t, b = c.t, c.b
    scale_q = 0.125
    has_bias = (variant == "l0")
    if has_bias:
        for i, nm in enumerate(("bq_in", "bk_in")):
            for ch in range(8):
                p.dma("sync", t["bqk"][:, i * 8 + ch:i * 8 + ch + 1],
                      t[nm][ch * 128:(ch + 1) * 128, :], reads=(b[nm],), writes=(b["bqk"],))
        p.ts(t["bqk"][:, 0:8], t["bqk"][:, 0:8], scale_q, None, ALU.mult,
             reads=(b["bqk"],), writes=(b["bqk"],))
    fmsrc = "zn_fm_d" if variant == "l3" else "xn_fm_d"
    for which, wname, dram_out, scl in (("q", "wq_in", "qx_d", scale_q),
                                        ("k", "wk_in", "kx_d", 1.0)):
        wsb, wsbB = t["wk1"], b["wk1"]
        load_wchunks(c, p, wsb, wsbB, wname, 1024, 1024)
        for ho in range(8):
            stg, stgB = t["w16a"], b["w16a"]
            for jt in range(T // 512):
                ps, psB = t["psB"], b["psB"]
                slab, slabB = t["w16b"], b["w16b"]
                p.dma("sync", slab.rearrange("p (c tk) -> p c tk", c=8),
                      t[fmsrc][0:1024, jt * 512:(jt + 1) * 512].rearrange(
                          "(c p) tk -> p c tk", p=128),
                      reads=(b[fmsrc],), writes=(slabB,))
                for ch in range(8):
                    p.mm(ps[:, 0:512], wsb[:, ch * 1024 + ho * 128: ch * 1024 + (ho + 1) * 128],
                         slab[:, ch * 512:(ch + 1) * 512], start=(ch == 0), stop=(ch == 7),
                         reads=(wsbB, slabB), writes=(psB,))
                bias_ap = t["bqk"][:, (0 if which == "q" else 8) + ho:
                                   (0 if which == "q" else 8) + ho + 1] if has_bias else 0.0
                extra = (b["bqk"],) if has_bias else ()
                p.act(stg[:, jt * 512:(jt + 1) * 512], ps[:, 0:512], AF.Identity,
                      bias=bias_ap, scale=(scl if which == "q" else 1.0),
                      reads=(psB,) + extra, writes=(stgB,))
            h0, h1 = 2 * ho, 2 * ho + 1
            p.dma("gpsimd", t[dram_out][h0 * 64:h0 * 64 + 64, :], stg[0:64, 0:T],
                  reads=(stgB,), writes=(b[dram_out],))
            p.dma("gpsimd", t[dram_out][h1 * 64:h1 * 64 + 64, :], stg[64:128, 0:T],
                  reads=(stgB,), writes=(b[dram_out],))


def _qkv_tokmajor(c, p, variant):
    t, b = c.t, c.b
    KR = 66 if variant == "l1" else 64
    if variant == "l2":
        rsb, rsbB = t["lnsq"], b["lnsq"]
        p.dma("sync", rsb[0:1, 0:16], t["l2_temp"][0:1, :], reads=(b["l2_temp"],), writes=(rsbB,))
        p.op("vector", lambda e, rsb=rsb: e.reciprocal(rsb[0:1, 16:32], rsb[0:1, 0:16]),
             reads=(rsbB,), writes=(rsbB,))
        ps, psB = t["psB"], b["psB"]
        p.mm(ps[:, 0:16], t["onesr"][0:1, 0:128], rsb[0:1, 16:32], start=True, stop=True,
             reads=(rsbB, b["onesr"]), writes=(psB,))
        tb_, tbB = t["tempb"], b["tempb"]
        p.copy(tb_[:, 0:16], ps[:, 0:16], reads=(psB,), writes=(tbB,))
    for which, wname, dram_out in (("q", "wq_in", "qx_d"), ("k", "wk_in", "kx_d")):
        wsb, wsbB = t["wk1"], b["wk1"]
        rows = 1025 if variant == "l1" else 1024
        load_wchunks(c, p, wsb, wsbB, wname, rows, 1024)
        for tt in range(NT):
            qt_, qtB = t["w16a"], b["w16a"]
            for j2 in range(2):
                ps, psB = t["psB"], b["psB"]
                for ch in range(8):
                    xf, xfB = t["xf"], b["xf"]
                    fm_ld(c, p, xf, xfB, ch, tt * 128, (tt + 1) * 128)
                    p.mm(ps[:, 0:512], xf[0:128, 0:128],
                         wsb[:, ch * 1024 + j2 * 512: ch * 1024 + (j2 + 1) * 512],
                         start=(ch == 0), stop=(ch == 7 and variant != "l1"),
                         reads=(wsbB, xfB), writes=(psB,))
                if variant == "l1":
                    xf, xfB = t["xf"], b["xf"]
                    fm_ld(c, p, xf, xfB, 8, tt * 128, (tt + 1) * 128)
                    p.mm(ps[:, 0:512], xf[0:1, 0:128],
                         wsb[0:1, 8 * 1024 + j2 * 512: 8 * 1024 + (j2 + 1) * 512],
                         start=False, stop=True, reads=(wsbB, xfB), writes=(psB,))
                p.copy(qt_[:, j2 * 512:(j2 + 1) * 512], ps[:, 0:512], reads=(psB,), writes=(qtB,))
            if variant == "l1":
                sq, sqB = t["lnsq"], b["lnsq"]
                p.act(sq[:, 0:1024], qt_[:, 0:1024], AF.Square, reads=(qtB,), writes=(sqB,))
                q2, q2B = t["a3"], b["a3"]
                for h in range(H):
                    p.red(q2[:, h:h + 1], sq[:, h * 64:(h + 1) * 64], ALU.add,
                          reads=(sqB,), writes=(q2B,))
                pt, ptB = t["psB"], b["psB"]
                p.tr(pt[0:16, 0:128], q2[:, 0:16], t["identf"][:, :],
                     reads=(q2B, b["identf"]), writes=(ptB,))
                p.copy(t["q2row"][0:16, tt * 128:(tt + 1) * 128], pt[0:16, 0:128],
                       reads=(ptB,), writes=(b["q2row"],))
                if which == "q":
                    p.ts(qt_[:, 0:1024], qt_[:, 0:1024], -2.0, None, ALU.mult,
                         reads=(qtB,), writes=(qtB,))
            else:
                sq, sqB = t["lnsq"], b["lnsq"]
                p.act(sq[:, 0:1024], qt_[:, 0:1024], AF.Square, reads=(qtB,), writes=(sqB,))
                st_, stB = t["stat"], b["stat"]
                p.red(st_[:, 0:1], sq[:, 0:1024], ALU.add, reads=(sqB,), writes=(stB,))
                p.act(st_[:, 1:2], st_[:, 0:1], AF.Sqrt, reads=(stB,), writes=(stB,))
                p.ts(st_[:, 1:2], st_[:, 1:2], 1e-12, None, ALU.max, reads=(stB,), writes=(stB,))
                p.op("vector", lambda e, st_=st_: e.reciprocal(st_[:, 2:3], st_[:, 1:2]),
                     reads=(stB,), writes=(stB,))
                p.ts(st_[:, 2:3], st_[:, 2:3], 0.125, None, ALU.mult, reads=(stB,), writes=(stB,))
                p.ts(qt_[:, 0:1024], qt_[:, 0:1024], st_[:, 2:3], None, ALU.mult,
                     reads=(stB,), writes=(qtB,))
                if which == "q":
                    for h in range(H):
                        p.ts(qt_[:, h * 64:(h + 1) * 64], qt_[:, h * 64:(h + 1) * 64],
                             t["tempb"][:, h:h + 1], None, ALU.mult,
                             reads=(b["tempb"],), writes=(qtB,))
            for ch in range(8):
                pt, ptB = t["psT"], b["psT"]
                p.tr(pt[:, 0:128], qt_[:, ch * 128:(ch + 1) * 128], t["ident"][:, :],
                     reads=(qtB, b["ident"]), writes=(ptB,))
                o2, o2B = t["w16b"], b["w16b"]
                p.copy(o2[:, ch * 128:(ch + 1) * 128], pt[:, 0:128], reads=(ptB,), writes=(o2B,))
            for ch in range(8):
                h0, h1 = 2 * ch, 2 * ch + 1
                p.dma("gpsimd", t[dram_out][h0 * KR:h0 * KR + 64, tt * 128:(tt + 1) * 128],
                      t["w16b"][0:64, ch * 128:(ch + 1) * 128], reads=(b["w16b"],), writes=(b[dram_out],))
                p.dma("gpsimd", t[dram_out][h1 * KR:h1 * KR + 64, tt * 128:(tt + 1) * 128],
                      t["w16b"][64:128, ch * 128:(ch + 1) * 128], reads=(b["w16b"],), writes=(b[dram_out],))
        if variant == "l1":
            sqrow = 64 if which == "q" else 65
            onerow = 65 if which == "q" else 64
            p.dma("gpsimd",
                  t[dram_out].ap().rearrange("(h r) tk -> h r tk", r=KR)[:, sqrow:sqrow + 1, :],
                  t["q2row"][0:16, 0:T], reads=(b["q2row"],), writes=(b[dram_out],))
            for h in range(H):
                p.dma("gpsimd", t[dram_out][h * KR + onerow:h * KR + onerow + 1, :],
                      t["onesrb"][0:1, 0:T], reads=(b["onesrb"],), writes=(b[dram_out],))


def _v_produce(c, p, variant):
    t, b = c.t, c.b
    wsb, wsbB = t["wk1"], b["wk1"]
    has_bias = variant in ("l0", "l1")
    rows = 1025 if has_bias else 1024
    load_wchunks(c, p, wsb, wsbB, "wv_in", rows, 1024)
    fmsrc = "zn_fm_d" if variant == "l3" else "xn_fm_d"
    for tt in range(NT):
        stg, stgB = t["w16a"], b["w16a"]
        p.memset(stg[:, 0:1040], 1.0, writes=(stgB,))
        slab, slabB = t["w16b"], b["w16b"]
        p.dma("sync", slab[:, 0:1024].rearrange("p (c tk) -> p c tk", c=8),
              t[fmsrc][0:1024, tt * 128:(tt + 1) * 128].rearrange("(c p) tk -> p c tk", p=128),
              reads=(b[fmsrc],), writes=(slabB,))
        for j2 in range(2):
            ps, psB = t["psB"], b["psB"]
            for ch in range(8):
                p.mm(ps[:, 0:512], slab[:, ch * 128:(ch + 1) * 128],
                     wsb[:, ch * 1024 + j2 * 512: ch * 1024 + (j2 + 1) * 512],
                     start=(ch == 0), stop=(ch == 7 and not has_bias),
                     reads=(wsbB, slabB), writes=(psB,))
            if has_bias:
                p.mm(ps[:, 0:512], t["onesrb"][0:1, 0:128],
                     wsb[0:1, 8 * 1024 + j2 * 512: 8 * 1024 + (j2 + 1) * 512],
                     start=False, stop=True, reads=(wsbB, b["onesrb"]), writes=(psB,))
            for hh in range(8):
                hgl = j2 * 8 + hh
                p.copy(stg[:, hgl * 65:hgl * 65 + 64], ps[:, hh * 64:(hh + 1) * 64],
                       reads=(psB,), writes=(stgB,))
        p.dma("gpsimd", t["vx_d"][tt * 128:(tt + 1) * 128, :], stg[:, 0:1040],
              reads=(stgB,), writes=(b["vx_d"],))


def _attn_bounds(c, p, KR):
    t, b = c.t, c.b
    row, rowB = t["row4"], b["row4"]
    for side, dram, nn, ncols in (("q", "qx_d", 0, T), ("k", "kg_d", 16, 2 * T)):
        for h in range(H):
            kf, kfB = t["kf"], b["kf"]
            if side == "q":
                p.dma("sync", kf[0:64, 0:T], t[dram][h * 64:h * 64 + 64, :],
                      reads=(b[dram],), writes=(kfB,))
            else:
                for r in range(2):
                    p.dma("sync", kf[0:64, r * T:(r + 1) * T],
                          t[dram][r * H * KR + h * KR: r * H * KR + h * KR + 64, :],
                          reads=(b[dram],), writes=(kfB,))
            sq, sqB = t["w16a"], b["w16a"]
            p.act(sq[0:64, 0:ncols], kf[0:64, 0:ncols], AF.Square, reads=(kfB,), writes=(sqB,))
            ps, psB = t["psS"], b["psS"]
            for half in range(ncols // 2048):
                for j in range(4):
                    p.mm(ps[0:1, j * 512:(j + 1) * 512], t["onescb"][0:64, 0:1],
                         sq[0:64, half * 2048 + j * 512: half * 2048 + (j + 1) * 512],
                         start=True, stop=True, reads=(sqB, b["onescb"]), writes=(psB,))
                if half == 0:
                    p.red(row[0:1, nn + h:nn + h + 1], ps[0:1, 0:2048], ALU.max,
                          reads=(psB,), writes=(rowB,))
                else:
                    p.red(row[0:1, 32 + h:32 + h + 1], ps[0:1, 0:2048], ALU.max,
                          reads=(psB,), writes=(rowB,))
                    p.tt(row[0:1, nn + h:nn + h + 1], row[0:1, nn + h:nn + h + 1],
                         row[0:1, 32 + h:32 + h + 1], ALU.max, reads=(rowB,), writes=(rowB,))
    p.tt(row[0:1, 32:48], row[0:1, 0:16], row[0:1, 16:32], ALU.mult, reads=(rowB,), writes=(rowB,))
    p.act(row[0:1, 32:48], row[0:1, 32:48], AF.Sqrt, reads=(rowB,), writes=(rowB,))
    p.ts(row[0:1, 32:48], row[0:1, 32:48], -1.0, None, ALU.mult, reads=(rowB,), writes=(rowB,))
    ps, psB = t["psB"], b["psB"]
    p.mm(ps[:, 0:H], t["onesr"][0:1, 0:128], row[0:1, 32:48], start=True, stop=True,
         reads=(rowB, b["onesr"]), writes=(psB,))
    p.copy(t["bb"][:, 0:H], ps[:, 0:H], reads=(psB,), writes=(b["bb"],))


def _attn_core(c, p, variant, KR):
    t, b = c.t, c.b
    for h in range(H):
        kf, kfB = t["kf"], b["kf"]
        for r in range(2):
            p.dma("sync", kf[0:KR, r * T:(r + 1) * T],
                  t["kg_d"][r * H * KR + h * KR: r * H * KR + (h + 1) * KR, :],
                  reads=(b["kg_d"],), writes=(kfB,))
        vh, vhB = t["vh"], b["vh"]
        for kc in range(NKC):
            p.dma("sync", vh[:, kc * 65:(kc + 1) * 65],
                  t["vg_d"][kc * 128:(kc + 1) * 128, h * 65:(h + 1) * 65],
                  reads=(b["vg_d"],), writes=(vhB,))
        qh, qhB = t["qh"], b["qh"]
        p.dma("sync", qh[0:KR, 0:T], t["qx_d"][h * KR:(h + 1) * KR, :],
              reads=(b["qx_d"],), writes=(qhB,))
        for qt in range(NQT):
            pb, pbB = t["pb"], b["pb"]
            for kc in range(NKC):
                psA, psAB = c.psbank(kc % 4)
                p.mm(psA[:, 0:QT], kf[0:KR, kc * 128:(kc + 1) * 128],
                     qh[0:KR, qt * QT:(qt + 1) * QT], start=True, stop=True,
                     reads=(kfB, qhB, b["psS"]), writes=(psAB,))
                if variant == "l1":
                    u, uB = t["a3"], b["a3"]
                    p.act(u[:, 0:QT], psA[:, 0:QT], AF.Abs, reads=(psAB,), writes=(uB,))
                    p.act(u[:, QT:2 * QT], u[:, 0:QT], AF.Sqrt, reads=(uB,), writes=(uB,))
                    p.act(pb[:, kc * QT:(kc + 1) * QT], u[:, QT:2 * QT], AF.Exp, scale=-1.0,
                          reads=(uB,), writes=(pbB,))
                else:
                    p.act(pb[:, kc * QT:(kc + 1) * QT], psA[:, 0:QT], AF.Exp,
                          bias=t["bb"][:, h:h + 1], reads=(psAB, b["bb"]), writes=(pbB,))
            po, poB = t["psB"], b["psB"]
            for kc in range(NKC):
                p.mm(po[0:65, 0:QT], vh[:, kc * 65:(kc + 1) * 65],
                     pb[:, kc * QT:(kc + 1) * QT], start=(kc == 0), stop=(kc == NKC - 1),
                     reads=(vhB, pbB), writes=(poB,))
            hp, half = h // 2, h % 2
            ost, ostB = t["ostg"], b["ostg"]
            p.act(ost[0:64, 0:QT], po[0:64, 0:QT], AF.Copy, reads=(poB,), writes=(ostB,))
            rs, rsB = t["rsum"], b["rsum"]
            p.op("vector", lambda e, rs=rs, po=po: e.reciprocal(rs[0:1, 0:QT], po[64:65, 0:QT]),
                 reads=(poB,), writes=(rsB,))
            bc, bcB = c.psbank(0)
            p.mm(bc[0:64, 0:QT], t["onesr"][0:1, 0:64], rs[0:1, 0:QT],
                 start=True, stop=True, reads=(rsB, b["onesr"], b["psS"]), writes=(bcB,))
            p.tt(ost[0:64, 0:QT], ost[0:64, 0:QT], bc[0:64, 0:QT], ALU.mult,
                 reads=(bcB,), writes=(ostB,))
            p.dma("gpsimd", t["ofm_d"][h * 64:(h + 1) * 64, qt * QT:(qt + 1) * QT],
                  ost[0:64, 0:QT], reads=(ostB,), writes=(b["ofm_d"],))


def _attn_epilogue(c, p, variant):
    t, b = c.t, c.b
    wsb, wsbB = t["wk1"], b["wk1"]
    load_wchunks(c, p, wsb, wsbB, "wo_in", 1025, 1024)
    for tt in range(NT):
        so, soB = t["a1"], b["a1"]
        if variant == "l3":
            p.dma("sync", so[:, 0:D], t["xn32_d"][tt * 128:(tt + 1) * 128, :],
                  reads=(b["xn32_d"],), writes=(soB,))
        ofc, ofcB = t["w16a"], b["w16a"]
        p.dma("sync", ofc[:, 0:1024].rearrange("p (c tk) -> p c tk", c=8),
              t["ofm_d"][0:1024, tt * 128:(tt + 1) * 128].rearrange(
                  "(c p) tk -> p c tk", p=128),
              reads=(b["ofm_d"],), writes=(ofcB,))
        for j2 in range(2):
            ps, psB = t["psS"], b["psS"]
            for ch in range(8):
                p.mm(ps[:, 0:512], ofc[:, ch * 128:(ch + 1) * 128],
                     wsb[:, ch * 1024 + j2 * 512: ch * 1024 + (j2 + 1) * 512],
                     start=(ch == 0), stop=False, reads=(ofcB, wsbB), writes=(psB,))
            p.mm(ps[:, 0:512], t["onesrb"][0:1, 0:128],
                 wsb[0:1, 8 * 1024 + j2 * 512: 8 * 1024 + (j2 + 1) * 512],
                 start=False, stop=True, reads=(b["onesrb"], wsbB), writes=(psB,))
            if variant == "l3":
                p.tt(so[:, j2 * 512:(j2 + 1) * 512], so[:, j2 * 512:(j2 + 1) * 512],
                     ps[:, 0:512], ALU.add, reads=(psB,), writes=(soB,))
            else:
                p.copy(so[:, j2 * 512:(j2 + 1) * 512], ps[:, 0:512], reads=(psB,), writes=(soB,))
        p.dma("gpsimd", t["s_d"][tt * 128:(tt + 1) * 128, :], so[:, 0:D],
              reads=(soB,), writes=(b["s_d"],))



def emit_ffn0(c, p):
    t, b = c.t, c.b
    CAP = 4096
    wsb, wsbB = t["wk1"], b["wk1"]
    load_wchunks(c, p, wsb, wsbB, "f0_wp_aug", 1025, 1024)
    for tt in range(NT):
        stg, stgB = t["w16a"], b["w16a"]
        slab, slabB = t["w16b"], b["w16b"]
        p.dma("sync", slab[:, 0:1024].rearrange("p (c tk) -> p c tk", c=8),
              t["xn_fm_d"][0:1024, tt * 128:(tt + 1) * 128].rearrange(
                  "(c p) tk -> p c tk", p=128),
              reads=(b["xn_fm_d"],), writes=(slabB,))
        for j2 in range(2):
            ps, psB = t["psB"], b["psB"]
            for ch in range(8):
                p.mm(ps[:, 0:512], slab[:, ch * 128:(ch + 1) * 128],
                     wsb[:, ch * 1024 + j2 * 512: ch * 1024 + (j2 + 1) * 512],
                     start=(ch == 0), stop=False, reads=(wsbB, slabB), writes=(psB,))
            p.mm(ps[:, 0:512], t["onesrb"][0:1, 0:128],
                 wsb[0:1, 8 * 1024 + j2 * 512: 8 * 1024 + (j2 + 1) * 512],
                 start=False, stop=True, reads=(wsbB, b["onesrb"]), writes=(psB,))
            p.copy(stg[:, j2 * 512:(j2 + 1) * 512], ps[:, 0:512], reads=(psB,), writes=(stgB,))
        p.dma("gpsimd", t["xflat_d"][tt * 128:(tt + 1) * 128, :], stg[:, 0:D],
              reads=(stgB,), writes=(b["xflat_d"],))
    p.coll("AllGather", ALU.bypass, ALL8, t["xflat_d"].ap(), t["xflatall_d"].ap(),
           reads=(b["xflat_d"],), writes=(b["xflatall_d"],))

    gsb, gsbB = t["gf32"], b["gf32"]
    wg, wgB = t["wg32"], b["wg32"]
    for ch in range(8):
        p.dma("sync", wg[:, ch * 8:(ch + 1) * 8], t["f0_wg"][ch * 128:(ch + 1) * 128, :],
              reads=(b["f0_wg"],), writes=(wgB,))
        p.dma("sync", wg[:, 64 + ch:64 + ch + 1], t["f0_bp"][ch * 128:(ch + 1) * 128, :],
              reads=(b["f0_bp"],), writes=(wgB,))
    ps, psB = t["psB"], b["psB"]
    p.memset(gsb[:, 0:80], 0.0, writes=(gsbB,))
    for jch in range(8):
        wpt, wptB = t["wf32s"], b["wf32s"]
        p.dma("sync", wpt[:, 0:1024], t["f0_wpT"][jch * 128:(jch + 1) * 128, :],
              reads=(b["f0_wpT"],), writes=(wptB,))
        for ich in range(8):
            p.mm(ps[:, ich * 8:(ich + 1) * 8], wpt[:, ich * 128:(ich + 1) * 128],
                 wg[:, jch * 8:(jch + 1) * 8], start=True, stop=True,
                 reads=(wptB, wgB), writes=(psB,))
        p.mm(ps[0:1, 64:72], wg[:, 64 + jch:64 + jch + 1], wg[:, jch * 8:(jch + 1) * 8],
             start=True, stop=True, reads=(wgB,), writes=(psB,))
        p.tt(gsb[:, 0:64], gsb[:, 0:64], ps[:, 0:64], ALU.add, reads=(psB,), writes=(gsbB,))
        p.tt(gsb[0:1, 64:72], gsb[0:1, 64:72], ps[0:1, 64:72], ALU.add,
             reads=(psB,), writes=(gsbB,))
    if DEBUG:
        p.dma("gpsimd", t["dbg_G"][:, :], gsb[:, 0:80], reads=(gsbB,), writes=(b["dbg_G"],))
    lg, lgB = t["psS"], b["psS"]
    lgs, lgsB = t["lnsq"], b["lnsq"]
    p.memset(lgs[0:8, 0:T], 0.0, writes=(lgsB,))
    for ich in range(8):
        strip, stripB = t["a2"], b["a2"]
        xa, xaB = t["a1"], b["a1"]
        p.dma("sync", xa[:, 0:2048].rearrange("p (tk q) -> p tk q", tk=16),
              t["xn32_d"][0:T, ich * 128:(ich + 1) * 128].rearrange(
                  "(tk p) q -> p tk q", p=128),
              reads=(b["xn32_d"],), writes=(xaB,))
        for tt in range(NT):
            pt, ptB = t["psB"], b["psB"]
            p.tr(pt[:, 0:128], xa[:, tt * 128:(tt + 1) * 128], t["identf"][:, :],
                 reads=(xaB, b["identf"]), writes=(ptB,))
            p.copy(strip[:, tt * 128:(tt + 1) * 128], pt[:, 0:128], reads=(ptB,), writes=(stripB,))
        if DEBUG and ich == 7:
            p.dma("gpsimd", t["dbg_strip"][:, :], strip[:, 0:2048],
                  reads=(stripB,), writes=(b["dbg_strip"],))
        for jt in range(4):
            p.mm(lg[0:8, jt * 512:(jt + 1) * 512], gsb[:, ich * 8:(ich + 1) * 8],
                 strip[:, jt * 512:(jt + 1) * 512], start=True, stop=True,
                 reads=(gsbB, stripB), writes=(lgB,))
        p.tt(lgs[0:8, 0:T], lgs[0:8, 0:T], lg[0:8, 0:T], ALU.add, reads=(lgB,), writes=(lgsB,))
    for jt in range(4):
        p.mm(lg[0:8, jt * 512:(jt + 1) * 512], gsb[0:1, 64:72],
             t["onesr"][0:1, 0:512], start=True, stop=True,
             reads=(gsbB, b["onesr"]), writes=(lgB,))
    p.tt(lgs[0:8, 0:T], lgs[0:8, 0:T], lg[0:8, 0:T], ALU.add, reads=(lgB,), writes=(lgsB,))
    lt, ltB = t["ltok"], b["ltok"]
    for tt in range(NT):
        pt, ptB = t["psB"], b["psB"]
        p.tr(pt[0:128, 0:8], lgs[0:8, tt * 128:(tt + 1) * 128], t["identf"][0:8, 0:8],
             reads=(lgsB, b["identf"]), writes=(ptB,))
        p.copy(lt[:, tt * 8:(tt + 1) * 8], pt[0:128, 0:8], reads=(ptB,), writes=(ltB,))

    _ffn0_route(c, p, CAP)

    _ffn0_expert(c, p, CAP)

    _ffn0_out(c, p, CAP)


def _ffn0_route(c, p, CAP):
    t, b = c.t, c.b
    lt, ltB = t["ltok"], b["ltok"]
    m8, m8B = t["m8"], b["m8"]
    est, estB = t["est"], b["est"]
    zz8, zz8B = t["zz8"], b["zz8"]
    p.memset(zz8[:, 0:8], 0.0, writes=(zz8B,))
    cps, cpsB = t["psB"], b["psB"]
    for tt in range(NT):
        lg8 = lt[:, tt * 8:(tt + 1) * 8]
        mx, mxB = t["sc8"], b["sc8"]
        p.op("vector", lambda e, mx=mx, lg8=lg8: e.max(mx[:, 0:8], lg8), reads=(ltB,), writes=(mxB,))
        eq, eqB = t["sc8b"], b["sc8b"]
        p.ts(eq[:, 0:8], lg8, mx[:, 0:1], None, ALU.is_equal, reads=(ltB, mxB), writes=(eqB,))
        pf, pfB = t["sc8c"], b["sc8c"]
        p.op("vector", lambda e, pf=pf, eq=eq, zz8=zz8: e.tensor_tensor_scan(
            pf[:, 0:8], eq[:, 0:8], zz8[:, 0:8], 0.0, ALU.add, ALU.add),
            reads=(eqB, zz8B), writes=(pfB,))
        p.ts(pf[:, 0:8], pf[:, 0:8], 1.0, None, ALU.is_equal, reads=(pfB,), writes=(pfB,))
        msk = m8[:, tt * 8:(tt + 1) * 8]
        p.tt(msk, pf[:, 0:8], eq[:, 0:8], ALU.mult, reads=(pfB, eqB), writes=(m8B,))
        p.tt(eq[:, 0:8], msk, t["iota8"][:, 0:8], ALU.mult, reads=(m8B, b["iota8"]), writes=(eqB,))
        p.red(est[:, tt:tt + 1], eq[:, 0:8], ALU.add, reads=(eqB,), writes=(estB,))
        p.mm(cps[0:1, 0:8], t["onesc"][:, 0:1], msk, start=(tt == 0), stop=(tt == NT - 1),
             reads=(b["onesc"], m8B), writes=(cpsB,))
    cnt, cntB = t["sc8"], b["sc8"]
    p.copy(cnt[0:1, 0:8], cps[0:1, 0:8], reads=(cpsB,), writes=(cntB,))
    p.dma("gpsimd", t["cnt_d"][0:1, :], cnt[0:1, 0:8], reads=(cntB,), writes=(b["cnt_d"],))
    p.coll("AllGather", ALU.bypass, ALL8, t["cnt_d"].ap(), t["cntall_d"].ap(),
           reads=(b["cnt_d"],), writes=(b["cntall_d"],))
    call, callB = t["sc8b"], b["sc8b"]
    p.dma("sync", call[0:8, 0:8], t["cntall_d"][:, :], reads=(b["cntall_d"],), writes=(callB,))
    ps, psB = t["psB"], b["psB"]
    p.mm(ps[0:1, 0:8], t["premask"][0:8, 0:1], call[0:8, 0:8], start=True, stop=True,
         reads=(callB, b["premask"]), writes=(psB,))
    offs, offsB = t["sc8c"], b["sc8c"]
    p.copy(offs[0:1, 0:8], ps[0:1, 0:8], reads=(psB,), writes=(offsB,))
    neg, negB = t["a1"], b["a1"]
    p.memset(neg[:, 0:256], -1.0, writes=(negB,))
    p.dma("gpsimd", t["idxsc_d"].ap().rearrange("(p f) one -> p (f one)", p=128),
          neg[:, 0:256], reads=(negB,), writes=(b["idxsc_d"],))
    for tt in range(NT):
        msk = t["m8"][:, tt * 8:(tt + 1) * 8]
        rps, rpsB = t["psB"], b["psB"]
        p.mm(rps[:, 0:8], t["cum"][:, :], msk, start=True, stop=False,
             reads=(b["cum"], b["m8"]), writes=(rpsB,))
        p.mm(rps[:, 0:8], t["onesr"][0:1, 0:128], offs[0:1, 0:8], start=False, stop=True,
             reads=(b["onesr"], offsB), writes=(rpsB,))
        rr, rrB = t["sc8"], b["sc8"]
        p.tt(rr[:, 0:8], rps[:, 0:8], msk, ALU.mult, reads=(rpsB, b["m8"]), writes=(rrB,))
        p.red(t["est"][:, NT + tt:NT + tt + 1], rr[:, 0:8], ALU.add, reads=(rrB,), writes=(b["est"],))
        tsp, tspB = t["psB"], b["psB"]
        p.mm(tsp[0:1, 8:16], t["onesc"][:, 0:1], msk, start=True, stop=True,
             reads=(b["onesc"], b["m8"]), writes=(tspB,))
        p.tt(offs[0:1, 0:8], offs[0:1, 0:8], tsp[0:1, 8:16], ALU.add,
             reads=(tspB,), writes=(offsB,))
    for tt in range(NT):
        pos, posB = t["sc8"], b["sc8"]
        rk = t["est"][:, NT + tt:NT + tt + 1]
        p.ts(pos[:, 0:1], t["est"][:, tt:tt + 1], float(CAP), None, ALU.mult,
             reads=(b["est"],), writes=(posB,))
        p.tt(pos[:, 0:1], pos[:, 0:1], rk, ALU.add, reads=(b["est"],), writes=(posB,))
        p.ts(pos[:, 1:2], rk, float(CAP), None, ALU.is_ge, reads=(b["est"],), writes=(posB,))
        p.ts(pos[:, 1:2], pos[:, 1:2], 1e9, None, ALU.mult, reads=(posB,), writes=(posB,))
        p.tt(pos[:, 0:1], pos[:, 0:1], pos[:, 1:2], ALU.add, reads=(posB,), writes=(posB,))
        pi, piB = t["idxm"], b["idxm"]
        p.copy(pi[:, 32 + tt:33 + tt], pos[:, 0:1], reads=(posB,), writes=(piB,))
        p.dma("sync", t["tid_sb"][:, 0:1],
              t["tok_ids"][tt * 128:(tt + 1) * 128, :], reads=(b["tok_ids"],), writes=(b["tid_sb"],))
        p.iscatter(t["idxsc_d"].ap(), pi[:, 32 + tt:33 + tt], t["tid_sb"][:, 0:1], 32767,
                   reads=(piB, b["tid_sb"], b["idxsc_d"]), writes=(b["idxsc_d"],))
    p.coll("AllReduce", ALU.max, ALL8, t["idxsc_d"].ap(), t["idxall_d"].ap(),
           reads=(b["idxsc_d"],), writes=(b["idxall_d"],))
    im, imB = t["idxm"], b["idxm"]
    tmp32, tmp32B = t["g32t"], b["g32t"]
    p.igather(tmp32[0:32, 0:128],
              t["idxall_d"].ap().rearrange("(r c) one -> r (c one)", c=128),
              t["rowidx_sb"][0:32, 0:1], 255,
              reads=(b["idxall_d"], b["rowidx_sb"]), writes=(tmp32B,))
    pt, ptB = t["psB"], b["psB"]
    p.tr(pt[0:128, 0:32], tmp32[0:32, 0:128], t["identf"][0:32, 0:32],
         reads=(tmp32B, b["identf"]), writes=(ptB,))
    imf, imfB = t["a1"], b["a1"]
    p.copy(imf[:, 0:32], pt[0:128, 0:32], reads=(ptB,), writes=(imfB,))
    p.ts(imf[:, 32:64], imf[:, 0:32], 0.0, None, ALU.is_lt, reads=(imfB,), writes=(imfB,))
    p.ts(imf[:, 32:64], imf[:, 32:64], 2e9, None, ALU.mult, reads=(imfB,), writes=(imfB,))
    p.tt(imf[:, 0:32], imf[:, 0:32], imf[:, 32:64], ALU.add, reads=(imfB,), writes=(imfB,))
    p.copy(im[:, 0:32], imf[:, 0:32], reads=(imfB,), writes=(imB,))


def _ffn0_expert(c, p, CAP):
    t, b = c.t, c.b
    g1, g1B = t["g1bc"], b["g1bc"]
    b1, b1B = t["b1bc"], b["b1bc"]
    for name, dst, dstB in (("f0_ln1g", g1, g1B), ("f0_ln1b", b1, b1B)):
        rsb, rsbB = t["lnsq"], b["lnsq"]
        p.dma("sync", rsb[0:1, 0:4096], t[name][0:1, :], reads=(b[name],), writes=(rsbB,))
        for j in range(8):
            ps, psB = t["psB"], b["psB"]
            p.mm(ps[:, 0:512], t["onesr"][0:1, 0:128], rsb[0:1, j * 512:(j + 1) * 512],
                 start=True, stop=True, reads=(rsbB, b["onesr"]), writes=(psB,))
            p.copy(dst[:, j * 512:(j + 1) * 512], ps[:, 0:512], reads=(psB,), writes=(dstB,))
    for half in range(2):
        w1, w1B = t["wbig"], b["wbig"]
        srcw, srcwB = c.tb("f0_w1_aug")
        for i in range(8):
            p.dma("gpsimd", w1[:, i * 2048:(i + 1) * 2048],
                  srcw[i * 128:(i + 1) * 128, half * 2048:(half + 1) * 2048],
                  reads=(srcwB,), writes=(w1B,))
        p.dma("gpsimd", w1[0:1, 8 * 2048:9 * 2048],
              srcw[1024:1025, half * 2048:(half + 1) * 2048], reads=(srcwB,), writes=(w1B,))
        for sc in range(32):
            tok, tokB = t["w16a"], b["w16a"]
            p.memset(tok[:, 0:1024], 0.0, writes=(tokB,))
            p.igather(tok[:, 0:1024], t["xflatall_d"].ap(), t["idxm"][:, sc:sc + 1], 16383,
                      reads=(b["xflatall_d"], b["idxm"]), writes=(tokB,))
            tfm, tfmB = t["w16b"], b["w16b"]
            for ch in range(8):
                pt, ptB = t["psT"], b["psT"]
                p.tr(pt[:, 0:128], tok[:, ch * 128:(ch + 1) * 128], t["ident"][:, :],
                     reads=(tokB, b["ident"]), writes=(ptB,))
                p.copy(tfm[:, ch * 128:(ch + 1) * 128], pt[:, 0:128], reads=(ptB,), writes=(tfmB,))
            ps, psB = t["psS"], b["psS"]
            for j in range(4):
                for ch in range(8):
                    p.mm(ps[:, j * 512:(j + 1) * 512], tfm[:, ch * 128:(ch + 1) * 128],
                         w1[:, ch * 2048 + j * 512: ch * 2048 + (j + 1) * 512],
                         start=(ch == 0), stop=False, reads=(tfmB, w1B), writes=(psB,))
                p.mm(ps[:, j * 512:(j + 1) * 512], t["onesrb"][0:1, 0:128],
                     w1[0:1, 8 * 2048 + j * 512: 8 * 2048 + (j + 1) * 512],
                     start=False, stop=True, reads=(b["onesrb"], w1B), writes=(psB,))
            hh, hhB = t["a2"], b["a2"]
            p.copy(hh[:, 0:2048], ps[:, 0:2048], reads=(psB,), writes=(hhB,))
            p.dma("gpsimd", t["hh_d"][sc * 128:(sc + 1) * 128, half * 2048:(half + 1) * 2048],
                  hh[:, 0:2048], reads=(hhB,), writes=(b["hh_d"],))
    for sc in range(32):
        hacc, haccB = t["a1"], b["a1"]
        p.dma("sync", hacc[:, 0:4096], t["hh_d"][sc * 128:(sc + 1) * 128, :],
              reads=(b["hh_d"],), writes=(haccB,))
        emit_ln_stats(c, p, hacc[:, 0:4096], (haccB,), 4096, 1e-5)
        st, stB = t["stat"], b["stat"]
        gel, gelB = t["w16a"], b["w16a"]
        p.ts(gel[:, 0:4096], hacc[:, 0:4096], st[:, 0:1], st[:, 3:4], ALU.subtract, ALU.mult,
             reads=(haccB, stB), writes=(gelB,))
        p.tt(gel[:, 0:4096], gel[:, 0:4096], g1[:, 0:4096], ALU.mult,
             reads=(g1B,), writes=(gelB,))
        p.tt(gel[:, 0:4096], gel[:, 0:4096], b1[:, 0:4096], ALU.add,
             reads=(b1B,), writes=(gelB,))
        p.act(gel[:, 0:4096], gel[:, 0:4096], AF.Gelu, reads=(gelB,), writes=(gelB,))
        p.dma("gpsimd", t["gel_d"][sc * 128:(sc + 1) * 128, :], gel[:, 0:4096],
              reads=(gelB,), writes=(b["gel_d"],))
    w2, w2B = t["wbig"], b["wbig"]
    load_wchunks(c, p, w2, w2B, "f0_w2_aug", 4097, 1024)
    for sc in range(32):
        gel, gelB = t["w16a"], b["w16a"]
        p.dma("sync", gel[:, 0:4096], t["gel_d"][sc * 128:(sc + 1) * 128, :],
              reads=(b["gel_d"],), writes=(gelB,))
        gfm, gfmB = t["w16b"], b["w16b"]
        for ch in range(32):
            pt, ptB = t["psT"], b["psT"]
            p.tr(pt[:, 0:128], gel[:, ch * 128:(ch + 1) * 128], t["ident"][:, :],
                 reads=(gelB, b["ident"]), writes=(ptB,))
            p.copy(gfm[:, ch * 128:(ch + 1) * 128], pt[:, 0:128], reads=(ptB,), writes=(gfmB,))
        stg, stgB = t["w16c"], b["w16c"]
        for j2 in range(2):
            ps, psB = t["psB"], b["psB"]
            for ch in range(32):
                p.mm(ps[:, 0:512], gfm[:, ch * 128:(ch + 1) * 128],
                     w2[:, ch * 1024 + j2 * 512: ch * 1024 + (j2 + 1) * 512],
                     start=(ch == 0), stop=False, reads=(gfmB, w2B), writes=(psB,))
            p.mm(ps[:, 0:512], t["onesrb"][0:1, 0:128],
                 w2[0:1, 32 * 1024 + j2 * 512: 32 * 1024 + (j2 + 1) * 512],
                 start=False, stop=True, reads=(b["onesrb"], w2B), writes=(psB,))
            p.copy(stg[:, j2 * 512:(j2 + 1) * 512], ps[:, 0:512], reads=(psB,), writes=(stgB,))
        p.dma("gpsimd", t["slots_d"][sc * 128:(sc + 1) * 128, :], stg[:, 0:D],
              reads=(stgB,), writes=(b["slots_d"],))
    p.coll("AllGather", ALU.bypass, ALL8, t["slots_d"].ap(), t["slotsall_d"].ap(),
           reads=(b["slots_d"],), writes=(b["slotsall_d"],))


def _ffn0_out(c, p, CAP):
    t, b = c.t, c.b
    gg, ggB = t["g1bc"], b["g1bc"]
    bb2, bb2B = t["b1bc"], b["b1bc"]
    for name, dst, dstB in (("f0_lng", gg, ggB), ("f0_lnb", bb2, bb2B)):
        rsb, rsbB = t["lnsq"], b["lnsq"]
        p.dma("sync", rsb[0:1, 0:1024], t[name][0:1, :], reads=(b[name],), writes=(rsbB,))
        for j in range(2):
            ps, psB = t["psB"], b["psB"]
            p.mm(ps[:, 0:512], t["onesr"][0:1, 0:128], rsb[0:1, j * 512:(j + 1) * 512],
                 start=True, stop=True, reads=(rsbB, b["onesr"]), writes=(psB,))
            p.copy(dst[:, j * 512:(j + 1) * 512], ps[:, 0:512], reads=(psB,), writes=(dstB,))
    pix, pixB = t["idxm"], b["idxm"]
    for tt in range(NT):
        og, ogB = t["w16a"], b["w16a"]
        p.memset(og[:, 0:1024], 0.0, writes=(ogB,))
        p.igather(og[:, 0:1024], t["slotsall_d"].ap(), pix[:, 32 + tt:33 + tt], 8 * CAP - 1,
                  reads=(b["slotsall_d"], pixB), writes=(ogB,))
        ofl, oflB = t["a1"], b["a1"]
        p.copy(ofl[:, 0:D], og[:, 0:D], reads=(ogB,), writes=(oflB,))
        emit_ln_stats(c, p, ofl[:, 0:D], (oflB,), D, 1e-5)
        st, stB = t["stat"], b["stat"]
        p.ts(ofl[:, 0:D], ofl[:, 0:D], st[:, 0:1], st[:, 3:4], ALU.subtract, ALU.mult,
             reads=(stB,), writes=(oflB,))
        p.tt(ofl[:, 0:D], ofl[:, 0:D], gg[:, 0:D], ALU.mult, reads=(ggB,), writes=(oflB,))
        p.tt(ofl[:, 0:D], ofl[:, 0:D], bb2[:, 0:D], ALU.add, reads=(bb2B,), writes=(oflB,))
        p.dma("gpsimd", t["g_d"][tt * 128:(tt + 1) * 128, :], ofl[:, 0:D],
              reads=(oflB,), writes=(b["g_d"],))


def _quantize_absmean(c, p, src_name, dst_name, rows, cols):
    t, b = c.t, c.b
    nch = rows // 128
    acc, accB = t["a3"], b["a3"]
    p.memset(acc[:, 0:1], 0.0, writes=(accB,))
    for i in range(nch):
        xa, xaB = t["a1"], b["a1"]
        p.dma("sync", xa[:, 0:cols], t[src_name][i * 128:(i + 1) * 128, :],
              reads=(b[src_name],), writes=(xaB,))
        p.op("vector", lambda e, xa=xa, cols=cols: e.tensor_reduce(
            t["stat"][:, 4:5], xa[:, 0:cols], AX.X, ALU.add, apply_absolute_value=True),
            reads=(xaB,), writes=(b["stat"],))
        p.tt(acc[:, 0:1], acc[:, 0:1], t["stat"][:, 4:5], ALU.add,
             reads=(b["stat"],), writes=(accB,))
    ps, psB = t["psB"], b["psB"]
    p.mm(ps[0:1, 0:1], acc[:, 0:1], t["onesc"][:, 0:1], start=True, stop=True,
         reads=(accB, b["onesc"]), writes=(psB,))
    th, thB = t["stat"], b["stat"]
    p.ts(th[0:1, 5:6], ps[0:1, 0:1], 0.5 / (rows * cols), None, ALU.mult,
         reads=(psB,), writes=(thB,))
    p.ts(th[0:1, 5:6], th[0:1, 5:6], 0.5e-6, None, ALU.add, reads=(thB,), writes=(thB,))
    p.mm(ps[:, 4:5], t["onesr"][0:1, 0:128], th[0:1, 5:6], start=True, stop=True,
         reads=(thB, b["onesr"]), writes=(psB,))
    thb, thbB = t["a3"], b["a3"]
    p.copy(thb[:, 1:2], ps[:, 4:5], reads=(psB,), writes=(thbB,))
    for i in range(nch):
        xa, xaB = t["a1"], b["a1"]
        p.dma("sync", xa[:, 0:cols], t[src_name][i * 128:(i + 1) * 128, :],
              reads=(b[src_name],), writes=(xaB,))
        sg, sgB = t["w16a"], b["w16a"]
        p.act(sg[:, 0:cols], xa[:, 0:cols], AF.Sign, reads=(xaB,), writes=(sgB,))
        aw, awB = t["lnsq"], b["lnsq"]
        p.act(aw[:, 0:cols], xa[:, 0:cols], AF.Abs, reads=(xaB,), writes=(awB,))
        p.ts(aw[:, 0:cols], aw[:, 0:cols], thb[:, 1:2], None, ALU.is_gt,
             reads=(thbB,), writes=(awB,))
        wq, wqB = t["w16b"], b["w16b"]
        p.tt(wq[:, 0:cols], sg[:, 0:cols], aw[:, 0:cols], ALU.mult,
             reads=(sgB, awB), writes=(wqB,))
        p.dma("gpsimd", t[dst_name][i * 128:(i + 1) * 128, :], wq[:, 0:cols],
              reads=(wqB,), writes=(b[dst_name],))


def emit_ffn1(c, p):
    t, b = c.t, c.b
    _quantize_absmean(c, p, "f1_wfcT", "f1_wq1_d", 1024, 4096)
    _quantize_absmean(c, p, "f1_wprojT", "f1_wq2_d", 4096, 1024)
    for j in range(8):
        wsb, wsbB = t["wk1"], b["wk1"]
        for ch in range(8):
            p.dma("gpsimd", wsb[:, ch * 512:(ch + 1) * 512],
                  t["f1_wq1_d"][ch * 128:(ch + 1) * 128, j * 512:(j + 1) * 512],
                  reads=(b["f1_wq1_d"],), writes=(wsbB,))
        for tt in range(NT):
            ps, psB = t["psB"], b["psB"]
            slab, slabB = t["w16b"], b["w16b"]
            p.dma("sync", slab[:, 0:1024].rearrange("p (cc tk) -> p cc tk", cc=8),
                  t["xn_fm_d"][0:1024, tt * 128:(tt + 1) * 128].rearrange(
                      "(cc p) tk -> p cc tk", p=128),
                  reads=(b["xn_fm_d"],), writes=(slabB,))
            for ch in range(8):
                p.mm(ps[:, 0:512], slab[:, ch * 128:(ch + 1) * 128], wsb[:, ch * 512:(ch + 1) * 512],
                     start=(ch == 0), stop=(ch == 7), reads=(wsbB, slabB), writes=(psB,))
            hg, hgB = t["a2"], b["a2"]
            p.act(hg[:, 0:512], ps[:, 0:512], AF.Gelu, reads=(psB,), writes=(hgB,))
            p.dma("gpsimd", t["f1_h_d"][tt * 128:(tt + 1) * 128, j * 512:(j + 1) * 512],
                  hg[:, 0:512], reads=(hgB,), writes=(b["f1_h_d"],))
    for name, lo in (("f1_lnw", 0), ("f1_lnb", 4096)):
        rsb, rsbB = t["lnsq"], b["lnsq"]
        p.dma("sync", rsb[0:1, 0:4096], t[name][0:1, :], reads=(b[name],), writes=(rsbB,))
        for jj in range(8):
            ps, psB = t["psB"], b["psB"]
            p.mm(ps[:, 0:512], t["onesr"][0:1, 0:128], rsb[0:1, jj * 512:(jj + 1) * 512],
                 start=True, stop=True, reads=(rsbB, b["onesr"]), writes=(psB,))
            p.copy(t["wk1"][:, lo + jj * 512:lo + (jj + 1) * 512], ps[:, 0:512],
                   reads=(psB,), writes=(b["wk1"],))
    for tt in range(NT):
        ha, haB = t["a1"], b["a1"]
        p.dma("sync", ha[:, 0:4096], t["f1_h_d"][tt * 128:(tt + 1) * 128, :],
              reads=(b["f1_h_d"],), writes=(haB,))
        emit_ln_stats(c, p, ha[:, 0:4096], (haB,), 4096, 1e-5)
        st_, stB = t["stat"], b["stat"]
        hn, hnB = t["w16a"], b["w16a"]
        p.ts(hn[:, 0:4096], ha[:, 0:4096], st_[:, 0:1], st_[:, 3:4], ALU.subtract, ALU.mult,
             reads=(haB, stB), writes=(hnB,))
        p.tt(hn[:, 0:4096], hn[:, 0:4096], t["wk1"][:, 0:4096], ALU.mult,
             reads=(b["wk1"],), writes=(hnB,))
        p.tt(hn[:, 0:4096], hn[:, 0:4096], t["wk1"][:, 4096:8192], ALU.add,
             reads=(b["wk1"],), writes=(hnB,))
        o2, o2B = t["w16b"], b["w16b"]
        for ch in range(32):
            pt, ptB = t["psT"], b["psT"]
            p.tr(pt[:, 0:128], hn[:, ch * 128:(ch + 1) * 128], t["ident"][:, :],
                 reads=(hnB, b["ident"]), writes=(ptB,))
            p.copy(o2[:, ch * 128:(ch + 1) * 128], pt[:, 0:128], reads=(ptB,), writes=(o2B,))
        for ch in range(32):
            p.dma("gpsimd", t["f1_hnf_d"][ch * 128:(ch + 1) * 128, tt * 128:(tt + 1) * 128],
                  o2[:, ch * 128:(ch + 1) * 128], reads=(o2B,), writes=(b["f1_hnf_d"],))
    for j2 in range(2):
        wsb, wsbB = t["wbig1"], b["wbig1"]
        for ch in range(32):
            p.dma("gpsimd", wsb[:, ch * 512:(ch + 1) * 512],
                  t["f1_wq2_d"][ch * 128:(ch + 1) * 128, j2 * 512:(j2 + 1) * 512],
                  reads=(b["f1_wq2_d"],), writes=(wsbB,))
        for tt in range(NT):
            ps, psB = t["psB"], b["psB"]
            slab, slabB = t["w16a"], b["w16a"]
            p.dma("sync", slab[:, 0:4096].rearrange("p (cc tk) -> p cc tk", cc=32),
                  t["f1_hnf_d"][0:4096, tt * 128:(tt + 1) * 128].rearrange(
                      "(cc p) tk -> p cc tk", p=128),
                  reads=(b["f1_hnf_d"],), writes=(slabB,))
            for ch in range(32):
                p.mm(ps[:, 0:512], slab[:, ch * 128:(ch + 1) * 128], wsb[:, ch * 512:(ch + 1) * 512],
                     start=(ch == 0), stop=(ch == 31), reads=(wsbB, slabB), writes=(psB,))
            go, goB = t["a2"], b["a2"]
            p.copy(go[:, 0:512], ps[:, 0:512], reads=(psB,), writes=(goB,))
            p.dma("gpsimd", t["g_d"][tt * 128:(tt + 1) * 128, j2 * 512:(j2 + 1) * 512],
                  go[:, 0:512], reads=(goB,), writes=(b["g_d"],))


HJ2 = [512, 512, 512, 512, 512, 256]


def emit_ffn2(c, p):
    t, b = c.t, c.b
    NTOK = 8 * T
    p.coll("AllGather", ALU.bypass, ALL8, t["xn_fm_d"].ap(), t["xnfm_all_d"].ap(),
           reads=(b["xn_fm_d"],), writes=(b["xnfm_all_d"],))

    wg, wgB = t["wg32"], b["wg32"]
    for ch in range(8):
        p.dma("sync", wg[:, ch * 7:(ch + 1) * 7], t["f2_wr"][ch * 128:(ch + 1) * 128, :],
              reads=(b["f2_wr"],), writes=(wgB,))
    lgs, lgsB = t["lnsq"], b["lnsq"]
    p.memset(lgs[0:7, 0:T], 0.0, writes=(lgsB,))
    lg, lgB = t["psS"], b["psS"]
    for ich in range(8):
        strip, stripB = t["a2"], b["a2"]
        xa, xaB = t["a1"], b["a1"]
        p.dma("sync", xa[:, 0:2048].rearrange("p (tk q) -> p tk q", tk=16),
              t["xn32_d"][0:T, ich * 128:(ich + 1) * 128].rearrange(
                  "(tk p) q -> p tk q", p=128),
              reads=(b["xn32_d"],), writes=(xaB,))
        for tt in range(NT):
            pt, ptB = t["psB"], b["psB"]
            p.tr(pt[:, 0:128], xa[:, tt * 128:(tt + 1) * 128], t["identf"][:, :],
                 reads=(xaB, b["identf"]), writes=(ptB,))
            p.copy(strip[:, tt * 128:(tt + 1) * 128], pt[:, 0:128], reads=(ptB,), writes=(stripB,))
        for jt in range(4):
            p.mm(lg[0:7, jt * 512:(jt + 1) * 512], wg[:, ich * 7:(ich + 1) * 7],
                 strip[:, jt * 512:(jt + 1) * 512], start=True, stop=True,
                 reads=(wgB, stripB), writes=(lgB,))
        p.tt(lgs[0:7, 0:T], lgs[0:7, 0:T], lg[0:7, 0:T], ALU.add, reads=(lgB,), writes=(lgsB,))
    rbc, rbcB = t["sc8"], b["sc8"]
    p.dma("sync", rbc[0:7, 0:1], t["f2_rb"][:, :], reads=(b["f2_rb"],), writes=(rbcB,))
    p.ts(lgs[0:7, 0:T], lgs[0:7, 0:T], rbc[0:7, 0:1], None, ALU.mult,
         reads=(rbcB,), writes=(lgsB,))
    p.act(lgs[0:7, 0:T], lgs[0:7, 0:T], AF.Sigmoid, reads=(lgsB,), writes=(lgsB,))
    zz8, zz8B = t["zz8"], b["zz8"]
    p.memset(zz8[:, 0:8], 0.0, writes=(zz8B,))
    for tt in range(NT):
        pt, ptB = t["psB"], b["psB"]
        p.tr(pt[0:128, 0:7], lgs[0:7, tt * 128:(tt + 1) * 128], t["identf"][0:7, 0:7],
             reads=(lgsB, b["identf"]), writes=(ptB,))
        pr, prB = t["ltok"], b["ltok"]
        p.memset(pr[:, tt * 8 + 7:tt * 8 + 8], -1e30, writes=(prB,))
        p.copy(pr[:, tt * 8:tt * 8 + 7], pt[0:128, 0:7], reads=(ptB,), writes=(prB,))
        lg8 = pr[:, tt * 8:(tt + 1) * 8]
        mx, mxB = t["sc8"], b["sc8"]
        p.op("vector", lambda e, mx=mx, lg8=lg8: e.max(mx[:, 0:8], lg8), reads=(prB,), writes=(mxB,))
        eq, eqB = t["sc8b"], b["sc8b"]
        pf, pfB = t["sc8c"], b["sc8c"]
        m1_, m1B = t["m8"], b["m8"]
        p.ts(eq[:, 0:8], lg8, mx[:, 0:1], None, ALU.is_equal, reads=(prB, mxB), writes=(eqB,))
        p.op("vector", lambda e, pf=pf, eq=eq, zz8=zz8: e.tensor_tensor_scan(
            pf[:, 0:8], eq[:, 0:8], zz8[:, 0:8], 0.0, ALU.add, ALU.add),
            reads=(eqB, zz8B), writes=(pfB,))
        msk1 = m1_[:, tt * 8:(tt + 1) * 8]
        p.ts(msk1, pf[:, 0:8], 1.0, None, ALU.is_equal, reads=(pfB,), writes=(m1B,))
        p.tt(msk1, msk1, eq[:, 0:8], ALU.mult, reads=(eqB,), writes=(m1B,))
        tie, tieB = t["stat"], b["stat"]
        p.ts(tie[:, 6:7], mx[:, 1:2], None, None, ALU.bypass, reads=(mxB,), writes=(tieB,)) \
            if False else None
        p.tt(tie[:, 6:7], mx[:, 0:1], mx[:, 1:2], ALU.is_equal, reads=(mxB,), writes=(tieB,))
        mt, mtB = t["est"], b["est"]
        p.ts(mt[:, 0:8], pf[:, 0:8], 2.0, None, ALU.is_equal, reads=(pfB,), writes=(mtB,))
        p.tt(mt[:, 0:8], mt[:, 0:8], eq[:, 0:8], ALU.mult, reads=(eqB,), writes=(mtB,))
        p.ts(eq[:, 0:8], lg8, mx[:, 1:2], None, ALU.is_equal, reads=(prB, mxB), writes=(eqB,))
        p.op("vector", lambda e, pf=pf, eq=eq, zz8=zz8: e.tensor_tensor_scan(
            pf[:, 0:8], eq[:, 0:8], zz8[:, 0:8], 0.0, ALU.add, ALU.add),
            reads=(eqB, zz8B), writes=(pfB,))
        p.ts(pf[:, 0:8], pf[:, 0:8], 1.0, None, ALU.is_equal, reads=(pfB,), writes=(pfB,))
        p.tt(pf[:, 0:8], pf[:, 0:8], eq[:, 0:8], ALU.mult, reads=(eqB,), writes=(pfB,))
        p.ts(mt[:, 0:8], mt[:, 0:8], tie[:, 6:7], None, ALU.mult, reads=(tieB,), writes=(mtB,))
        p.ts(tie[:, 7:8], tie[:, 6:7], -1.0, None, ALU.mult, ALU.bypass, reads=(tieB,), writes=(tieB,)) \
            if False else None
        p.ts(tie[:, 7:8], tie[:, 6:7], -1.0, None, ALU.mult, reads=(tieB,), writes=(tieB,))
        p.ts(tie[:, 7:8], tie[:, 7:8], 1.0, None, ALU.add, reads=(tieB,), writes=(tieB,))
        p.ts(pf[:, 0:8], pf[:, 0:8], tie[:, 7:8], None, ALU.mult, reads=(tieB,), writes=(pfB,))
        p.tt(mt[:, 0:8], mt[:, 0:8], pf[:, 0:8], ALU.add, reads=(pfB,), writes=(mtB,))
        p.tt(tie[:, 4:5], mx[:, 0:1], mx[:, 1:2], ALU.add, reads=(mxB,), writes=(tieB,))
        p.op("vector", lambda e, tie=tie: e.reciprocal(tie[:, 5:6], tie[:, 4:5]),
             reads=(tieB,), writes=(tieB,))
        p.tt(tie[:, 4:5], mx[:, 0:1], tie[:, 5:6], ALU.mult, reads=(mxB,), writes=(tieB,))
        p.tt(tie[:, 5:6], mx[:, 1:2], tie[:, 5:6], ALU.mult, reads=(mxB, tieB), writes=(tieB,))
        p.ts(msk1, msk1, tie[:, 4:5], None, ALU.mult, reads=(tieB,), writes=(m1B,))
        p.ts(mt[:, 0:8], mt[:, 0:8], tie[:, 5:6], None, ALU.mult, reads=(tieB,), writes=(mtB,))
        ww, wwB = t["sc8"], b["sc8"]
        p.tt(ww[:, 0:8], msk1, mt[:, 0:8], ALU.add, reads=(m1B, mtB), writes=(wwB,))
        p.memset(ww[:, 7:8], 1.0, writes=(wwB,))
        p.dma("gpsimd", t["wx_d"][tt * 128:(tt + 1) * 128, :], ww[:, 0:8],
              reads=(wwB,), writes=(b["wx_d"],))
    p.coll("AllGather", ALU.bypass, ALL8, t["wx_d"].ap(), t["wall_d"].ap(),
           reads=(b["wx_d"],), writes=(b["wall_d"],))
    selb, selbB = t["iota8"], b["iota8"]
    rsb, rsbB = t["lnsq"], b["lnsq"]
    p.dma("sync", rsb[0:1, 0:8], t["f2_sel"][0:1, :], reads=(b["f2_sel"],), writes=(rsbB,))
    ps, psB = t["psB"], b["psB"]
    p.mm(ps[:, 0:8], t["onesr"][0:1, 0:128], rsb[0:1, 0:8], start=True, stop=True,
         reads=(rsbB, b["onesr"]), writes=(psB,))
    p.copy(selb[:, 0:8], ps[:, 0:8], reads=(psB,), writes=(selbB,))

    col0 = 0
    for j, wj in enumerate(HJ2):
        wsb, wsbB = t["wk1"], b["wk1"]
        wub, wubB = t["wbig1"], b["wbig1"]
        for ch in range(8):
            p.dma("gpsimd", wsb[:, ch * wj:(ch + 1) * wj],
                  t["f2_wg_w"][ch * 128:(ch + 1) * 128, col0:col0 + wj],
                  reads=(b["f2_wg_w"],), writes=(wsbB,))
            p.dma("gpsimd", wub[:, ch * wj:(ch + 1) * wj],
                  t["f2_wu_w"][ch * 128:(ch + 1) * 128, col0:col0 + wj],
                  reads=(b["f2_wu_w"],), writes=(wubB,))
        for tg in range(NTOK // 128):
            rr, tt2 = tg // 16, tg % 16
            xfn = "xq8a" if tg % 2 == 0 else "xq8b"
            xq, xqB = t[xfn], b[xfn]
            p.dma("sync", xq[:, 0:1024].rearrange("p (c tk) -> p c tk", c=8),
                  t["xnfm_all_d"][rr * 1025: rr * 1025 + 1024,
                                  tt2 * 128:(tt2 + 1) * 128].rearrange(
                      "(c p) tk -> p c tk", p=128),
                  reads=(b["xnfm_all_d"],), writes=(xqB,))
            psg, psgB = c.psbank(0)
            psu, psuB = c.psbank(1)
            for ch in range(8):
                p.mm(psg[:, 0:wj], xq[:, ch * 128:(ch + 1) * 128], wsb[:, ch * wj:(ch + 1) * wj],
                     start=(ch == 0), stop=(ch == 7),
                     reads=(wsbB, xqB, b["psS"]), writes=(psgB,))
            for ch in range(8):
                p.mm(psu[:, 0:wj], xq[:, ch * 128:(ch + 1) * 128], wub[:, ch * wj:(ch + 1) * wj],
                     start=(ch == 0), stop=(ch == 7),
                     reads=(wubB, xqB, b["psS"]), writes=(psuB,))
            sg, sgB = t["w16a"], b["w16a"]
            p.act(sg[:, 0:wj], psg[:, 0:wj], AF.Silu, reads=(psgB,), writes=(sgB,))
            p.tt(sg[:, 1024:1024 + wj], sg[:, 0:wj], psu[:, 0:wj], ALU.mult,
                 reads=(psuB, sgB), writes=(sgB,))
            p.dma("gpsimd", t["f2_mid_d"][tg * 128:(tg + 1) * 128, col0:col0 + wj],
                  sg[:, 1024:1024 + wj], reads=(sgB,), writes=(b["f2_mid_d"],))
        col0 += wj

    for tg in range(NTOK // 128):
        md, mdB = t["w16a"], b["w16a"]
        p.dma("sync", md[:, 0:2816], t["f2_mid_d"][tg * 128:(tg + 1) * 128, :],
              reads=(b["f2_mid_d"],), writes=(mdB,))
        o2, o2B = t["w16b"], b["w16b"]
        for ch in range(22):
            pt, ptB = t["psT"], b["psT"]
            p.tr(pt[:, 0:128], md[:, ch * 128:(ch + 1) * 128], t["ident"][:, :],
                 reads=(mdB, b["ident"]), writes=(ptB,))
            p.copy(o2[:, ch * 128:(ch + 1) * 128], pt[:, 0:128], reads=(ptB,), writes=(o2B,))
        for ch in range(22):
            p.dma("gpsimd", t["f2_midf_d"][ch * 128:(ch + 1) * 128, tg * 128:(tg + 1) * 128],
                  o2[:, ch * 128:(ch + 1) * 128], reads=(o2B,), writes=(b["f2_midf_d"],))

    for j2 in range(2):
        wsb, wsbB = t["wbig1"], b["wbig1"]
        for ch in range(22):
            p.dma("gpsimd", wsb[:, ch * 512:(ch + 1) * 512],
                  t["f2_wd_w"][ch * 128:(ch + 1) * 128, j2 * 512:(j2 + 1) * 512],
                  reads=(b["f2_wd_w"],), writes=(wsbB,))
        for tg in range(NTOK // 128):
            ps, psB = t["psB"], b["psB"]
            xfn = "xq8a" if tg % 2 == 0 else "xq8b"
            xq, xqB = t[xfn], b[xfn]
            p.dma("sync", xq[:, 0:22 * 128].rearrange("p (c tk) -> p c tk", c=22),
                  t["f2_midf_d"][0:2816, tg * 128:(tg + 1) * 128].rearrange(
                      "(c p) tk -> p c tk", p=128),
                  reads=(b["f2_midf_d"],), writes=(xqB,))
            for ch in range(22):
                p.mm(ps[:, 0:512], xq[:, ch * 128:(ch + 1) * 128], wsb[:, ch * 512:(ch + 1) * 512],
                     start=(ch == 0), stop=(ch == 21), reads=(wsbB, xqB), writes=(psB,))
            wv8, wv8B = t["sc8"], b["sc8"]
            p.dma("sync", wv8[:, 0:8], t["wall_d"][tg * 128:(tg + 1) * 128, :],
                  reads=(b["wall_d"],), writes=(wv8B,))
            p.tt(wv8[:, 0:8], wv8[:, 0:8], t["iota8"][:, 0:8], ALU.mult,
                 reads=(b["iota8"],), writes=(wv8B,))
            p.red(t["stat"][:, 6:7], wv8[:, 0:8], ALU.add, reads=(wv8B,), writes=(b["stat"],))
            oo, ooB = t["a2"], b["a2"]
            p.ts(oo[:, 0:512], ps[:, 0:512], t["stat"][:, 6:7], None, ALU.mult,
                 reads=(psB, b["stat"]), writes=(ooB,))
            p.dma("gpsimd", t["f2_out_d"][tg * 128:(tg + 1) * 128, j2 * 512:(j2 + 1) * 512],
                  oo[:, 0:512], reads=(ooB,), writes=(b["f2_out_d"],))
    p.coll("ReduceScatter", ALU.add, ALL8, t["f2_out_d"].ap(), t["g_d"].ap(),
           reads=(b["f2_out_d"],), writes=(b["g_d"],))


def emit_ffn3(c, p):
    t, b = c.t, c.b
    NP = 2048
    wph, wphB = t["wbig1"], b["wbig1"]
    load_wchunks(c, p, wph, wphB, "f3_phi", 1024, NP)
    nrm, nrmB = t["row4"], b["row4"]
    acc, accB = t["psS"], b["psS"]
    sq, sqB = t["lnsq"], b["lnsq"]
    lacc, laccB = t["a2"], b["a2"]
    p.memset(lacc[0:1, 0:NP], 0.0, writes=(laccB,))
    for ch in range(8):
        p.act(sq[:, 0:NP], wph[:, ch * NP:(ch + 1) * NP], AF.Square, reads=(wphB,), writes=(sqB,))
        for jt in range(4):
            p.mm(acc[0:1, jt * 512:(jt + 1) * 512], t["onesc"][:, 0:1],
                 sq[:, jt * 512:(jt + 1) * 512], start=True, stop=True,
                 reads=(sqB, b["onesc"]), writes=(accB,))
        p.tt(lacc[0:1, 0:NP], lacc[0:1, 0:NP], acc[0:1, 0:NP], ALU.add,
             reads=(accB,), writes=(laccB,))
    p.red(nrm[0:1, 0:1], lacc[0:1, 0:NP], ALU.max, reads=(laccB,), writes=(nrmB,))
    p.act(nrm[0:1, 0:1], nrm[0:1, 0:1], AF.Sqrt, reads=(nrmB,), writes=(nrmB,))
    p.ts(nrm[0:1, 0:1], nrm[0:1, 0:1], 1.0001, None, ALU.mult, reads=(nrmB,), writes=(nrmB,))
    ps, psB = t["psB"], b["psB"]
    p.mm(ps[:, 16:17], t["onesr"][0:1, 0:128], nrm[0:1, 0:1], start=True, stop=True,
         reads=(nrmB, b["onesr"]), writes=(psB,))
    phm, phmB = t["stat"], b["stat"]
    p.copy(phm[:, 6:7], ps[:, 16:17], reads=(psB,), writes=(phmB,))
    for tt in range(NT):
        ps2, ps2B = t["psS"], b["psS"]
        slab, slabB = t["w16b"], b["w16b"]
        p.dma("sync", slab[:, 0:1024].rearrange("p (cc tk) -> p cc tk", cc=8),
              t["xn_fm_d"][0:1024, tt * 128:(tt + 1) * 128].rearrange(
                  "(cc p) tk -> p cc tk", p=128),
              reads=(b["xn_fm_d"],), writes=(slabB,))
        for jt in range(4):
            for ch in range(8):
                p.mm(ps2[:, jt * 512:(jt + 1) * 512], slab[:, ch * 128:(ch + 1) * 128],
                     wph[:, ch * NP + jt * 512: ch * NP + (jt + 1) * 512],
                     start=(ch == 0), stop=(ch == 7), reads=(wphB, slabB), writes=(ps2B,))
        la, laB = t["a1"], b["a1"]
        p.copy(la[:, 0:NP], ps2[:, 0:NP], reads=(ps2B,), writes=(laB,))
        p.dma("gpsimd", t["l_d"][tt * 128:(tt + 1) * 128, :], la[:, 0:NP],
              reads=(laB,), writes=(b["l_d"],))
        xa, xaB = t["a2"], b["a2"]
        p.dma("sync", xa[:, 0:D], t["xn32_d"][tt * 128:(tt + 1) * 128, :],
              reads=(b["xn32_d"],), writes=(xaB,))
        p.act(sq[:, 0:D], xa[:, 0:D], AF.Square, reads=(xaB,), writes=(sqB,))
        p.red(t["stat"][:, 4:5], sq[:, 0:D], ALU.add, reads=(sqB,), writes=(b["stat"],))
        p.act(t["stat"][:, 4:5], t["stat"][:, 4:5], AF.Sqrt, reads=(b["stat"],), writes=(b["stat"],))
        p.tt(t["stat"][:, 5:6], t["stat"][:, 4:5], phm[:, 6:7], ALU.mult,
             reads=(phmB,), writes=(b["stat"],))
        p.dma("gpsimd", t["bm_d"][tt * 128:(tt + 1) * 128, :], t["stat"][:, 5:6],
              reads=(b["stat"],), writes=(b["bm_d"],))
    p.coll("AllReduce", ALU.max, GRP4, t["bm_d"].ap(), t["bmax_d"].ap(),
           reads=(b["bm_d"],), writes=(b["bmax_d"],))
    for tt in range(NT):
        la, laB = t["a1"], b["a1"]
        p.dma("sync", la[:, 0:NP], t["l_d"][tt * 128:(tt + 1) * 128, :],
              reads=(b["l_d"],), writes=(laB,))
        bm, bmB = t["stat"], b["stat"]
        p.dma("sync", bm[:, 4:5], t["bmax_d"][tt * 128:(tt + 1) * 128, :],
              reads=(b["bmax_d"],), writes=(bmB,))
        p.ts(bm[:, 5:6], bm[:, 4:5], -1.0, None, ALU.mult, reads=(bmB,), writes=(bmB,))
        ea, eaB = t["a2"], b["a2"]
        p.act(ea[:, 0:NP], la[:, 0:NP], AF.Exp, bias=bm[:, 5:6], reads=(laB, bmB), writes=(eaB,))
        p.dma("gpsimd", t["e_d"][tt * 128:(tt + 1) * 128, :], ea[:, 0:NP],
              reads=(eaB,), writes=(b["e_d"],))
    p.coll("AllReduce", ALU.add, GRP4, t["e_d"].ap(), t["esum_d"].ap(),
           reads=(b["e_d"],), writes=(b["esum_d"],))
    for tt in range(NT):
        ea, eaB = t["a1"], b["a1"]
        p.dma("sync", ea[:, 0:NP], t["e_d"][tt * 128:(tt + 1) * 128, :],
              reads=(b["e_d"],), writes=(eaB,))
        es, esB = t["a2"], b["a2"]
        p.dma("sync", es[:, 0:NP], t["esum_d"][tt * 128:(tt + 1) * 128, :],
              reads=(b["esum_d"],), writes=(esB,))
        p.op("vector", lambda e, es=es: e.reciprocal(es[:, 0:NP], es[:, 0:NP]),
             reads=(esB,), writes=(esB,))
        dd, ddB = t["w16a"], b["w16a"]
        p.tt(dd[:, 0:NP], ea[:, 0:NP], es[:, 0:NP], ALU.mult, reads=(eaB, esB), writes=(ddB,))
        p.dma("gpsimd", t["d_d"][tt * 128:(tt + 1) * 128, :], dd[:, 0:NP],
              reads=(ddB,), writes=(b["d_d"],))
    for ch in range(16):
        for jt in range(2):
            ps3, ps3B = t["psB"], b["psB"]
            for tt in range(NT):
                df, dfB = t["w16c"], b["w16c"]
                p.dma("sync", df[0:128, 0:128],
                      t["d_d"][tt * 128:(tt + 1) * 128, ch * 128:(ch + 1) * 128],
                      reads=(b["d_d"],), writes=(dfB,))
                xf, xfB = t["xf"], b["xf"]
                p.dma("sync", xf[0:128, 0:512],
                      t["xn_tok_d"][tt * 128:(tt + 1) * 128, jt * 512:(jt + 1) * 512],
                      reads=(b["xn_tok_d"],), writes=(xfB,))
                p.mm(ps3[:, 0:512], df[0:128, 0:128], xf[0:128, 0:512],
                     start=(tt == 0), stop=(tt == NT - 1), reads=(xfB, dfB), writes=(ps3B,))
            so, soB = t["a2"], b["a2"]
            p.copy(so[:, 0:512], ps3[:, 0:512], reads=(ps3B,), writes=(soB,))
            p.dma("gpsimd", t["slf_d"][ch * 128:(ch + 1) * 128, jt * 512:(jt + 1) * 512],
                  so[:, 0:512], reads=(soB,), writes=(b["slf_d"],))
    p.coll("AllReduce", ALU.add, PAIRS, t["slf_d"].ap(), t["slfsum_d"].ap(),
           reads=(b["slf_d"],), writes=(b["slfsum_d"],))
    p.coll("AllGather", ALU.bypass, GRP4, t["slfsum_d"].ap(), t["slall_d"].ap(),
           reads=(b["slfsum_d"],), writes=(b["slall_d"],))
    for g in range(8):
        ms, msB = t["a1"], b["a1"]
        p.igather(ms[:, 0:1024], t["slall_d"].ap(), t["yidx_sb"][:, 16 + g:17 + g], 8191,
                  reads=(b["slall_d"], b["yidx_sb"]), writes=(msB,))
        mb, mbB = t["w16a"], b["w16a"]
        p.copy(mb[:, 0:1024], ms[:, 0:1024], reads=(msB,), writes=(mbB,))
        for ch in range(8):
            pt, ptB = t["psT"], b["psT"]
            p.tr(pt[:, 0:128], mb[:, ch * 128:(ch + 1) * 128], t["ident"][:, :],
                 reads=(mbB, b["ident"]), writes=(ptB,))
            p.copy(t["wbig1"][:, ch * 1024 + g * 128:ch * 1024 + (g + 1) * 128], pt[:, 0:128],
                   reads=(ptB,), writes=(b["wbig1"],))
    for nloc in range(8):
        wsb, wsbB = t["wk1"], b["wk1"]
        for ch in range(8):
            p.dma("gpsimd", wsb[:, ch * 1024:(ch + 1) * 1024],
                  t["f3_w"][nloc * 1024 + ch * 128: nloc * 1024 + (ch + 1) * 128, :],
                  reads=(b["f3_w"],), writes=(wsbB,))
        brow, browB = t["w16c"], b["w16c"]
        p.dma("gpsimd", brow[0:1, 0:1024], t["f3_b"][nloc:nloc + 1, :],
              reads=(b["f3_b"],), writes=(browB,))
        for bb_ in range(4):
            loc0 = bb_ * 256 + nloc * 32
            for j2 in range(2):
                psy, psyB = t["psB"], b["psB"]
                for ch in range(8):
                    p.mm(psy[0:32, 0:512],
                         t["wbig1"][:, ch * 1024 + loc0: ch * 1024 + loc0 + 32],
                         wsb[:, ch * 1024 + j2 * 512: ch * 1024 + (j2 + 1) * 512],
                         start=(ch == 0), stop=False, reads=(b["wbig1"], wsbB), writes=(psyB,))
                p.mm(psy[0:32, 0:512], t["onesrb"][0:1, 0:32],
                     brow[0:1, j2 * 512:(j2 + 1) * 512], start=False, stop=True,
                     reads=(b["onesrb"], browB), writes=(psyB,))
                yo, yoB = t["w16a"], b["w16a"]
                p.copy(yo[0:32, 0:512], psy[0:32, 0:512], reads=(psyB,), writes=(yoB,))
                p.dma("gpsimd",
                      t["y_loc_d"][bb_ * 256 + nloc * 32: bb_ * 256 + (nloc + 1) * 32,
                                   j2 * 512:(j2 + 1) * 512],
                      yo[0:32, 0:512], reads=(yoB,), writes=(b["y_loc_d"],))
    p.coll("AllGather", ALU.bypass, ALL8, t["y_loc_d"].ap(), t["yall_d"].ap(),
           reads=(b["y_loc_d"],), writes=(b["yall_d"],))
    for j in range(16):
        yb, ybB = t["w16a"], b["w16a"]
        p.igather(yb[:, 0:1024], t["yall_d"].ap(), t["yidx_sb"][:, j:j + 1], 8191,
                  reads=(b["yall_d"], b["yidx_sb"]), writes=(ybB,))
        p.dma("gpsimd", t["yb_d"][j * 128:(j + 1) * 128, :], yb[:, 0:1024],
              reads=(ybB,), writes=(b["yb_d"],))
    for tt in range(NT):
        la, laB = t["a1"], b["a1"]
        p.dma("sync", la[:, 0:NP], t["l_d"][tt * 128:(tt + 1) * 128, :],
              reads=(b["l_d"],), writes=(laB,))
        st_, stB = t["stat"], b["stat"]
        p.red(st_[:, 4:5], la[:, 0:NP], ALU.max, reads=(laB,), writes=(stB,))
        p.ts(st_[:, 5:6], st_[:, 4:5], -1.0, None, ALU.mult, reads=(stB,), writes=(stB,))
        ca, caB = t["a2"], b["a2"]
        p.act(ca[:, 0:NP], la[:, 0:NP], AF.Exp, bias=st_[:, 5:6], reads=(laB, stB), writes=(caB,))
        p.red(st_[:, 6:7], ca[:, 0:NP], ALU.add, reads=(caB,), writes=(stB,))
        p.op("vector", lambda e, st_=st_: e.reciprocal(st_[:, 7:8], st_[:, 6:7]),
             reads=(stB,), writes=(stB,))
        cb, cbB = t["w16a"], b["w16a"]
        p.ts(cb[:, 0:NP], ca[:, 0:NP], st_[:, 7:8], None, ALU.mult, reads=(caB, stB), writes=(cbB,))
        cf, cfB = t["w16b"], b["w16b"]
        for ch in range(16):
            pt, ptB = t["psT"], b["psT"]
            p.tr(pt[:, 0:128], cb[:, ch * 128:(ch + 1) * 128], t["ident"][:, :],
                 reads=(cbB, b["ident"]), writes=(ptB,))
            p.copy(cf[:, ch * 128:(ch + 1) * 128], pt[:, 0:128], reads=(ptB,), writes=(cfB,))
        for j2 in range(2):
            pso, psoB = t["psB"], b["psB"]
            for j in range(16):
                yf, yfB = t["w16c"], b["w16c"]
                p.dma("sync", yf[0:128, 0:512],
                      t["yb_d"][j * 128:(j + 1) * 128, j2 * 512:(j2 + 1) * 512],
                      reads=(b["yb_d"],), writes=(yfB,))
                p.mm(pso[:, 0:512], cf[:, j * 128:(j + 1) * 128], yf[0:128, 0:512],
                     start=(j == 0), stop=(j == 15), reads=(cfB, yfB), writes=(psoB,))
            go, goB = t["a2"], b["a2"]
            p.copy(go[:, 0:512], pso[:, 0:512], reads=(psoB,), writes=(goB,))
            p.dma("gpsimd", t["g_d"][tt * 128:(tt + 1) * 128, j2 * 512:(j2 + 1) * 512],
                  go[:, 0:512], reads=(goB,), writes=(b["g_d"],))


DEBUG = False


def emit_debug_dumps0(c, p):
    t, b = c.t, c.b
    p.dma("gpsimd", t["dbg_xflat"][:, :], t["xflat_d"].ap(), reads=(b["xflat_d"],),
          writes=(b["dbg_xflat"],))
    for tt in range(NT):
        p.dma("gpsimd", t["dbg_lg"][tt * 128:(tt + 1) * 128, :], t["ltok"][:, tt * 8:(tt + 1) * 8],
              reads=(b["ltok"],), writes=(b["dbg_lg"],))
        p.dma("gpsimd", t["dbg_est"][tt * 128:(tt + 1) * 128, 0:1], t["est"][:, tt:tt + 1],
              reads=(b["est"],), writes=(b["dbg_est"],), allow_slow_non_contiguous=True)
        p.dma("gpsimd", t["dbg_est"][tt * 128:(tt + 1) * 128, 1:2], t["est"][:, NT + tt:NT + tt + 1],
              reads=(b["est"],), writes=(b["dbg_est"],), allow_slow_non_contiguous=True)
    p.dma("gpsimd", t["dbg_slots"][:, :], t["slots_d"].ap(), reads=(b["slots_d"],),
          writes=(b["dbg_slots"],))
    p.dma("gpsimd", t["dbg_idx"][:, :], t["idxall_d"].ap(), reads=(b["idxall_d"],),
          writes=(b["dbg_idx"],))


def emit_debug_dumps(c, p):
    t, b = c.t, c.b
    for nm in ("s", "g", "xn32"):
        if "dbg_" + nm not in t:
            continue
        p.dma("gpsimd", t["dbg_" + nm][:, :], t[nm.replace("xn32", "xn32") + "_d"].ap(),
              reads=(b[nm + "_d"] if nm != "xn32" else b["xn32_d"],),
              writes=(b["dbg_" + nm],))


def emit_combine(c, p):
    t, b = c.t, c.b
    if DEBUG:
        emit_debug_dumps(c, p)
        if "dbg_xflat" in c.t:
            emit_debug_dumps0(c, p)
    for tt in range(NT):
        r = slice(tt * 128, (tt + 1) * 128)
        xa, xaB = t["a1"], b["a1"]
        sa, saB = t["a2"], b["a2"]
        ga, gaB = t["a3"], b["a3"]
        p.dma("sync", xa[:, 0:D], t["x_in"][r, :], reads=(b["x_in"],), writes=(xaB,))
        p.dma("sync", sa[:, 0:D], t["s_d"][r, :], reads=(b["s_d"],), writes=(saB,))
        p.dma("sync", ga[:, 0:D], t["g_d"][r, :], reads=(b["g_d"],), writes=(gaB,))
        p.act(ga[:, 0:D], ga[:, 0:D], AF.Silu, reads=(gaB,), writes=(gaB,))
        p.tt(sa[:, 0:D], sa[:, 0:D], ga[:, 0:D], ALU.mult, reads=(gaB,), writes=(saB,))
        p.tt(xa[:, 0:D], xa[:, 0:D], sa[:, 0:D], ALU.add, reads=(saB,), writes=(xaB,))
        p.dma("gpsimd", t["x_out"][r, :], xa[:, 0:D], reads=(xaB,), writes=(b["x_out"],))



def build_block_program(blk):
    nc = bass.Bass("TRN2", target_bir_lowering=False, debug=False)
    es = contextlib.ExitStack()
    p = Pr(nc)
    c = Ctx(nc, es, p)

    c.dram("x_in", [T, D], F32, kind="ExternalInput")
    c.dram("x_out", [T, D], F32, kind="ExternalOutput")
    c.dram("ident_in", [128, 128], F32, kind="ExternalInput")
    c.dram("cum_in", [128, 128], F32, kind="ExternalInput")
    c.dram("xn_tok_d", [T, D], BF16)
    c.dram("xn_fm_d", [1025, T], BF16)
    c.dram("xn32_d", [T, D], F32)
    c.dram("s_d", [T, D], F32)
    c.dram("g_d", [T, D], F32)
    p.fence_bufs = (c.dram("fence_in", [1, 8], F32), c.dram("fence_out", [1, 8], F32))
    if DEBUG:
        KRD = 66 if blk == 1 else 64
        for nm in ("dbg_s", "dbg_g", "dbg_xn32"):
            c.dram(nm, [T, D], F32, kind="ExternalOutput")
        c.dram("dbg_qx", [H * KRD, T], F32, kind="ExternalOutput")
        c.dram("dbg_kx", [H * KRD, T], F32, kind="ExternalOutput")
        c.dram("dbg_vx", [T, 1040], F32, kind="ExternalOutput")
        c.dram("dbg_kg", [2 * H * KRD, T], F32, kind="ExternalOutput")
        c.dram("dbg_vg", [2 * T, 1040], F32, kind="ExternalOutput")
        if blk == 0:
            c.dram("dbg_G", [128, 80], F32, kind="ExternalOutput")
            c.dram("dbg_strip", [128, 2048], F32, kind="ExternalOutput")
            c.dram("dbg_xflat", [T, D], F32, kind="ExternalOutput")
            c.dram("dbg_lg", [T, 8], F32, kind="ExternalOutput")
            c.dram("dbg_est", [T, 2], F32, kind="ExternalOutput")
            c.dram("dbg_slots", [4096, D], F32, kind="ExternalOutput")
            c.dram("dbg_idx", [8 * 4096, 1], F32, kind="ExternalOutput")

    KR = 66 if blk == 1 else 64
    c.dram("qx_d", [H * KR, T], BF16)
    c.dram("kx_d", [H * KR, T], BF16)
    c.dram("vx_d", [T, 1040], BF16)
    c.dram("kg_d", [2 * H * KR, T], BF16)
    c.dram("vg_d", [2 * T, 1040], BF16)
    c.dram("ofm_d", [1024, T], BF16)

    if blk == 0:
        for nm, shp in (("wq_in", [1024, 1024]), ("wk_in", [1024, 1024]),
                        ("wv_in", [1025, 1024]), ("wo_in", [1025, 1024]),
                        ("bq_in", [1024, 1]), ("bk_in", [1024, 1])):
            c.dram(nm, shp, F32, kind="ExternalInput")
    elif blk == 1:
        for nm, shp in (("wq_in", [1025, 1024]), ("wk_in", [1025, 1024]),
                        ("wv_in", [1025, 1024]), ("wo_in", [1025, 1024]),
                        ("f1_wfcT", [1024, 4096]), ("f1_wprojT", [4096, 1024]),
                        ("f1_lnw", [1, 4096]), ("f1_lnb", [1, 4096])):
            c.dram(nm, shp, F32, kind="ExternalInput")
        c.dram("f1_wq1_d", [1024, 4096], BF16)
        c.dram("f1_wq2_d", [4096, 1024], BF16)
        c.dram("f1_h_d", [T, 4096], F32)
        c.dram("f1_hnf_d", [4096, T], BF16)
    elif blk == 2:
        for nm, shp in (("wq_in", [1024, 1024]), ("wk_in", [1024, 1024]),
                        ("wv_in", [1024, 1024]), ("wo_in", [1025, 1024]),
                        ("l2_temp", [1, 16]),
                        ("f2_wr", [1024, 7]), ("f2_rb", [7, 1]), ("f2_sel", [1, 8]),
                        ("f2_wg_w", [1024, 2816]), ("f2_wu_w", [1024, 2816]),
                        ("f2_wd_w", [2816, 1024])):
            c.dram(nm, shp, F32, kind="ExternalInput")
        c.dram("xnfm_all_d", [8 * 1025, T], BF16)
        c.dram("wx_d", [T, 8], F32)
        c.dram("wall_d", [8 * T, 8], F32)
        c.dram("f2_mid_d", [8 * T, 2816], BF16)
        c.dram("f2_midf_d", [2816, 8 * T], BF16)
        c.dram("f2_out_d", [8 * T, 1024], F32)
    elif blk == 3:
        for nm, shp in (("wq_in", [1024, 1024]), ("wk_in", [1024, 1024]),
                        ("wv_in", [1024, 1024]), ("wo_in", [1025, 1024]),
                        ("l3_lns", [1, 1024]), ("l3_lnb", [1, 1024]),
                        ("f3_phi", [1024, 2048]), ("f3_w", [8 * 1024, 1024]),
                        ("f3_b", [8, 1024])):
            c.dram(nm, shp, F32, kind="ExternalInput")
        c.dram("f3_yidx", [128, 32], I32, kind="ExternalInput")
        c.dram("zn_fm_d", [1025, T], BF16)
        c.dram("l_d", [T, 2048], F32)
        c.dram("bm_d", [T, 1], F32)
        c.dram("bmax_d", [T, 1], F32)
        c.dram("e_d", [T, 2048], F32)
        c.dram("esum_d", [T, 2048], F32)
        c.dram("d_d", [T, 2048], BF16)
        c.dram("slf_d", [2048, 1024], F32)
        c.dram("slfsum_d", [2048, 1024], F32)
        c.dram("slall_d", [8192, 1024], F32)
        c.dram("y_loc_d", [1024, 1024], BF16)
        c.dram("yall_d", [8192, 1024], BF16)
        c.dram("yb_d", [T, 1024], BF16)
    if blk == 0:
        for nm, shp in (("f0_wp_aug", [1025, 1024]), ("f0_wpT", [1024, 1024]),
                        ("f0_wg", [1024, 8]), ("f0_bp", [1024, 1]),
                        ("f0_w1_aug", [1025, 4096]), ("f0_w2_aug", [4097, 1024]),
                        ("f0_ln1g", [1, 4096]), ("f0_ln1b", [1, 4096]),
                        ("f0_lng", [1, 1024]), ("f0_lnb", [1, 1024]),
                        ("tok_ids", [T, 1]), ("premask", [8, 1]), ("iota8_in", [1, 8])):
            c.dram(nm, shp, F32, kind="ExternalInput")
        c.dram("rowidx", [32, 1], I32, kind="ExternalInput")
        c.dram("xflat_d", [T, D], BF16)
        c.dram("xflatall_d", [8 * T, D], BF16)
        c.dram("cnt_d", [1, 8], F32)
        c.dram("cntall_d", [8, 8], F32)
        c.dram("idxsc_d", [8 * 4096, 1], F32)
        c.dram("idxall_d", [8 * 4096, 1], F32)
        c.dram("hh_d", [4096, 4096], F32)
        c.dram("gel_d", [4096, 4096], BF16)
        c.dram("slots_d", [4096, D], BF16)
        c.dram("slotsall_d", [8 * 4096, D], BF16)

    c.sb("cst", [128, 1], F32)
    nc.const_aps.register(F32, 1e-5, c.t["cst"][:, 0:1])
    c.sb("ident", [128, 128], BF16)
    c.sb("identf", [128, 128], F32)
    c.sb("cum", [128, 128], F32)
    c.sb("onesr", [1, 512], F32)
    c.sb("onesrb", [1, T], BF16)
    c.sb("onesc", [128, 1], F32)
    c.sb("onescb", [128, 1], BF16)
    c.sb("stat", [128, 8], F32)
    c.sb("lnsq", [128, 4096], F32)
    c.sb("a1", [128, 4096], F32)
    c.sb("a2", [128, 2048], F32)
    c.sb("a3", [128, 1024], F32)
    c.sb("rsum", [1, 256], F32)
    c.sb("w16a", [128, 4096], BF16)
    c.sb("w16b", [128, 4096], BF16)
    c.sb("w16c", [128, 1024], BF16)
    c.sb("wk1", [128, 9216], BF16)
    c.sb("xf", [128, 512], BF16)
    c.sb("kf", [128, 2 * T], BF16)
    c.sb("qh", [128, T], BF16)
    c.sb("vh", [128, NKC * 65], BF16)
    c.sb("pb", [128, NKC * QT], BF16)
    c.sb("ostg", [64, 256], BF16)
    c.sb("bb", [128, H], F32)
    c.sb("row4", [1, 64], F32)
    c.sb("bqk", [128, 16], F32)
    if blk == 0:
        c.sb("gf32", [128, 80], F32)
        c.sb("wf32s", [128, 1024], F32)
        c.sb("wg32", [128, 72], F32)
        c.sb("ltok", [128, NT * 8], F32)
        c.sb("m8", [128, NT * 8], F32)
        c.sb("est", [128, 2 * NT], F32)
        c.sb("zz8", [128, 8], F32)
        c.sb("sc8", [128, 8], F32)
        c.sb("sc8b", [128, 8], F32)
        c.sb("sc8c", [128, 8], F32)
        c.sb("iota8", [128, 8], F32)
        c.sb("idxm", [128, 64], I32)
        c.sb("rowidx_sb", [32, 1], I32)
        c.sb("g32t", [32, 128], F32)
        c.sb("tid_sb", [128, 1], F32)
        c.t["g1bc"] = c.t["wk1"][:, 0:4096]
        c.b["g1bc"] = c.b["wk1"]
        c.t["b1bc"] = c.t["wk1"][:, 4096:8192]
        c.b["b1bc"] = c.b["wk1"]
        c.sb("wbig", [128, 9 * 2048 + 16 * 1024], BF16)
        c.sb("premask_sb", [8, 1], F32)
    if blk == 1:
        c.sb("q2row", [16, T], F32)
        c.sb("wbig1", [128, 32 * 512], BF16)
    if blk == 2:
        c.sb("wg32", [128, 72], F32)
        c.sb("ltok", [128, NT * 8], F32)
        c.sb("m8", [128, NT * 8], F32)
        c.sb("est", [128, 2 * NT], F32)
        c.sb("zz8", [128, 8], F32)
        c.sb("sc8", [128, 8], F32)
        c.sb("sc8b", [128, 8], F32)
        c.sb("sc8c", [128, 8], F32)
        c.sb("iota8", [128, 8], F32)
        c.sb("tempb", [128, 16], F32)
        c.sb("wbig1", [128, 22 * 512], BF16)
    if blk == 3:
        c.sb("zlng", [128, 2048], F32)
        c.sb("wbig1", [128, 8 * 2048], BF16)
        c.sb("yidx_sb", [128, 32], I32)
    c.ps("psS", [128, 2048], F32)
    for _j in range(4):
        c.b["psS%d" % _j] = Buf("psS%d" % _j)
    c.ps("psB", [128, 1024], F32)
    c.ps("psT", [128, 512], BF16)

    emit_prelude(c, p)
    if blk == 0:
        rsb, rsbB = c.t["a3"], c.b["a3"]
        p.dma("sync", rsb[0:1, 0:8], c.t["iota8_in"][0:1, :],
              reads=(c.b["iota8_in"],), writes=(rsbB,))
        ps, psB = c.t["psB"], c.b["psB"]
        p.mm(ps[:, 0:8], c.t["onesr"][0:1, 0:128], rsb[0:1, 0:8], start=True, stop=True,
             reads=(rsbB, c.b["onesr"]), writes=(psB,))
        p.copy(c.t["iota8"][:, 0:8], ps[:, 0:8], reads=(psB,), writes=(c.b["iota8"],))
        p.dma("sync", c.t["premask_sb"][0:8, 0:1], c.t["premask"][:, :],
              reads=(c.b["premask"],), writes=(c.b["premask_sb"],))
        c.t["premask"], c.b["premask"] = c.t["premask_sb"], c.b["premask_sb"]
        p.dma("sync", c.t["rowidx_sb"][0:32, 0:1], c.t["rowidx"][:, :],
              reads=(c.b["rowidx"],), writes=(c.b["rowidx_sb"],))

    if blk == 3:
        p.dma("sync", c.t["yidx_sb"][:, 0:32], c.t["f3_yidx"][:, :],
              reads=(c.b["f3_yidx"],), writes=(c.b["yidx_sb"],))
    emit_ln_phase(c, p)
    if blk == 0:
        emit_attn(c, p, "l0")
        emit_ffn0(c, p)
    elif blk == 1:
        emit_attn(c, p, "l1")
        emit_ffn1(c, p)
    elif blk == 2:
        emit_attn(c, p, "l2")
        emit_ffn2(c, p)
    else:
        emit_attn(c, p, "l3")
        emit_ffn3(c, p)
    emit_combine(c, p)

    with contextlib.ExitStack() as es2:
        sems = {e: es2.enter_context(nc.semaphore("sem_" + e)) for e in ENGINES}
        block = es2.enter_context(nc.Block())
        p.emit(block, sems)
    es.close()
    return nc



def _host_consts(core):
    ident = np.eye(128, dtype=np.float32)
    cum = np.triu(np.ones((128, 128), np.float32), 1)
    tok_ids = (np.arange(T, dtype=np.float32) + core * T).reshape(T, 1)
    premask = (np.arange(8) < core).astype(np.float32).reshape(8, 1)
    iota8 = np.arange(8, dtype=np.float32).reshape(1, 8)
    rowidx = (core * 32 + np.arange(32, dtype=np.int32)).reshape(32, 1)
    return {"ident_in": ident, "cum_in": cum, "tok_ids": tok_ids,
            "premask": premask, "iota8_in": iota8, "rowidx": rowidx}


def _block0_weights(inputs, core):
    f32 = np.float32
    wqkv = np.asarray(inputs["L0_attn_w_qkv"], f32)
    bqkv = np.asarray(inputs["L0_attn_b_qkv"], f32)
    d = {
        "wq_in": np.ascontiguousarray(wqkv[:, 0:1024]),
        "wk_in": np.ascontiguousarray(wqkv[:, 1024:2048]),
        "wv_in": np.ascontiguousarray(
            np.vstack([wqkv[:, 2048:3072], bqkv[2048:3072][None, :]])),
        "wo_in": np.ascontiguousarray(
            np.vstack([np.asarray(inputs["L0_attn_w_proj"], f32),
                       np.asarray(inputs["L0_attn_b_proj"], f32)[None, :]])),
        "bq_in": np.ascontiguousarray(bqkv[0:1024].reshape(1024, 1)),
        "bk_in": np.ascontiguousarray(bqkv[1024:2048].reshape(1024, 1)),
        "f0_wp_aug": np.ascontiguousarray(
            np.vstack([np.asarray(inputs["L0_ffn_W_proj"], f32),
                       np.asarray(inputs["L0_ffn_b_proj"], f32)[None, :]])),
        "f0_wpT": np.ascontiguousarray(np.asarray(inputs["L0_ffn_W_proj"], f32).T),
        "f0_wg": np.ascontiguousarray(np.asarray(inputs["L0_ffn_W_gate"], f32)),
        "f0_bp": np.ascontiguousarray(
            np.asarray(inputs["L0_ffn_b_proj"], f32).reshape(1024, 1)),
        "f0_w1_aug": np.ascontiguousarray(
            np.vstack([np.asarray(inputs["L0_ffn_W1"], f32)[core],
                       np.asarray(inputs["L0_ffn_b1"], f32)[core][None, :]])),
        "f0_w2_aug": np.ascontiguousarray(
            np.vstack([np.asarray(inputs["L0_ffn_W2"], f32)[core],
                       np.asarray(inputs["L0_ffn_b2"], f32)[core][None, :]])),
        "f0_ln1g": np.asarray(inputs["L0_ffn_ln1_g"], f32)[core][None, :].copy(),
        "f0_ln1b": np.asarray(inputs["L0_ffn_ln1_b"], f32)[core][None, :].copy(),
        "f0_lng": np.asarray(inputs["L0_ffn_ln_g"], f32)[None, :].copy(),
        "f0_lnb": np.asarray(inputs["L0_ffn_ln_b"], f32)[None, :].copy(),
    }
    return d


def shard_x(x):
    out = []
    for c in range(NCORE):
        bb, hh = c // 2, c % 2
        out.append(np.ascontiguousarray(np.asarray(x, np.float32)[bb, hh * T:(hh + 1) * T, :]))
    return out


def unshard_x(shards):
    x = np.empty((B_, S_, D), np.float32)
    for c in range(NCORE):
        bb, hh = c // 2, c % 2
        x[bb, hh * T:(hh + 1) * T, :] = shards[c]
    return x


def _block1_weights(inputs, core):
    f32 = np.float32
    def aug(w, bias):
        return np.ascontiguousarray(np.vstack([np.asarray(w, f32).T,
                                               np.asarray(bias, f32)[None, :]]))
    return {
        "wq_in": aug(inputs["L1_attn_wq"], inputs["L1_attn_bq"]),
        "wk_in": aug(inputs["L1_attn_wk"], inputs["L1_attn_bk"]),
        "wv_in": aug(inputs["L1_attn_wv"], inputs["L1_attn_bv"]),
        "wo_in": aug(inputs["L1_attn_wo"], inputs["L1_attn_bo"]),
        "f1_wfcT": np.ascontiguousarray(np.asarray(inputs["L1_ffn_W_fc"], f32).T),
        "f1_wprojT": np.ascontiguousarray(np.asarray(inputs["L1_ffn_W_proj"], f32).T),
        "f1_lnw": np.asarray(inputs["L1_ffn_ln_w"], f32)[None, :].copy(),
        "f1_lnb": np.asarray(inputs["L1_ffn_ln_b"], f32)[None, :].copy(),
    }


def _block2_weights(inputs, core):
    f32 = np.float32
    qkvT = np.ascontiguousarray(np.asarray(inputs["L2_attn_qkv_w"], f32).T)
    if core < 7:
        wg = np.asarray(inputs["L2_ffn_routed_gate"], f32)[core]
        wu = np.asarray(inputs["L2_ffn_routed_up"], f32)[core]
        wd = np.asarray(inputs["L2_ffn_routed_down"], f32)[core]
    else:
        wg = np.asarray(inputs["L2_ffn_shared_gate"], f32)
        wu = np.asarray(inputs["L2_ffn_shared_up"], f32)
        wd = np.asarray(inputs["L2_ffn_shared_down"], f32)
    sel = np.zeros((1, 8), f32)
    sel[0, min(core, 7)] = 1.0
    return {
        "wq_in": np.ascontiguousarray(qkvT[:, 0:1024]),
        "wk_in": np.ascontiguousarray(qkvT[:, 1024:2048]),
        "wv_in": np.ascontiguousarray(qkvT[:, 2048:3072]),
        "wo_in": np.ascontiguousarray(
            np.vstack([np.asarray(inputs["L2_attn_proj_w"], f32).T,
                       np.asarray(inputs["L2_attn_proj_b"], f32)[None, :]])),
        "l2_temp": np.asarray(inputs["L2_attn_temperature"], f32).reshape(1, 16).copy(),
        "f2_wr": np.ascontiguousarray(np.asarray(inputs["L2_ffn_Wr"], f32)),
        "f2_rb": np.asarray(inputs["L2_ffn_routing_bias"], f32).reshape(7, 1).copy(),
        "f2_sel": sel,
        "f2_wg_w": np.ascontiguousarray(wg),
        "f2_wu_w": np.ascontiguousarray(wu),
        "f2_wd_w": np.ascontiguousarray(wd),
    }


def _block3_weights(inputs, core):
    f32 = np.float32
    wqkv = np.asarray(inputs["L3_attn_w_qkv"], f32).reshape(1024, 16, 64, 3)
    bcore = core // 2
    yidx = np.zeros((128, 32), np.int32)
    for j in range(16):
        npidx = j * 128 + np.arange(128)
        n = npidx // 32
        pp = npidx % 32
        r = n // 8
        yidx[:, j] = r * 1024 + bcore * 256 + (n % 8) * 32 + pp
    for g in range(8):
        rloc = g * 128 + np.arange(128)
        bb = rloc // 256
        rem = rloc % 256
        yidx[:, 16 + g] = bb * 2048 + core * 256 + rem
    return {
        "wq_in": np.ascontiguousarray(wqkv[..., 0].reshape(1024, 1024)),
        "wk_in": np.ascontiguousarray(wqkv[..., 1].reshape(1024, 1024)),
        "wv_in": np.ascontiguousarray(wqkv[..., 2].reshape(1024, 1024)),
        "wo_in": np.ascontiguousarray(
            np.vstack([np.asarray(inputs["L3_attn_w_proj"], f32),
                       np.asarray(inputs["L3_attn_b_proj"], f32)[None, :]])),
        "l3_lns": np.asarray(inputs["L3_attn_ln_scale"], f32)[None, :].copy(),
        "l3_lnb": np.asarray(inputs["L3_attn_ln_bias"], f32)[None, :].copy(),
        "f3_phi": np.ascontiguousarray(np.asarray(inputs["L3_ffn_phi"], f32).reshape(1024, 2048)),
        "f3_w": np.ascontiguousarray(
            np.asarray(inputs["L3_ffn_W"], f32)[core * 8:(core + 1) * 8].reshape(8 * 1024, 1024)),
        "f3_b": np.ascontiguousarray(np.asarray(inputs["L3_ffn_b"], f32)[core * 8:(core + 1) * 8]),
        "f3_yidx": yidx,
    }


_BLOCKS = [0, 1, 2, 3]
FUSED = True


ATTN_WNAMES = {
    0: ("wq_in", "wk_in", "wv_in", "wo_in", "bq_in", "bk_in"),
    1: ("wq_in", "wk_in", "wv_in", "wo_in"),
    2: ("wq_in", "wk_in", "wv_in", "wo_in", "l2_temp", "f2_wr", "f2_rb", "f2_sel",
        "f2_wg_w", "f2_wu_w", "f2_wd_w"),
    3: ("wq_in", "wk_in", "wv_in", "wo_in", "l3_lns", "l3_lnb", "f3_phi", "f3_w",
        "f3_b", "f3_yidx"),
}
WSHAPES = {
    0: {"wq_in": [1024, 1024], "wk_in": [1024, 1024], "wv_in": [1025, 1024],
        "wo_in": [1025, 1024], "bq_in": [1024, 1], "bk_in": [1024, 1]},
    1: {"wq_in": [1025, 1024], "wk_in": [1025, 1024], "wv_in": [1025, 1024],
        "wo_in": [1025, 1024]},
    2: {"wq_in": [1024, 1024], "wk_in": [1024, 1024], "wv_in": [1024, 1024],
        "wo_in": [1025, 1024], "l2_temp": [1, 16], "f2_wr": [1024, 7],
        "f2_rb": [7, 1], "f2_sel": [1, 8], "f2_wg_w": [1024, 2816],
        "f2_wu_w": [1024, 2816], "f2_wd_w": [2816, 1024]},
    3: {"wq_in": [1024, 1024], "wk_in": [1024, 1024], "wv_in": [1024, 1024],
        "wo_in": [1025, 1024], "l3_lns": [1, 1024], "l3_lnb": [1, 1024],
        "f3_phi": [1024, 2048], "f3_w": [8 * 1024, 1024], "f3_b": [8, 1024]},
}


def build_full_program():
    nc = bass.Bass("TRN2", target_bir_lowering=False, debug=False)
    es = contextlib.ExitStack()
    p = Pr(nc)
    c = Ctx(nc, es, p)

    c.dram("x_in0", [T, D], F32, kind="ExternalInput")
    c.dram("x_out3", [T, D], F32, kind="ExternalOutput")
    c.dram("xh1", [T, D], F32)
    c.dram("xh2", [T, D], F32)
    c.dram("xh3", [T, D], F32)
    c.dram("ident_in", [128, 128], F32, kind="ExternalInput")
    c.dram("cum_in", [128, 128], F32, kind="ExternalInput")
    c.dram("xn_tok_d", [T, D], BF16)
    c.dram("xn_fm_d", [1025, T], BF16)
    c.dram("xn32_d", [T, D], F32)
    c.dram("s_d", [T, D], F32)
    c.dram("g_d", [T, D], F32)
    p.fence_bufs = (c.dram("fence_in", [1, 8], F32), c.dram("fence_out", [1, 8], F32))

    KRX = 66
    c.dram("qx_d", [H * KRX, T], BF16)
    c.dram("kx_d", [H * KRX, T], BF16)
    c.dram("vx_d", [T, 1040], BF16)
    c.dram("kg_d", [2 * H * KRX, T], BF16)
    c.dram("vg_d", [2 * T, 1040], BF16)
    c.dram("ofm_d", [1024, T], BF16)

    for blk in range(4):
        for nm in ATTN_WNAMES[blk]:
            if nm == "f3_yidx":
                c.dram(f"b{blk}_{nm}", [128, 32], I32, kind="ExternalInput")
            else:
                c.dram(f"b{blk}_{nm}", WSHAPES[blk][nm], F32, kind="ExternalInput")
    for nm, shp in (("f0_wp_aug", [1025, 1024]), ("f0_wpT", [1024, 1024]),
                    ("f0_wg", [1024, 8]), ("f0_bp", [1024, 1]),
                    ("f0_w1_aug", [1025, 4096]), ("f0_w2_aug", [4097, 1024]),
                    ("f0_ln1g", [1, 4096]), ("f0_ln1b", [1, 4096]),
                    ("f0_lng", [1, 1024]), ("f0_lnb", [1, 1024]),
                    ("tok_ids", [T, 1]), ("premask", [8, 1]), ("iota8_in", [1, 8])):
        c.dram(nm, shp, F32, kind="ExternalInput")
    c.dram("rowidx", [32, 1], I32, kind="ExternalInput")
    for nm, shp in (("f1_wfcT", [1024, 4096]), ("f1_wprojT", [4096, 1024]),
                    ("f1_lnw", [1, 4096]), ("f1_lnb", [1, 4096])):
        c.dram(nm, shp, F32, kind="ExternalInput")
    c.dram("xflat_d", [T, D], BF16)
    c.dram("xflatall_d", [8 * T, D], BF16)
    c.dram("cnt_d", [1, 8], F32)
    c.dram("cntall_d", [8, 8], F32)
    c.dram("idxsc_d", [8 * 4096, 1], F32)
    c.dram("idxall_d", [8 * 4096, 1], F32)
    c.dram("hh_d", [4096, 4096], F32)
    c.dram("gel_d", [4096, 4096], BF16)
    c.dram("slots_d", [4096, D], BF16)
    c.dram("slotsall_d", [8 * 4096, D], BF16)
    c.dram("f1_wq1_d", [1024, 4096], BF16)
    c.dram("f1_wq2_d", [4096, 1024], BF16)
    c.dram("f1_h_d", [T, 4096], F32)
    c.dram("f1_hnf_d", [4096, T], BF16)
    c.dram("xnfm_all_d", [8 * 1025, T], BF16)
    c.dram("wx_d", [T, 8], F32)
    c.dram("wall_d", [8 * T, 8], F32)
    c.dram("f2_mid_d", [8 * T, 2816], BF16)
    c.dram("f2_midf_d", [2816, 8 * T], BF16)
    c.dram("f2_out_d", [8 * T, 1024], F32)
    c.dram("zn_fm_d", [1025, T], BF16)
    c.dram("l_d", [T, 2048], F32)
    c.dram("bm_d", [T, 1], F32)
    c.dram("bmax_d", [T, 1], F32)
    c.dram("e_d", [T, 2048], F32)
    c.dram("esum_d", [T, 2048], F32)
    c.dram("d_d", [T, 2048], BF16)
    c.dram("slf_d", [2048, 1024], F32)
    c.dram("slfsum_d", [2048, 1024], F32)
    c.dram("slall_d", [8192, 1024], F32)
    c.dram("y_loc_d", [1024, 1024], BF16)
    c.dram("yall_d", [8192, 1024], BF16)
    c.dram("yb_d", [T, 1024], BF16)

    c.sb("cst", [128, 1], F32)
    nc.const_aps.register(F32, 1e-5, c.t["cst"][:, 0:1])
    c.sb("ident", [128, 128], BF16)
    c.sb("identf", [128, 128], F32)
    c.sb("cum", [128, 128], F32)
    c.sb("onesr", [1, 512], F32)
    c.sb("onesrb", [1, T], BF16)
    c.sb("onesc", [128, 1], F32)
    c.sb("onescb", [128, 1], BF16)
    c.sb("stat", [128, 8], F32)
    c.sb("lnsq", [128, 4096], F32)
    c.sb("a1", [128, 4096], F32)
    c.sb("a2", [128, 2048], F32)
    c.sb("a3", [128, 1024], F32)
    c.sb("rsum", [1, 256], F32)
    c.sb("w16a", [128, 4096], BF16)
    c.sb("w16b", [128, 4096], BF16)
    c.sb("w16c", [128, 1024], BF16)
    c.sb("wk1", [128, 9216], BF16)
    c.sb("xf", [128, 512], BF16)
    c.sb("kf", [128, 2 * T], BF16)
    c.sb("qh", [128, T], BF16)
    c.sb("vh", [128, NKC * 65], BF16)
    c.sb("pb", [128, NKC * QT], BF16)
    c.sb("ostg", [64, 256], BF16)
    c.sb("bb", [128, H], F32)
    c.sb("row4", [1, 64], F32)
    c.sb("bqk", [128, 16], F32)
    c.sb("gf32", [128, 80], F32)
    c.sb("wf32s", [128, 1024], F32)
    c.sb("wg32", [128, 72], F32)
    c.sb("ltok", [128, NT * 8], F32)
    c.sb("m8", [128, NT * 8], F32)
    c.sb("est", [128, 2 * NT], F32)
    c.sb("zz8", [128, 8], F32)
    c.sb("sc8", [128, 8], F32)
    c.sb("sc8b", [128, 8], F32)
    c.sb("sc8c", [128, 8], F32)
    c.sb("iota8", [128, 8], F32)
    c.sb("idxm", [128, 64], I32)
    c.sb("rowidx_sb", [32, 1], I32)
    c.sb("g32t", [32, 128], F32)
    c.sb("tid_sb", [128, 1], F32)
    c.t["g1bc"] = c.t["wk1"][:, 0:4096]
    c.b["g1bc"] = c.b["wk1"]
    c.t["b1bc"] = c.t["wk1"][:, 4096:8192]
    c.b["b1bc"] = c.b["wk1"]
    c.sb("premask_sb", [8, 1], F32)
    c.sb("wbig", [128, 9 * 2048 + 16 * 1024], BF16)
    c.t["wbig1"] = c.t["wbig"][:, 0:16 * 1024]
    c.b["wbig1"] = c.b["wbig"]
    c.t["xq8a"] = c.t["kf"][:, 0:2816]
    c.b["xq8a"] = c.b["kf"]
    c.t["xq8b"] = c.t["pb"][:, 0:2816]
    c.b["xq8b"] = c.b["pb"]
    c.sb("aux8k", [128, 2048], F32)
    c.t["q2row"] = c.t["aux8k"][0:16, 0:T]
    c.b["q2row"] = c.b["aux8k"]
    c.t["zlng"] = c.t["aux8k"]
    c.b["zlng"] = c.b["aux8k"]
    c.sb("tempb", [128, 16], F32)
    c.sb("yidx_sb", [128, 32], I32)
    c.ps("psS", [128, 2048], F32)
    for _j in range(4):
        c.b["psS%d" % _j] = Buf("psS%d" % _j)
    c.ps("psB", [128, 1024], F32)
    c.ps("psT", [128, 512], BF16)

    emit_prelude(c, p)
    rsb, rsbB = c.t["a3"], c.b["a3"]
    p.dma("sync", rsb[0:1, 0:8], c.t["iota8_in"][0:1, :], reads=(c.b["iota8_in"],), writes=(rsbB,))
    ps, psB = c.t["psB"], c.b["psB"]
    p.mm(ps[:, 0:8], c.t["onesr"][0:1, 0:128], rsb[0:1, 0:8], start=True, stop=True,
         reads=(rsbB, c.b["onesr"]), writes=(psB,))
    p.copy(c.t["iota8"][:, 0:8], ps[:, 0:8], reads=(psB,), writes=(c.b["iota8"],))
    p.dma("sync", c.t["premask_sb"][0:8, 0:1], c.t["premask"][:, :],
          reads=(c.b["premask"],), writes=(c.b["premask_sb"],))
    c.t["premask"], c.b["premask"] = c.t["premask_sb"], c.b["premask_sb"]
    p.dma("sync", c.t["rowidx_sb"][0:32, 0:1], c.t["rowidx"][:, :],
          reads=(c.b["rowidx"],), writes=(c.b["rowidx_sb"],))

    xins = ["x_in0", "xh1", "xh2", "xh3"]
    xouts = ["xh1", "xh2", "xh3", "x_out3"]
    variants = ["l0", "l1", "l2", "l3"]
    iota_tok = None
    for blk in range(4):
        c.t["x_in"], c.b["x_in"] = c.t[xins[blk]], c.b[xins[blk]]
        c.t["x_out"], c.b["x_out"] = c.t[xouts[blk]], c.b[xouts[blk]]
        for nm in ATTN_WNAMES[blk]:
            c.t[nm], c.b[nm] = c.t[f"b{blk}_{nm}"], c.b[f"b{blk}_{nm}"]
        if blk == 3:
            p.dma("sync", c.t["yidx_sb"][:, 0:32], c.t["f3_yidx"][:, :],
                  reads=(c.b["f3_yidx"],), writes=(c.b["yidx_sb"],))
        emit_ln_phase(c, p)
        emit_attn(c, p, variants[blk])
        [emit_ffn0, emit_ffn1, emit_ffn2, emit_ffn3][blk](c, p)
        emit_combine(c, p)

    with contextlib.ExitStack() as es2:
        sems = {e: es2.enter_context(nc.semaphore("sem_" + e)) for e in ENGINES}
        block = es2.enter_context(nc.Block())
        p.emit(block, sems)
    es.close()
    return nc


def kernel(**inputs):
    xs = shard_x(inputs["L0_attn_x"])
    if FUSED and _BLOCKS == [0, 1, 2, 3]:
        nc = build_full_program()
        wfns = {0: _block0_weights, 1: _block1_weights, 2: _block2_weights,
                3: _block3_weights}
        in_maps = []
        for core in range(NCORE):
            m = {"x_in0": xs[core]}
            m.update(_host_consts(core))
            for blk in range(4):
                wd = wfns[blk](inputs, core)
                for k, v in wd.items():
                    m[f"b{blk}_{k}" if k in ATTN_WNAMES[blk] else k] = v
            in_maps.append(m)
        res = run_bass_kernel_spmd(nc, in_maps, core_ids=list(range(NCORE)))
        return unshard_x([np.asarray(r["x_out3"], np.float32) for r in res.results])
    for blk in _BLOCKS:
        nc = build_block_program(blk)
        wfn = {0: _block0_weights, 1: _block1_weights, 2: _block2_weights,
               3: _block3_weights}[blk]
        in_maps = []
        for core in range(NCORE):
            m = {"x_in": xs[core]}
            m.update(_host_consts(core))
            m.update(wfn(inputs, core))
            in_maps.append(m)
        res = run_bass_kernel_spmd(nc, in_maps, core_ids=list(range(NCORE)))
        xs = [np.asarray(r["x_out"], np.float32) for r in res.results]
    return unshard_x(xs)
